# version 01
# baseline (speedup 1.0000x reference)
import numpy as np
import ml_dtypes
import concourse.bass as bass
import concourse.mybir as mybir
from concourse.bass_utils import run_bass_kernel_spmd

F32 = mybir.dt.float32
BF16 = mybir.dt.bfloat16
AF = mybir.ActivationFunctionType
OP = mybir.AluOpType
AX = mybir.AxisListType

NCORES = 8
D_MODEL = 2048
D_INNER = 4096
NCH = D_INNER // NCORES
NQ = NCH // 128

BF = ml_dtypes.bfloat16


class Prog:
    ENGINES = ("sync", "tensor", "vector", "scalar", "gpsimd")

    def __init__(self):
        self.ops = {e: [] for e in self.ENGINES}
        self.cnt = {}

    def emit(self, eng, fn, wait=(), inc=None, inc_amt=1):
        tok = None
        if inc is not None:
            self.cnt[inc] = self.cnt.get(inc, 0) + inc_amt
            tok = (inc, self.cnt[inc])
        waits = tuple(w for w in wait if w is not None)
        self.ops[eng].append((fn, waits, inc, inc_amt))
        return tok

    def build(self, nc, block):
        sems = {name: nc.semaphore(name).__enter__() for name in self.cnt}
        for e in self.ENGINES:
            if not self.ops[e]:
                continue
            ops = self.ops[e]

            def mk(ops=ops, e=e):
                def body(eng):
                    wm = {}
                    for fn, waits, inc, amt in ops:
                        for (s, v) in waits:
                            if wm.get(("w", s), 0) < v:
                                eng.wait_ge(sems[s], v)
                                wm[("w", s)] = v
                        instr = fn(eng)
                        if inc is not None:
                            instr.then_inc(sems[inc], amt)
                return body

            getattr(block, e)(mk())


def tmax(*toks):
    toks = [t for t in toks if t is not None]
    if not toks:
        return None
    sems = {t[0] for t in toks}
    assert len(sems) == 1, sems
    return max(toks, key=lambda t: t[1])


def _c(a):
    return np.ascontiguousarray(a)


def host_prep_A(inputs):
    W_in = inputs["L0_attn_W_in"]
    W_dt = inputs["L0_attn_W_dt"]
    b_dt = inputs["L0_attn_b_dt"]
    conv_w = inputs["L0_attn_conv_w"]
    conv_b = inputs["L0_attn_conv_b"]
    A_log = inputs["L0_attn_A_log"]
    D_param = inputs["L0_attn_D_param"]
    x = inputs["L0_attn_hidden_states"][0]
    xT = _c(x.T.astype(np.float32))

    ind_x = np.zeros((128, NQ * 128), np.float32)
    for q in range(NQ):
        for m in range(128):
            ind_x[16 * q + (m % 16), q * 128 + m] = 1.0
    ind_B = np.zeros((128, 64 * 128), np.float32)
    for j in range(64):
        ind_B[64 + j, j * 128:(j + 1) * 128] = 1.0
    ind_C = np.zeros((128, 16 * 128), np.float32)
    for n in range(16):
        for pp in range(128):
            ind_C[(pp // 16) * 16 + n, n * 128 + pp] = 1.0

    per_core = []
    for k in range(NCORES):
        cs = slice(NCH * k, NCH * (k + 1))
        rows = np.concatenate([
            np.arange(cs.start, cs.stop),
            np.arange(4096 + 64 * k, 4096 + 64 * (k + 1)),
            np.arange(4608 + 64 * k, 4608 + 64 * (k + 1)),
            np.arange(5120 + NCH * k, 5120 + NCH * (k + 1)),
            np.arange(9216, 9344),
        ])
        per_core.append(dict(
            xT=xT,
            winT=_c(W_in[rows].T),
            wdtT=_c(W_dt[cs].T),
            bdt=_c(b_dt[cs].reshape(NQ, 128).T),
            alogT=_c(A_log[cs].reshape(NQ, 128, 16).transpose(1, 0, 2).reshape(128, NQ * 16)),
            convw=_c(conv_w[cs, 0, :].reshape(NQ, 128, 4).transpose(1, 0, 2).reshape(128, NQ * 4)),
            convb=_c(conv_b[cs].reshape(NQ, 128).T),
            dpar=_c(D_param[cs].reshape(NQ, 128).T),
            ind_x=ind_x, ind_B=ind_B, ind_C=ind_C,
        ))
    return per_core


def build_A(T=8192, Tc=512, STAGE=5):
    nck = T // Tc
    nc = bass.Bass("TRN2", target_bir_lowering=False, debug=False)
    xT = nc.dram_tensor("xT", [D_MODEL, T], F32, kind="ExternalInput").ap()
    winT = nc.dram_tensor("winT", [D_MODEL, 1280], F32, kind="ExternalInput").ap()
    wdtT = nc.dram_tensor("wdtT", [128, NCH], F32, kind="ExternalInput").ap()
    bdt = nc.dram_tensor("bdt", [128, NQ], F32, kind="ExternalInput").ap()
    alogT = nc.dram_tensor("alogT", [128, NQ * 16], F32, kind="ExternalInput").ap()
    convw = nc.dram_tensor("convw", [128, NQ * 4], F32, kind="ExternalInput").ap()
    convb = nc.dram_tensor("convb", [128, NQ], F32, kind="ExternalInput").ap()
    dpar = nc.dram_tensor("dpar", [128, NQ], F32, kind="ExternalInput").ap()
    ind_x = nc.dram_tensor("ind_x", [128, NQ * 128], F32, kind="ExternalInput").ap()
    ind_B = nc.dram_tensor("ind_B", [128, 64 * 128], F32, kind="ExternalInput").ap()
    ind_C = nc.dram_tensor("ind_C", [128, 16 * 128], F32, kind="ExternalInput").ap()
    yg = nc.dram_tensor("yg", [NCH, T], BF16, kind="ExternalOutput").ap()

    p = Prog()
    import contextlib
    es = contextlib.ExitStack()
    wl = es.enter_context(nc.sbuf_tensor("wl", [128, 16 * 1280], BF16))
    wdt_sb = es.enter_context(nc.sbuf_tensor("wdt_sb", [128, NCH], BF16))
    indx_sb = es.enter_context(nc.sbuf_tensor("indx_sb", [128, NQ * 128], BF16))
    indB_sb = es.enter_context(nc.sbuf_tensor("indB_sb", [128, 64 * 128], BF16))
    indC_sb = es.enter_context(nc.sbuf_tensor("indC_sb", [128, 16 * 128], BF16))
    bdt2 = es.enter_context(nc.sbuf_tensor("bdt2", [128, NQ], F32))
    acol = es.enter_context(nc.sbuf_tensor("acol", [128, NQ * 16], F32))
    cw = es.enter_context(nc.sbuf_tensor("cw", [128, NQ * 4], F32))
    cb = es.enter_context(nc.sbuf_tensor("cb", [128, NQ], F32))
    dp = es.enter_context(nc.sbuf_tensor("dp", [128, NQ], F32))
    xc = es.enter_context(nc.sbuf_tensor("xc", [128, 2, 16 * Tc], BF16))
    zf = es.enter_context(nc.sbuf_tensor("zf", [128, NQ, Tc], F32))
    xBb = es.enter_context(nc.sbuf_tensor("xBb", [128, Tc], BF16))
    Cbuf = es.enter_context(nc.sbuf_tensor("Cbuf", [128, NQ, Tc], BF16))
    dtlb = es.enter_context(nc.sbuf_tensor("dtlb", [128, Tc], BF16))
    delta = es.enter_context(nc.sbuf_tensor("delta", [128, NQ, Tc], F32))
    ubuf = es.enter_context(nc.sbuf_tensor("ubuf", [128, NQ, Tc], F32))
    dubuf = es.enter_context(nc.sbuf_tensor("dubuf", [128, NQ, Tc], F32))
    ybuf = es.enter_context(nc.sbuf_tensor("ybuf", [128, NQ, Tc], F32))
    xrb = es.enter_context(nc.sbuf_tensor("xrb", [128, NQ, Tc + 3], F32))
    cacc = es.enter_context(nc.sbuf_tensor("cacc", [128, 2, Tc], F32))
    sp1 = es.enter_context(nc.sbuf_tensor("sp1", [128, Tc], F32))
    sp2 = es.enter_context(nc.sbuf_tensor("sp2", [128, Tc], F32))
    sp3 = es.enter_context(nc.sbuf_tensor("sp3", [128, Tc], F32))
    dAb = es.enter_context(nc.sbuf_tensor("dAb", [128, 2, Tc], F32))
    d1b = es.enter_context(nc.sbuf_tensor("d1b", [128, 2, Tc], F32))
    snb = es.enter_context(nc.sbuf_tensor("snb", [128, 2, Tc], F32))
    prodb = es.enter_context(nc.sbuf_tensor("prodb", [128, 2, Tc], F32))
    stateb = es.enter_context(nc.sbuf_tensor("stateb", [128, 64], F32))
    szb = es.enter_context(nc.sbuf_tensor("szb", [128, Tc], F32))
    ygb = es.enter_context(nc.sbuf_tensor("ygb", [128, 2, Tc], BF16))
    psmm = es.enter_context(nc.psum_tensor("psmm", [128, 2, 512], F32))
    psx = es.enter_context(nc.psum_tensor("psx", [128, 2, 512], F32))
    psB = es.enter_context(nc.psum_tensor("psB", [128, 2, 512], F32))
    psC = es.enter_context(nc.psum_tensor("psC", [128, 2, 512], F32))
    if True:
        for d in range(16):
            p.emit("gpsimd", lambda g, d=d: g.dma_start(
                out=wl[:, d * 1280:(d + 1) * 1280], in_=winT[128 * d:128 * (d + 1), :]),
                inc="dw", inc_amt=16)
        p.emit("gpsimd", lambda g: g.dma_start(out=wdt_sb[:], in_=wdtT), inc="dw", inc_amt=16)
        p.emit("gpsimd", lambda g: g.dma_start(out=indx_sb[:], in_=ind_x), inc="dw", inc_amt=16)
        p.emit("gpsimd", lambda g: g.dma_start(out=indB_sb[:], in_=ind_B), inc="dw", inc_amt=16)
        t_pre_g = p.emit("gpsimd", lambda g: g.dma_start(out=indC_sb[:], in_=ind_C), inc="dw", inc_amt=16)
        p.emit("sync", lambda s: s.dma_start(out=bdt2[:], in_=bdt), inc="dws", inc_amt=16)
        p.emit("sync", lambda s: s.dma_start(out=acol[:], in_=alogT), inc="dws", inc_amt=16)
        p.emit("sync", lambda s: s.dma_start(out=cw[:], in_=convw), inc="dws", inc_amt=16)
        p.emit("sync", lambda s: s.dma_start(out=cb[:], in_=convb), inc="dws", inc_amt=16)
        t_pre_s = p.emit("sync", lambda s: s.dma_start(out=dp[:], in_=dpar), inc="dws", inc_amt=16)
        t_wdt = t_indx = t_indB = t_indC = t_pre_g
        t_bdt = t_alog = t_cw = t_cb = t_dp = t_pre_s
        t_b2 = p.emit("vector", lambda v: v.tensor_scalar_mul(bdt2[:], bdt2[:], 2.0),
                      wait=[t_bdt], inc="v")
        t_ae = p.emit("scalar", lambda s: s.activation(out=acol[:], in_=acol[:], func=AF.Exp),
                      wait=[t_alog], inc="s")
        t_an = p.emit("vector", lambda v: v.tensor_scalar_mul(acol[:], acol[:], -1.0),
                      wait=[t_ae], inc="v")
        t_st0 = p.emit("vector", lambda v: v.memset(stateb[:], 0.0), inc="v")
        t_hist0 = p.emit("vector", lambda v: v.memset(xrb[:], 0.0), inc="v")

        xload = [None] * (nck + 2)
        xfree = [None] * (nck + 2)

        def load_chunk(k):
            tk = None
            sem = "dx%d" % (k % 2)
            for d in range(16):
                tk = p.emit("gpsimd", lambda g, d=d, k=k: g.dma_start(
                    out=xc[:, k % 2, d * Tc:(d + 1) * Tc],
                    in_=xT[128 * d:128 * (d + 1), k * Tc:(k + 1) * Tc]),
                    wait=[xfree[k - 2] if k >= 2 else None],
                    inc=sem, inc_amt=16)
            return tk

        xload[0] = load_chunk(0)
        if nck > 1:
            xload[1] = load_chunk(1)

        evac_free = [None, None]
        psx_free = [None, None]
        state_w = [None] * 64
        B_rd = [None, None]
        C_rd = [None, None]
        A_rd = [None, None]
        D_rd = [None, None]
        S_rd = [None, None]
        P_rd = [None, None]
        delta_war = [t_hist0] * NQ
        du_war = [None] * NQ
        u_war = [None] * NQ
        z_war = [None] * NQ
        xB_war = None
        C_war = [None] * NQ
        dtl_war = None
        xrb_war = [t_hist0] * NQ
        cacc_free = [None, None]
        sp_free = [t_hist0, None, None]
        szb_free = None
        ygb_free = [None, None]
        y_war = [None] * NQ
        sp1_hold = None

        store_last = [None, None]
        last_store = None
        for k in range(nck):
            mm_tok = [None] * 10
            ev = [None] * 10
            for j in range(10):
                slot = j % 2

                def mmz(t, j=j, slot=slot, k=k):
                    ins = None
                    for d in range(16):
                        ins = t.matmul(psmm[:, slot, :],
                                       wl[:, d * 1280 + j * 128:d * 1280 + (j + 1) * 128],
                                       xc[:, k % 2, d * Tc:(d + 1) * Tc],
                                       start=(d == 0), stop=(d == 15))
                    return ins

                mm_tok[j] = p.emit("tensor", mmz,
                                   wait=[xload[k], evac_free[slot],
                                         t_pre_g if k == 0 and j == 0 else None],
                                   inc="t")
                if j <= 3:
                    dst, war = zf[:, j, :], z_war[j]
                elif j == 4:
                    dst, war = xBb[:], xB_war
                elif j <= 8:
                    dst, war = Cbuf[:, j - 5, :], C_war[j - 5]
                else:
                    dst, war = dtlb[:], dtl_war
                ev[j] = p.emit("scalar",
                               lambda s, dst=dst, slot=slot: s.copy(out=dst, in_=psmm[:, slot, :]),
                               wait=[mm_tok[j], war], inc="s")
                evac_free[slot] = ev[j]
            if k + 2 < nck:
                xfree[k] = mm_tok[9]
                xload[k + 2] = load_chunk(k + 2)

            t_delta = [None] * NQ
            t_u = [None] * NQ
            t_du = [None] * NQ
            for q in range(NQ if STAGE >= 2 else 0):
                slot = q % 2
                mmd = p.emit("tensor", lambda t, q=q, slot=slot: t.matmul(
                    psx[:, slot, :], wdt_sb[:, q * 128:(q + 1) * 128], dtlb[:],
                    start=True, stop=True),
                    wait=[ev[9], t_wdt, psx_free[slot]], inc="t")
                if q == NQ - 1:
                    dtl_war = mmd
                a1 = p.emit("scalar", lambda s, slot=slot, q=q: s.activation(
                    out=sp1[:], in_=psx[:, slot, :], func=AF.Abs, bias=bdt2[:, q:q + 1]),
                    wait=[mmd, t_b2, sp_free[0]], inc="s")
                e1 = p.emit("scalar", lambda s: s.activation(
                    out=sp2[:], in_=sp1[:], func=AF.Exp, scale=-1.0),
                    wait=[a1, sp_free[1]], inc="s")
                l1 = p.emit("scalar", lambda s: s.activation(
                    out=sp2[:], in_=sp2[:], func=AF.Ln, bias=1.0),
                    wait=[e1], inc="s")
                t0 = p.emit("vector", lambda v, slot=slot, q=q: v.tensor_scalar_add(
                    sp3[:], psx[:, slot, :], bdt2[:, q:q + 1]),
                    wait=[mmd, a1, t_b2, sp_free[2]], inc="v")
                t1 = p.emit("vector", lambda v: v.tensor_add(sp3[:], sp3[:], sp1[:]),
                            wait=[t0, a1], inc="v")
                t_delta[q] = p.emit("vector", lambda v, q=q: v.scalar_tensor_tensor(
                    delta[:, q, :], sp3[:], 0.5, sp2[:], OP.mult, OP.add),
                    wait=[t1, l1, delta_war[q]], inc="v")
                sp_free[0] = t1
                sp_free[1] = t_delta[q]
                sp_free[2] = t_delta[q]
                psx_free[slot] = t1

                if STAGE < 3:
                    continue
                mmx = p.emit("tensor", lambda t, q=q, slot=slot: t.matmul(
                    psx[:, slot, :], indx_sb[:, q * 128:(q + 1) * 128], xBb[:],
                    start=True, stop=True),
                    wait=[ev[4], t_indx, psx_free[slot]], inc="t")
                exr = p.emit("vector", lambda v, q=q, slot=slot: v.tensor_copy(
                    xrb[:, q, 3:Tc + 3], psx[:, slot, :]),
                    wait=[mmx, xrb_war[q]], inc="v")
                psx_free[slot] = exr
                c0 = p.emit("vector", lambda v, q=q: v.tensor_scalar(
                    cacc[:, 0, :], xrb[:, q, 0:Tc], cw[:, 4 * q:4 * q + 1], None, OP.mult),
                    wait=[exr, t_cw, cacc_free[0]], inc="v")
                c1 = p.emit("vector", lambda v, q=q: v.scalar_tensor_tensor(
                    cacc[:, 1, :], xrb[:, q, 1:Tc + 1], cw[:, 4 * q + 1:4 * q + 2],
                    cacc[:, 0, :], OP.mult, OP.add),
                    wait=[c0, cacc_free[1]], inc="v")
                c2 = p.emit("vector", lambda v, q=q: v.scalar_tensor_tensor(
                    cacc[:, 0, :], xrb[:, q, 2:Tc + 2], cw[:, 4 * q + 2:4 * q + 3],
                    cacc[:, 1, :], OP.mult, OP.add),
                    wait=[c1], inc="v")
                c3 = p.emit("vector", lambda v, q=q: v.scalar_tensor_tensor(
                    cacc[:, 1, :], xrb[:, q, 3:Tc + 3], cw[:, 4 * q + 3:4 * q + 4],
                    cacc[:, 0, :], OP.mult, OP.add),
                    wait=[c2], inc="v")
                hq = p.emit("vector", lambda v, q=q: v.tensor_copy(
                    xrb[:, q, 0:3], xrb[:, q, Tc:Tc + 3]),
                    wait=[c3], inc="v")
                xrb_war[q] = hq
                t_u[q] = p.emit("scalar", lambda s, q=q: s.activation(
                    out=ubuf[:, q, :], in_=cacc[:, 1, :], func=AF.Silu, bias=cb[:, q:q + 1]),
                    wait=[c3, t_cb, u_war[q]], inc="s")
                cacc_free[0] = c3
                cacc_free[1] = t_u[q]
                t_du[q] = p.emit("vector", lambda v, q=q: v.tensor_mul(
                    dubuf[:, q, :], delta[:, q, :], ubuf[:, q, :]),
                    wait=[t_delta[q], t_u[q], du_war[q]], inc="v")

            tok_yadd = [None] * NQ
            dA_cur = [None] * 64
            for i in range(64 if STAGE >= 4 else 0):
                n, q = divmod(i, NQ)
                j = 16 * q + n
                sl = i % 2
                tB = p.emit("tensor", lambda t, j=j, sl=sl: t.matmul(
                    psB[:, sl, :], indB_sb[:, j * 128:(j + 1) * 128], xBb[:],
                    start=True, stop=True),
                    wait=[ev[4], t_indB, B_rd[sl]], inc="t")
                tC = p.emit("tensor", lambda t, n=n, q=q, sl=sl: t.matmul(
                    psC[:, sl, :], indC_sb[:, n * 128:(n + 1) * 128], Cbuf[:, q, :],
                    start=True, stop=True),
                    wait=[ev[5 + q], t_indC, C_rd[sl]], inc="t")
                if n == 15:
                    C_war[q] = tC
                if i == 63:
                    xB_war = tB
                tA = p.emit("scalar", lambda s, q=q, n=n, sl=sl: s.activation(
                    out=dAb[:, sl, :], in_=delta[:, q, :], func=AF.Exp,
                    scale=acol[:, q * 16 + n:q * 16 + n + 1]),
                    wait=[t_delta[q], t_an, A_rd[sl]], inc="s")
                dA_cur[i] = tA
                if n == 15:
                    delta_war[q] = tA
                td1 = p.emit("vector", lambda v, q=q, sl=sl: v.tensor_mul(
                    d1b[:, sl, :], dubuf[:, q, :], psB[:, sl, :]),
                    wait=[t_du[q], tB, D_rd[sl]], inc="v")
                B_rd[sl] = td1
                if n == 15:
                    du_war[q] = td1
                tscan = p.emit("vector", lambda v, i=i, sl=sl: v.tensor_tensor_scan(
                    snb[:, sl, :], dAb[:, sl, :], d1b[:, sl, :],
                    stateb[:, i:i + 1], OP.mult, OP.add),
                    wait=[tA, td1, state_w[i], S_rd[sl], t_st0], inc="v")
                A_rd[sl] = tscan
                D_rd[sl] = tscan
                state_w[i] = p.emit("vector", lambda v, i=i, sl=sl: v.tensor_copy(
                    stateb[:, i:i + 1], snb[:, sl, Tc - 1:Tc]),
                    wait=[tscan], inc="v")
                if STAGE < 5:
                    S_rd[sl] = state_w[i]
                    C_rd[sl] = tC
                    continue
                if n == 0:
                    tprod = p.emit("vector", lambda v, q=q, sl=sl: v.tensor_mul(
                        ybuf[:, q, :], snb[:, sl, :], psC[:, sl, :]),
                        wait=[tscan, tC, y_war[q]], inc="v")
                    tok_yadd[q] = tprod
                else:
                    tprod = p.emit("vector", lambda v, sl=sl: v.tensor_mul(
                        prodb[:, sl, :], snb[:, sl, :], psC[:, sl, :]),
                        wait=[tscan, tC, P_rd[sl]], inc="v")
                    ty = p.emit("vector", lambda v, q=q, sl=sl: v.tensor_add(
                        ybuf[:, q, :], ybuf[:, q, :], prodb[:, sl, :]),
                        wait=[tprod, tok_yadd[q]], inc="v")
                    tok_yadd[q] = ty
                    P_rd[sl] = ty
                C_rd[sl] = tprod
                S_rd[sl] = tmax(tprod, state_w[i])

            if STAGE < 5:
                for q in range(NQ):
                    sl = q % 2
                    if STAGE >= 3:
                        src, stok = ubuf, t_u[q]
                    elif STAGE >= 2:
                        src, stok = delta, t_delta[q]
                    else:
                        src, stok = zf, ev[q]
                    ygt = p.emit("vector", lambda v, q=q, sl=sl, src=src: v.tensor_copy(
                        ygb[:, sl, :], src[:, q, :]),
                        wait=[stok, ygb_free[sl]], inc="v")
                    if STAGE >= 2:
                        delta_war[q] = tmax(delta_war[q], None) if False else delta_war[q]
                    y_war[q] = ygt
                    if STAGE >= 3:
                        u_war[q] = ygt
                    if STAGE == 2:
                        delta_war[q] = ygt
                    if STAGE < 2:
                        z_war[q] = ygt
                    last_store = p.emit("sync", lambda s, q=q, k=k, sl=sl: s.dma_start(
                        out=yg[128 * q:128 * (q + 1), k * Tc:(k + 1) * Tc], in_=ygb[:, sl, :]),
                        wait=[ygt], inc="ds%d" % sl, inc_amt=16)
                    ygb_free[sl] = last_store
                    store_last[sl] = last_store
                continue
            for q in range(NQ):
                sl = q % 2
                pr = p.emit("vector", lambda v, q=q, sl=sl: v.tensor_scalar(
                    prodb[:, sl, :], ubuf[:, q, :], dp[:, q:q + 1], None, OP.mult),
                    wait=[t_u[q], t_dp, P_rd[sl]], inc="v")
                ya = p.emit("vector", lambda v, q=q, sl=sl: v.tensor_add(
                    ybuf[:, q, :], ybuf[:, q, :], prodb[:, sl, :]),
                    wait=[pr, tok_yadd[q]], inc="v")
                P_rd[sl] = ya
                u_war[q] = pr
                sz = p.emit("scalar", lambda s, q=q: s.activation(
                    out=szb[:], in_=zf[:, q, :], func=AF.Silu),
                    wait=[ev[q], szb_free], inc="s")
                z_war[q] = sz
                ygt = p.emit("vector", lambda v, q=q, sl=sl: v.tensor_mul(
                    ygb[:, sl, :], ybuf[:, q, :], szb[:]),
                    wait=[ya, sz, ygb_free[sl]], inc="v")
                szb_free = ygt
                y_war[q] = ygt
                last_store = p.emit("sync", lambda s, q=q, k=k, sl=sl: s.dma_start(
                    out=yg[128 * q:128 * (q + 1), k * Tc:(k + 1) * Tc], in_=ygb[:, sl, :]),
                    wait=[ygt], inc="ds%d" % sl, inc_amt=16)
                ygb_free[sl] = last_store
                store_last[sl] = last_store

        p.emit("gpsimd", lambda g: None, wait=[store_last[0], store_last[1]])
        with nc.Block() as block:
            p.build(nc, block)
    es.close()
    return nc


def run_A(inputs, T=8192, Tc=512, STAGE=5):
    per_core = host_prep_A(inputs)
    if T != 8192:
        for d in per_core:
            d["xT"] = _c(d["xT"][:, :T])
    nc = build_A(T=T, Tc=Tc, STAGE=STAGE)
    res = run_bass_kernel_spmd(nc, per_core, core_ids=list(range(NCORES)))
    return np.concatenate([r["yg"] for r in res.results], axis=0)


def kernel(**inputs):
    raise NotImplementedError("later launches pending")


def host_prep_B1(inputs, yg_full, t=1024):
    xT = _c(inputs["L0_attn_hidden_states"][0].T.astype(np.float32))
    ident = np.eye(128, dtype=np.float32)
    ones = np.ones((128, 128), np.float32)
    ind8 = np.zeros((8, 8 * 128), np.float32)
    for j in range(8):
        ind8[j, j * 128:(j + 1) * 128] = 1.0
    amp = inputs["L0_ffn_amplitudes"]
    shared = dict(
        woutT=_c(inputs["L0_attn_W_out"].T.astype(np.float32)),
        wrout=_c(inputs["L0_ffn_w_router"].astype(np.float32)),
        wgate=_c(inputs["L0_ffn_w_gate"].astype(np.float32)),
        wup=_c(inputs["L0_ffn_w_up"].astype(np.float32)),
        wdown=_c(inputs["L0_ffn_w_down"].astype(np.float32)),
        wsg=_c(inputs["L0_ffn_shared_w_gate"].astype(np.float32)),
        wsu=_c(inputs["L0_ffn_shared_w_up"].astype(np.float32)),
        wsd=_c(inputs["L0_ffn_shared_w_down"].astype(np.float32)),
        a0=_c(amp[:, :, 0].astype(np.float32)),
        a1=_c(amp[:, :, 1].astype(np.float32)),
        esc=_c(inputs["L0_ffn_expert_scale"].astype(np.float32).reshape(128, 1)),
        ident=ident, ones=ones, ind8=ind8,
    )
    per_core = []
    for k in range(NCORES):
        ts = slice(t * k, t * (k + 1))
        per_core.append(dict(ygT=_c(yg_full[:, ts]), xme=_c(xT[:, ts]), **shared))
    return per_core


def build_B1(t=1024):
    TH = t // 512
    nc = bass.Bass("TRN2", target_bir_lowering=False, debug=False)
    ygT = nc.dram_tensor("ygT", [D_INNER, t], BF16, kind="ExternalInput").ap()
    xme = nc.dram_tensor("xme", [D_MODEL, t], F32, kind="ExternalInput").ap()
    woutT = nc.dram_tensor("woutT", [D_INNER, D_MODEL], F32, kind="ExternalInput").ap()
    wrout = nc.dram_tensor("wrout", [D_MODEL, 128], F32, kind="ExternalInput").ap()
    wgate = nc.dram_tensor("wgate", [D_MODEL, 8192], F32, kind="ExternalInput").ap()
    wup = nc.dram_tensor("wup", [D_MODEL, 8192], F32, kind="ExternalInput").ap()
    wdown = nc.dram_tensor("wdown", [1024, D_MODEL], F32, kind="ExternalInput").ap()
    wsg = nc.dram_tensor("wsg", [D_MODEL, D_MODEL], F32, kind="ExternalInput").ap()
    wsu = nc.dram_tensor("wsu", [D_MODEL, D_MODEL], F32, kind="ExternalInput").ap()
    wsd = nc.dram_tensor("wsd", [D_MODEL, D_MODEL], F32, kind="ExternalInput").ap()
    a0 = nc.dram_tensor("a0", [128, 8], F32, kind="ExternalInput").ap()
    a1 = nc.dram_tensor("a1", [128, 8], F32, kind="ExternalInput").ap()
    esc = nc.dram_tensor("esc", [128, 1], F32, kind="ExternalInput").ap()
    ident = nc.dram_tensor("ident", [128, 128], F32, kind="ExternalInput").ap()
    ones = nc.dram_tensor("ones", [128, 128], F32, kind="ExternalInput").ap()
    ind8 = nc.dram_tensor("ind8", [8, 8 * 128], F32, kind="ExternalInput").ap()
    x2T = nc.dram_tensor("x2T", [D_MODEL, t], F32, kind="ExternalOutput").ap()

    p = Prog()
    import contextlib
    es = contextlib.ExitStack()
    SB = lambda nm, sh, dt: es.enter_context(nc.sbuf_tensor(nm, sh, dt))
    wslab = SB("wslab", [128, 8192], BF16)
    resA = SB("resA", [128, 16, t], F32)
    x1b = SB("x1b", [128, 16, t], BF16)
    bp = SB("bp", [128, 24 * t], BF16)
    ygs = SB("ygs", [128, 2, t], BF16)
    wos = SB("wos", [128, 4, 512], BF16)
    srow = SB("srow", [1, 6, 512], F32)
    lgf = SB("lgf", [128, t], F32)
    lgTt = SB("lgTt", [128, 128], F32)
    Eb = SB("Eb", [128, 128], F32)
    Ea = SB("Ea", [128, 128], F32)
    Ec = SB("Ec", [128, 128], F32)
    Mb = SB("Mb", [128, 128], F32)
    vc = SB("vc", [128, 8], F32)
    cnb = SB("cnb", [128, 128], F32)
    cnf = SB("cnf", [128, t], BF16)
    mixs = SB("mixs", [128, 8], F32)
    mixb = SB("mixb", [128, 8], BF16)
    a1s = SB("a1s", [128, 8], F32)
    esb = SB("esb", [128, 1], F32)
    efb = SB("efb", [8, t], BF16)
    ind8b = SB("ind8b", [8, 8 * 128], BF16)
    idb = SB("idb", [128, 128], F32)
    onesb = SB("onesb", [128, 128], F32)
    wrb = SB("wrb", [128, 16 * 128], BF16)
    stg = SB("stg", [128, 2, 512], F32)
    tb = SB("tb", [128, 2, 512], BF16)
    pall = es.enter_context(nc.psum_tensor("pall", [128, 8, 512], F32))
    ebs = bp[:, 0:8 * t]
    sgv = bp[:, 8 * t:16 * t]
    sgs = bp[:, 0:16 * t]
    comb = bp[:, 16 * t:24 * t]
    sq = lgf[:, 0:512]
    bank_rd = [None] * 8

    p.emit("sync", lambda s: s.dma_start(out=idb[:], in_=ident), inc="dws", inc_amt=16)
    p.emit("sync", lambda s: s.dma_start(out=onesb[:], in_=ones), inc="dws", inc_amt=16)
    t_pre_s = p.emit("sync", lambda s: s.dma_start(out=esb[:], in_=esc), inc="dws", inc_amt=16)
    p.emit("gpsimd", lambda g: g.dma_start(out=ind8b[:], in_=ind8), inc="dw", inc_amt=16)
    p.emit("gpsimd", lambda g: g.dma_start(out=mixs[:], in_=a0), inc="dw", inc_amt=16)
    p.emit("gpsimd", lambda g: g.dma_start(out=a1s[:], in_=a1), inc="dw", inc_amt=16)
    t_pre_g = p.emit("gpsimd", lambda g: g.dma_start(
        out=wrb[:].rearrange("p (a b) -> p a b", a=16),
        in_=wrout.rearrange("(a p) b -> p a b", p=128)), inc="dw", inc_amt=16)

    s0 = p.emit("scalar", lambda s: s.activation(out=Eb[:, 0:8], in_=mixs[:], func=AF.Square),
                wait=[t_pre_g], inc="s")
    s1 = p.emit("scalar", lambda s: s.activation(out=Eb[:, 8:16], in_=a1s[:], func=AF.Square),
                wait=[t_pre_g, s0], inc="s")
    v0 = p.emit("vector", lambda v: v.tensor_add(mixs[:], Eb[:, 0:8], Eb[:, 8:16]), wait=[s1], inc="v")
    v1_ = p.emit("vector", lambda v: v.reduce_sum(vc[:, 0:1], mixs[:], axis=AX.X), wait=[v0], inc="v")
    v2_ = p.emit("vector", lambda v: v.tensor_scalar_add(vc[:, 0:1], vc[:, 0:1], 1e-8), wait=[v1_], inc="v")
    v3_ = p.emit("vector", lambda v: v.reciprocal(vc[:, 1:2], vc[:, 0:1]), wait=[v2_], inc="v")
    v4_ = p.emit("vector", lambda v: v.tensor_scalar(mixs[:], mixs[:], vc[:, 1:2], None, OP.mult),
                 wait=[v3_], inc="v")
    v5_ = p.emit("vector", lambda v: v.tensor_scalar(mixs[:], mixs[:], esb[:], None, OP.mult),
                 wait=[v4_, t_pre_s], inc="v")
    t_mix = p.emit("vector", lambda v: v.tensor_copy(mixb[:], mixs[:]), wait=[v5_], inc="v")

    pre1_tok = {}
    wo_cons = {}
    yg_cons = {}
    stg_rd = [t_pre_s, t_pre_s]
    for jg in range(4):
        for k in range(32):
            slw = k % 4
            gate_w = wo_cons.get((jg, k - 4)) if k >= 4 else wo_cons.get((jg - 1, 28 + k))
            wld = p.emit("gpsimd", lambda g, jg=jg, k=k, slw=slw: g.dma_start(
                out=wos[:, slw, :], in_=woutT[128 * k:128 * (k + 1), 512 * jg:512 * (jg + 1)]),
                wait=[gate_w], inc="dwo%d" % slw, inc_amt=16)
            sly = k % 2
            gate_y = yg_cons.get((jg, k - 2)) if k >= 2 else yg_cons.get((jg - 1, 30 + k))
            yld = p.emit("gpsimd", lambda g, k=k, sly=sly: g.dma_start(
                out=ygs[:, sly, :], in_=ygT[128 * k:128 * (k + 1), :]),
                wait=[gate_y], inc="dyg%d" % sly, inc_amt=16)
            last = None
            for jj in range(4):
                for th in range(TH):
                    b = jj * 2 + th
                    last = p.emit("tensor", lambda tt, k=k, jj=jj, th=th, b=b, slw=slw, sly=sly: tt.matmul(
                        pall[:, b, :], wos[:, slw, 128 * jj:128 * (jj + 1)],
                        ygs[:, sly, 512 * th:512 * (th + 1)],
                        start=(k == 0), stop=(k == 31)),
                        wait=[wld, yld, bank_rd[b] if k == 0 else None], inc="t")
            wo_cons[(jg, k)] = last
            yg_cons[(jg, k)] = last
        for jj in range(4):
            for th in range(TH):
                b = jj * 2 + th
                j = 4 * jg + jj
                sl = (jg * 8 + jj * 2 + th) % 2
                ld = p.emit("sync", lambda s, j=j, th=th, sl=sl: s.dma_start(
                    out=stg[:, sl, :], in_=xme[128 * j:128 * (j + 1), 512 * th:512 * (th + 1)]),
                    wait=[stg_rd[sl]], inc="dst%d" % sl, inc_amt=16)
                tk = p.emit("vector", lambda v, j=j, th=th, b=b, sl=sl: v.tensor_add(
                    resA[:, j, 512 * th:512 * (th + 1)], stg[:, sl, :], pall[:, b, :]),
                    wait=[wo_cons[(jg, 31)], ld], inc="v")
                pre1_tok[(j, th)] = tk
                stg_rd[sl] = tk
                bank_rd[b] = tk

    srow_rd = [None]
    def emit_ln(src_ready, x1b_out):
        apply_toks = {}
        sq_rd = [None]
        for th in range(TH):
            mm1 = mm2 = None
            for j in range(16):
                sqt = p.emit("scalar", lambda s, j=j, th=th: s.activation(
                    out=sq, in_=resA[:, j, 512 * th:512 * (th + 1)], func=AF.Square),
                    wait=[src_ready[(j, th)], sq_rd[0]], inc="s")
                mm1 = p.emit("tensor", lambda tt, j=j, th=th: tt.matmul(
                    pall[0:1, 0, :], onesb[:, 0:1], resA[:, j, 512 * th:512 * (th + 1)],
                    start=(j == 0), stop=(j == 15)),
                    wait=[src_ready[(j, th)], bank_rd[0] if j == 0 else None], inc="t")
                mm2 = p.emit("tensor", lambda tt, j=j: tt.matmul(
                    pall[0:1, 1, :], onesb[:, 0:1], sq,
                    start=(j == 0), stop=(j == 15)),
                    wait=[sqt, bank_rd[1] if j == 0 else None], inc="t")
                sq_rd[0] = mm2
            a = p.emit("vector", lambda v: v.tensor_scalar(
                srow[0:1, 0, :], pall[0:1, 0, :], 1.0 / 2048.0, None, OP.mult),
                wait=[mm1, srow_rd[0]], inc="v")
            bank_rd[0] = a
            b2 = p.emit("vector", lambda v: v.tensor_mul(
                srow[0:1, 2, :], srow[0:1, 0, :], srow[0:1, 0, :]), wait=[a], inc="v")
            c = p.emit("vector", lambda v: v.scalar_tensor_tensor(
                srow[0:1, 3, :], pall[0:1, 1, :], 1.0 / 2048.0, srow[0:1, 2, :],
                OP.mult, OP.subtract),
                wait=[mm2, b2], inc="v")
            bank_rd[1] = c
            c2 = p.emit("vector", lambda v: v.tensor_scalar_add(
                srow[0:1, 3, :], srow[0:1, 3, :], 1e-5), wait=[c], inc="v")
            d = p.emit("scalar", lambda s: s.activation(
                out=srow[0:1, 4, :], in_=srow[0:1, 3, :], func=AF.Sqrt),
                wait=[c2], inc="s")
            e = p.emit("vector", lambda v: v.reciprocal(srow[0:1, 1, :], srow[0:1, 4, :]),
                       wait=[d], inc="v")
            bm = p.emit("tensor", lambda tt: tt.matmul(
                pall[:, 2, :], onesb[0:1, :], srow[0:1, 0, :], start=True, stop=True),
                wait=[a, bank_rd[2]], inc="t")
            br = p.emit("tensor", lambda tt: tt.matmul(
                pall[:, 3, :], onesb[0:1, :], srow[0:1, 1, :], start=True, stop=True),
                wait=[e, bank_rd[3]], inc="t")
            srow_rd[0] = br
            f2 = None
            for j in range(16):
                f1 = p.emit("vector", lambda v, j=j, th=th: v.tensor_sub(
                    resA[:, j, 512 * th:512 * (th + 1)],
                    resA[:, j, 512 * th:512 * (th + 1)], pall[:, 2, :]),
                    wait=[bm], inc="v")
                f2 = p.emit("vector", lambda v, j=j, th=th: v.tensor_mul(
                    resA[:, j, 512 * th:512 * (th + 1)],
                    resA[:, j, 512 * th:512 * (th + 1)], pall[:, 3, :]),
                    wait=[f1, br], inc="v")
                tokj = f2
                if x1b_out is not None:
                    tokj = p.emit("vector", lambda v, j=j, th=th: v.tensor_copy(
                        x1b_out[:, j, 512 * th:512 * (th + 1)],
                        resA[:, j, 512 * th:512 * (th + 1)]),
                        wait=[f2], inc="v")
                apply_toks[(j, th)] = tokj
            bank_rd[2] = f2
            bank_rd[3] = f2
        return apply_toks

    x1_tok = emit_ln(pre1_tok, x1b)

    lg_ev = [None] * TH
    for th in range(TH):
        mm = None
        for d in range(16):
            mm = p.emit("tensor", lambda tt, d=d, th=th: tt.matmul(
                pall[:, 4, :], wrb[:, 128 * d:128 * (d + 1)], x1b[:, d, 512 * th:512 * (th + 1)],
                start=(d == 0), stop=(d == 15)),
                wait=[x1_tok[(d, th)], t_pre_g, bank_rd[4] if d == 0 else None], inc="t")
        lg_ev[th] = p.emit("scalar", lambda s, th=th: s.copy(
            out=lgf[:, 512 * th:512 * (th + 1)], in_=pall[:, 4, :]), wait=[mm], inc="s")
        bank_rd[4] = lg_ev[th]
    cn_done = [None] * (t // 128)
    lgT_rd = [None]
    for tt_ in range(t // 128):
        th = (tt_ * 128) // 512
        tr1 = p.emit("tensor", lambda tt, tt_=tt_: tt.transpose(
            pall[:, 5, 0:128], lgf[:, 128 * tt_:128 * (tt_ + 1)], idb[:]),
            wait=[lg_ev[th], bank_rd[5]], inc="t")
        ev1 = p.emit("vector", lambda v: v.tensor_copy(lgTt[:], pall[:, 5, 0:128]),
                     wait=[tr1, lgT_rd[0]], inc="v")
        bank_rd[5] = ev1
        m0 = p.emit("vector", lambda v: v.reduce_max(vc[:, 0:1], lgTt[:], axis=AX.X, negate=True),
                    wait=[ev1], inc="v")
        ex = p.emit("scalar", lambda s: s.activation(out=Eb[:], in_=lgTt[:], func=AF.Exp,
                                                     bias=vc[:, 0:1]),
                    wait=[m0], inc="s")
        lgT_rd[0] = ex
        sS = p.emit("vector", lambda v: v.reduce_sum(vc[:, 1:2], Eb[:], axis=AX.X), wait=[ex], inc="v")
        vv1 = p.emit("vector", lambda v: v.reduce_max(vc[:, 2:3], Eb[:], axis=AX.X), wait=[ex], inc="v")
        mk1 = p.emit("vector", lambda v: v.tensor_scalar(Mb[:], Eb[:], vc[:, 2:3], None, OP.is_ge),
                     wait=[vv1], inc="v")
        e1 = p.emit("vector", lambda v: v.scalar_tensor_tensor(Ea[:], Mb[:], -1e30, Eb[:], OP.mult, OP.add),
                    wait=[mk1], inc="v")
        vv2 = p.emit("vector", lambda v: v.reduce_max(vc[:, 3:4], Ea[:], axis=AX.X), wait=[e1], inc="v")
        mk2 = p.emit("vector", lambda v: v.tensor_scalar(Mb[:], Ea[:], vc[:, 3:4], None, OP.is_ge),
                     wait=[vv2], inc="v")
        e2 = p.emit("vector", lambda v: v.scalar_tensor_tensor(Ec[:], Mb[:], -1e30, Ea[:], OP.mult, OP.add),
                    wait=[mk2], inc="v")
        vv3 = p.emit("vector", lambda v: v.reduce_max(vc[:, 4:5], Ec[:], axis=AX.X), wait=[e2], inc="v")
        mk3 = p.emit("vector", lambda v: v.tensor_scalar(Mb[:], Ec[:], vc[:, 4:5], None, OP.is_ge),
                     wait=[vv3], inc="v")
        e3 = p.emit("vector", lambda v: v.scalar_tensor_tensor(Ea[:], Mb[:], -1e30, Ec[:], OP.mult, OP.add),
                    wait=[mk3], inc="v")
        vv4 = p.emit("vector", lambda v: v.reduce_max(vc[:, 5:6], Ea[:], axis=AX.X), wait=[e3], inc="v")
        t1 = p.emit("vector", lambda v: v.tensor_add(vc[:, 6:7], vc[:, 2:3], vc[:, 3:4]),
                    wait=[vv2, vv1], inc="v")
        t2 = p.emit("vector", lambda v: v.tensor_add(vc[:, 7:8], vc[:, 6:7], vc[:, 4:5]),
                    wait=[t1, vv3], inc="v")
        t3 = p.emit("vector", lambda v: v.tensor_add(vc[:, 6:7], vc[:, 7:8], vc[:, 5:6]),
                    wait=[t2, vv4], inc="v")
        den = p.emit("vector", lambda v: v.scalar_tensor_tensor(vc[:, 7:8], vc[:, 1:2], 1e-8,
                                                                vc[:, 6:7], OP.mult, OP.add),
                     wait=[t3, sS], inc="v")
        rd = p.emit("vector", lambda v: v.reciprocal(vc[:, 6:7], vc[:, 7:8]), wait=[den], inc="v")
        mk4 = p.emit("vector", lambda v: v.tensor_scalar(Mb[:], Eb[:], vc[:, 5:6], None, OP.is_ge),
                     wait=[vv4], inc="v")
        cm = p.emit("vector", lambda v: v.tensor_mul(cnb[:], Eb[:], Mb[:]), wait=[mk4], inc="v")
        cs = p.emit("vector", lambda v: v.tensor_scalar(cnb[:], cnb[:], vc[:, 6:7], None, OP.mult),
                    wait=[cm, rd], inc="v")
        tr2 = p.emit("tensor", lambda tt, tt_=tt_: tt.transpose(
            pall[:, 5, 0:128], cnb[:], idb[:]), wait=[cs, bank_rd[5]], inc="t")
        ev2 = p.emit("vector", lambda v, tt_=tt_: v.tensor_copy(
            cnf[:, 128 * tt_:128 * (tt_ + 1)], pall[:, 5, 0:128]), wait=[tr2], inc="v")
        bank_rd[5] = ev2
        cn_done[tt_] = ev2
    ef_ev = [None] * TH
    for th in range(TH):
        mm = p.emit("tensor", lambda tt, th=th: tt.matmul(
            pall[0:8, 6, :], mixb[:], cnf[:, 512 * th:512 * (th + 1)], start=True, stop=True),
            wait=[t_mix, bank_rd[6]] + cn_done[4 * th:4 * (th + 1)], inc="t")
        ef_ev[th] = p.emit("vector", lambda v, th=th: v.tensor_copy(
            efb[:, 512 * th:512 * (th + 1)], pall[0:8, 6, :]), wait=[mm], inc="v")
        bank_rd[6] = ef_ev[th]
    eb_tok = {}
    for k8 in range(8):
        for th in range(TH):
            b = 4 + (k8 * TH + th) % 2
            mm = p.emit("tensor", lambda tt, k8=k8, th=th, b=b: tt.matmul(
                pall[:, b, :], ind8b[:, 128 * k8:128 * (k8 + 1)], efb[0:8, 512 * th:512 * (th + 1)],
                start=True, stop=True),
                wait=[ef_ev[th], t_pre_g, bank_rd[b]], inc="t")
            ev = p.emit("scalar", lambda s, k8=k8, th=th, b=b: s.copy(
                out=ebs[:, k8 * t + 512 * th:k8 * t + 512 * (th + 1)], in_=pall[:, b, :]),
                wait=[mm], inc="s")
            bank_rd[b] = ev
            eb_tok[(k8, th)] = ev

    def slab_load(src, cols, gate, n_dma=16, rows=128):
        tk = None
        for d in range(n_dma):
            w = 8192 // n_dma
            tk = p.emit("gpsimd", lambda g, d=d, w=w: g.dma_start(
                out=wslab[:, w * d:w * (d + 1)], in_=src[rows * d:rows * (d + 1), cols]),
                wait=[gate if d == 0 else None], inc="dsl", inc_amt=16)
        return tk

    comb_tok = {}
    slab_cons = None
    sgv_rd = {}
    tb_rd = [None, None]
    for k8 in range(8):
        sg_tok = {}
        for ih in range(2):
            ldg = slab_load(wgate, slice(1024 * k8 + 512 * ih, 1024 * k8 + 512 * (ih + 1)), slab_cons)
            for i4 in range(4):
                i = 4 * ih + i4
                for th in range(TH):
                    b = (i4 * TH + th) % 4
                    mm = p.emit("tensor", lambda tt, i4=i4, th=th, b=b: [tt.matmul(
                        pall[:, b, :], wslab[:, 512 * d + 128 * i4:512 * d + 128 * (i4 + 1)],
                        x1b[:, d, 512 * th:512 * (th + 1)],
                        start=(d == 0), stop=(d == 15)) for d in range(16)][-1],
                        wait=[ldg, x1_tok[(15, th)], bank_rd[b]], inc="t")
                    sg_tok[(i, th)] = p.emit("scalar", lambda s, i=i, th=th, b=b: s.activation(
                        out=sgv[:, i * t + 512 * th:i * t + 512 * (th + 1)], in_=pall[:, b, :],
                        func=AF.Silu),
                        wait=[mm, sgv_rd.get((i, th))], inc="s")
                    bank_rd[b] = sg_tok[(i, th)]
                    slab_cons = mm
        for ih in range(2):
            ldu = slab_load(wup, slice(1024 * k8 + 512 * ih, 1024 * k8 + 512 * (ih + 1)), slab_cons)
            for i4 in range(4):
                i = 4 * ih + i4
                for th in range(TH):
                    b = (i4 * TH + th) % 4
                    mm = p.emit("tensor", lambda tt, i4=i4, th=th, b=b: [tt.matmul(
                        pall[:, b, :], wslab[:, 512 * d + 128 * i4:512 * d + 128 * (i4 + 1)],
                        x1b[:, d, 512 * th:512 * (th + 1)],
                        start=(d == 0), stop=(d == 15)) for d in range(16)][-1],
                        wait=[ldu, bank_rd[b]], inc="t")
                    slab_cons = mm
                    sltb = (i * TH + th) % 2
                    upe = p.emit("vector", lambda v, th=th, b=b, k8=k8, sltb=sltb: v.tensor_mul(
                        tb[:, sltb, :], pall[:, b, :],
                        ebs[:, k8 * t + 512 * th:k8 * t + 512 * (th + 1)]),
                        wait=[mm, eb_tok[(k8, th)], tb_rd[sltb]], inc="v")
                    bank_rd[b] = upe
                    if k8 == 0:
                        pr = p.emit("vector", lambda v, i=i, th=th, sltb=sltb: v.tensor_mul(
                            comb[:, i * t + 512 * th:i * t + 512 * (th + 1)],
                            sgv[:, i * t + 512 * th:i * t + 512 * (th + 1)], tb[:, sltb, :]),
                            wait=[upe, sg_tok[(i, th)]], inc="v")
                        comb_tok[(i, th)] = pr
                        tb_rd[sltb] = pr
                        sgv_rd[(i, th)] = pr
                    else:
                        pr = p.emit("vector", lambda v, i=i, th=th, sltb=sltb: v.tensor_mul(
                            tb[:, sltb, :], sgv[:, i * t + 512 * th:i * t + 512 * (th + 1)],
                            tb[:, sltb, :]),
                            wait=[upe, sg_tok[(i, th)]], inc="v")
                        sgv_rd[(i, th)] = pr
                        ad = p.emit("vector", lambda v, i=i, th=th, sltb=sltb: v.tensor_add(
                            comb[:, i * t + 512 * th:i * t + 512 * (th + 1)],
                            comb[:, i * t + 512 * th:i * t + 512 * (th + 1)], tb[:, sltb, :]),
                            wait=[pr, comb_tok[(i, th)]], inc="v")
                        comb_tok[(i, th)] = ad
                        tb_rd[sltb] = ad

    down_tok = {}
    for fh in range(2):
        ldd = slab_load(wdown, slice(1024 * fh, 1024 * (fh + 1)), slab_cons, n_dma=8)
        for f8 in range(8):
            f = 8 * fh + f8
            for th in range(TH):
                b = (f8 * TH + th) % 4
                mm = p.emit("tensor", lambda tt, f8=f8, th=th, b=b: [tt.matmul(
                    pall[:, b, :], wslab[:, 1024 * ic + 128 * f8:1024 * ic + 128 * (f8 + 1)],
                    comb[:, ic * t + 512 * th:ic * t + 512 * (th + 1)],
                    start=(ic == 0), stop=(ic == 7)) for ic in range(8)][-1],
                    wait=[ldd, bank_rd[b]] + [comb_tok[(ic, th)] for ic in range(8)], inc="t")
                slab_cons = mm
                ra = p.emit("vector", lambda v, f=f, th=th, b=b: v.tensor_add(
                    resA[:, f, 512 * th:512 * (th + 1)],
                    resA[:, f, 512 * th:512 * (th + 1)], pall[:, b, :]),
                    wait=[mm, x1_tok[(f, th)]], inc="v")
                bank_rd[b] = ra
                down_tok[(f, th)] = ra

    sgs_tok = {}
    for qt in range(4):
        ldsg = slab_load(wsg, slice(512 * qt, 512 * (qt + 1)), slab_cons)
        for f4 in range(4):
            f = 4 * qt + f4
            for th in range(TH):
                b = (f4 * TH + th) % 4
                mm = p.emit("tensor", lambda tt, f4=f4, th=th, b=b: [tt.matmul(
                    pall[:, b, :], wslab[:, 512 * d + 128 * f4:512 * d + 128 * (f4 + 1)],
                    x1b[:, d, 512 * th:512 * (th + 1)],
                    start=(d == 0), stop=(d == 15)) for d in range(16)][-1],
                    wait=[ldsg, bank_rd[b]], inc="t")
                slab_cons = mm
                sgs_tok[(f, th)] = p.emit("scalar", lambda s, f=f, th=th, b=b: s.activation(
                    out=sgs[:, f * t + 512 * th:f * t + 512 * (th + 1)], in_=pall[:, b, :],
                    func=AF.Silu),
                    wait=[mm], inc="s")
                bank_rd[b] = sgs_tok[(f, th)]
    for qt in range(4):
        ldsu = slab_load(wsu, slice(512 * qt, 512 * (qt + 1)), slab_cons)
        for f4 in range(4):
            f = 4 * qt + f4
            for th in range(TH):
                b = (f4 * TH + th) % 4
                mm = p.emit("tensor", lambda tt, f4=f4, th=th, b=b: [tt.matmul(
                    pall[:, b, :], wslab[:, 512 * d + 128 * f4:512 * d + 128 * (f4 + 1)],
                    x1b[:, d, 512 * th:512 * (th + 1)],
                    start=(d == 0), stop=(d == 15)) for d in range(16)][-1],
                    wait=[ldsu, bank_rd[b]], inc="t")
                slab_cons = mm
                up2 = p.emit("vector", lambda v, f=f, th=th, b=b: v.tensor_mul(
                    sgs[:, f * t + 512 * th:f * t + 512 * (th + 1)],
                    sgs[:, f * t + 512 * th:f * t + 512 * (th + 1)], pall[:, b, :]),
                    wait=[mm, sgs_tok[(f, th)]], inc="v")
                bank_rd[b] = up2
                sgs_tok[(f, th)] = up2
    for qt in range(4):
        ldsd = slab_load(wsd, slice(512 * qt, 512 * (qt + 1)), slab_cons)
        for f4 in range(4):
            f = 4 * qt + f4
            for th in range(TH):
                b = (f4 * TH + th) % 4
                mm = p.emit("tensor", lambda tt, f4=f4, th=th, b=b: [tt.matmul(
                    pall[:, b, :], wslab[:, 512 * d + 128 * f4:512 * d + 128 * (f4 + 1)],
                    sgs[:, d * t + 512 * th:d * t + 512 * (th + 1)],
                    start=(d == 0), stop=(d == 15)) for d in range(16)][-1],
                    wait=[ldsd, bank_rd[b]] + [sgs_tok[(d_, th)] for d_ in range(16)], inc="t")
                slab_cons = mm
                ra = p.emit("vector", lambda v, f=f, th=th, b=b: v.tensor_add(
                    resA[:, f, 512 * th:512 * (th + 1)],
                    resA[:, f, 512 * th:512 * (th + 1)], pall[:, b, :]),
                    wait=[mm, down_tok[(f, th)]], inc="v")
                bank_rd[b] = ra
                down_tok[(f, th)] = ra

    x2_tok = emit_ln(down_tok, None)
    st_last = None
    for j in range(16):
        st_last = p.emit("sync", lambda s, j=j: s.dma_start(
            out=x2T[128 * j:128 * (j + 1), :], in_=resA[:, j, :]),
            wait=[x2_tok[(j, 0)], x2_tok[(j, TH - 1)]], inc="dso", inc_amt=16)
    p.emit("gpsimd", lambda g: None, wait=[st_last])
    with nc.Block() as block:
        p.build(nc, block)
    es.close()
    return nc


def run_B1(inputs, yg_full, t=1024):
    per_core = host_prep_B1(inputs, yg_full, t=t)
    nc = build_B1(t=t)
    res = run_bass_kernel_spmd(nc, per_core, core_ids=list(range(NCORES)))
    return np.concatenate([r["x2T"] for r in res.results], axis=1)

def host_prep_B2(inputs, x2_full, t=1024):
    HALO = 64
    T = x2_full.shape[1]
    def depair(n_heads):
        perm = []
        for h in range(n_heads):
            perm += [h * 128 + 2 * i for i in range(64)]
            perm += [h * 128 + 2 * i + 1 for i in range(64)]
        return np.array(perm)
    wqp = _c(inputs["L1_attn_wq"].astype(np.float32)[:, depair(16)])
    wkp = _c(inputs["L1_attn_wk"].astype(np.float32)[:, depair(4)])
    wv = _c(inputs["L1_attn_wv"].astype(np.float32))
    wo = _c(inputs["L1_attn_wo"].astype(np.float32))
    sink = _c(inputs["L1_attn_sink"].astype(np.float32).reshape(1, 16))
    qkv = inputs["L1_ffn_qkv_weight"].astype(np.float32)
    rows = np.concatenate([np.arange((h // 4) * 768 + 640, (h // 4) * 768 + 768)
                           for h in range(16)])
    wvselT = _c(qkv[rows].T)
    projT = _c(inputs["L1_ffn_proj_weight"].astype(np.float32).T)
    ident = np.eye(128, dtype=np.float32)
    ones = np.ones((128, 128), np.float32)
    ind8 = np.zeros((8, 8 * 128), np.float32)
    for j in range(8):
        ind8[j, j * 128:(j + 1) * 128] = 1.0
    permM = np.zeros((128, 128), np.float32)
    for i in range(64):
        permM[64 + i, i] = 1.0
        permM[i, 64 + i] = 1.0
    lnp = np.stack([inputs["L1_ffn_ln1_weight"], inputs["L1_ffn_ln1_bias"],
                    inputs["L1_ffn_ln2_weight"], inputs["L1_ffn_ln2_bias"]]
                   ).astype(np.float32).reshape(4, 16, 128).transpose(2, 0, 1)
    lnp = _c(lnp)
    band = np.zeros((128, 256), np.float32)
    for i in range(128):
        for j in range(256):
            if j - 128 <= i <= j:
                band[i, j] = 1.0
    shared = dict(
        wqp=wqp, wkp=wkp, wv=wv, wo=wo, sinkr=sink, wvselT=wvselT, projT=projT,
        wre=_c(inputs["L1_ffn_router_weight"].astype(np.float32).reshape(16, 128, 8).transpose(1, 0, 2).reshape(128, 128)),
        mw1=_c(inputs["L1_ffn_moe_w1"].astype(np.float32)),
        mw2=_c(inputs["L1_ffn_moe_w2"].astype(np.float32)),
        ident=ident, ones=ones, ind8=ind8, permM=permM, lnp=lnp,
    )
    per_core = []
    for c in range(NCORES):
        t0 = t * c
        x2h = np.zeros((D_MODEL, t + 2 * HALO), np.float32)
        lo, hi = t0 - HALO, t0 + t + HALO
        slo, shi = max(lo, 0), min(hi, T)
        x2h[:, slo - lo:shi - lo] = x2_full[:, slo:shi]
        pos = np.arange(lo, hi, dtype=np.float64)
        inv = 1.0 / (10000.0 ** (np.arange(64) / 64.0))
        f = pos[None, :] * inv[:, None]
        cos = np.cos(f); sin = np.sin(f)
        cs1 = np.concatenate([cos, cos], 0).astype(np.float32)
        cs2 = np.concatenate([-sin, sin], 0).astype(np.float32)
        m0 = band.copy()
        m7 = band.copy()
        if c == 0:
            for j in range(256):
                if j - HALO + 0 < 0:
                    m0[:, j] = 0.0
        if c == NCORES - 1:
            for j in range(256):
                if t0 + 128 * 7 - HALO + j >= T:
                    m7[:, j] = 0.0
        per_core.append(dict(x2h=_c(x2h), cs1=cs1, cs2=cs2,
                             mask0=m0, maskI=band, mask7=m7, **shared))
    return per_core


def build_B2(t=1024):
    TH = t // 512
    HALO = 64
    th_ = t + 2 * HALO
    nc = bass.Bass("TRN2", target_bir_lowering=False, debug=False)
    x2h = nc.dram_tensor("x2h", [D_MODEL, th_], F32, kind="ExternalInput").ap()
    wqp = nc.dram_tensor("wqp", [D_MODEL, 2048], F32, kind="ExternalInput").ap()
    wkp = nc.dram_tensor("wkp", [D_MODEL, 512], F32, kind="ExternalInput").ap()
    wv = nc.dram_tensor("wv", [D_MODEL, 512], F32, kind="ExternalInput").ap()
    wo = nc.dram_tensor("wo", [D_MODEL, 2048], F32, kind="ExternalInput").ap()
    sinkr = nc.dram_tensor("sinkr", [1, 16], F32, kind="ExternalInput").ap()
    wvselT = nc.dram_tensor("wvselT", [D_MODEL, 2048], F32, kind="ExternalInput").ap()
    projT = nc.dram_tensor("projT", [D_MODEL, 2048], F32, kind="ExternalInput").ap()
    wre = nc.dram_tensor("wre", [128, 128], F32, kind="ExternalInput").ap()
    mw1 = nc.dram_tensor("mw1", [8, D_MODEL, 4096], F32, kind="ExternalInput").ap()
    mw2 = nc.dram_tensor("mw2", [8, 4096, D_MODEL], F32, kind="ExternalInput").ap()
    ident = nc.dram_tensor("ident", [128, 128], F32, kind="ExternalInput").ap()
    ones = nc.dram_tensor("ones", [128, 128], F32, kind="ExternalInput").ap()
    ind8 = nc.dram_tensor("ind8", [8, 8 * 128], F32, kind="ExternalInput").ap()
    permM = nc.dram_tensor("permM", [128, 128], F32, kind="ExternalInput").ap()
    lnp = nc.dram_tensor("lnp", [128, 4 * 16], F32, kind="ExternalInput").ap()
    cs1 = nc.dram_tensor("cs1", [128, th_], F32, kind="ExternalInput").ap()
    cs2 = nc.dram_tensor("cs2", [128, th_], F32, kind="ExternalInput").ap()
    mask0 = nc.dram_tensor("mask0", [128, 256], F32, kind="ExternalInput").ap()
    maskI = nc.dram_tensor("maskI", [128, 256], F32, kind="ExternalInput").ap()
    mask7 = nc.dram_tensor("mask7", [128, 256], F32, kind="ExternalInput").ap()
    outT = nc.dram_tensor("outT", [D_MODEL, t], F32, kind="ExternalOutput").ap()

    p = Prog()
    import contextlib
    es = contextlib.ExitStack()
    SB = lambda nm, sh, dt: es.enter_context(nc.sbuf_tensor(nm, sh, dt))
    XR = SB("XR", [128, 16 * th_], BF16)
    P1 = SB("P1", [128, 16, t], F32)
    ob = SB("ob", [128, 16 * t], BF16)
    kbuf = SB("kbuf", [128, 4 * th_], BF16)
    krob = SB("krob", [128, 4 * th_], BF16)
    vbuf = SB("vbuf", [128, 4 * th_], BF16)
    qhb = SB("qhb", [128, t], BF16)
    qrob = SB("qrob", [128, t], BF16)
    wslab = SB("wslab", [128, 8192], BF16)
    cs1b = SB("cs1b", [128, th_], BF16)
    cs2b = SB("cs2b", [128, th_], BF16)
    maskb = SB("maskb", [128, 3, 256], BF16)
    scr = SB("scr", [128, 2, 512], F32)
    srow = SB("srow", [1, 3, 512], F32)
    sq = SB("sq", [128, 512], F32)
    stg = SB("stg", [128, 2, 512], F32)
    attb = SB("attb", [128, 256], BF16)
    attTb = SB("attTb", [128, 256], BF16)
    g8 = SB("g8", [128, 8], F32)
    gE = SB("gE", [128, 8], F32)
    gE1 = SB("gE1", [128, 8], F32)
    gM = SB("gM", [128, 8], F32)
    rs = SB("rs", [128, 4], F32)
    konst = SB("konst", [128, 1], F32)
    idb = SB("idb", [128, 128], F32)
    idbB = SB("idbB", [128, 128], BF16)
    onesb = SB("onesb", [128, 128], F32)
    onesbB = SB("onesbB", [128, 128], BF16)
    ind8b = SB("ind8b", [8, 8 * 128], BF16)
    permB = SB("permB", [128, 128], BF16)
    lnpb = SB("lnpb", [128, 4 * 16], F32)
    sinkbc = SB("sinkbc", [128, 16], F32)
    sinkb = SB("sinkb", [1, 16], F32)
    pall = es.enter_context(nc.psum_tensor("pall", [128, 7, 512], F32))
    pbt = es.enter_context(nc.psum_tensor("pbt", [128, 1024], BF16))
    x2hb = XR[:].rearrange("p (a b) -> p a b", a=16)
    lnreg = XR[:, 0:16 * t].rearrange("p (a b) -> p a b", a=16)
    gelu = ob[:].rearrange("p (a b) -> p a b", a=16)
    bank_rd = [None] * 8

    p.emit("sync", lambda s: s.dma_start(out=idb[:], in_=ident), inc="dws", inc_amt=16)
    p.emit("sync", lambda s: s.dma_start(out=onesb[:], in_=ones), inc="dws", inc_amt=16)
    p.emit("gpsimd", lambda g: g.dma_start(out=maskb[:, 0, :], in_=mask0), inc="dw", inc_amt=16)
    p.emit("gpsimd", lambda g: g.dma_start(out=maskb[:, 1, :], in_=maskI), inc="dw", inc_amt=16)
    p.emit("gpsimd", lambda g: g.dma_start(out=maskb[:, 2, :], in_=mask7), inc="dw", inc_amt=16)
    p.emit("sync", lambda s: s.dma_start(out=lnpb[:], in_=lnp), inc="dws", inc_amt=16)
    t_pre_s = p.emit("sync", lambda s: s.dma_start(out=sinkb[:], in_=sinkr), inc="dws", inc_amt=16)
    p.emit("gpsimd", lambda g: g.dma_start(out=idbB[:], in_=ident), inc="dw", inc_amt=16)
    p.emit("gpsimd", lambda g: g.dma_start(out=onesbB[:], in_=ones), inc="dw", inc_amt=16)
    p.emit("gpsimd", lambda g: g.dma_start(out=ind8b[:], in_=ind8), inc="dw", inc_amt=16)
    p.emit("gpsimd", lambda g: g.dma_start(out=permB[:], in_=permM), inc="dw", inc_amt=16)
    p.emit("gpsimd", lambda g: g.dma_start(out=cs1b[:], in_=cs1), inc="dw", inc_amt=16)
    p.emit("gpsimd", lambda g: g.dma_start(out=cs2b[:], in_=cs2), inc="dw", inc_amt=16)
    t_pre_g = None
    for d in range(16):
        t_pre_g = p.emit("gpsimd", lambda g, d=d: g.dma_start(
            out=x2hb[:, d, :], in_=x2h[128 * d:128 * (d + 1), :]), inc="dw", inc_amt=16)
    t_k0 = p.emit("vector", lambda v: v.memset(konst[:], 0.08838834764831845), inc="v")
    mmsk = p.emit("tensor", lambda tt: tt.matmul(
        pall[:, 6, 0:16], onesb[0:1, :], sinkb[:], start=True, stop=True),
        wait=[t_pre_s, ("dws", 6 * 16) if False else None], inc="t")
    t_sink = p.emit("vector", lambda v: v.tensor_copy(sinkbc[:], pall[:, 6, 0:16]),
                    wait=[mmsk], inc="v")
    bank_rd[6] = t_sink

    def slab_load(src_fn, n_dma, gate):
        tk = None
        w = 8192 // n_dma
        for d in range(n_dma):
            tk = p.emit("gpsimd", lambda g, d=d, w=w: g.dma_start(
                out=wslab[:, w * d:w * (d + 1)], in_=src_fn(d)),
                wait=[gate if d == 0 else None], inc="dsl", inc_amt=16)
        return tk

    slab_cons = None
    TC = [(0, 512), (512, 1024), (1024, 1152)]
    kv_ev = {}
    for which, (wsrc, dst) in enumerate([(wkp, kbuf), (wv, vbuf)]):
        ld = slab_load(lambda d, wsrc=wsrc: wsrc[128 * d:128 * (d + 1), :], 16, slab_cons)
        for g4 in range(4):
            for ci, (c0, c1) in enumerate(TC):
                b = (g4 + ci) % 4
                mm = p.emit("tensor", lambda tt, g4=g4, c0=c0, c1=c1, b=b: [tt.matmul(
                    pall[:, b, 0:c1 - c0], wslab[:, 512 * d + 128 * g4:512 * d + 128 * (g4 + 1)],
                    x2hb[:, d, c0:c1], start=(d == 0), stop=(d == 15)) for d in range(16)][-1],
                    wait=[ld, t_pre_g, bank_rd[b]], inc="t")
                slab_cons = mm
                ev = p.emit("scalar", lambda s, dst=dst, g4=g4, c0=c0, c1=c1, b=b: s.copy(
                    out=dst[:, g4 * th_ + c0:g4 * th_ + c1], in_=pall[:, b, 0:c1 - c0]),
                    wait=[mm], inc="s")
                bank_rd[b] = ev
                kv_ev[(which, g4, ci)] = ev
    kro_tok = {}
    scr_rd = [None, None]
    for g4 in range(4):
        mm = p.emit("tensor", lambda tt, g4=g4: [tt.matmul(
            pall[:, 4 + (g4 % 2), 0:512] if c1 - c0 == 512 else pall[:, 4 + (g4 % 2), 0:c1 - c0],
            permB[:], kbuf[:, g4 * th_ + c0:g4 * th_ + c1], start=True, stop=True)
            for (c0, c1) in [TC[0]]][-1],
            wait=[kv_ev[(0, g4, 0)], bank_rd[4 + (g4 % 2)]], inc="t")
        toks = []
        for ci, (c0, c1) in enumerate(TC):
            b = 4 + (g4 + ci) % 2
            if ci > 0 or True:
                mm = p.emit("tensor", lambda tt, g4=g4, c0=c0, c1=c1, b=b: tt.matmul(
                    pall[:, b, 0:c1 - c0], permB[:], kbuf[:, g4 * th_ + c0:g4 * th_ + c1],
                    start=True, stop=True),
                    wait=[kv_ev[(0, g4, ci)], bank_rd[b]], inc="t")
            t1 = p.emit("vector", lambda v, g4=g4, c0=c0, c1=c1: v.tensor_mul(
                krob[:, g4 * th_ + c0:g4 * th_ + c1], kbuf[:, g4 * th_ + c0:g4 * th_ + c1],
                cs1b[:, c0:c1]),
                wait=[kv_ev[(0, g4, ci)], kro_tok.get((g4 - 1, ci), None)], inc="v")
            t2 = p.emit("vector", lambda v, c0=c0, c1=c1, b=b, sl=ci % 2: v.tensor_mul(
                scr[:, sl, 0:c1 - c0], pall[:, b, 0:c1 - c0], cs2b[:, c0:c1]),
                wait=[mm, scr_rd[ci % 2]], inc="v")
            bank_rd[b] = t2
            t3 = p.emit("vector", lambda v, g4=g4, c0=c0, c1=c1, sl=ci % 2: v.tensor_add(
                krob[:, g4 * th_ + c0:g4 * th_ + c1], krob[:, g4 * th_ + c0:g4 * th_ + c1],
                scr[:, sl, 0:c1 - c0]),
                wait=[t1, t2], inc="v")
            scr_rd[ci % 2] = t3
            toks.append(t3)
            kro_tok[(g4, ci)] = t3
    vT_tok = {}
    for g4 in range(4):
        for kt in range(9):
            mm = p.emit("tensor", lambda tt, g4=g4, kt=kt: tt.transpose(
                pbt[:, 0:128] if kt % 2 == 0 else pbt[:, 128:256],
                vbuf[:, g4 * th_ + 128 * kt:g4 * th_ + 128 * (kt + 1)], idbB[:]),
                wait=[kv_ev[(1, g4, min(kt // 4, 2))], kv_ev[(1, g4, 2)],
                      bank_rd[7], kro_tok[(g4, 2)] if kt == 0 else None], inc="t")
            ev = p.emit("vector", lambda v, g4=g4, kt=kt: v.tensor_copy(
                kbuf[:, g4 * th_ + 128 * kt:g4 * th_ + 128 * (kt + 1)],
                pbt[:, 0:128] if kt % 2 == 0 else pbt[:, 128:256]),
                wait=[mm], inc="v")
            bank_rd[7] = ev
            vT_tok[(g4, kt)] = ev
    vT = kbuf

    o_tok = {}
    att_rd = [None]
    attT_rd = [None]
    qh_rd = [None]
    qro_rd = [None]
    ldq_cons = slab_cons
    for h in range(16):
        g4 = h // 4
        ldq = slab_load(lambda d, h=h: wqp[128 * d:128 * (d + 1),
                                           512 * (h // 4):512 * (h // 4 + 1)],
                        16, ldq_cons) if h % 4 == 0 else ldq
        if h % 4 == 0:
            ldq_cons_next = None
        hh = h % 4
        for thc in range(TH):
            b = thc % 2
            mm = p.emit("tensor", lambda tt, hh=hh, thc=thc, b=b: [tt.matmul(
                pall[:, b, :], wslab[:, 512 * d + 128 * hh:512 * d + 128 * (hh + 1)],
                x2hb[:, d, HALO + 512 * thc:HALO + 512 * (thc + 1)],
                start=(d == 0), stop=(d == 15)) for d in range(16)][-1],
                wait=[ldq, t_pre_g, bank_rd[b]], inc="t")
            ldq_cons_next = mm
            ev = p.emit("scalar", lambda s, thc=thc, b=b: s.copy(
                out=qhb[:, 512 * thc:512 * (thc + 1)], in_=pall[:, b, :]),
                wait=[mm, qh_rd[0]], inc="s")
            bank_rd[b] = ev
            if thc == TH - 1:
                qev = ev
        if h % 4 == 3:
            ldq_cons = ldq_cons_next
        b = 2
        mm = p.emit("tensor", lambda tt, b=b: [tt.matmul(
            pall[:, b, :], permB[:], qhb[:, 512 * c:512 * (c + 1)], start=True, stop=True)
            for c in [0]][-1],
            wait=[qev, bank_rd[b]], inc="t")
        rope_toks = []
        for c in range(TH):
            b = 2 + c % 2
            if c > 0:
                mm = p.emit("tensor", lambda tt, c=c, b=b: tt.matmul(
                    pall[:, b, :], permB[:], qhb[:, 512 * c:512 * (c + 1)], start=True, stop=True),
                    wait=[qev, bank_rd[b]], inc="t")
            t1 = p.emit("vector", lambda v, c=c: v.tensor_mul(
                qrob[:, 512 * c:512 * (c + 1)], qhb[:, 512 * c:512 * (c + 1)],
                cs1b[:, HALO + 512 * c:HALO + 512 * (c + 1)]),
                wait=[qev, qro_rd[0]], inc="v")
            sl = c % 2
            t2 = p.emit("vector", lambda v, c=c, b=b, sl=sl: v.tensor_mul(
                scr[:, sl, 0:512], pall[:, b, :], cs2b[:, HALO + 512 * c:HALO + 512 * (c + 1)]),
                wait=[mm, scr_rd[sl]], inc="v")
            bank_rd[b] = t2
            t3 = p.emit("vector", lambda v, c=c, sl=sl: v.tensor_add(
                qrob[:, 512 * c:512 * (c + 1)], qrob[:, 512 * c:512 * (c + 1)], scr[:, sl, 0:512]),
                wait=[t1, t2], inc="v")
            scr_rd[sl] = t3
            rope_toks.append(t3)
        qh_rd[0] = rope_toks[-1]
        for i in range(8):
            mi = 0 if i == 0 else (2 if i == 7 else 1)
            b = 4 + i % 2
            sc = p.emit("tensor", lambda tt, h=h, i=i, g4=g4, b=b: tt.matmul(
                pall[:, b, 0:256], qrob[:, 128 * i:128 * (i + 1)],
                krob[:, g4 * th_ + 128 * i:g4 * th_ + 128 * i + 256],
                start=True, stop=True),
                wait=[rope_toks[(128 * i) // 512], rope_toks[(128 * i + 127) // 512],
                      kro_tok[(g4, 2)], bank_rd[b]], inc="t")
            m0 = p.emit("vector", lambda v, b=b: v.reduce_max(rs[:, 0:1], pall[:, b, 0:256],
                                                              axis=AX.X, negate=True),
                        wait=[sc], inc="v")
            bb = p.emit("vector", lambda v, h=h: v.scalar_tensor_tensor(
                rs[:, 1:2], rs[:, 0:1], 0.08838834764831845, sinkbc[:, h:h + 1],
                OP.mult, OP.add),
                wait=[m0, t_sink], inc="v")
            ex = p.emit("scalar", lambda s, b=b: s.activation(
                out=scr[:, 0, 0:256], in_=pall[:, b, 0:256], func=AF.Exp,
                scale=konst[:], bias=rs[:, 1:2]),
                wait=[bb, t_k0, scr_rd[0]], inc="s")
            bank_rd[b] = ex
            mk = p.emit("vector", lambda v, mi=mi: v.tensor_mul(
                scr[:, 1, 0:256], scr[:, 0, 0:256], maskb[:, mi, :]),
                wait=[ex, scr_rd[1]], inc="v")
            scr_rd[0] = mk
            sm = p.emit("vector", lambda v: v.reduce_sum(rs[:, 2:3], scr[:, 1, 0:256], axis=AX.X),
                        wait=[mk], inc="v")
            rc = p.emit("vector", lambda v: v.reciprocal(rs[:, 3:4], rs[:, 2:3]), wait=[sm], inc="v")
            at = p.emit("vector", lambda v: v.tensor_scalar(
                attb[:], scr[:, 1, 0:256], rs[:, 3:4], None, OP.mult),
                wait=[rc, att_rd[0]], inc="v")
            scr_rd[1] = at
            ev2 = None
            for half in range(2):
                tr = p.emit("tensor", lambda tt, half=half: tt.transpose(
                    pbt[:, 128 * half:128 * (half + 1)], attb[:, 128 * half:128 * (half + 1)],
                    idbB[:]),
                    wait=[at, bank_rd[7]], inc="t")
                ev2 = p.emit("vector", lambda v, half=half: v.tensor_copy(
                    attTb[:, 128 * half:128 * (half + 1)], pbt[:, 128 * half:128 * (half + 1)]),
                    wait=[tr, attT_rd[0] if half == 0 else None], inc="v")
                bank_rd[7] = ev2
            att_rd[0] = ev2
            av = p.emit("tensor", lambda tt, g4=g4, i=i: [tt.matmul(
                pall[:, 6, 0:128], vT[:, g4 * th_ + 128 * (i + kk):g4 * th_ + 128 * (i + kk + 1)],
                attTb[:, 128 * kk:128 * (kk + 1)], start=(kk == 0), stop=(kk == 1))
                for kk in range(2)][-1],
                wait=[ev2, vT_tok[(g4, i)], vT_tok[(g4, i + 1)], bank_rd[6]], inc="t")
            attT_rd[0] = av
            oe = p.emit("scalar", lambda s, h=h, i=i: s.copy(
                out=ob[:, h * t + 128 * i:h * t + 128 * (i + 1)], in_=pall[:, 6, 0:128]),
                wait=[av, o_tok.get((h, i))], inc="s")
            bank_rd[6] = oe
            o_tok[(h, i)] = oe
        qro_rd[0] = sc
    slab_cons = ldq_cons

    pre3_tok = {}
    stg_rd = [None, None]
    for qt in range(4):
        ldo = slab_load(lambda d, qt=qt: wo[128 * d:128 * (d + 1), 512 * qt:512 * (qt + 1)],
                        16, slab_cons if qt == 0 else slab_cons)
        for f4 in range(4):
            f = 4 * qt + f4
            for thc in range(TH):
                b = (f4 * TH + thc) % 4
                mm = p.emit("tensor", lambda tt, f4=f4, thc=thc, b=b: [tt.matmul(
                    pall[:, b, :], wslab[:, 512 * d + 128 * f4:512 * d + 128 * (f4 + 1)],
                    ob[:, d * t + 512 * thc:d * t + 512 * (thc + 1)],
                    start=(d == 0), stop=(d == 15)) for d in range(16)][-1],
                    wait=[ldo, bank_rd[b], o_tok[(15, 7)]], inc="t")
                slab_cons = mm
                sl = (f * TH + thc) % 2
                ld = p.emit("sync", lambda s, f=f, thc=thc, sl=sl: s.dma_start(
                    out=stg[:, sl, :], in_=x2h[128 * f:128 * (f + 1),
                                               HALO + 512 * thc:HALO + 512 * (thc + 1)]),
                    wait=[stg_rd[sl]], inc="dst%d" % sl, inc_amt=16)
                tk = p.emit("vector", lambda v, f=f, thc=thc, b=b, sl=sl: v.tensor_add(
                    P1[:, f, 512 * thc:512 * (thc + 1)], stg[:, sl, :], pall[:, b, :]),
                    wait=[mm, ld], inc="v")
                bank_rd[b] = tk
                stg_rd[sl] = tk
                pre3_tok[(f, thc)] = tk

    srow_rd = [None]
    def emit_ln(src_ready, dst, gamma_beta=None, src_bf=None, inplace=True):
        apply_toks = {}
        sq_rd = [None]
        for thc in range(TH):
            mm1 = mm2 = None
            for j in range(16):
                srcap = (P1[:, j, 512 * thc:512 * (thc + 1)] if src_bf is None
                         else src_bf[:, j, 512 * thc:512 * (thc + 1)])
                sqt = p.emit("scalar", lambda s, srcap=srcap: s.activation(
                    out=sq[:], in_=srcap, func=AF.Square),
                    wait=[src_ready[(j, thc)], sq_rd[0]], inc="s")
                mm1 = p.emit("tensor", lambda tt, srcap=srcap, j=j: tt.matmul(
                    pall[0:1, 0, :], onesb[:, 0:1] if src_bf is None else onesbB[:, 0:1],
                    srcap, start=(j == 0), stop=(j == 15)),
                    wait=[src_ready[(j, thc)], bank_rd[0] if j == 0 else None], inc="t")
                mm2 = p.emit("tensor", lambda tt, j=j: tt.matmul(
                    pall[0:1, 1, :], onesb[:, 0:1], sq[:], start=(j == 0), stop=(j == 15)),
                    wait=[sqt, bank_rd[1] if j == 0 else None], inc="t")
                sq_rd[0] = mm2
            a = p.emit("vector", lambda v: v.tensor_scalar(
                srow[0:1, 0, :], pall[0:1, 0, :], 1.0 / 2048.0, None, OP.mult),
                wait=[mm1, srow_rd[0]], inc="v")
            bank_rd[0] = a
            b2 = p.emit("vector", lambda v: v.tensor_mul(
                srow[0:1, 2, :], srow[0:1, 0, :], srow[0:1, 0, :]), wait=[a], inc="v")
            c = p.emit("vector", lambda v: v.scalar_tensor_tensor(
                srow[0:1, 2, :], pall[0:1, 1, :], 1.0 / 2048.0, srow[0:1, 2, :],
                OP.mult, OP.subtract),
                wait=[mm2, b2], inc="v")
            bank_rd[1] = c
            c2 = p.emit("vector", lambda v: v.tensor_scalar_add(
                srow[0:1, 2, :], srow[0:1, 2, :], 1e-5), wait=[c], inc="v")
            d = p.emit("scalar", lambda s: s.activation(
                out=srow[0:1, 1, :], in_=srow[0:1, 2, :], func=AF.Sqrt), wait=[c2], inc="s")
            e = p.emit("vector", lambda v: v.reciprocal(srow[0:1, 2, :], srow[0:1, 1, :]),
                       wait=[d], inc="v")
            bm = p.emit("tensor", lambda tt: tt.matmul(
                pall[:, 2, :], onesb[0:1, :], srow[0:1, 0, :], start=True, stop=True),
                wait=[a, bank_rd[2]], inc="t")
            br = p.emit("tensor", lambda tt: tt.matmul(
                pall[:, 3, :], onesb[0:1, :], srow[0:1, 2, :], start=True, stop=True),
                wait=[e, bank_rd[3]], inc="t")
            srow_rd[0] = br
            f2 = None
            for j in range(16):
                srcap = (P1[:, j, 512 * thc:512 * (thc + 1)] if src_bf is None
                         else src_bf[:, j, 512 * thc:512 * (thc + 1)])
                f1 = p.emit("vector", lambda v, srcap=srcap: v.tensor_sub(
                    scr[:, 0, 0:512], srcap, pall[:, 2, :]),
                    wait=[bm, scr_rd[0], src_ready[(j, thc)]], inc="v")
                if gamma_beta is None and dst is None:
                    f2 = p.emit("vector", lambda v, j=j, thc=thc: v.tensor_mul(
                        P1[:, j, 512 * thc:512 * (thc + 1)], scr[:, 0, 0:512], pall[:, 3, :]),
                        wait=[f1, br], inc="v")
                    tokj = f2
                else:
                    f2 = p.emit("vector", lambda v: v.tensor_mul(
                        scr[:, 1, 0:512], scr[:, 0, 0:512], pall[:, 3, :]),
                        wait=[f1, br, scr_rd[1]], inc="v")
                    gcol, bcol = gamma_beta
                    tokj = p.emit("vector", lambda v, j=j, thc=thc, gcol=gcol, bcol=bcol: v.tensor_scalar(
                        dst[:, j, 512 * thc:512 * (thc + 1)], scr[:, 1, 0:512],
                        lnpb[:, gcol * 16 + j:gcol * 16 + j + 1],
                        lnpb[:, bcol * 16 + j:bcol * 16 + j + 1],
                        OP.mult, OP.add),
                        wait=[f2], inc="v")
                    scr_rd[1] = tokj
                scr_rd[0] = f2
                apply_toks[(j, thc)] = tokj
            bank_rd[2] = f2
            bank_rd[3] = f2
        return apply_toks

    x3_tok = emit_ln(pre3_tok, None)
    ln1_tok = emit_ln(x3_tok, lnreg, gamma_beta=(0, 1))

    av2_tok = {}
    for qt in range(4):
        ldv = slab_load(lambda d, qt=qt: wvselT[128 * d:128 * (d + 1), 512 * qt:512 * (qt + 1)],
                        16, slab_cons)
        for f4 in range(4):
            f = 4 * qt + f4
            for thc in range(TH):
                b = (f4 * TH + thc) % 4
                mm = p.emit("tensor", lambda tt, f4=f4, thc=thc, b=b: [tt.matmul(
                    pall[:, b, :], wslab[:, 512 * d + 128 * f4:512 * d + 128 * (f4 + 1)],
                    lnreg[:, d, 512 * thc:512 * (thc + 1)],
                    start=(d == 0), stop=(d == 15)) for d in range(16)][-1],
                    wait=[ldv, bank_rd[b]] + [ln1_tok[(d, thc)] for d in range(16)], inc="t")
                slab_cons = mm
                ev = p.emit("scalar", lambda s, f=f, thc=thc, b=b: s.copy(
                    out=ob[:, f * t + 512 * thc:f * t + 512 * (thc + 1)], in_=pall[:, b, :]),
                    wait=[mm], inc="s")
                bank_rd[b] = ev
                av2_tok[(f, thc)] = ev
    had_tok = {}
    pre_tok = {}
    for qt in range(4):
        ldp = slab_load(lambda d, qt=qt: projT[128 * d:128 * (d + 1), 512 * qt:512 * (qt + 1)],
                        16, slab_cons)
        for f4 in range(4):
            f = 4 * qt + f4
            for thc in range(TH):
                b = (f4 * TH + thc) % 4
                mm = p.emit("tensor", lambda tt, f4=f4, thc=thc, b=b: [tt.matmul(
                    pall[:, b, :], wslab[:, 512 * d + 128 * f4:512 * d + 128 * (f4 + 1)],
                    ob[:, d * t + 512 * thc:d * t + 512 * (thc + 1)],
                    start=(d == 0), stop=(d == 15)) for d in range(16)][-1],
                    wait=[ldp, bank_rd[b]] + [av2_tok[(d, thc)] for d in range(16)], inc="t")
                slab_cons = mm
                hd = p.emit("vector", lambda v, f=f, thc=thc, b=b: v.tensor_add(
                    lnreg[:, f, 512 * thc:512 * (thc + 1)],
                    P1[:, f, 512 * thc:512 * (thc + 1)], pall[:, b, :]),
                    wait=[mm, ln1_tok[(f, thc)], x3_tok[(f, thc)]], inc="v")
                pr = p.emit("vector", lambda v, f=f, thc=thc, b=b: v.scalar_tensor_tensor(
                    P1[:, f, 512 * thc:512 * (thc + 1)],
                    P1[:, f, 512 * thc:512 * (thc + 1)], 2.0, pall[:, b, :],
                    OP.mult, OP.add),
                    wait=[hd], inc="v")
                bank_rd[b] = pr
                had_tok[(f, thc)] = hd
                pre_tok[(f, thc)] = pr

    ln2_tok = emit_ln(had_tok, lnreg, gamma_beta=(2, 3), src_bf=lnreg)

    lgG = wslab[:, 4096:6144].bitcast(F32)[0:8, :]
    gTb = wslab[0:8, 6144:7168]
    ldr = p.emit("gpsimd", lambda g: g.dma_start(
        out=wslab[:, 0:128], in_=wre), wait=[slab_cons], inc="dsl", inc_amt=16)
    lg_ev = [None] * TH
    for thc in range(TH):
        mm = None
        for d in range(16):
            mm = p.emit("tensor", lambda tt, d=d, thc=thc: tt.matmul(
                pall[0:8, 4, :], wslab[:, 8 * d:8 * (d + 1)],
                lnreg[:, d, 512 * thc:512 * (thc + 1)],
                start=(d == 0), stop=(d == 15)),
                wait=[ldr, ln2_tok[(d, thc)], bank_rd[4] if d == 0 else None], inc="t")
        lg_ev[thc] = p.emit("vector", lambda v, thc=thc: v.tensor_copy(
            lgG[:, 512 * thc:512 * (thc + 1)], pall[0:8, 4, :]), wait=[mm], inc="v")
        bank_rd[4] = lg_ev[thc]
    gT_done = []
    g8_rd = [None]
    for tt_ in range(t // 128):
        thc = (tt_ * 128) // 512
        tr = p.emit("tensor", lambda tt, tt_=tt_: tt.transpose(
            pall[:, 5, 0:8], lgG[0:8, 128 * tt_:128 * (tt_ + 1)], idb[0:8, 0:8]),
            wait=[lg_ev[thc], bank_rd[5]], inc="t")
        ev = p.emit("vector", lambda v: v.tensor_copy(g8[:], pall[:, 5, 0:8]),
                    wait=[tr, g8_rd[0]], inc="v")
        bank_rd[5] = ev
        m0 = p.emit("vector", lambda v: v.reduce_max(rs[:, 0:1], g8[:], axis=AX.X, negate=True),
                    wait=[ev], inc="v")
        exg = p.emit("scalar", lambda s: s.activation(out=gE[:], in_=g8[:], func=AF.Exp,
                                                      bias=rs[:, 0:1]),
                     wait=[m0], inc="s")
        g8_rd[0] = exg
        vv1 = p.emit("vector", lambda v: v.reduce_max(rs[:, 1:2], gE[:], axis=AX.X),
                     wait=[exg], inc="v")
        mk1 = p.emit("vector", lambda v: v.tensor_scalar(gM[:], gE[:], rs[:, 1:2], None, OP.is_ge),
                     wait=[vv1], inc="v")
        e1 = p.emit("vector", lambda v: v.scalar_tensor_tensor(gE1[:], gM[:], -1e30, gE[:],
                                                               OP.mult, OP.add),
                    wait=[mk1], inc="v")
        vv2 = p.emit("vector", lambda v: v.reduce_max(rs[:, 2:3], gE1[:], axis=AX.X),
                     wait=[e1], inc="v")
        dn = p.emit("vector", lambda v: v.tensor_add(rs[:, 3:4], rs[:, 1:2], rs[:, 2:3]),
                    wait=[vv2], inc="v")
        rc = p.emit("vector", lambda v: v.reciprocal(rs[:, 3:4], rs[:, 3:4]), wait=[dn], inc="v")
        mk2 = p.emit("vector", lambda v: v.tensor_scalar(gM[:], gE[:], rs[:, 2:3], None, OP.is_ge),
                     wait=[vv2, e1], inc="v")
        gm = p.emit("vector", lambda v: v.tensor_mul(gE[:], gE[:], gM[:]), wait=[mk2], inc="v")
        gs = p.emit("vector", lambda v: v.tensor_scalar(gE[:], gE[:], rs[:, 3:4], None, OP.mult),
                    wait=[gm, rc], inc="v")
        tr2 = p.emit("tensor", lambda tt: tt.transpose(
            pall[0:8, 5, 0:128], gE[:], idb[:]),
            wait=[gs, bank_rd[5]], inc="t")
        ev2 = p.emit("vector", lambda v, tt_=tt_: v.tensor_copy(
            gTb[0:8, 128 * tt_:128 * (tt_ + 1)], pall[0:8, 5, 0:128]),
            wait=[tr2], inc="v")
        bank_rd[5] = ev2
        g8_rd[0] = tr2
        gT_done.append(ev2)
    def gbv(e):
        if e < 4:
            return krob[:, e * t:(e + 1) * t]
        return vbuf[:, (e - 4) * t:(e - 3) * t]
    gb_tok = {}
    for e in range(8):
        for thc in range(TH):
            b = 4 + (e * TH + thc) % 2
            mm = p.emit("tensor", lambda tt, e=e, thc=thc, b=b: tt.matmul(
                pall[:, b, :], ind8b[:, 128 * e:128 * (e + 1)],
                gTb[0:8, 512 * thc:512 * (thc + 1)], start=True, stop=True),
                wait=[bank_rd[b], t_pre_g] + gT_done[4 * thc:4 * (thc + 1)], inc="t")
            ev = p.emit("scalar", lambda s, e=e, thc=thc, b=b: s.copy(
                out=gbv(e)[:, 512 * thc:512 * (thc + 1)], in_=pall[:, b, :]),
                wait=[mm, kro_tok.get((3, 2)) if e == 0 and thc == 0 else None], inc="s")
            bank_rd[b] = ev
            gb_tok[(e, thc)] = ev
            slab_cons = mm

    gelu_tok = {}
    for e in range(8):
        for half in range(2):
            for s8 in range(4):
                fbase = 2048 * half + 512 * s8
                ldw1 = slab_load(lambda d, e=e, fbase=fbase: mw1[e, 128 * d:128 * (d + 1),
                                                                 fbase:fbase + 512],
                                 16, slab_cons)
                for f4 in range(4):
                    lc = 4 * s8 + f4
                    for thc in range(TH):
                        b = (f4 * TH + thc) % 4
                        mm = p.emit("tensor", lambda tt, f4=f4, thc=thc, b=b: [tt.matmul(
                            pall[:, b, :], wslab[:, 512 * d + 128 * f4:512 * d + 128 * (f4 + 1)],
                            lnreg[:, d, 512 * thc:512 * (thc + 1)],
                            start=(d == 0), stop=(d == 15)) for d in range(16)][-1],
                            wait=[ldw1, bank_rd[b]] +
                                 ([ln2_tok[(d, thc)] for d in range(16)] if e == 0 else []),
                            inc="t")
                        slab_cons = mm
                        ge = p.emit("scalar", lambda s, lc=lc, thc=thc, b=b: s.activation(
                            out=gelu[:, lc, 512 * thc:512 * (thc + 1)], in_=pall[:, b, :],
                            func=AF.Gelu),
                            wait=[mm, gelu_tok.get((half, lc, thc)),
                                  av2_tok[(lc, thc)] if e == 0 and half == 0 else None], inc="s")
                        gmul = p.emit("vector", lambda v, lc=lc, thc=thc, e=e: v.tensor_mul(
                            gelu[:, lc, 512 * thc:512 * (thc + 1)],
                            gelu[:, lc, 512 * thc:512 * (thc + 1)],
                            gbv(e)[:, 512 * thc:512 * (thc + 1)]),
                            wait=[ge, gb_tok[(e, thc)]], inc="v")
                        bank_rd[b] = gmul
                        gelu_tok[(half, lc, thc)] = gmul
            for qt in range(4):
                ldw2 = slab_load(lambda d, e=e, half=half, qt=qt: mw2[
                    e, 2048 * half + 128 * d:2048 * half + 128 * (d + 1),
                    512 * qt:512 * (qt + 1)], 16, slab_cons)
                for f4 in range(4):
                    f = 4 * qt + f4
                    for thc in range(TH):
                        b = (f4 * TH + thc) % 4
                        mm = p.emit("tensor", lambda tt, f4=f4, thc=thc, b=b: [tt.matmul(
                            pall[:, b, :], wslab[:, 512 * d + 128 * f4:512 * d + 128 * (f4 + 1)],
                            gelu[:, d, 512 * thc:512 * (thc + 1)],
                            start=(d == 0), stop=(d == 15)) for d in range(16)][-1],
                            wait=[ldw2, bank_rd[b]] +
                                 [gelu_tok[(half, d, thc)] for d in range(16)], inc="t")
                        slab_cons = mm
                        ra = p.emit("vector", lambda v, f=f, thc=thc, b=b: v.tensor_add(
                            P1[:, f, 512 * thc:512 * (thc + 1)],
                            P1[:, f, 512 * thc:512 * (thc + 1)], pall[:, b, :]),
                            wait=[mm, pre_tok[(f, thc)]], inc="v")
                        bank_rd[b] = ra
                        pre_tok[(f, thc)] = ra

    x4_tok = emit_ln(pre_tok, None)
    st_last = None
    for j in range(16):
        st_last = p.emit("sync", lambda s, j=j: s.dma_start(
            out=outT[128 * j:128 * (j + 1), :], in_=P1[:, j, :]),
            wait=[x4_tok[(j, 0)], x4_tok[(j, TH - 1)]], inc="dso", inc_amt=16)
    p.emit("gpsimd", lambda g: None, wait=[st_last])
    with nc.Block() as block:
        p.build(nc, block)
    es.close()
    return nc


def run_B2(inputs, x2_full, t=1024):
    per_core = host_prep_B2(inputs, x2_full, t=t)
    nc = build_B2(t=t)
    res = run_bass_kernel_spmd(nc, per_core, core_ids=list(range(NCORES)))
    return np.concatenate([r["outT"] for r in res.results], axis=1)


def kernel(**inputs):
    yg = run_A(inputs)
    x2 = run_B1(inputs, yg)
    out = run_B2(inputs, x2)
    return np.ascontiguousarray(out.T).reshape(1, 8192, 2048)


# version 02
# speedup vs baseline: 1.0863x; 1.0863x over previous
import numpy as np
import ml_dtypes
import concourse.bass as bass
import concourse.mybir as mybir
from concourse.bass_utils import run_bass_kernel_spmd

F32 = mybir.dt.float32
BF16 = mybir.dt.bfloat16
AF = mybir.ActivationFunctionType
OP = mybir.AluOpType
AX = mybir.AxisListType

NCORES = 8
D_MODEL = 2048
D_INNER = 4096
NCH = D_INNER // NCORES
NQ = NCH // 128

BF = ml_dtypes.bfloat16


class Prog:
    ENGINES = ("sync", "tensor", "vector", "scalar", "gpsimd")

    def __init__(self):
        self.ops = {e: [] for e in self.ENGINES}
        self.cnt = {}

    def emit(self, eng, fn, wait=(), inc=None, inc_amt=1):
        tok = None
        if inc is not None:
            self.cnt[inc] = self.cnt.get(inc, 0) + inc_amt
            tok = (inc, self.cnt[inc])
        waits = tuple(w for w in wait if w is not None)
        self.ops[eng].append((fn, waits, inc, inc_amt))
        return tok

    def build(self, nc, block):
        sems = {name: nc.semaphore(name).__enter__() for name in self.cnt}
        for e in self.ENGINES:
            if not self.ops[e]:
                continue
            ops = self.ops[e]

            def mk(ops=ops, e=e):
                def body(eng):
                    wm = {}
                    for fn, waits, inc, amt in ops:
                        for (s, v) in waits:
                            if wm.get(("w", s), 0) < v:
                                eng.wait_ge(sems[s], v)
                                wm[("w", s)] = v
                        instr = fn(eng)
                        if inc is not None:
                            instr.then_inc(sems[inc], amt)
                return body

            getattr(block, e)(mk())


def tmax(*toks):
    toks = [t for t in toks if t is not None]
    if not toks:
        return None
    sems = {t[0] for t in toks}
    assert len(sems) == 1, sems
    return max(toks, key=lambda t: t[1])


def _c(a):
    return np.ascontiguousarray(a)


def host_prep_A(inputs):
    W_in = inputs["L0_attn_W_in"]
    W_dt = inputs["L0_attn_W_dt"]
    b_dt = inputs["L0_attn_b_dt"]
    conv_w = inputs["L0_attn_conv_w"]
    conv_b = inputs["L0_attn_conv_b"]
    A_log = inputs["L0_attn_A_log"]
    D_param = inputs["L0_attn_D_param"]
    x = inputs["L0_attn_hidden_states"][0]
    xT = _c(x.T.astype(np.float32))

    ind_x = np.zeros((128, NQ * 128), np.float32)
    for q in range(NQ):
        for m in range(128):
            ind_x[16 * q + (m % 16), q * 128 + m] = 1.0
    ind_B = np.zeros((128, 64 * 128), np.float32)
    for j in range(64):
        ind_B[64 + j, j * 128:(j + 1) * 128] = 1.0
    ind_C = np.zeros((128, 16 * 128), np.float32)
    for n in range(16):
        for pp in range(128):
            ind_C[(pp // 16) * 16 + n, n * 128 + pp] = 1.0

    per_core = []
    for k in range(NCORES):
        cs = slice(NCH * k, NCH * (k + 1))
        rows = np.concatenate([
            np.arange(cs.start, cs.stop),
            np.arange(4096 + 64 * k, 4096 + 64 * (k + 1)),
            np.arange(4608 + 64 * k, 4608 + 64 * (k + 1)),
            np.arange(5120 + NCH * k, 5120 + NCH * (k + 1)),
            np.arange(9216, 9344),
        ])
        per_core.append(dict(
            xT=xT,
            winT=_c(W_in[rows].T),
            wdtT=_c(W_dt[cs].T),
            bdt=_c(b_dt[cs].reshape(NQ, 128).T),
            alogT=_c(A_log[cs].reshape(NQ, 128, 16).transpose(1, 0, 2).reshape(128, NQ * 16)),
            convw=_c(conv_w[cs, 0, :].reshape(NQ, 128, 4).transpose(1, 0, 2).reshape(128, NQ * 4)),
            convb=_c(conv_b[cs].reshape(NQ, 128).T),
            dpar=_c(D_param[cs].reshape(NQ, 128).T),
            ind_x=ind_x, ind_B=ind_B, ind_C=ind_C,
        ))
    return per_core


def build_A(T=8192, Tc=512, STAGE=5):
    nck = T // Tc
    nc = bass.Bass("TRN2", target_bir_lowering=False, debug=False)
    xT = nc.dram_tensor("xT", [D_MODEL, T], F32, kind="ExternalInput").ap()
    winT = nc.dram_tensor("winT", [D_MODEL, 1280], F32, kind="ExternalInput").ap()
    wdtT = nc.dram_tensor("wdtT", [128, NCH], F32, kind="ExternalInput").ap()
    bdt = nc.dram_tensor("bdt", [128, NQ], F32, kind="ExternalInput").ap()
    alogT = nc.dram_tensor("alogT", [128, NQ * 16], F32, kind="ExternalInput").ap()
    convw = nc.dram_tensor("convw", [128, NQ * 4], F32, kind="ExternalInput").ap()
    convb = nc.dram_tensor("convb", [128, NQ], F32, kind="ExternalInput").ap()
    dpar = nc.dram_tensor("dpar", [128, NQ], F32, kind="ExternalInput").ap()
    ind_x = nc.dram_tensor("ind_x", [128, NQ * 128], F32, kind="ExternalInput").ap()
    ind_B = nc.dram_tensor("ind_B", [128, 64 * 128], F32, kind="ExternalInput").ap()
    ind_C = nc.dram_tensor("ind_C", [128, 16 * 128], F32, kind="ExternalInput").ap()
    yg = nc.dram_tensor("yg", [NCH, T], BF16, kind="ExternalOutput").ap()

    p = Prog()
    import contextlib
    es = contextlib.ExitStack()
    wl = es.enter_context(nc.sbuf_tensor("wl", [128, 16 * 1280], BF16))
    wdt_sb = es.enter_context(nc.sbuf_tensor("wdt_sb", [128, NCH], BF16))
    indx_sb = es.enter_context(nc.sbuf_tensor("indx_sb", [128, NQ * 128], BF16))
    indB_sb = es.enter_context(nc.sbuf_tensor("indB_sb", [128, 64 * 128], BF16))
    indC_sb = es.enter_context(nc.sbuf_tensor("indC_sb", [128, 16 * 128], BF16))
    bdt2 = es.enter_context(nc.sbuf_tensor("bdt2", [128, NQ], F32))
    acol = es.enter_context(nc.sbuf_tensor("acol", [128, NQ * 16], F32))
    cw = es.enter_context(nc.sbuf_tensor("cw", [128, NQ * 4], F32))
    cb = es.enter_context(nc.sbuf_tensor("cb", [128, NQ], F32))
    dp = es.enter_context(nc.sbuf_tensor("dp", [128, NQ], F32))
    xc = es.enter_context(nc.sbuf_tensor("xc", [128, 2, 16 * Tc], BF16))
    zf = es.enter_context(nc.sbuf_tensor("zf", [128, NQ, Tc], F32))
    xBb = es.enter_context(nc.sbuf_tensor("xBb", [128, Tc], BF16))
    Cbuf = es.enter_context(nc.sbuf_tensor("Cbuf", [128, NQ, Tc], BF16))
    dtlb = es.enter_context(nc.sbuf_tensor("dtlb", [128, Tc], BF16))
    delta = es.enter_context(nc.sbuf_tensor("delta", [128, NQ, Tc], F32))
    ubuf = es.enter_context(nc.sbuf_tensor("ubuf", [128, NQ, Tc], F32))
    dubuf = es.enter_context(nc.sbuf_tensor("dubuf", [128, NQ, Tc], F32))
    ybuf = es.enter_context(nc.sbuf_tensor("ybuf", [128, NQ, Tc], F32))
    xrb = es.enter_context(nc.sbuf_tensor("xrb", [128, NQ, Tc + 3], F32))
    cacc = es.enter_context(nc.sbuf_tensor("cacc", [128, 2, Tc], F32))
    sp1 = es.enter_context(nc.sbuf_tensor("sp1", [128, Tc], F32))
    sp2 = es.enter_context(nc.sbuf_tensor("sp2", [128, Tc], F32))
    sp3 = es.enter_context(nc.sbuf_tensor("sp3", [128, Tc], F32))
    dAb = es.enter_context(nc.sbuf_tensor("dAb", [128, 2, Tc], F32))
    d1b = es.enter_context(nc.sbuf_tensor("d1b", [128, 2, Tc], F32))
    snb = es.enter_context(nc.sbuf_tensor("snb", [128, 2, Tc], F32))
    prodb = es.enter_context(nc.sbuf_tensor("prodb", [128, 2, Tc], F32))
    stateb = es.enter_context(nc.sbuf_tensor("stateb", [128, 64], F32))
    szb = es.enter_context(nc.sbuf_tensor("szb", [128, Tc], F32))
    ygb = es.enter_context(nc.sbuf_tensor("ygb", [128, 2, Tc], BF16))
    psmm = es.enter_context(nc.psum_tensor("psmm", [128, 2, 512], F32))
    psx = es.enter_context(nc.psum_tensor("psx", [128, 2, 512], F32))
    psB = es.enter_context(nc.psum_tensor("psB", [128, 2, 512], F32))
    psC = es.enter_context(nc.psum_tensor("psC", [128, 2, 512], F32))
    if True:
        for d in range(16):
            p.emit("gpsimd", lambda g, d=d: g.dma_start(
                out=wl[:, d * 1280:(d + 1) * 1280], in_=winT[128 * d:128 * (d + 1), :]),
                inc="dw", inc_amt=16)
        p.emit("gpsimd", lambda g: g.dma_start(out=wdt_sb[:], in_=wdtT), inc="dw", inc_amt=16)
        p.emit("gpsimd", lambda g: g.dma_start(out=indx_sb[:], in_=ind_x), inc="dw", inc_amt=16)
        p.emit("gpsimd", lambda g: g.dma_start(out=indB_sb[:], in_=ind_B), inc="dw", inc_amt=16)
        t_pre_g = p.emit("gpsimd", lambda g: g.dma_start(out=indC_sb[:], in_=ind_C), inc="dw", inc_amt=16)
        p.emit("sync", lambda s: s.dma_start(out=bdt2[:], in_=bdt), inc="dws", inc_amt=16)
        p.emit("sync", lambda s: s.dma_start(out=acol[:], in_=alogT), inc="dws", inc_amt=16)
        p.emit("sync", lambda s: s.dma_start(out=cw[:], in_=convw), inc="dws", inc_amt=16)
        p.emit("sync", lambda s: s.dma_start(out=cb[:], in_=convb), inc="dws", inc_amt=16)
        t_pre_s = p.emit("sync", lambda s: s.dma_start(out=dp[:], in_=dpar), inc="dws", inc_amt=16)
        t_wdt = t_indx = t_indB = t_indC = t_pre_g
        t_bdt = t_alog = t_cw = t_cb = t_dp = t_pre_s
        t_b2 = p.emit("vector", lambda v: v.tensor_scalar_mul(bdt2[:], bdt2[:], 2.0),
                      wait=[t_bdt], inc="v")
        t_ae = p.emit("scalar", lambda s: s.activation(out=acol[:], in_=acol[:], func=AF.Exp),
                      wait=[t_alog], inc="s")
        t_an = p.emit("vector", lambda v: v.tensor_scalar_mul(acol[:], acol[:], -1.0),
                      wait=[t_ae], inc="v")
        t_st0 = p.emit("vector", lambda v: v.memset(stateb[:], 0.0), inc="v")
        t_hist0 = p.emit("vector", lambda v: v.memset(xrb[:], 0.0), inc="v")

        xload = [None] * (nck + 2)
        xfree = [None] * (nck + 2)

        def load_chunk(k):
            tk = None
            sem = "dx%d" % (k % 2)
            for d in range(16):
                tk = p.emit("gpsimd", lambda g, d=d, k=k: g.dma_start(
                    out=xc[:, k % 2, d * Tc:(d + 1) * Tc],
                    in_=xT[128 * d:128 * (d + 1), k * Tc:(k + 1) * Tc]),
                    wait=[xfree[k - 2] if k >= 2 else None],
                    inc=sem, inc_amt=16)
            return tk

        xload[0] = load_chunk(0)
        if nck > 1:
            xload[1] = load_chunk(1)

        evac_free = [None, None]
        psx_free = [None, None]
        state_w = [None] * 64
        B_rd = [None, None]
        C_rd = [None, None]
        A_rd = [None, None]
        D_rd = [None, None]
        S_rd = [None, None]
        P_rd = [None, None]
        delta_war = [t_hist0] * NQ
        du_war = [None] * NQ
        u_war = [None] * NQ
        z_war = [None] * NQ
        xB_war = None
        C_war = [None] * NQ
        dtl_war = None
        xrb_war = [t_hist0] * NQ
        cacc_free = [None, None]
        sp_free = [t_hist0, None, None]
        szb_free = None
        ygb_free = [None, None]
        y_war = [None] * NQ
        sp1_hold = None

        store_last = [None, None]
        last_store = None
        for k in range(nck):
            mm_tok = [None] * 10
            ev = [None] * 10
            for j in range(10):
                slot = j % 2

                def mmz(t, j=j, slot=slot, k=k):
                    ins = None
                    for d in range(16):
                        ins = t.matmul(psmm[:, slot, :],
                                       wl[:, d * 1280 + j * 128:d * 1280 + (j + 1) * 128],
                                       xc[:, k % 2, d * Tc:(d + 1) * Tc],
                                       start=(d == 0), stop=(d == 15))
                    return ins

                mm_tok[j] = p.emit("tensor", mmz,
                                   wait=[xload[k], evac_free[slot],
                                         t_pre_g if k == 0 and j == 0 else None],
                                   inc="t")
                if j <= 3:
                    dst, war = zf[:, j, :], z_war[j]
                elif j == 4:
                    dst, war = xBb[:], xB_war
                elif j <= 8:
                    dst, war = Cbuf[:, j - 5, :], C_war[j - 5]
                else:
                    dst, war = dtlb[:], dtl_war
                ev[j] = p.emit("scalar",
                               lambda s, dst=dst, slot=slot: s.copy(out=dst, in_=psmm[:, slot, :]),
                               wait=[mm_tok[j], war], inc="s")
                evac_free[slot] = ev[j]
            if k + 2 < nck:
                xfree[k] = mm_tok[9]
                xload[k + 2] = load_chunk(k + 2)

            t_delta = [None] * NQ
            t_u = [None] * NQ
            t_du = [None] * NQ
            for q in range(NQ if STAGE >= 2 else 0):
                slot = q % 2
                mmd = p.emit("tensor", lambda t, q=q, slot=slot: t.matmul(
                    psx[:, slot, :], wdt_sb[:, q * 128:(q + 1) * 128], dtlb[:],
                    start=True, stop=True),
                    wait=[ev[9], t_wdt, psx_free[slot]], inc="t")
                if q == NQ - 1:
                    dtl_war = mmd
                a1 = p.emit("scalar", lambda s, slot=slot, q=q: s.activation(
                    out=sp1[:], in_=psx[:, slot, :], func=AF.Abs, bias=bdt2[:, q:q + 1]),
                    wait=[mmd, t_b2, sp_free[0]], inc="s")
                e1 = p.emit("scalar", lambda s: s.activation(
                    out=sp2[:], in_=sp1[:], func=AF.Exp, scale=-1.0),
                    wait=[a1, sp_free[1]], inc="s")
                l1 = p.emit("scalar", lambda s: s.activation(
                    out=sp2[:], in_=sp2[:], func=AF.Ln, bias=1.0),
                    wait=[e1], inc="s")
                t0 = p.emit("vector", lambda v, slot=slot, q=q: v.tensor_scalar_add(
                    sp3[:], psx[:, slot, :], bdt2[:, q:q + 1]),
                    wait=[mmd, a1, t_b2, sp_free[2]], inc="v")
                t1 = p.emit("vector", lambda v: v.tensor_add(sp3[:], sp3[:], sp1[:]),
                            wait=[t0, a1], inc="v")
                t_delta[q] = p.emit("vector", lambda v, q=q: v.scalar_tensor_tensor(
                    delta[:, q, :], sp3[:], 0.5, sp2[:], OP.mult, OP.add),
                    wait=[t1, l1, delta_war[q]], inc="v")
                sp_free[0] = t1
                sp_free[1] = t_delta[q]
                sp_free[2] = t_delta[q]
                psx_free[slot] = t1

                if STAGE < 3:
                    continue
                mmx = p.emit("tensor", lambda t, q=q, slot=slot: t.matmul(
                    psx[:, slot, :], indx_sb[:, q * 128:(q + 1) * 128], xBb[:],
                    start=True, stop=True),
                    wait=[ev[4], t_indx, psx_free[slot]], inc="t")
                exr = p.emit("vector", lambda v, q=q, slot=slot: v.tensor_copy(
                    xrb[:, q, 3:Tc + 3], psx[:, slot, :]),
                    wait=[mmx, xrb_war[q]], inc="v")
                psx_free[slot] = exr
                c0 = p.emit("vector", lambda v, q=q: v.tensor_scalar(
                    cacc[:, 0, :], xrb[:, q, 0:Tc], cw[:, 4 * q:4 * q + 1], None, OP.mult),
                    wait=[exr, t_cw, cacc_free[0]], inc="v")
                c1 = p.emit("vector", lambda v, q=q: v.scalar_tensor_tensor(
                    cacc[:, 1, :], xrb[:, q, 1:Tc + 1], cw[:, 4 * q + 1:4 * q + 2],
                    cacc[:, 0, :], OP.mult, OP.add),
                    wait=[c0, cacc_free[1]], inc="v")
                c2 = p.emit("vector", lambda v, q=q: v.scalar_tensor_tensor(
                    cacc[:, 0, :], xrb[:, q, 2:Tc + 2], cw[:, 4 * q + 2:4 * q + 3],
                    cacc[:, 1, :], OP.mult, OP.add),
                    wait=[c1], inc="v")
                c3 = p.emit("vector", lambda v, q=q: v.scalar_tensor_tensor(
                    cacc[:, 1, :], xrb[:, q, 3:Tc + 3], cw[:, 4 * q + 3:4 * q + 4],
                    cacc[:, 0, :], OP.mult, OP.add),
                    wait=[c2], inc="v")
                hq = p.emit("vector", lambda v, q=q: v.tensor_copy(
                    xrb[:, q, 0:3], xrb[:, q, Tc:Tc + 3]),
                    wait=[c3], inc="v")
                xrb_war[q] = hq
                t_u[q] = p.emit("scalar", lambda s, q=q: s.activation(
                    out=ubuf[:, q, :], in_=cacc[:, 1, :], func=AF.Silu, bias=cb[:, q:q + 1]),
                    wait=[c3, t_cb, u_war[q]], inc="s")
                cacc_free[0] = c3
                cacc_free[1] = t_u[q]
                t_du[q] = p.emit("vector", lambda v, q=q: v.tensor_mul(
                    dubuf[:, q, :], delta[:, q, :], ubuf[:, q, :]),
                    wait=[t_delta[q], t_u[q], du_war[q]], inc="v")

            tok_yadd = [None] * NQ
            dA_cur = [None] * 64
            for i in range(64 if STAGE >= 4 else 0):
                n, q = divmod(i, NQ)
                j = 16 * q + n
                sl = i % 2
                tB = p.emit("tensor", lambda t, j=j, sl=sl: t.matmul(
                    psB[:, sl, :], indB_sb[:, j * 128:(j + 1) * 128], xBb[:],
                    start=True, stop=True),
                    wait=[ev[4], t_indB, B_rd[sl]], inc="t")
                tC = p.emit("tensor", lambda t, n=n, q=q, sl=sl: t.matmul(
                    psC[:, sl, :], indC_sb[:, n * 128:(n + 1) * 128], Cbuf[:, q, :],
                    start=True, stop=True),
                    wait=[ev[5 + q], t_indC, C_rd[sl]], inc="t")
                if n == 15:
                    C_war[q] = tC
                if i == 63:
                    xB_war = tB
                tA = p.emit("scalar", lambda s, q=q, n=n, sl=sl: s.activation(
                    out=dAb[:, sl, :], in_=delta[:, q, :], func=AF.Exp,
                    scale=acol[:, q * 16 + n:q * 16 + n + 1]),
                    wait=[t_delta[q], t_an, A_rd[sl]], inc="s")
                dA_cur[i] = tA
                if n == 15:
                    delta_war[q] = tA
                td1 = p.emit("vector", lambda v, q=q, sl=sl: v.tensor_mul(
                    d1b[:, sl, :], dubuf[:, q, :], psB[:, sl, :]),
                    wait=[t_du[q], tB, D_rd[sl]], inc="v")
                B_rd[sl] = td1
                if n == 15:
                    du_war[q] = td1
                tscan = p.emit("vector", lambda v, i=i, sl=sl: v.tensor_tensor_scan(
                    snb[:, sl, :], dAb[:, sl, :], d1b[:, sl, :],
                    stateb[:, i:i + 1], OP.mult, OP.add),
                    wait=[tA, td1, state_w[i], S_rd[sl], t_st0], inc="v")
                A_rd[sl] = tscan
                D_rd[sl] = tscan
                state_w[i] = p.emit("vector", lambda v, i=i, sl=sl: v.tensor_copy(
                    stateb[:, i:i + 1], snb[:, sl, Tc - 1:Tc]),
                    wait=[tscan], inc="v")
                if STAGE < 5:
                    S_rd[sl] = state_w[i]
                    C_rd[sl] = tC
                    continue
                if n == 0:
                    tprod = p.emit("vector", lambda v, q=q, sl=sl: v.tensor_mul(
                        ybuf[:, q, :], snb[:, sl, :], psC[:, sl, :]),
                        wait=[tscan, tC, y_war[q]], inc="v")
                    tok_yadd[q] = tprod
                else:
                    tprod = p.emit("vector", lambda v, sl=sl: v.tensor_mul(
                        prodb[:, sl, :], snb[:, sl, :], psC[:, sl, :]),
                        wait=[tscan, tC, P_rd[sl]], inc="v")
                    ty = p.emit("vector", lambda v, q=q, sl=sl: v.tensor_add(
                        ybuf[:, q, :], ybuf[:, q, :], prodb[:, sl, :]),
                        wait=[tprod, tok_yadd[q]], inc="v")
                    tok_yadd[q] = ty
                    P_rd[sl] = ty
                C_rd[sl] = tprod
                S_rd[sl] = tmax(tprod, state_w[i])

            if STAGE < 5:
                for q in range(NQ):
                    sl = q % 2
                    if STAGE >= 3:
                        src, stok = ubuf, t_u[q]
                    elif STAGE >= 2:
                        src, stok = delta, t_delta[q]
                    else:
                        src, stok = zf, ev[q]
                    ygt = p.emit("vector", lambda v, q=q, sl=sl, src=src: v.tensor_copy(
                        ygb[:, sl, :], src[:, q, :]),
                        wait=[stok, ygb_free[sl]], inc="v")
                    if STAGE >= 2:
                        delta_war[q] = tmax(delta_war[q], None) if False else delta_war[q]
                    y_war[q] = ygt
                    if STAGE >= 3:
                        u_war[q] = ygt
                    if STAGE == 2:
                        delta_war[q] = ygt
                    if STAGE < 2:
                        z_war[q] = ygt
                    last_store = p.emit("sync", lambda s, q=q, k=k, sl=sl: s.dma_start(
                        out=yg[128 * q:128 * (q + 1), k * Tc:(k + 1) * Tc], in_=ygb[:, sl, :]),
                        wait=[ygt], inc="ds%d" % sl, inc_amt=16)
                    ygb_free[sl] = last_store
                    store_last[sl] = last_store
                continue
            for q in range(NQ):
                sl = q % 2
                pr = p.emit("vector", lambda v, q=q, sl=sl: v.tensor_scalar(
                    prodb[:, sl, :], ubuf[:, q, :], dp[:, q:q + 1], None, OP.mult),
                    wait=[t_u[q], t_dp, P_rd[sl]], inc="v")
                ya = p.emit("vector", lambda v, q=q, sl=sl: v.tensor_add(
                    ybuf[:, q, :], ybuf[:, q, :], prodb[:, sl, :]),
                    wait=[pr, tok_yadd[q]], inc="v")
                P_rd[sl] = ya
                u_war[q] = pr
                sz = p.emit("scalar", lambda s, q=q: s.activation(
                    out=szb[:], in_=zf[:, q, :], func=AF.Silu),
                    wait=[ev[q], szb_free], inc="s")
                z_war[q] = sz
                ygt = p.emit("vector", lambda v, q=q, sl=sl: v.tensor_mul(
                    ygb[:, sl, :], ybuf[:, q, :], szb[:]),
                    wait=[ya, sz, ygb_free[sl]], inc="v")
                szb_free = ygt
                y_war[q] = ygt
                last_store = p.emit("sync", lambda s, q=q, k=k, sl=sl: s.dma_start(
                    out=yg[128 * q:128 * (q + 1), k * Tc:(k + 1) * Tc], in_=ygb[:, sl, :]),
                    wait=[ygt], inc="ds%d" % sl, inc_amt=16)
                ygb_free[sl] = last_store
                store_last[sl] = last_store

        p.emit("gpsimd", lambda g: None, wait=[store_last[0], store_last[1]])
        with nc.Block() as block:
            p.build(nc, block)
    es.close()
    return nc


def run_A(inputs, T=8192, Tc=512, STAGE=5):
    per_core = host_prep_A(inputs)
    if T != 8192:
        for d in per_core:
            d["xT"] = _c(d["xT"][:, :T])
    nc = build_A(T=T, Tc=Tc, STAGE=STAGE)
    res = run_bass_kernel_spmd(nc, per_core, core_ids=list(range(NCORES)))
    return np.concatenate([r["yg"] for r in res.results], axis=0)


def kernel(**inputs):
    raise NotImplementedError("later launches pending")


def host_prep_B1(inputs, yg_full, t=1024):
    xT = _c(inputs["L0_attn_hidden_states"][0].T.astype(np.float32))
    ident = np.eye(128, dtype=np.float32)
    ones = np.ones((128, 128), np.float32)
    ind8 = np.zeros((8, 8 * 128), np.float32)
    for j in range(8):
        ind8[j, j * 128:(j + 1) * 128] = 1.0
    amp = inputs["L0_ffn_amplitudes"]
    shared = dict(
        woutT=_c(inputs["L0_attn_W_out"].T.astype(np.float32)),
        wrout=_c(inputs["L0_ffn_w_router"].astype(np.float32)),
        wgate=_c(inputs["L0_ffn_w_gate"].astype(np.float32)),
        wup=_c(inputs["L0_ffn_w_up"].astype(np.float32)),
        wdown=_c(inputs["L0_ffn_w_down"].astype(np.float32)),
        wsg=_c(inputs["L0_ffn_shared_w_gate"].astype(np.float32)),
        wsu=_c(inputs["L0_ffn_shared_w_up"].astype(np.float32)),
        wsd=_c(inputs["L0_ffn_shared_w_down"].astype(np.float32)),
        a0=_c(amp[:, :, 0].astype(np.float32)),
        a1=_c(amp[:, :, 1].astype(np.float32)),
        esc=_c(inputs["L0_ffn_expert_scale"].astype(np.float32).reshape(128, 1)),
        ident=ident, ones=ones, ind8=ind8,
    )
    per_core = []
    for k in range(NCORES):
        ts = slice(t * k, t * (k + 1))
        per_core.append(dict(ygT=_c(yg_full[:, ts]), xme=_c(xT[:, ts]), **shared))
    return per_core


def build_B1(t=1024):
    TH = t // 512
    nc = bass.Bass("TRN2", target_bir_lowering=False, debug=False)
    ygT = nc.dram_tensor("ygT", [D_INNER, t], BF16, kind="ExternalInput").ap()
    xme = nc.dram_tensor("xme", [D_MODEL, t], F32, kind="ExternalInput").ap()
    woutT = nc.dram_tensor("woutT", [D_INNER, D_MODEL], F32, kind="ExternalInput").ap()
    wrout = nc.dram_tensor("wrout", [D_MODEL, 128], F32, kind="ExternalInput").ap()
    wgate = nc.dram_tensor("wgate", [D_MODEL, 8192], F32, kind="ExternalInput").ap()
    wup = nc.dram_tensor("wup", [D_MODEL, 8192], F32, kind="ExternalInput").ap()
    wdown = nc.dram_tensor("wdown", [1024, D_MODEL], F32, kind="ExternalInput").ap()
    wsg = nc.dram_tensor("wsg", [D_MODEL, D_MODEL], F32, kind="ExternalInput").ap()
    wsu = nc.dram_tensor("wsu", [D_MODEL, D_MODEL], F32, kind="ExternalInput").ap()
    wsd = nc.dram_tensor("wsd", [D_MODEL, D_MODEL], F32, kind="ExternalInput").ap()
    a0 = nc.dram_tensor("a0", [128, 8], F32, kind="ExternalInput").ap()
    a1 = nc.dram_tensor("a1", [128, 8], F32, kind="ExternalInput").ap()
    esc = nc.dram_tensor("esc", [128, 1], F32, kind="ExternalInput").ap()
    ident = nc.dram_tensor("ident", [128, 128], F32, kind="ExternalInput").ap()
    ones = nc.dram_tensor("ones", [128, 128], F32, kind="ExternalInput").ap()
    ind8 = nc.dram_tensor("ind8", [8, 8 * 128], F32, kind="ExternalInput").ap()
    x2T = nc.dram_tensor("x2T", [D_MODEL, t], F32, kind="ExternalOutput").ap()

    p = Prog()
    import contextlib
    es = contextlib.ExitStack()
    SB = lambda nm, sh, dt: es.enter_context(nc.sbuf_tensor(nm, sh, dt))
    wslab = SB("wslab", [128, 8192], BF16)
    resA = SB("resA", [128, 16, t], F32)
    x1b = SB("x1b", [128, 16, t], BF16)
    bp = SB("bp", [128, 24 * t], BF16)
    ygs = SB("ygs", [128, 2, t], BF16)
    wos = SB("wos", [128, 4, 512], BF16)
    srow = SB("srow", [1, 6, 512], F32)
    lgf = SB("lgf", [128, t], F32)
    lgTt = SB("lgTt", [128, 128], F32)
    Eb = SB("Eb", [128, 128], F32)
    Ea = SB("Ea", [128, 128], F32)
    Ec = SB("Ec", [128, 128], F32)
    Mb = SB("Mb", [128, 128], F32)
    vc = SB("vc", [128, 8], F32)
    cnb = SB("cnb", [128, 128], F32)
    cnf = SB("cnf", [128, t], BF16)
    mixs = SB("mixs", [128, 8], F32)
    mixb = SB("mixb", [128, 8], BF16)
    a1s = SB("a1s", [128, 8], F32)
    esb = SB("esb", [128, 1], F32)
    efb = SB("efb", [8, t], BF16)
    ind8b = SB("ind8b", [8, 8 * 128], BF16)
    idb = SB("idb", [128, 128], F32)
    onesb = SB("onesb", [128, 128], F32)
    wrb = SB("wrb", [128, 16 * 128], BF16)
    stg = SB("stg", [128, 2, 512], F32)
    tb = SB("tb", [128, 2, 512], BF16)
    pall = es.enter_context(nc.psum_tensor("pall", [128, 8, 512], F32))
    ebs = bp[:, 0:8 * t]
    sgv = bp[:, 8 * t:16 * t]
    sgs = bp[:, 0:16 * t]
    comb = bp[:, 16 * t:24 * t]
    sq = lgf[:, 0:512]
    bank_rd = [None] * 8

    p.emit("sync", lambda s: s.dma_start(out=idb[:], in_=ident), inc="dws", inc_amt=16)
    p.emit("sync", lambda s: s.dma_start(out=onesb[:], in_=ones), inc="dws", inc_amt=16)
    t_pre_s = p.emit("sync", lambda s: s.dma_start(out=esb[:], in_=esc), inc="dws", inc_amt=16)
    p.emit("gpsimd", lambda g: g.dma_start(out=ind8b[:], in_=ind8), inc="dw", inc_amt=16)
    p.emit("gpsimd", lambda g: g.dma_start(out=mixs[:], in_=a0), inc="dw", inc_amt=16)
    p.emit("gpsimd", lambda g: g.dma_start(out=a1s[:], in_=a1), inc="dw", inc_amt=16)
    t_pre_g = p.emit("gpsimd", lambda g: g.dma_start(
        out=wrb[:].rearrange("p (a b) -> p a b", a=16),
        in_=wrout.rearrange("(a p) b -> p a b", p=128)), inc="dw", inc_amt=16)

    s0 = p.emit("scalar", lambda s: s.activation(out=Eb[:, 0:8], in_=mixs[:], func=AF.Square),
                wait=[t_pre_g], inc="s")
    s1 = p.emit("scalar", lambda s: s.activation(out=Eb[:, 8:16], in_=a1s[:], func=AF.Square),
                wait=[t_pre_g, s0], inc="s")
    v0 = p.emit("vector", lambda v: v.tensor_add(mixs[:], Eb[:, 0:8], Eb[:, 8:16]), wait=[s1], inc="v")
    v1_ = p.emit("vector", lambda v: v.reduce_sum(vc[:, 0:1], mixs[:], axis=AX.X), wait=[v0], inc="v")
    v2_ = p.emit("vector", lambda v: v.tensor_scalar_add(vc[:, 0:1], vc[:, 0:1], 1e-8), wait=[v1_], inc="v")
    v3_ = p.emit("vector", lambda v: v.reciprocal(vc[:, 1:2], vc[:, 0:1]), wait=[v2_], inc="v")
    v4_ = p.emit("vector", lambda v: v.tensor_scalar(mixs[:], mixs[:], vc[:, 1:2], None, OP.mult),
                 wait=[v3_], inc="v")
    v5_ = p.emit("vector", lambda v: v.tensor_scalar(mixs[:], mixs[:], esb[:], None, OP.mult),
                 wait=[v4_, t_pre_s], inc="v")
    t_mix = p.emit("vector", lambda v: v.tensor_copy(mixb[:], mixs[:]), wait=[v5_], inc="v")

    pre1_tok = {}
    wo_cons = {}
    yg_cons = {}
    stg_rd = [t_pre_s, t_pre_s]
    for jg in range(4):
        for k in range(32):
            slw = k % 4
            gate_w = wo_cons.get((jg, k - 4)) if k >= 4 else wo_cons.get((jg - 1, 28 + k))
            wld = p.emit("gpsimd", lambda g, jg=jg, k=k, slw=slw: g.dma_start(
                out=wos[:, slw, :], in_=woutT[128 * k:128 * (k + 1), 512 * jg:512 * (jg + 1)]),
                wait=[gate_w], inc="dwo%d" % slw, inc_amt=16)
            sly = k % 2
            gate_y = yg_cons.get((jg, k - 2)) if k >= 2 else yg_cons.get((jg - 1, 30 + k))
            yld = p.emit("gpsimd", lambda g, k=k, sly=sly: g.dma_start(
                out=ygs[:, sly, :], in_=ygT[128 * k:128 * (k + 1), :]),
                wait=[gate_y], inc="dyg%d" % sly, inc_amt=16)
            last = None
            for jj in range(4):
                for th in range(TH):
                    b = jj * 2 + th
                    last = p.emit("tensor", lambda tt, k=k, jj=jj, th=th, b=b, slw=slw, sly=sly: tt.matmul(
                        pall[:, b, :], wos[:, slw, 128 * jj:128 * (jj + 1)],
                        ygs[:, sly, 512 * th:512 * (th + 1)],
                        start=(k == 0), stop=(k == 31)),
                        wait=[wld, yld, bank_rd[b] if k == 0 else None], inc="t")
            wo_cons[(jg, k)] = last
            yg_cons[(jg, k)] = last
        for jj in range(4):
            for th in range(TH):
                b = jj * 2 + th
                j = 4 * jg + jj
                sl = (jg * 8 + jj * 2 + th) % 2
                ld = p.emit("sync", lambda s, j=j, th=th, sl=sl: s.dma_start(
                    out=stg[:, sl, :], in_=xme[128 * j:128 * (j + 1), 512 * th:512 * (th + 1)]),
                    wait=[stg_rd[sl]], inc="dst%d" % sl, inc_amt=16)
                tk = p.emit("vector", lambda v, j=j, th=th, b=b, sl=sl: v.tensor_add(
                    resA[:, j, 512 * th:512 * (th + 1)], stg[:, sl, :], pall[:, b, :]),
                    wait=[wo_cons[(jg, 31)], ld], inc="v")
                pre1_tok[(j, th)] = tk
                stg_rd[sl] = tk
                bank_rd[b] = tk

    srow_rd = [None]
    def emit_ln(src_ready, x1b_out):
        apply_toks = {}
        sq_rd = [None]
        for th in range(TH):
            mm1 = mm2 = None
            for j in range(16):
                sqt = p.emit("scalar", lambda s, j=j, th=th: s.activation(
                    out=sq, in_=resA[:, j, 512 * th:512 * (th + 1)], func=AF.Square),
                    wait=[src_ready[(j, th)], sq_rd[0]], inc="s")
                mm1 = p.emit("tensor", lambda tt, j=j, th=th: tt.matmul(
                    pall[0:1, 0, :], onesb[:, 0:1], resA[:, j, 512 * th:512 * (th + 1)],
                    start=(j == 0), stop=(j == 15)),
                    wait=[src_ready[(j, th)], bank_rd[0] if j == 0 else None], inc="t")
                mm2 = p.emit("tensor", lambda tt, j=j: tt.matmul(
                    pall[0:1, 1, :], onesb[:, 0:1], sq,
                    start=(j == 0), stop=(j == 15)),
                    wait=[sqt, bank_rd[1] if j == 0 else None], inc="t")
                sq_rd[0] = mm2
            a = p.emit("vector", lambda v: v.tensor_scalar(
                srow[0:1, 0, :], pall[0:1, 0, :], 1.0 / 2048.0, None, OP.mult),
                wait=[mm1, srow_rd[0]], inc="v")
            bank_rd[0] = a
            b2 = p.emit("vector", lambda v: v.tensor_mul(
                srow[0:1, 2, :], srow[0:1, 0, :], srow[0:1, 0, :]), wait=[a], inc="v")
            c = p.emit("vector", lambda v: v.scalar_tensor_tensor(
                srow[0:1, 3, :], pall[0:1, 1, :], 1.0 / 2048.0, srow[0:1, 2, :],
                OP.mult, OP.subtract),
                wait=[mm2, b2], inc="v")
            bank_rd[1] = c
            c2 = p.emit("vector", lambda v: v.tensor_scalar_add(
                srow[0:1, 3, :], srow[0:1, 3, :], 1e-5), wait=[c], inc="v")
            d = p.emit("scalar", lambda s: s.activation(
                out=srow[0:1, 4, :], in_=srow[0:1, 3, :], func=AF.Sqrt),
                wait=[c2], inc="s")
            e = p.emit("vector", lambda v: v.reciprocal(srow[0:1, 1, :], srow[0:1, 4, :]),
                       wait=[d], inc="v")
            bm = p.emit("tensor", lambda tt: tt.matmul(
                pall[:, 2, :], onesb[0:1, :], srow[0:1, 0, :], start=True, stop=True),
                wait=[a, bank_rd[2]], inc="t")
            br = p.emit("tensor", lambda tt: tt.matmul(
                pall[:, 3, :], onesb[0:1, :], srow[0:1, 1, :], start=True, stop=True),
                wait=[e, bank_rd[3]], inc="t")
            srow_rd[0] = br
            f2 = None
            for j in range(16):
                f1 = p.emit("vector", lambda v, j=j, th=th: v.tensor_sub(
                    resA[:, j, 512 * th:512 * (th + 1)],
                    resA[:, j, 512 * th:512 * (th + 1)], pall[:, 2, :]),
                    wait=[bm], inc="v")
                f2 = p.emit("vector", lambda v, j=j, th=th: v.tensor_mul(
                    resA[:, j, 512 * th:512 * (th + 1)],
                    resA[:, j, 512 * th:512 * (th + 1)], pall[:, 3, :]),
                    wait=[f1, br], inc="v")
                tokj = f2
                if x1b_out is not None:
                    tokj = p.emit("vector", lambda v, j=j, th=th: v.tensor_copy(
                        x1b_out[:, j, 512 * th:512 * (th + 1)],
                        resA[:, j, 512 * th:512 * (th + 1)]),
                        wait=[f2], inc="v")
                apply_toks[(j, th)] = tokj
            bank_rd[2] = f2
            bank_rd[3] = f2
        return apply_toks

    x1_tok = emit_ln(pre1_tok, x1b)

    lg_ev = [None] * TH
    for th in range(TH):
        mm = None
        for d in range(16):
            mm = p.emit("tensor", lambda tt, d=d, th=th: tt.matmul(
                pall[:, 4, :], wrb[:, 128 * d:128 * (d + 1)], x1b[:, d, 512 * th:512 * (th + 1)],
                start=(d == 0), stop=(d == 15)),
                wait=[x1_tok[(d, th)], t_pre_g, bank_rd[4] if d == 0 else None], inc="t")
        lg_ev[th] = p.emit("scalar", lambda s, th=th: s.copy(
            out=lgf[:, 512 * th:512 * (th + 1)], in_=pall[:, 4, :]), wait=[mm], inc="s")
        bank_rd[4] = lg_ev[th]
    cn_done = [None] * (t // 128)
    lgT_rd = [None]
    for tt_ in range(t // 128):
        th = (tt_ * 128) // 512
        tr1 = p.emit("tensor", lambda tt, tt_=tt_: tt.transpose(
            pall[:, 5, 0:128], lgf[:, 128 * tt_:128 * (tt_ + 1)], idb[:]),
            wait=[lg_ev[th], bank_rd[5]], inc="t")
        ev1 = p.emit("vector", lambda v: v.tensor_copy(lgTt[:], pall[:, 5, 0:128]),
                     wait=[tr1, lgT_rd[0]], inc="v")
        bank_rd[5] = ev1
        m0 = p.emit("vector", lambda v: v.reduce_max(vc[:, 0:1], lgTt[:], axis=AX.X, negate=True),
                    wait=[ev1], inc="v")
        ex = p.emit("scalar", lambda s: s.activation(out=Eb[:], in_=lgTt[:], func=AF.Exp,
                                                     bias=vc[:, 0:1]),
                    wait=[m0], inc="s")
        lgT_rd[0] = ex
        sS = p.emit("vector", lambda v: v.reduce_sum(vc[:, 1:2], Eb[:], axis=AX.X), wait=[ex], inc="v")
        vv1 = p.emit("vector", lambda v: v.reduce_max(vc[:, 2:3], Eb[:], axis=AX.X), wait=[ex], inc="v")
        mk1 = p.emit("vector", lambda v: v.tensor_scalar(Mb[:], Eb[:], vc[:, 2:3], None, OP.is_ge),
                     wait=[vv1], inc="v")
        e1 = p.emit("vector", lambda v: v.scalar_tensor_tensor(Ea[:], Mb[:], -1e30, Eb[:], OP.mult, OP.add),
                    wait=[mk1], inc="v")
        vv2 = p.emit("vector", lambda v: v.reduce_max(vc[:, 3:4], Ea[:], axis=AX.X), wait=[e1], inc="v")
        mk2 = p.emit("vector", lambda v: v.tensor_scalar(Mb[:], Ea[:], vc[:, 3:4], None, OP.is_ge),
                     wait=[vv2], inc="v")
        e2 = p.emit("vector", lambda v: v.scalar_tensor_tensor(Ec[:], Mb[:], -1e30, Ea[:], OP.mult, OP.add),
                    wait=[mk2], inc="v")
        vv3 = p.emit("vector", lambda v: v.reduce_max(vc[:, 4:5], Ec[:], axis=AX.X), wait=[e2], inc="v")
        mk3 = p.emit("vector", lambda v: v.tensor_scalar(Mb[:], Ec[:], vc[:, 4:5], None, OP.is_ge),
                     wait=[vv3], inc="v")
        e3 = p.emit("vector", lambda v: v.scalar_tensor_tensor(Ea[:], Mb[:], -1e30, Ec[:], OP.mult, OP.add),
                    wait=[mk3], inc="v")
        vv4 = p.emit("vector", lambda v: v.reduce_max(vc[:, 5:6], Ea[:], axis=AX.X), wait=[e3], inc="v")
        t1 = p.emit("vector", lambda v: v.tensor_add(vc[:, 6:7], vc[:, 2:3], vc[:, 3:4]),
                    wait=[vv2, vv1], inc="v")
        t2 = p.emit("vector", lambda v: v.tensor_add(vc[:, 7:8], vc[:, 6:7], vc[:, 4:5]),
                    wait=[t1, vv3], inc="v")
        t3 = p.emit("vector", lambda v: v.tensor_add(vc[:, 6:7], vc[:, 7:8], vc[:, 5:6]),
                    wait=[t2, vv4], inc="v")
        den = p.emit("vector", lambda v: v.scalar_tensor_tensor(vc[:, 7:8], vc[:, 1:2], 1e-8,
                                                                vc[:, 6:7], OP.mult, OP.add),
                     wait=[t3, sS], inc="v")
        rd = p.emit("vector", lambda v: v.reciprocal(vc[:, 6:7], vc[:, 7:8]), wait=[den], inc="v")
        mk4 = p.emit("vector", lambda v: v.tensor_scalar(Mb[:], Eb[:], vc[:, 5:6], None, OP.is_ge),
                     wait=[vv4], inc="v")
        cm = p.emit("vector", lambda v: v.tensor_mul(cnb[:], Eb[:], Mb[:]), wait=[mk4], inc="v")
        cs = p.emit("vector", lambda v: v.tensor_scalar(cnb[:], cnb[:], vc[:, 6:7], None, OP.mult),
                    wait=[cm, rd], inc="v")
        tr2 = p.emit("tensor", lambda tt, tt_=tt_: tt.transpose(
            pall[:, 5, 0:128], cnb[:], idb[:]), wait=[cs, bank_rd[5]], inc="t")
        ev2 = p.emit("vector", lambda v, tt_=tt_: v.tensor_copy(
            cnf[:, 128 * tt_:128 * (tt_ + 1)], pall[:, 5, 0:128]), wait=[tr2], inc="v")
        bank_rd[5] = ev2
        cn_done[tt_] = ev2
    ef_ev = [None] * TH
    for th in range(TH):
        mm = p.emit("tensor", lambda tt, th=th: tt.matmul(
            pall[0:8, 6, :], mixb[:], cnf[:, 512 * th:512 * (th + 1)], start=True, stop=True),
            wait=[t_mix, bank_rd[6]] + cn_done[4 * th:4 * (th + 1)], inc="t")
        ef_ev[th] = p.emit("vector", lambda v, th=th: v.tensor_copy(
            efb[:, 512 * th:512 * (th + 1)], pall[0:8, 6, :]), wait=[mm], inc="v")
        bank_rd[6] = ef_ev[th]
    eb_tok = {}
    for k8 in range(8):
        for th in range(TH):
            b = 4 + (k8 * TH + th) % 2
            mm = p.emit("tensor", lambda tt, k8=k8, th=th, b=b: tt.matmul(
                pall[:, b, :], ind8b[:, 128 * k8:128 * (k8 + 1)], efb[0:8, 512 * th:512 * (th + 1)],
                start=True, stop=True),
                wait=[ef_ev[th], t_pre_g, bank_rd[b]], inc="t")
            ev = p.emit("scalar", lambda s, k8=k8, th=th, b=b: s.copy(
                out=ebs[:, k8 * t + 512 * th:k8 * t + 512 * (th + 1)], in_=pall[:, b, :]),
                wait=[mm], inc="s")
            bank_rd[b] = ev
            eb_tok[(k8, th)] = ev

    def slab_load(src, cols, gate, n_dma=16, rows=128):
        tk = None
        for d in range(n_dma):
            w = 8192 // n_dma
            tk = p.emit("gpsimd", lambda g, d=d, w=w: g.dma_start(
                out=wslab[:, w * d:w * (d + 1)], in_=src[rows * d:rows * (d + 1), cols]),
                wait=[gate if d == 0 else None], inc="dsl", inc_amt=16)
        return tk

    comb_tok = {}
    slab_cons = None
    sgv_rd = {}
    tb_rd = [None, None]
    for k8 in range(8):
        sg_tok = {}
        for ih in range(2):
            ldg = slab_load(wgate, slice(1024 * k8 + 512 * ih, 1024 * k8 + 512 * (ih + 1)), slab_cons)
            for i4 in range(4):
                i = 4 * ih + i4
                for th in range(TH):
                    b = (i4 * TH + th) % 4
                    mm = p.emit("tensor", lambda tt, i4=i4, th=th, b=b: [tt.matmul(
                        pall[:, b, :], wslab[:, 512 * d + 128 * i4:512 * d + 128 * (i4 + 1)],
                        x1b[:, d, 512 * th:512 * (th + 1)],
                        start=(d == 0), stop=(d == 15)) for d in range(16)][-1],
                        wait=[ldg, x1_tok[(15, th)], bank_rd[b]], inc="t")
                    sg_tok[(i, th)] = p.emit("scalar", lambda s, i=i, th=th, b=b: s.activation(
                        out=sgv[:, i * t + 512 * th:i * t + 512 * (th + 1)], in_=pall[:, b, :],
                        func=AF.Silu),
                        wait=[mm, sgv_rd.get((i, th))], inc="s")
                    bank_rd[b] = sg_tok[(i, th)]
                    slab_cons = mm
        for ih in range(2):
            ldu = slab_load(wup, slice(1024 * k8 + 512 * ih, 1024 * k8 + 512 * (ih + 1)), slab_cons)
            for i4 in range(4):
                i = 4 * ih + i4
                for th in range(TH):
                    b = (i4 * TH + th) % 4
                    mm = p.emit("tensor", lambda tt, i4=i4, th=th, b=b: [tt.matmul(
                        pall[:, b, :], wslab[:, 512 * d + 128 * i4:512 * d + 128 * (i4 + 1)],
                        x1b[:, d, 512 * th:512 * (th + 1)],
                        start=(d == 0), stop=(d == 15)) for d in range(16)][-1],
                        wait=[ldu, bank_rd[b]], inc="t")
                    slab_cons = mm
                    sltb = (i * TH + th) % 2
                    upe = p.emit("vector", lambda v, th=th, b=b, k8=k8, sltb=sltb: v.tensor_mul(
                        tb[:, sltb, :], pall[:, b, :],
                        ebs[:, k8 * t + 512 * th:k8 * t + 512 * (th + 1)]),
                        wait=[mm, eb_tok[(k8, th)], tb_rd[sltb]], inc="v")
                    bank_rd[b] = upe
                    if k8 == 0:
                        pr = p.emit("vector", lambda v, i=i, th=th, sltb=sltb: v.tensor_mul(
                            comb[:, i * t + 512 * th:i * t + 512 * (th + 1)],
                            sgv[:, i * t + 512 * th:i * t + 512 * (th + 1)], tb[:, sltb, :]),
                            wait=[upe, sg_tok[(i, th)]], inc="v")
                        comb_tok[(i, th)] = pr
                        tb_rd[sltb] = pr
                        sgv_rd[(i, th)] = pr
                    else:
                        pr = p.emit("vector", lambda v, i=i, th=th, sltb=sltb: v.tensor_mul(
                            tb[:, sltb, :], sgv[:, i * t + 512 * th:i * t + 512 * (th + 1)],
                            tb[:, sltb, :]),
                            wait=[upe, sg_tok[(i, th)]], inc="v")
                        sgv_rd[(i, th)] = pr
                        ad = p.emit("vector", lambda v, i=i, th=th, sltb=sltb: v.tensor_add(
                            comb[:, i * t + 512 * th:i * t + 512 * (th + 1)],
                            comb[:, i * t + 512 * th:i * t + 512 * (th + 1)], tb[:, sltb, :]),
                            wait=[pr, comb_tok[(i, th)]], inc="v")
                        comb_tok[(i, th)] = ad
                        tb_rd[sltb] = ad

    down_tok = {}
    for fh in range(2):
        ldd = slab_load(wdown, slice(1024 * fh, 1024 * (fh + 1)), slab_cons, n_dma=8)
        for f8 in range(8):
            f = 8 * fh + f8
            for th in range(TH):
                b = (f8 * TH + th) % 4
                mm = p.emit("tensor", lambda tt, f8=f8, th=th, b=b: [tt.matmul(
                    pall[:, b, :], wslab[:, 1024 * ic + 128 * f8:1024 * ic + 128 * (f8 + 1)],
                    comb[:, ic * t + 512 * th:ic * t + 512 * (th + 1)],
                    start=(ic == 0), stop=(ic == 7)) for ic in range(8)][-1],
                    wait=[ldd, bank_rd[b]] + [comb_tok[(ic, th)] for ic in range(8)], inc="t")
                slab_cons = mm
                ra = p.emit("vector", lambda v, f=f, th=th, b=b: v.tensor_add(
                    resA[:, f, 512 * th:512 * (th + 1)],
                    resA[:, f, 512 * th:512 * (th + 1)], pall[:, b, :]),
                    wait=[mm, x1_tok[(f, th)]], inc="v")
                bank_rd[b] = ra
                down_tok[(f, th)] = ra

    sgs_tok = {}
    for qt in range(4):
        ldsg = slab_load(wsg, slice(512 * qt, 512 * (qt + 1)), slab_cons)
        for f4 in range(4):
            f = 4 * qt + f4
            for th in range(TH):
                b = (f4 * TH + th) % 4
                mm = p.emit("tensor", lambda tt, f4=f4, th=th, b=b: [tt.matmul(
                    pall[:, b, :], wslab[:, 512 * d + 128 * f4:512 * d + 128 * (f4 + 1)],
                    x1b[:, d, 512 * th:512 * (th + 1)],
                    start=(d == 0), stop=(d == 15)) for d in range(16)][-1],
                    wait=[ldsg, bank_rd[b]], inc="t")
                slab_cons = mm
                sgs_tok[(f, th)] = p.emit("scalar", lambda s, f=f, th=th, b=b: s.activation(
                    out=sgs[:, f * t + 512 * th:f * t + 512 * (th + 1)], in_=pall[:, b, :],
                    func=AF.Silu),
                    wait=[mm], inc="s")
                bank_rd[b] = sgs_tok[(f, th)]
    for qt in range(4):
        ldsu = slab_load(wsu, slice(512 * qt, 512 * (qt + 1)), slab_cons)
        for f4 in range(4):
            f = 4 * qt + f4
            for th in range(TH):
                b = (f4 * TH + th) % 4
                mm = p.emit("tensor", lambda tt, f4=f4, th=th, b=b: [tt.matmul(
                    pall[:, b, :], wslab[:, 512 * d + 128 * f4:512 * d + 128 * (f4 + 1)],
                    x1b[:, d, 512 * th:512 * (th + 1)],
                    start=(d == 0), stop=(d == 15)) for d in range(16)][-1],
                    wait=[ldsu, bank_rd[b]], inc="t")
                slab_cons = mm
                up2 = p.emit("vector", lambda v, f=f, th=th, b=b: v.tensor_mul(
                    sgs[:, f * t + 512 * th:f * t + 512 * (th + 1)],
                    sgs[:, f * t + 512 * th:f * t + 512 * (th + 1)], pall[:, b, :]),
                    wait=[mm, sgs_tok[(f, th)]], inc="v")
                bank_rd[b] = up2
                sgs_tok[(f, th)] = up2
    for qt in range(4):
        ldsd = slab_load(wsd, slice(512 * qt, 512 * (qt + 1)), slab_cons)
        for f4 in range(4):
            f = 4 * qt + f4
            for th in range(TH):
                b = (f4 * TH + th) % 4
                mm = p.emit("tensor", lambda tt, f4=f4, th=th, b=b: [tt.matmul(
                    pall[:, b, :], wslab[:, 512 * d + 128 * f4:512 * d + 128 * (f4 + 1)],
                    sgs[:, d * t + 512 * th:d * t + 512 * (th + 1)],
                    start=(d == 0), stop=(d == 15)) for d in range(16)][-1],
                    wait=[ldsd, bank_rd[b]] + [sgs_tok[(d_, th)] for d_ in range(16)], inc="t")
                slab_cons = mm
                ra = p.emit("vector", lambda v, f=f, th=th, b=b: v.tensor_add(
                    resA[:, f, 512 * th:512 * (th + 1)],
                    resA[:, f, 512 * th:512 * (th + 1)], pall[:, b, :]),
                    wait=[mm, down_tok[(f, th)]], inc="v")
                bank_rd[b] = ra
                down_tok[(f, th)] = ra

    x2_tok = emit_ln(down_tok, None)
    st_last = None
    for j in range(16):
        st_last = p.emit("sync", lambda s, j=j: s.dma_start(
            out=x2T[128 * j:128 * (j + 1), :], in_=resA[:, j, :]),
            wait=[x2_tok[(j, 0)], x2_tok[(j, TH - 1)]], inc="dso", inc_amt=16)
    p.emit("gpsimd", lambda g: None, wait=[st_last])
    with nc.Block() as block:
        p.build(nc, block)
    es.close()
    return nc


def run_B1(inputs, yg_full, t=1024):
    per_core = host_prep_B1(inputs, yg_full, t=t)
    nc = build_B1(t=t)
    res = run_bass_kernel_spmd(nc, per_core, core_ids=list(range(NCORES)))
    return np.concatenate([r["x2T"] for r in res.results], axis=1)

def host_prep_B2(inputs, x2_full, t=1024):
    HALO = 64
    T = x2_full.shape[1]
    def depair(n_heads):
        perm = []
        for h in range(n_heads):
            perm += [h * 128 + 2 * i for i in range(64)]
            perm += [h * 128 + 2 * i + 1 for i in range(64)]
        return np.array(perm)
    wqp = _c(inputs["L1_attn_wq"].astype(np.float32)[:, depair(16)])
    wkp = _c(inputs["L1_attn_wk"].astype(np.float32)[:, depair(4)])
    wv = _c(inputs["L1_attn_wv"].astype(np.float32))
    wo = _c(inputs["L1_attn_wo"].astype(np.float32))
    sink = _c(inputs["L1_attn_sink"].astype(np.float32).reshape(1, 16))
    qkv = inputs["L1_ffn_qkv_weight"].astype(np.float32)
    rows = np.concatenate([np.arange((h // 4) * 768 + 640, (h // 4) * 768 + 768)
                           for h in range(16)])
    wvselT = _c(qkv[rows].T)
    projT = _c(inputs["L1_ffn_proj_weight"].astype(np.float32).T)
    ident = np.eye(128, dtype=np.float32)
    ones = np.ones((128, 128), np.float32)
    ind8 = np.zeros((8, 8 * 128), np.float32)
    for j in range(8):
        ind8[j, j * 128:(j + 1) * 128] = 1.0
    permM = np.zeros((128, 128), np.float32)
    for i in range(64):
        permM[64 + i, i] = 1.0
        permM[i, 64 + i] = 1.0
    lnp = np.stack([inputs["L1_ffn_ln1_weight"], inputs["L1_ffn_ln1_bias"],
                    inputs["L1_ffn_ln2_weight"], inputs["L1_ffn_ln2_bias"]]
                   ).astype(np.float32).reshape(4, 16, 128).transpose(2, 0, 1)
    lnp = _c(lnp)
    band = np.zeros((128, 256), np.float32)
    for i in range(128):
        for j in range(256):
            if j - 128 <= i <= j:
                band[i, j] = 1.0
    shared = dict(
        wqp=wqp, wkp=wkp, wv=wv, wo=wo, sinkr=sink, wvselT=wvselT, projT=projT,
        wre=_c(inputs["L1_ffn_router_weight"].astype(np.float32).reshape(16, 128, 8).transpose(1, 0, 2).reshape(128, 128)),
        mw1=_c(inputs["L1_ffn_moe_w1"].astype(np.float32)),
        mw2=_c(inputs["L1_ffn_moe_w2"].astype(np.float32)),
        ident=ident, ones=ones, ind8=ind8, permM=permM, lnp=lnp,
    )
    per_core = []
    for c in range(NCORES):
        t0 = t * c
        x2h = np.zeros((D_MODEL, t + 2 * HALO), np.float32)
        lo, hi = t0 - HALO, t0 + t + HALO
        slo, shi = max(lo, 0), min(hi, T)
        x2h[:, slo - lo:shi - lo] = x2_full[:, slo:shi]
        pos = np.arange(lo, hi, dtype=np.float64)
        inv = 1.0 / (10000.0 ** (np.arange(64) / 64.0))
        f = pos[None, :] * inv[:, None]
        cos = np.cos(f); sin = np.sin(f)
        cs1 = np.concatenate([cos, cos], 0).astype(np.float32)
        cs2 = np.concatenate([-sin, sin], 0).astype(np.float32)
        m0 = band.copy()
        m7 = band.copy()
        if c == 0:
            for j in range(256):
                if j - HALO + 0 < 0:
                    m0[:, j] = 0.0
        if c == NCORES - 1:
            for j in range(256):
                if t0 + 128 * 7 - HALO + j >= T:
                    m7[:, j] = 0.0
        per_core.append(dict(x2h=_c(x2h), cs1=cs1, cs2=cs2,
                             mask0=m0, maskI=band, mask7=m7, **shared))
    return per_core


def build_B2(t=1024):
    TH = t // 512
    HALO = 64
    th_ = t + 2 * HALO
    nc = bass.Bass("TRN2", target_bir_lowering=False, debug=False)
    x2h = nc.dram_tensor("x2h", [D_MODEL, th_], F32, kind="ExternalInput").ap()
    wqp = nc.dram_tensor("wqp", [D_MODEL, 2048], F32, kind="ExternalInput").ap()
    wkp = nc.dram_tensor("wkp", [D_MODEL, 512], F32, kind="ExternalInput").ap()
    wv = nc.dram_tensor("wv", [D_MODEL, 512], F32, kind="ExternalInput").ap()
    wo = nc.dram_tensor("wo", [D_MODEL, 2048], F32, kind="ExternalInput").ap()
    sinkr = nc.dram_tensor("sinkr", [1, 16], F32, kind="ExternalInput").ap()
    wvselT = nc.dram_tensor("wvselT", [D_MODEL, 2048], F32, kind="ExternalInput").ap()
    projT = nc.dram_tensor("projT", [D_MODEL, 2048], F32, kind="ExternalInput").ap()
    wre = nc.dram_tensor("wre", [128, 128], F32, kind="ExternalInput").ap()
    mw1 = nc.dram_tensor("mw1", [8, D_MODEL, 4096], F32, kind="ExternalInput").ap()
    mw2 = nc.dram_tensor("mw2", [8, 4096, D_MODEL], F32, kind="ExternalInput").ap()
    ident = nc.dram_tensor("ident", [128, 128], F32, kind="ExternalInput").ap()
    ones = nc.dram_tensor("ones", [128, 128], F32, kind="ExternalInput").ap()
    ind8 = nc.dram_tensor("ind8", [8, 8 * 128], F32, kind="ExternalInput").ap()
    permM = nc.dram_tensor("permM", [128, 128], F32, kind="ExternalInput").ap()
    lnp = nc.dram_tensor("lnp", [128, 4 * 16], F32, kind="ExternalInput").ap()
    cs1 = nc.dram_tensor("cs1", [128, th_], F32, kind="ExternalInput").ap()
    cs2 = nc.dram_tensor("cs2", [128, th_], F32, kind="ExternalInput").ap()
    mask0 = nc.dram_tensor("mask0", [128, 256], F32, kind="ExternalInput").ap()
    maskI = nc.dram_tensor("maskI", [128, 256], F32, kind="ExternalInput").ap()
    mask7 = nc.dram_tensor("mask7", [128, 256], F32, kind="ExternalInput").ap()
    outT = nc.dram_tensor("outT", [D_MODEL, t], F32, kind="ExternalOutput").ap()

    p = Prog()
    import contextlib
    es = contextlib.ExitStack()
    SB = lambda nm, sh, dt: es.enter_context(nc.sbuf_tensor(nm, sh, dt))
    XR = SB("XR", [128, 16 * th_], BF16)
    P1 = SB("P1", [128, 16, t], F32)
    ob = SB("ob", [128, 16 * t], BF16)
    kbuf = SB("kbuf", [128, 4 * th_], BF16)
    krob = SB("krob", [128, 4 * th_], BF16)
    vbuf = SB("vbuf", [128, 4 * th_], BF16)
    qhb = SB("qhb", [128, t], BF16)
    qrob = SB("qrob", [128, t], BF16)
    wslab = SB("wslab", [128, 8192], BF16)
    cs1b = SB("cs1b", [128, th_], BF16)
    cs2b = SB("cs2b", [128, th_], BF16)
    maskb = SB("maskb", [128, 3, 256], BF16)
    scr = SB("scr", [128, 2, 512], F32)
    srow = SB("srow", [1, 3, 512], F32)
    sq = SB("sq", [128, 512], F32)
    stg = SB("stg", [128, 2, 512], F32)
    attb = SB("attb", [128, 256], BF16)
    attTb = SB("attTb", [128, 256], BF16)
    g8 = SB("g8", [128, 8], F32)
    gE = SB("gE", [128, 8], F32)
    gE1 = SB("gE1", [128, 8], F32)
    gM = SB("gM", [128, 8], F32)
    rs = SB("rs", [128, 4], F32)
    konst = SB("konst", [128, 1], F32)
    idb = SB("idb", [128, 128], F32)
    idbB = SB("idbB", [128, 128], BF16)
    onesb = SB("onesb", [128, 128], F32)
    onesbB = SB("onesbB", [128, 128], BF16)
    ind8b = SB("ind8b", [8, 8 * 128], BF16)
    permB = SB("permB", [128, 128], BF16)
    lnpb = SB("lnpb", [128, 4 * 16], F32)
    sinkbc = SB("sinkbc", [128, 16], F32)
    sinkb = SB("sinkb", [1, 16], F32)
    pall = es.enter_context(nc.psum_tensor("pall", [128, 7, 512], F32))
    pbt = es.enter_context(nc.psum_tensor("pbt", [128, 1024], BF16))
    x2hb = XR[:].rearrange("p (a b) -> p a b", a=16)
    lnreg = XR[:, 0:16 * t].rearrange("p (a b) -> p a b", a=16)
    gelu = ob[:].rearrange("p (a b) -> p a b", a=16)
    bank_rd = [None] * 8

    p.emit("sync", lambda s: s.dma_start(out=idb[:], in_=ident), inc="dws", inc_amt=16)
    p.emit("sync", lambda s: s.dma_start(out=onesb[:], in_=ones), inc="dws", inc_amt=16)
    p.emit("gpsimd", lambda g: g.dma_start(out=maskb[:, 0, :], in_=mask0), inc="dw", inc_amt=16)
    p.emit("gpsimd", lambda g: g.dma_start(out=maskb[:, 1, :], in_=maskI), inc="dw", inc_amt=16)
    p.emit("gpsimd", lambda g: g.dma_start(out=maskb[:, 2, :], in_=mask7), inc="dw", inc_amt=16)
    p.emit("sync", lambda s: s.dma_start(out=lnpb[:], in_=lnp), inc="dws", inc_amt=16)
    t_pre_s = p.emit("sync", lambda s: s.dma_start(out=sinkb[:], in_=sinkr), inc="dws", inc_amt=16)
    p.emit("gpsimd", lambda g: g.dma_start(out=idbB[:], in_=ident), inc="dw", inc_amt=16)
    p.emit("gpsimd", lambda g: g.dma_start(out=onesbB[:], in_=ones), inc="dw", inc_amt=16)
    p.emit("gpsimd", lambda g: g.dma_start(out=ind8b[:], in_=ind8), inc="dw", inc_amt=16)
    p.emit("gpsimd", lambda g: g.dma_start(out=permB[:], in_=permM), inc="dw", inc_amt=16)
    p.emit("gpsimd", lambda g: g.dma_start(out=cs1b[:], in_=cs1), inc="dw", inc_amt=16)
    p.emit("gpsimd", lambda g: g.dma_start(out=cs2b[:], in_=cs2), inc="dw", inc_amt=16)
    t_pre_g = None
    for d in range(16):
        t_pre_g = p.emit("gpsimd", lambda g, d=d: g.dma_start(
            out=x2hb[:, d, :], in_=x2h[128 * d:128 * (d + 1), :]), inc="dw", inc_amt=16)
    t_k0 = p.emit("vector", lambda v: v.memset(konst[:], 0.08838834764831845), inc="v")
    mmsk = p.emit("tensor", lambda tt: tt.matmul(
        pall[:, 6, 0:16], onesb[0:1, :], sinkb[:], start=True, stop=True),
        wait=[t_pre_s, ("dws", 6 * 16) if False else None], inc="t")
    t_sink = p.emit("vector", lambda v: v.tensor_copy(sinkbc[:], pall[:, 6, 0:16]),
                    wait=[mmsk], inc="v")
    bank_rd[6] = t_sink

    def slab_load(src_fn, n_dma, gate):
        tk = None
        w = 8192 // n_dma
        for d in range(n_dma):
            tk = p.emit("gpsimd", lambda g, d=d, w=w: g.dma_start(
                out=wslab[:, w * d:w * (d + 1)], in_=src_fn(d)),
                wait=[gate if d == 0 else None], inc="dsl", inc_amt=16)
        return tk

    slab_cons = None
    TC = [(0, 512), (512, 1024), (1024, 1152)]
    kv_ev = {}
    for which, (wsrc, dst) in enumerate([(wkp, kbuf), (wv, vbuf)]):
        ld = slab_load(lambda d, wsrc=wsrc: wsrc[128 * d:128 * (d + 1), :], 16, slab_cons)
        for g4 in range(4):
            for ci, (c0, c1) in enumerate(TC):
                b = (g4 + ci) % 4
                mm = p.emit("tensor", lambda tt, g4=g4, c0=c0, c1=c1, b=b: [tt.matmul(
                    pall[:, b, 0:c1 - c0], wslab[:, 512 * d + 128 * g4:512 * d + 128 * (g4 + 1)],
                    x2hb[:, d, c0:c1], start=(d == 0), stop=(d == 15)) for d in range(16)][-1],
                    wait=[ld, t_pre_g, bank_rd[b]], inc="t")
                slab_cons = mm
                ev = p.emit("scalar", lambda s, dst=dst, g4=g4, c0=c0, c1=c1, b=b: s.copy(
                    out=dst[:, g4 * th_ + c0:g4 * th_ + c1], in_=pall[:, b, 0:c1 - c0]),
                    wait=[mm], inc="s")
                bank_rd[b] = ev
                kv_ev[(which, g4, ci)] = ev
    kro_tok = {}
    scr_rd = [None, None]
    for g4 in range(4):
        mm = p.emit("tensor", lambda tt, g4=g4: [tt.matmul(
            pall[:, 4 + (g4 % 2), 0:512] if c1 - c0 == 512 else pall[:, 4 + (g4 % 2), 0:c1 - c0],
            permB[:], kbuf[:, g4 * th_ + c0:g4 * th_ + c1], start=True, stop=True)
            for (c0, c1) in [TC[0]]][-1],
            wait=[kv_ev[(0, g4, 0)], bank_rd[4 + (g4 % 2)]], inc="t")
        toks = []
        for ci, (c0, c1) in enumerate(TC):
            b = 4 + (g4 + ci) % 2
            if ci > 0 or True:
                mm = p.emit("tensor", lambda tt, g4=g4, c0=c0, c1=c1, b=b: tt.matmul(
                    pall[:, b, 0:c1 - c0], permB[:], kbuf[:, g4 * th_ + c0:g4 * th_ + c1],
                    start=True, stop=True),
                    wait=[kv_ev[(0, g4, ci)], bank_rd[b]], inc="t")
            t1 = p.emit("vector", lambda v, g4=g4, c0=c0, c1=c1: v.tensor_mul(
                krob[:, g4 * th_ + c0:g4 * th_ + c1], kbuf[:, g4 * th_ + c0:g4 * th_ + c1],
                cs1b[:, c0:c1]),
                wait=[kv_ev[(0, g4, ci)], kro_tok.get((g4 - 1, ci), None)], inc="v")
            t2 = p.emit("vector", lambda v, c0=c0, c1=c1, b=b, sl=ci % 2: v.tensor_mul(
                scr[:, sl, 0:c1 - c0], pall[:, b, 0:c1 - c0], cs2b[:, c0:c1]),
                wait=[mm, scr_rd[ci % 2]], inc="v")
            bank_rd[b] = t2
            t3 = p.emit("vector", lambda v, g4=g4, c0=c0, c1=c1, sl=ci % 2: v.tensor_add(
                krob[:, g4 * th_ + c0:g4 * th_ + c1], krob[:, g4 * th_ + c0:g4 * th_ + c1],
                scr[:, sl, 0:c1 - c0]),
                wait=[t1, t2], inc="v")
            scr_rd[ci % 2] = t3
            toks.append(t3)
            kro_tok[(g4, ci)] = t3
    vT_tok = {}
    for g4 in range(4):
        for kt in range(9):
            mm = p.emit("tensor", lambda tt, g4=g4, kt=kt: tt.transpose(
                pbt[:, 0:128] if kt % 2 == 0 else pbt[:, 128:256],
                vbuf[:, g4 * th_ + 128 * kt:g4 * th_ + 128 * (kt + 1)], idbB[:]),
                wait=[kv_ev[(1, g4, min(kt // 4, 2))], kv_ev[(1, g4, 2)],
                      bank_rd[7], kro_tok[(g4, 2)] if kt == 0 else None], inc="t")
            ev = p.emit("vector", lambda v, g4=g4, kt=kt: v.tensor_copy(
                kbuf[:, g4 * th_ + 128 * kt:g4 * th_ + 128 * (kt + 1)],
                pbt[:, 0:128] if kt % 2 == 0 else pbt[:, 128:256]),
                wait=[mm], inc="v")
            bank_rd[7] = ev
            vT_tok[(g4, kt)] = ev
    vT = kbuf

    o_tok = {}
    att_rd = [None]
    attT_rd = [None]
    qh_rd = [None]
    qro_rd = [None]
    ldq_cons = slab_cons
    for h in range(16):
        g4 = h // 4
        ldq = slab_load(lambda d, h=h: wqp[128 * d:128 * (d + 1),
                                           512 * (h // 4):512 * (h // 4 + 1)],
                        16, ldq_cons) if h % 4 == 0 else ldq
        if h % 4 == 0:
            ldq_cons_next = None
        hh = h % 4
        for thc in range(TH):
            b = thc % 2
            mm = p.emit("tensor", lambda tt, hh=hh, thc=thc, b=b: [tt.matmul(
                pall[:, b, :], wslab[:, 512 * d + 128 * hh:512 * d + 128 * (hh + 1)],
                x2hb[:, d, HALO + 512 * thc:HALO + 512 * (thc + 1)],
                start=(d == 0), stop=(d == 15)) for d in range(16)][-1],
                wait=[ldq, t_pre_g, bank_rd[b]], inc="t")
            ldq_cons_next = mm
            ev = p.emit("scalar", lambda s, thc=thc, b=b: s.copy(
                out=qhb[:, 512 * thc:512 * (thc + 1)], in_=pall[:, b, :]),
                wait=[mm, qh_rd[0]], inc="s")
            bank_rd[b] = ev
            if thc == TH - 1:
                qev = ev
        if h % 4 == 3:
            ldq_cons = ldq_cons_next
        b = 2
        mm = p.emit("tensor", lambda tt, b=b: [tt.matmul(
            pall[:, b, :], permB[:], qhb[:, 512 * c:512 * (c + 1)], start=True, stop=True)
            for c in [0]][-1],
            wait=[qev, bank_rd[b]], inc="t")
        rope_toks = []
        for c in range(TH):
            b = 2 + c % 2
            if c > 0:
                mm = p.emit("tensor", lambda tt, c=c, b=b: tt.matmul(
                    pall[:, b, :], permB[:], qhb[:, 512 * c:512 * (c + 1)], start=True, stop=True),
                    wait=[qev, bank_rd[b]], inc="t")
            t1 = p.emit("vector", lambda v, c=c: v.tensor_mul(
                qrob[:, 512 * c:512 * (c + 1)], qhb[:, 512 * c:512 * (c + 1)],
                cs1b[:, HALO + 512 * c:HALO + 512 * (c + 1)]),
                wait=[qev, qro_rd[0]], inc="v")
            sl = c % 2
            t2 = p.emit("vector", lambda v, c=c, b=b, sl=sl: v.tensor_mul(
                scr[:, sl, 0:512], pall[:, b, :], cs2b[:, HALO + 512 * c:HALO + 512 * (c + 1)]),
                wait=[mm, scr_rd[sl]], inc="v")
            bank_rd[b] = t2
            t3 = p.emit("vector", lambda v, c=c, sl=sl: v.tensor_add(
                qrob[:, 512 * c:512 * (c + 1)], qrob[:, 512 * c:512 * (c + 1)], scr[:, sl, 0:512]),
                wait=[t1, t2], inc="v")
            scr_rd[sl] = t3
            rope_toks.append(t3)
        qh_rd[0] = rope_toks[-1]
        for i in range(8):
            mi = 0 if i == 0 else (2 if i == 7 else 1)
            b = 4 + i % 2
            sc = p.emit("tensor", lambda tt, h=h, i=i, g4=g4, b=b: tt.matmul(
                pall[:, b, 0:256], qrob[:, 128 * i:128 * (i + 1)],
                krob[:, g4 * th_ + 128 * i:g4 * th_ + 128 * i + 256],
                start=True, stop=True),
                wait=[rope_toks[(128 * i) // 512], rope_toks[(128 * i + 127) // 512],
                      kro_tok[(g4, 2)], bank_rd[b]], inc="t")
            m0 = p.emit("vector", lambda v, b=b: v.reduce_max(rs[:, 0:1], pall[:, b, 0:256],
                                                              axis=AX.X, negate=True),
                        wait=[sc], inc="v")
            bb = p.emit("vector", lambda v, h=h: v.scalar_tensor_tensor(
                rs[:, 1:2], rs[:, 0:1], 0.08838834764831845, sinkbc[:, h:h + 1],
                OP.mult, OP.add),
                wait=[m0, t_sink], inc="v")
            ex = p.emit("scalar", lambda s, b=b: s.activation(
                out=scr[:, 0, 0:256], in_=pall[:, b, 0:256], func=AF.Exp,
                scale=konst[:], bias=rs[:, 1:2]),
                wait=[bb, t_k0, scr_rd[0]], inc="s")
            bank_rd[b] = ex
            mk = p.emit("vector", lambda v, mi=mi: v.tensor_mul(
                scr[:, 1, 0:256], scr[:, 0, 0:256], maskb[:, mi, :]),
                wait=[ex, scr_rd[1]], inc="v")
            scr_rd[0] = mk
            sm = p.emit("vector", lambda v: v.reduce_sum(rs[:, 2:3], scr[:, 1, 0:256], axis=AX.X),
                        wait=[mk], inc="v")
            rc = p.emit("vector", lambda v: v.reciprocal(rs[:, 3:4], rs[:, 2:3]), wait=[sm], inc="v")
            at = p.emit("vector", lambda v: v.tensor_scalar(
                attb[:], scr[:, 1, 0:256], rs[:, 3:4], None, OP.mult),
                wait=[rc, att_rd[0]], inc="v")
            scr_rd[1] = at
            ev2 = None
            for half in range(2):
                tr = p.emit("tensor", lambda tt, half=half: tt.transpose(
                    pbt[:, 128 * half:128 * (half + 1)], attb[:, 128 * half:128 * (half + 1)],
                    idbB[:]),
                    wait=[at, bank_rd[7]], inc="t")
                ev2 = p.emit("vector", lambda v, half=half: v.tensor_copy(
                    attTb[:, 128 * half:128 * (half + 1)], pbt[:, 128 * half:128 * (half + 1)]),
                    wait=[tr, attT_rd[0] if half == 0 else None], inc="v")
                bank_rd[7] = ev2
            att_rd[0] = ev2
            av = p.emit("tensor", lambda tt, g4=g4, i=i: [tt.matmul(
                pall[:, 6, 0:128], vT[:, g4 * th_ + 128 * (i + kk):g4 * th_ + 128 * (i + kk + 1)],
                attTb[:, 128 * kk:128 * (kk + 1)], start=(kk == 0), stop=(kk == 1))
                for kk in range(2)][-1],
                wait=[ev2, vT_tok[(g4, i)], vT_tok[(g4, i + 1)], bank_rd[6]], inc="t")
            attT_rd[0] = av
            oe = p.emit("scalar", lambda s, h=h, i=i: s.copy(
                out=ob[:, h * t + 128 * i:h * t + 128 * (i + 1)], in_=pall[:, 6, 0:128]),
                wait=[av, o_tok.get((h, i))], inc="s")
            bank_rd[6] = oe
            o_tok[(h, i)] = oe
        qro_rd[0] = sc
    slab_cons = ldq_cons

    pre3_tok = {}
    stg_rd = [None, None]
    for qt in range(4):
        ldo = slab_load(lambda d, qt=qt: wo[128 * d:128 * (d + 1), 512 * qt:512 * (qt + 1)],
                        16, slab_cons if qt == 0 else slab_cons)
        for f4 in range(4):
            f = 4 * qt + f4
            for thc in range(TH):
                b = (f4 * TH + thc) % 4
                mm = p.emit("tensor", lambda tt, f4=f4, thc=thc, b=b: [tt.matmul(
                    pall[:, b, :], wslab[:, 512 * d + 128 * f4:512 * d + 128 * (f4 + 1)],
                    ob[:, d * t + 512 * thc:d * t + 512 * (thc + 1)],
                    start=(d == 0), stop=(d == 15)) for d in range(16)][-1],
                    wait=[ldo, bank_rd[b], o_tok[(15, 7)]], inc="t")
                slab_cons = mm
                sl = (f * TH + thc) % 2
                ld = p.emit("sync", lambda s, f=f, thc=thc, sl=sl: s.dma_start(
                    out=stg[:, sl, :], in_=x2h[128 * f:128 * (f + 1),
                                               HALO + 512 * thc:HALO + 512 * (thc + 1)]),
                    wait=[stg_rd[sl]], inc="dst%d" % sl, inc_amt=16)
                tk = p.emit("vector", lambda v, f=f, thc=thc, b=b, sl=sl: v.tensor_add(
                    P1[:, f, 512 * thc:512 * (thc + 1)], stg[:, sl, :], pall[:, b, :]),
                    wait=[mm, ld], inc="v")
                bank_rd[b] = tk
                stg_rd[sl] = tk
                pre3_tok[(f, thc)] = tk

    srow_rd = [None]
    def emit_ln(src_ready, dst, gamma_beta=None, src_bf=None, inplace=True):
        apply_toks = {}
        sq_rd = [None]
        for thc in range(TH):
            mm1 = mm2 = None
            for j in range(16):
                srcap = (P1[:, j, 512 * thc:512 * (thc + 1)] if src_bf is None
                         else src_bf[:, j, 512 * thc:512 * (thc + 1)])
                sqt = p.emit("scalar", lambda s, srcap=srcap: s.activation(
                    out=sq[:], in_=srcap, func=AF.Square),
                    wait=[src_ready[(j, thc)], sq_rd[0]], inc="s")
                mm1 = p.emit("tensor", lambda tt, srcap=srcap, j=j: tt.matmul(
                    pall[0:1, 0, :], onesb[:, 0:1] if src_bf is None else onesbB[:, 0:1],
                    srcap, start=(j == 0), stop=(j == 15)),
                    wait=[src_ready[(j, thc)], bank_rd[0] if j == 0 else None], inc="t")
                mm2 = p.emit("tensor", lambda tt, j=j: tt.matmul(
                    pall[0:1, 1, :], onesb[:, 0:1], sq[:], start=(j == 0), stop=(j == 15)),
                    wait=[sqt, bank_rd[1] if j == 0 else None], inc="t")
                sq_rd[0] = mm2
            a = p.emit("vector", lambda v: v.tensor_scalar(
                srow[0:1, 0, :], pall[0:1, 0, :], 1.0 / 2048.0, None, OP.mult),
                wait=[mm1, srow_rd[0]], inc="v")
            bank_rd[0] = a
            b2 = p.emit("vector", lambda v: v.tensor_mul(
                srow[0:1, 2, :], srow[0:1, 0, :], srow[0:1, 0, :]), wait=[a], inc="v")
            c = p.emit("vector", lambda v: v.scalar_tensor_tensor(
                srow[0:1, 2, :], pall[0:1, 1, :], 1.0 / 2048.0, srow[0:1, 2, :],
                OP.mult, OP.subtract),
                wait=[mm2, b2], inc="v")
            bank_rd[1] = c
            c2 = p.emit("vector", lambda v: v.tensor_scalar_add(
                srow[0:1, 2, :], srow[0:1, 2, :], 1e-5), wait=[c], inc="v")
            d = p.emit("scalar", lambda s: s.activation(
                out=srow[0:1, 1, :], in_=srow[0:1, 2, :], func=AF.Sqrt), wait=[c2], inc="s")
            e = p.emit("vector", lambda v: v.reciprocal(srow[0:1, 2, :], srow[0:1, 1, :]),
                       wait=[d], inc="v")
            bm = p.emit("tensor", lambda tt: tt.matmul(
                pall[:, 2, :], onesb[0:1, :], srow[0:1, 0, :], start=True, stop=True),
                wait=[a, bank_rd[2]], inc="t")
            br = p.emit("tensor", lambda tt: tt.matmul(
                pall[:, 3, :], onesb[0:1, :], srow[0:1, 2, :], start=True, stop=True),
                wait=[e, bank_rd[3]], inc="t")
            srow_rd[0] = br
            f2 = None
            for j in range(16):
                srcap = (P1[:, j, 512 * thc:512 * (thc + 1)] if src_bf is None
                         else src_bf[:, j, 512 * thc:512 * (thc + 1)])
                f1 = p.emit("vector", lambda v, srcap=srcap: v.tensor_sub(
                    scr[:, 0, 0:512], srcap, pall[:, 2, :]),
                    wait=[bm, scr_rd[0], src_ready[(j, thc)]], inc="v")
                if gamma_beta is None and dst is None:
                    f2 = p.emit("vector", lambda v, j=j, thc=thc: v.tensor_mul(
                        P1[:, j, 512 * thc:512 * (thc + 1)], scr[:, 0, 0:512], pall[:, 3, :]),
                        wait=[f1, br], inc="v")
                    tokj = f2
                else:
                    f2 = p.emit("vector", lambda v: v.tensor_mul(
                        scr[:, 1, 0:512], scr[:, 0, 0:512], pall[:, 3, :]),
                        wait=[f1, br, scr_rd[1]], inc="v")
                    gcol, bcol = gamma_beta
                    tokj = p.emit("vector", lambda v, j=j, thc=thc, gcol=gcol, bcol=bcol: v.tensor_scalar(
                        dst[:, j, 512 * thc:512 * (thc + 1)], scr[:, 1, 0:512],
                        lnpb[:, gcol * 16 + j:gcol * 16 + j + 1],
                        lnpb[:, bcol * 16 + j:bcol * 16 + j + 1],
                        OP.mult, OP.add),
                        wait=[f2], inc="v")
                    scr_rd[1] = tokj
                scr_rd[0] = f2
                apply_toks[(j, thc)] = tokj
            bank_rd[2] = f2
            bank_rd[3] = f2
        return apply_toks

    x3_tok = emit_ln(pre3_tok, None)
    ln1_tok = emit_ln(x3_tok, lnreg, gamma_beta=(0, 1))

    av2_tok = {}
    for qt in range(4):
        ldv = slab_load(lambda d, qt=qt: wvselT[128 * d:128 * (d + 1), 512 * qt:512 * (qt + 1)],
                        16, slab_cons)
        for f4 in range(4):
            f = 4 * qt + f4
            for thc in range(TH):
                b = (f4 * TH + thc) % 4
                mm = p.emit("tensor", lambda tt, f4=f4, thc=thc, b=b: [tt.matmul(
                    pall[:, b, :], wslab[:, 512 * d + 128 * f4:512 * d + 128 * (f4 + 1)],
                    lnreg[:, d, 512 * thc:512 * (thc + 1)],
                    start=(d == 0), stop=(d == 15)) for d in range(16)][-1],
                    wait=[ldv, bank_rd[b]] + [ln1_tok[(d, thc)] for d in range(16)], inc="t")
                slab_cons = mm
                ev = p.emit("scalar", lambda s, f=f, thc=thc, b=b: s.copy(
                    out=ob[:, f * t + 512 * thc:f * t + 512 * (thc + 1)], in_=pall[:, b, :]),
                    wait=[mm], inc="s")
                bank_rd[b] = ev
                av2_tok[(f, thc)] = ev
    had_tok = {}
    pre_tok = {}
    for qt in range(4):
        ldp = slab_load(lambda d, qt=qt: projT[128 * d:128 * (d + 1), 512 * qt:512 * (qt + 1)],
                        16, slab_cons)
        for f4 in range(4):
            f = 4 * qt + f4
            for thc in range(TH):
                b = (f4 * TH + thc) % 4
                mm = p.emit("tensor", lambda tt, f4=f4, thc=thc, b=b: [tt.matmul(
                    pall[:, b, :], wslab[:, 512 * d + 128 * f4:512 * d + 128 * (f4 + 1)],
                    ob[:, d * t + 512 * thc:d * t + 512 * (thc + 1)],
                    start=(d == 0), stop=(d == 15)) for d in range(16)][-1],
                    wait=[ldp, bank_rd[b]] + [av2_tok[(d, thc)] for d in range(16)], inc="t")
                slab_cons = mm
                hd = p.emit("vector", lambda v, f=f, thc=thc, b=b: v.tensor_add(
                    lnreg[:, f, 512 * thc:512 * (thc + 1)],
                    P1[:, f, 512 * thc:512 * (thc + 1)], pall[:, b, :]),
                    wait=[mm, ln1_tok[(f, thc)], x3_tok[(f, thc)]], inc="v")
                pr = p.emit("vector", lambda v, f=f, thc=thc, b=b: v.scalar_tensor_tensor(
                    P1[:, f, 512 * thc:512 * (thc + 1)],
                    P1[:, f, 512 * thc:512 * (thc + 1)], 2.0, pall[:, b, :],
                    OP.mult, OP.add),
                    wait=[hd], inc="v")
                bank_rd[b] = pr
                had_tok[(f, thc)] = hd
                pre_tok[(f, thc)] = pr

    ln2_tok = emit_ln(had_tok, lnreg, gamma_beta=(2, 3), src_bf=lnreg)

    lgG = wslab[:, 4096:6144].bitcast(F32)[0:8, :]
    gTb = wslab[0:8, 6144:7168]
    ldr = p.emit("gpsimd", lambda g: g.dma_start(
        out=wslab[:, 0:128], in_=wre), wait=[slab_cons], inc="dsl", inc_amt=16)
    lg_ev = [None] * TH
    for thc in range(TH):
        mm = None
        for d in range(16):
            mm = p.emit("tensor", lambda tt, d=d, thc=thc: tt.matmul(
                pall[0:8, 4, :], wslab[:, 8 * d:8 * (d + 1)],
                lnreg[:, d, 512 * thc:512 * (thc + 1)],
                start=(d == 0), stop=(d == 15)),
                wait=[ldr, ln2_tok[(d, thc)], bank_rd[4] if d == 0 else None], inc="t")
        lg_ev[thc] = p.emit("vector", lambda v, thc=thc: v.tensor_copy(
            lgG[:, 512 * thc:512 * (thc + 1)], pall[0:8, 4, :]), wait=[mm], inc="v")
        bank_rd[4] = lg_ev[thc]
    gT_done = []
    g8_rd = [None]
    for tt_ in range(t // 128):
        thc = (tt_ * 128) // 512
        tr = p.emit("tensor", lambda tt, tt_=tt_: tt.transpose(
            pall[:, 5, 0:8], lgG[0:8, 128 * tt_:128 * (tt_ + 1)], idb[0:8, 0:8]),
            wait=[lg_ev[thc], bank_rd[5]], inc="t")
        ev = p.emit("vector", lambda v: v.tensor_copy(g8[:], pall[:, 5, 0:8]),
                    wait=[tr, g8_rd[0]], inc="v")
        bank_rd[5] = ev
        m0 = p.emit("vector", lambda v: v.reduce_max(rs[:, 0:1], g8[:], axis=AX.X, negate=True),
                    wait=[ev], inc="v")
        exg = p.emit("scalar", lambda s: s.activation(out=gE[:], in_=g8[:], func=AF.Exp,
                                                      bias=rs[:, 0:1]),
                     wait=[m0], inc="s")
        g8_rd[0] = exg
        vv1 = p.emit("vector", lambda v: v.reduce_max(rs[:, 1:2], gE[:], axis=AX.X),
                     wait=[exg], inc="v")
        mk1 = p.emit("vector", lambda v: v.tensor_scalar(gM[:], gE[:], rs[:, 1:2], None, OP.is_ge),
                     wait=[vv1], inc="v")
        e1 = p.emit("vector", lambda v: v.scalar_tensor_tensor(gE1[:], gM[:], -1e30, gE[:],
                                                               OP.mult, OP.add),
                    wait=[mk1], inc="v")
        vv2 = p.emit("vector", lambda v: v.reduce_max(rs[:, 2:3], gE1[:], axis=AX.X),
                     wait=[e1], inc="v")
        dn = p.emit("vector", lambda v: v.tensor_add(rs[:, 3:4], rs[:, 1:2], rs[:, 2:3]),
                    wait=[vv2], inc="v")
        rc = p.emit("vector", lambda v: v.reciprocal(rs[:, 3:4], rs[:, 3:4]), wait=[dn], inc="v")
        mk2 = p.emit("vector", lambda v: v.tensor_scalar(gM[:], gE[:], rs[:, 2:3], None, OP.is_ge),
                     wait=[vv2, e1], inc="v")
        gm = p.emit("vector", lambda v: v.tensor_mul(gE[:], gE[:], gM[:]), wait=[mk2], inc="v")
        gs = p.emit("vector", lambda v: v.tensor_scalar(gE[:], gE[:], rs[:, 3:4], None, OP.mult),
                    wait=[gm, rc], inc="v")
        tr2 = p.emit("tensor", lambda tt: tt.transpose(
            pall[0:8, 5, 0:128], gE[:], idb[:]),
            wait=[gs, bank_rd[5]], inc="t")
        ev2 = p.emit("vector", lambda v, tt_=tt_: v.tensor_copy(
            gTb[0:8, 128 * tt_:128 * (tt_ + 1)], pall[0:8, 5, 0:128]),
            wait=[tr2], inc="v")
        bank_rd[5] = ev2
        g8_rd[0] = tr2
        gT_done.append(ev2)
    def gbv(e):
        if e < 4:
            return krob[:, e * t:(e + 1) * t]
        return vbuf[:, (e - 4) * t:(e - 3) * t]
    gb_tok = {}
    for e in range(8):
        for thc in range(TH):
            b = 4 + (e * TH + thc) % 2
            mm = p.emit("tensor", lambda tt, e=e, thc=thc, b=b: tt.matmul(
                pall[:, b, :], ind8b[:, 128 * e:128 * (e + 1)],
                gTb[0:8, 512 * thc:512 * (thc + 1)], start=True, stop=True),
                wait=[bank_rd[b], t_pre_g] + gT_done[4 * thc:4 * (thc + 1)], inc="t")
            ev = p.emit("scalar", lambda s, e=e, thc=thc, b=b: s.copy(
                out=gbv(e)[:, 512 * thc:512 * (thc + 1)], in_=pall[:, b, :]),
                wait=[mm, kro_tok.get((3, 2)) if e == 0 and thc == 0 else None], inc="s")
            bank_rd[b] = ev
            gb_tok[(e, thc)] = ev
            slab_cons = mm

    gelu_tok = {}
    for e in range(8):
        for half in range(2):
            for s8 in range(4):
                fbase = 2048 * half + 512 * s8
                ldw1 = slab_load(lambda d, e=e, fbase=fbase: mw1[e, 128 * d:128 * (d + 1),
                                                                 fbase:fbase + 512],
                                 16, slab_cons)
                for f4 in range(4):
                    lc = 4 * s8 + f4
                    for thc in range(TH):
                        b = (f4 * TH + thc) % 4
                        mm = p.emit("tensor", lambda tt, f4=f4, thc=thc, b=b: [tt.matmul(
                            pall[:, b, :], wslab[:, 512 * d + 128 * f4:512 * d + 128 * (f4 + 1)],
                            lnreg[:, d, 512 * thc:512 * (thc + 1)],
                            start=(d == 0), stop=(d == 15)) for d in range(16)][-1],
                            wait=[ldw1, bank_rd[b]] +
                                 ([ln2_tok[(d, thc)] for d in range(16)] if e == 0 else []),
                            inc="t")
                        slab_cons = mm
                        ge = p.emit("scalar", lambda s, lc=lc, thc=thc, b=b: s.activation(
                            out=gelu[:, lc, 512 * thc:512 * (thc + 1)], in_=pall[:, b, :],
                            func=AF.Gelu),
                            wait=[mm, gelu_tok.get((half, lc, thc)),
                                  av2_tok[(lc, thc)] if e == 0 and half == 0 else None], inc="s")
                        gmul = p.emit("vector", lambda v, lc=lc, thc=thc, e=e: v.tensor_mul(
                            gelu[:, lc, 512 * thc:512 * (thc + 1)],
                            gelu[:, lc, 512 * thc:512 * (thc + 1)],
                            gbv(e)[:, 512 * thc:512 * (thc + 1)]),
                            wait=[ge, gb_tok[(e, thc)]], inc="v")
                        bank_rd[b] = gmul
                        gelu_tok[(half, lc, thc)] = gmul
            for qt in range(4):
                ldw2 = slab_load(lambda d, e=e, half=half, qt=qt: mw2[
                    e, 2048 * half + 128 * d:2048 * half + 128 * (d + 1),
                    512 * qt:512 * (qt + 1)], 16, slab_cons)
                for f4 in range(4):
                    f = 4 * qt + f4
                    for thc in range(TH):
                        b = (f4 * TH + thc) % 4
                        mm = p.emit("tensor", lambda tt, f4=f4, thc=thc, b=b: [tt.matmul(
                            pall[:, b, :], wslab[:, 512 * d + 128 * f4:512 * d + 128 * (f4 + 1)],
                            gelu[:, d, 512 * thc:512 * (thc + 1)],
                            start=(d == 0), stop=(d == 15)) for d in range(16)][-1],
                            wait=[ldw2, bank_rd[b]] +
                                 [gelu_tok[(half, d, thc)] for d in range(16)], inc="t")
                        slab_cons = mm
                        ra = p.emit("vector", lambda v, f=f, thc=thc, b=b: v.tensor_add(
                            P1[:, f, 512 * thc:512 * (thc + 1)],
                            P1[:, f, 512 * thc:512 * (thc + 1)], pall[:, b, :]),
                            wait=[mm, pre_tok[(f, thc)]], inc="v")
                        bank_rd[b] = ra
                        pre_tok[(f, thc)] = ra

    x4_tok = emit_ln(pre_tok, None)
    st_last = None
    for j in range(16):
        st_last = p.emit("sync", lambda s, j=j: s.dma_start(
            out=outT[128 * j:128 * (j + 1), :], in_=P1[:, j, :]),
            wait=[x4_tok[(j, 0)], x4_tok[(j, TH - 1)]], inc="dso", inc_amt=16)
    p.emit("gpsimd", lambda g: None, wait=[st_last])
    with nc.Block() as block:
        p.build(nc, block)
    es.close()
    return nc


def run_B2(inputs, x2_full, t=1024):
    per_core = host_prep_B2(inputs, x2_full, t=t)
    nc = build_B2(t=t)
    res = run_bass_kernel_spmd(nc, per_core, core_ids=list(range(NCORES)))
    return np.concatenate([r["outT"] for r in res.results], axis=1)


def kernel(**inputs):
    out = run_fused(inputs)
    return np.ascontiguousarray(out.T).reshape(1, 8192, 2048)

def host_prep_fused(inputs, t=1024):
    A = host_prep_A(inputs)
    xT = A[0]["xT"]
    B1shared = host_prep_B1(inputs, np.zeros((1, 1), BF), t=t)[0]
    for k in ("ygT", "xme", "woutT"):
        B1shared.pop(k, None)
    B2pc = host_prep_B2(inputs, np.zeros((D_MODEL, NCORES * t), np.float32), t=t)
    woutT = _c(inputs["L0_attn_W_out"].T.astype(np.float32))
    per_core = []
    for c in range(NCORES):
        d = dict(A[c])
        d.update(B1shared)
        b2 = dict(B2pc[c])
        b2.pop("x2h", None)
        for k in ("ident", "ones", "ind8"):
            b2.pop(k, None)
        d.update(b2)
        d["xme"] = _c(xT[:, t * c:t * (c + 1)])
        d["woutTs"] = _c(woutT[NCH * c:NCH * (c + 1), :])
        hwL = np.zeros((128, 8), np.float32)
        hwR = np.zeros((128, 8), np.float32)
        if c > 0:
            hwL[:, c - 1] = 1.0
        if c < NCORES - 1:
            hwR[:, c + 1] = 1.0
        d["hwL"] = hwL
        d["hwR"] = hwR
        per_core.append(d)
    return per_core


def build_fused(T=8192, Tc=512, t=1024):
    nck = T // Tc
    TH = t // 512
    HALO = 64
    th_ = t + 2 * HALO
    nc = bass.Bass("TRN2", target_bir_lowering=False, debug=False)
    DR = {}
    def DI(name, shape, dt=F32):
        DR[name] = nc.dram_tensor(name, shape, dt, kind="ExternalInput").ap()
        return DR[name]
    xT = DI("xT", [D_MODEL, T]); winT = DI("winT", [D_MODEL, 1280])
    wdtT = DI("wdtT", [128, NCH]); bdt = DI("bdt", [128, NQ])
    alogT = DI("alogT", [128, NQ * 16]); convw = DI("convw", [128, NQ * 4])
    convb = DI("convb", [128, NQ]); dpar = DI("dpar", [128, NQ])
    ind_x = DI("ind_x", [128, NQ * 128]); ind_B = DI("ind_B", [128, 64 * 128])
    ind_C = DI("ind_C", [128, 16 * 128])
    woutTs = DI("woutTs", [NCH, D_MODEL]); xme = DI("xme", [D_MODEL, t])
    wrout = DI("wrout", [D_MODEL, 128]); wgate = DI("wgate", [D_MODEL, 8192])
    wup = DI("wup", [D_MODEL, 8192]); wdown = DI("wdown", [1024, D_MODEL])
    wsg = DI("wsg", [D_MODEL, D_MODEL]); wsu = DI("wsu", [D_MODEL, D_MODEL])
    wsd = DI("wsd", [D_MODEL, D_MODEL])
    a0 = DI("a0", [128, 8]); a1 = DI("a1", [128, 8]); esc = DI("esc", [128, 1])
    ident = DI("ident", [128, 128]); ones = DI("ones", [128, 128])
    ind8 = DI("ind8", [8, 8 * 128])
    wqp = DI("wqp", [D_MODEL, 2048]); wkp = DI("wkp", [D_MODEL, 512])
    wv = DI("wv", [D_MODEL, 512]); wo = DI("wo", [D_MODEL, 2048])
    sinkr = DI("sinkr", [1, 16]); wvselT = DI("wvselT", [D_MODEL, 2048])
    projT = DI("projT", [D_MODEL, 2048]); wre = DI("wre", [128, 128])
    mw1 = DI("mw1", [8, D_MODEL, 4096]); mw2 = DI("mw2", [8, 4096, D_MODEL])
    permM = DI("permM", [128, 128]); lnp = DI("lnp", [128, 4 * 16])
    cs1 = DI("cs1", [128, th_]); cs2 = DI("cs2", [128, th_])
    mask0 = DI("mask0", [128, 256]); maskI = DI("maskI", [128, 256])
    mask7 = DI("mask7", [128, 256])
    hwL = DI("hwL", [128, 8]); hwR = DI("hwR", [128, 8])
    outT = nc.dram_tensor("outT", [D_MODEL, t], F32, kind="ExternalOutput").ap()
    prt = nc.dram_tensor("prt", [NCORES * D_MODEL, t], F32, kind="Internal")
    rsout = nc.dram_tensor("rsout", [D_MODEL, t], F32, kind="Internal")
    edgebuf = nc.dram_tensor("edgebuf", [128, 16 * 128], BF16, kind="Internal")
    edgeall = nc.dram_tensor("edgeall", [NCORES * 128, 16 * 128], BF16, kind="Internal")
    rg = [list(range(NCORES))]

    p = Prog()
    import contextlib
    esS = contextlib.ExitStack()
    SBs = lambda nm, sh, dt: esS.enter_context(nc.sbuf_tensor(nm, sh, dt))
    resA = SBs("resA", [128, 16, t], F32)
    wslab = SBs("wslab", [128, 8192], BF16)
    idb = SBs("idb", [128, 128], F32)
    onesb = SBs("onesb", [128, 128], F32)
    idbB = SBs("idbB", [128, 128], BF16)
    onesbB = SBs("onesbB", [128, 128], BF16)
    ind8b = SBs("ind8b", [8, 8 * 128], BF16)
    permB = SBs("permB", [128, 128], BF16)
    lnpb = SBs("lnpb", [128, 4 * 16], F32)
    sinkbc = SBs("sinkbc", [128, 16], F32)
    sinkb = SBs("sinkb", [1, 16], F32)
    maskb = SBs("maskb", [128, 3, 256], BF16)
    hwLb = SBs("hwLb", [128, 8], F32)
    hwRb = SBs("hwRb", [128, 8], F32)
    resF = resA[:].rearrange("p a b -> p (a b)")

    p.emit("sync", lambda s: s.dma_start(out=idb[:], in_=ident), inc="dws", inc_amt=16)
    p.emit("sync", lambda s: s.dma_start(out=onesb[:], in_=ones), inc="dws", inc_amt=16)
    p.emit("sync", lambda s: s.dma_start(out=lnpb[:], in_=lnp), inc="dws", inc_amt=16)
    p.emit("sync", lambda s: s.dma_start(out=sinkb[:], in_=sinkr), inc="dws", inc_amt=16)
    p.emit("sync", lambda s: s.dma_start(out=hwLb[:], in_=hwL), inc="dws", inc_amt=16)
    t_pre_s = p.emit("sync", lambda s: s.dma_start(out=hwRb[:], in_=hwR), inc="dws", inc_amt=16)
    p.emit("gpsimd", lambda g: g.dma_start(out=idbB[:], in_=ident), inc="dw", inc_amt=16)
    p.emit("gpsimd", lambda g: g.dma_start(out=onesbB[:], in_=ones), inc="dw", inc_amt=16)
    p.emit("gpsimd", lambda g: g.dma_start(out=ind8b[:], in_=ind8), inc="dw", inc_amt=16)
    p.emit("gpsimd", lambda g: g.dma_start(out=permB[:], in_=permM), inc="dw", inc_amt=16)
    p.emit("gpsimd", lambda g: g.dma_start(out=maskb[:, 0, :], in_=mask0), inc="dw", inc_amt=16)
    p.emit("gpsimd", lambda g: g.dma_start(out=maskb[:, 1, :], in_=maskI), inc="dw", inc_amt=16)
    t_pre_gS = p.emit("gpsimd", lambda g: g.dma_start(out=maskb[:, 2, :], in_=mask7), inc="dw", inc_amt=16)
    esA = contextlib.ExitStack()
    SA = lambda nm, sh, dt: esA.enter_context(nc.sbuf_tensor(nm, sh, dt))
    wl = SA("wl", [128, 16 * 1280], BF16)
    xcA = SA("xcA", [128, 16 * Tc], BF16)
    wdt_sb = SA("wdt_sb", [128, NCH], BF16)
    indx_sb = SA("indx_sb", [128, NQ * 128], BF16)
    indB_sb = SA("indB_sb", [128, 64 * 128], BF16)
    indC_sb = SA("indC_sb", [128, 16 * 128], BF16)
    Cbuf = SA("Cbuf", [128, NQ, Tc], BF16)
    xBb = SA("xBb", [128, Tc], BF16)
    dtlb = SA("dtlb", [128, Tc], BF16)
    ygb = SA("ygb", [128, NQ, Tc], BF16)
    wouts_sb = SA("wouts_sb", [128, NQ * 2048], BF16)
    dAb = SA("dAb", [128, 2, Tc], F32)
    d1b = SA("d1b", [128, 2, Tc], F32)
    snb = SA("snb", [128, 2, Tc], F32)
    prodbA = SA("prodbA", [128, 2, Tc], F32)
    stateb = SA("stateb", [128, 64], F32)
    bdt2 = SA("bdt2", [128, NQ], F32)
    acol = SA("acol", [128, NQ * 16], F32)
    cwS = SA("cwS", [128, NQ * 4], F32)
    cbS = SA("cbS", [128, NQ], F32)
    dpS = SA("dpS", [128, NQ], F32)
    psmm = esA.enter_context(nc.psum_tensor("psmm", [128, 2, 512], F32))
    psx = esA.enter_context(nc.psum_tensor("psx", [128, 2, 512], F32))
    psB = esA.enter_context(nc.psum_tensor("psB", [128, 2, 512], F32))
    psC = esA.enter_context(nc.psum_tensor("psC", [128, 2, 512], F32))
    delta = resF[:, 0:2048].rearrange("p (a b) -> p a b", a=NQ)
    ubuf = resF[:, 2048:4096].rearrange("p (a b) -> p a b", a=NQ)
    dubuf = resF[:, 4096:6144].rearrange("p (a b) -> p a b", a=NQ)
    ybuf = resF[:, 6144:8192].rearrange("p (a b) -> p a b", a=NQ)
    zf = resF[:, 8192:10240].rearrange("p (a b) -> p a b", a=NQ)
    xrb = resF[:, 10240:12300].rearrange("p (a b) -> p a b", a=NQ)
    cacc = resF[:, 12300:13324].rearrange("p (a b) -> p a b", a=2)
    sp1 = resF[:, 13324:13836]
    sp2 = resF[:, 13836:14348]
    sp3 = resF[:, 14348:14860]
    szb = sp3
    a2s = resF[:, 14860:15884].rearrange("p (a b) -> p a b", a=2)
    xcv = [xcA[:].rearrange("p (a b) -> p a b", a=16),
           wslab[:].rearrange("p (a b) -> p a b", a=16)]

    for d in range(16):
        p.emit("gpsimd", lambda g, d=d: g.dma_start(
            out=wl[:, d * 1280:(d + 1) * 1280], in_=winT[128 * d:128 * (d + 1), :]),
            inc="dw", inc_amt=16)
    p.emit("gpsimd", lambda g: g.dma_start(out=wdt_sb[:], in_=wdtT), inc="dw", inc_amt=16)
    p.emit("gpsimd", lambda g: g.dma_start(out=indx_sb[:], in_=ind_x), inc="dw", inc_amt=16)
    p.emit("gpsimd", lambda g: g.dma_start(out=indB_sb[:], in_=ind_B), inc="dw", inc_amt=16)
    p.emit("gpsimd", lambda g: g.dma_start(out=indC_sb[:], in_=ind_C), inc="dw", inc_amt=16)
    for qq in range(NQ):
        p.emit("gpsimd", lambda g, qq=qq: g.dma_start(
            out=wouts_sb[:, qq * 2048:(qq + 1) * 2048], in_=woutTs[128 * qq:128 * (qq + 1), :]),
            inc="dw", inc_amt=16)
    t_pre_g = p.emit("gpsimd", lambda g: g.dma_start(out=acol[:], in_=alogT), inc="dw", inc_amt=16)
    p.emit("sync", lambda s: s.dma_start(out=bdt2[:], in_=bdt), inc="dws", inc_amt=16)
    p.emit("sync", lambda s: s.dma_start(out=cwS[:], in_=convw), inc="dws", inc_amt=16)
    p.emit("sync", lambda s: s.dma_start(out=cbS[:], in_=convb), inc="dws", inc_amt=16)
    t_pre_s = p.emit("sync", lambda s: s.dma_start(out=dpS[:], in_=dpar), inc="dws", inc_amt=16)
    t_bdt = t_alog = t_cw = t_cb = t_dp = t_pre_s
    t_wdt = t_indx = t_indB = t_indC = t_pre_g
    t_b2 = p.emit("vector", lambda v: v.tensor_scalar_mul(bdt2[:], bdt2[:], 2.0),
                  wait=[t_bdt], inc="v")
    t_ae = p.emit("scalar", lambda s: s.activation(out=acol[:], in_=acol[:], func=AF.Exp),
                  wait=[t_pre_g], inc="s")
    t_an = p.emit("vector", lambda v: v.tensor_scalar_mul(acol[:], acol[:], -1.0),
                  wait=[t_ae], inc="v")
    t_st0 = p.emit("vector", lambda v: v.memset(stateb[:], 0.0), inc="v")
    t_hist0 = p.emit("vector", lambda v: v.memset(xrb[:], 0.0), inc="v")

    xload = [None] * (nck + 2)
    xfree = [None] * (nck + 2)

    def load_chunk(k):
        tk = None
        sem = "dx%d" % (k % 2)
        for d in range(16):
            tk = p.emit("gpsimd", lambda g, d=d, k=k: g.dma_start(
                out=xcv[k % 2][:, d, :],
                in_=xT[128 * d:128 * (d + 1), k * Tc:(k + 1) * Tc]),
                wait=[xfree[k - 2] if k >= 2 else None],
                inc=sem, inc_amt=16)
        return tk

    xload[0] = load_chunk(0)
    if nck > 1:
        xload[1] = load_chunk(1)

    evac_free = [None, None]
    psx_free = [None, None]
    state_w = [None] * 64
    B_rd = [None, None]
    C_rd = [None, None]
    A_rd = [None, None]
    D_rd = [None, None]
    S_rd = [None, None]
    P_rd = [None, None]
    delta_war = [t_hist0] * NQ
    du_war = [None] * NQ
    u_war = [None] * NQ
    z_war = [None] * NQ
    xB_war = None
    C_war = [None] * NQ
    dtl_war = None
    xrb_war = [t_hist0] * NQ
    cacc_free = [None, None]
    sp_free = [t_hist0, None, None]
    szb_free = None
    y_war = [None] * NQ
    a2s_free = [None, None]
    a2st_free = [None, None]
    a2_gate = None
    for k in range(nck):
        mm_tok = [None] * 10
        ev = [None] * 10
        for j in range(10):
            slot = j % 2

            def mmz(tt, j=j, slot=slot, k=k):
                ins = None
                for d in range(16):
                    ins = tt.matmul(psmm[:, slot, :],
                                    wl[:, d * 1280 + j * 128:d * 1280 + (j + 1) * 128],
                                    xcv[k % 2][:, d, :],
                                    start=(d == 0), stop=(d == 15))
                return ins

            mm_tok[j] = p.emit("tensor", mmz,
                               wait=[xload[k], evac_free[slot],
                                     t_pre_g if k == 0 and j == 0 else None],
                               inc="t")
            if j <= 3:
                dst, war = zf[:, j, :], z_war[j]
            elif j == 4:
                dst, war = xBb[:], xB_war
            elif j <= 8:
                dst, war = Cbuf[:, j - 5, :], C_war[j - 5]
            else:
                dst, war = dtlb[:], dtl_war
            ev[j] = p.emit("scalar",
                           lambda s, dst=dst, slot=slot: s.copy(out=dst, in_=psmm[:, slot, :]),
                           wait=[mm_tok[j], war], inc="s")
            evac_free[slot] = ev[j]
        if k + 2 < nck:
            xfree[k] = mm_tok[9]
            xload[k + 2] = load_chunk(k + 2)

        t_delta = [None] * NQ
        t_u = [None] * NQ
        t_du = [None] * NQ
        for q in range(NQ):
            slot = q % 2
            mmd = p.emit("tensor", lambda tt, q=q, slot=slot: tt.matmul(
                psx[:, slot, :], wdt_sb[:, q * 128:(q + 1) * 128], dtlb[:],
                start=True, stop=True),
                wait=[ev[9], t_wdt, psx_free[slot]], inc="t")
            if q == NQ - 1:
                dtl_war = mmd
            a1_ = p.emit("scalar", lambda s, slot=slot, q=q: s.activation(
                out=sp1, in_=psx[:, slot, :], func=AF.Abs, bias=bdt2[:, q:q + 1]),
                wait=[mmd, t_b2, sp_free[0]], inc="s")
            e1 = p.emit("scalar", lambda s: s.activation(
                out=sp2, in_=sp1, func=AF.Exp, scale=-1.0),
                wait=[a1_, sp_free[1]], inc="s")
            l1 = p.emit("scalar", lambda s: s.activation(
                out=sp2, in_=sp2, func=AF.Ln, bias=1.0),
                wait=[e1], inc="s")
            t0 = p.emit("vector", lambda v, slot=slot, q=q: v.tensor_scalar_add(
                sp3, psx[:, slot, :], bdt2[:, q:q + 1]),
                wait=[mmd, a1_, t_b2, sp_free[2]], inc="v")
            t1 = p.emit("vector", lambda v: v.tensor_add(sp3, sp3, sp1),
                        wait=[t0, a1_], inc="v")
            t_delta[q] = p.emit("vector", lambda v, q=q: v.scalar_tensor_tensor(
                delta[:, q, :], sp3, 0.5, sp2, OP.mult, OP.add),
                wait=[t1, l1, delta_war[q]], inc="v")
            sp_free[0] = t1
            sp_free[1] = t_delta[q]
            sp_free[2] = t_delta[q]
            psx_free[slot] = t1

            mmx = p.emit("tensor", lambda tt, q=q, slot=slot: tt.matmul(
                psx[:, slot, :], indx_sb[:, q * 128:(q + 1) * 128], xBb[:],
                start=True, stop=True),
                wait=[ev[4], t_indx, psx_free[slot]], inc="t")
            exr = p.emit("vector", lambda v, q=q, slot=slot: v.tensor_copy(
                xrb[:, q, 3:Tc + 3], psx[:, slot, :]),
                wait=[mmx, xrb_war[q]], inc="v")
            psx_free[slot] = exr
            c0 = p.emit("vector", lambda v, q=q: v.tensor_scalar(
                cacc[:, 0, :], xrb[:, q, 0:Tc], cwS[:, 4 * q:4 * q + 1], None, OP.mult),
                wait=[exr, t_cw, cacc_free[0]], inc="v")
            c1 = p.emit("vector", lambda v, q=q: v.scalar_tensor_tensor(
                cacc[:, 1, :], xrb[:, q, 1:Tc + 1], cwS[:, 4 * q + 1:4 * q + 2],
                cacc[:, 0, :], OP.mult, OP.add),
                wait=[c0, cacc_free[1]], inc="v")
            c2 = p.emit("vector", lambda v, q=q: v.scalar_tensor_tensor(
                cacc[:, 0, :], xrb[:, q, 2:Tc + 2], cwS[:, 4 * q + 2:4 * q + 3],
                cacc[:, 1, :], OP.mult, OP.add),
                wait=[c1], inc="v")
            c3 = p.emit("vector", lambda v, q=q: v.scalar_tensor_tensor(
                cacc[:, 1, :], xrb[:, q, 3:Tc + 3], cwS[:, 4 * q + 3:4 * q + 4],
                cacc[:, 0, :], OP.mult, OP.add),
                wait=[c2], inc="v")
            hq = p.emit("vector", lambda v, q=q: v.tensor_copy(
                xrb[:, q, 0:3], xrb[:, q, Tc:Tc + 3]),
                wait=[c3], inc="v")
            xrb_war[q] = hq
            t_u[q] = p.emit("scalar", lambda s, q=q: s.activation(
                out=ubuf[:, q, :], in_=cacc[:, 1, :], func=AF.Silu, bias=cbS[:, q:q + 1]),
                wait=[c3, t_cb, u_war[q]], inc="s")
            cacc_free[0] = c3
            cacc_free[1] = t_u[q]
            t_du[q] = p.emit("vector", lambda v, q=q: v.tensor_mul(
                dubuf[:, q, :], delta[:, q, :], ubuf[:, q, :]),
                wait=[t_delta[q], t_u[q], du_war[q]], inc="v")

        tok_yadd = [None] * NQ
        for i in range(64):
            n, q = divmod(i, NQ)
            j = 16 * q + n
            sl = i % 2
            tB = p.emit("tensor", lambda tt, j=j, sl=sl: tt.matmul(
                psB[:, sl, :], indB_sb[:, j * 128:(j + 1) * 128], xBb[:],
                start=True, stop=True),
                wait=[ev[4], t_indB, B_rd[sl]], inc="t")
            tC = p.emit("tensor", lambda tt, n=n, q=q, sl=sl: tt.matmul(
                psC[:, sl, :], indC_sb[:, n * 128:(n + 1) * 128], Cbuf[:, q, :],
                start=True, stop=True),
                wait=[ev[5 + q], t_indC, C_rd[sl]], inc="t")
            if n == 15:
                C_war[q] = tC
            if i == 63:
                xB_war = tB
            tA = p.emit("scalar", lambda s, q=q, n=n, sl=sl: s.activation(
                out=dAb[:, sl, :], in_=delta[:, q, :], func=AF.Exp,
                scale=acol[:, q * 16 + n:q * 16 + n + 1]),
                wait=[t_delta[q], t_an, A_rd[sl]], inc="s")
            if n == 15:
                delta_war[q] = tA
            td1 = p.emit("vector", lambda v, q=q, sl=sl: v.tensor_mul(
                d1b[:, sl, :], dubuf[:, q, :], psB[:, sl, :]),
                wait=[t_du[q], tB, D_rd[sl]], inc="v")
            B_rd[sl] = td1
            if n == 15:
                du_war[q] = td1
            tscan = p.emit("vector", lambda v, i=i, sl=sl: v.tensor_tensor_scan(
                snb[:, sl, :], dAb[:, sl, :], d1b[:, sl, :],
                stateb[:, i:i + 1], OP.mult, OP.add),
                wait=[tA, td1, state_w[i], S_rd[sl], t_st0], inc="v")
            A_rd[sl] = tscan
            D_rd[sl] = tscan
            state_w[i] = p.emit("vector", lambda v, i=i, sl=sl: v.tensor_copy(
                stateb[:, i:i + 1], snb[:, sl, Tc - 1:Tc]),
                wait=[tscan], inc="v")
            if n == 0:
                tprod = p.emit("vector", lambda v, q=q, sl=sl: v.tensor_mul(
                    ybuf[:, q, :], snb[:, sl, :], psC[:, sl, :]),
                    wait=[tscan, tC, y_war[q]], inc="v")
                tok_yadd[q] = tprod
            else:
                tprod = p.emit("vector", lambda v, sl=sl: v.tensor_mul(
                    prodbA[:, sl, :], snb[:, sl, :], psC[:, sl, :]),
                    wait=[tscan, tC, P_rd[sl]], inc="v")
                ty = p.emit("vector", lambda v, q=q, sl=sl: v.tensor_add(
                    ybuf[:, q, :], ybuf[:, q, :], prodbA[:, sl, :]),
                    wait=[tprod, tok_yadd[q]], inc="v")
                tok_yadd[q] = ty
                P_rd[sl] = ty
            C_rd[sl] = tprod
            S_rd[sl] = tmax(tprod, state_w[i])

        ygt_tok = [None] * NQ
        for q in range(NQ):
            sl = q % 2
            pr = p.emit("vector", lambda v, q=q, sl=sl: v.tensor_scalar(
                prodbA[:, sl, :], ubuf[:, q, :], dpS[:, q:q + 1], None, OP.mult),
                wait=[t_u[q], t_dp, P_rd[sl]], inc="v")
            ya = p.emit("vector", lambda v, q=q, sl=sl: v.tensor_add(
                ybuf[:, q, :], ybuf[:, q, :], prodbA[:, sl, :]),
                wait=[pr, tok_yadd[q]], inc="v")
            P_rd[sl] = ya
            u_war[q] = pr
            sz = p.emit("scalar", lambda s, q=q: s.activation(
                out=szb, in_=zf[:, q, :], func=AF.Silu),
                wait=[ev[q], szb_free], inc="s")
            z_war[q] = sz
            ygt = p.emit("vector", lambda v, q=q: v.tensor_mul(
                ygb[:, q, :], ybuf[:, q, :], szb),
                wait=[ya, sz, a2_gate], inc="v")
            szb_free = ygt
            sp_free[2] = ygt
            y_war[q] = ygt
            ygt_tok[q] = ygt
        r_, cblk = divmod(k, TH)
        for f in range(16):
            slot = f % 2
            mm = p.emit("tensor", lambda tt, f=f, slot=slot: [tt.matmul(
                psmm[:, slot, :], wouts_sb[:, q2 * 2048 + 128 * f:q2 * 2048 + 128 * (f + 1)],
                ygb[:, q2, :], start=(q2 == 0), stop=(q2 == 3)) for q2 in range(NQ)][-1],
                wait=[ygt_tok[0], ygt_tok[1], ygt_tok[2], ygt_tok[3],
                      t_pre_g, evac_free[slot]], inc="t")
            eva = p.emit("scalar", lambda s, slot=slot: s.copy(
                out=a2s[:, slot, :], in_=psmm[:, slot, :]),
                wait=[mm, a2st_free[slot]], inc="s")
            evac_free[slot] = eva
            st = p.emit("sync", lambda s, f=f, r_=r_, cblk=cblk, slot=slot: s.dma_start(
                out=prt.ap()[2048 * r_ + 128 * f:2048 * r_ + 128 * (f + 1),
                             512 * cblk:512 * (cblk + 1)],
                in_=a2s[:, slot, :]),
                wait=[eva], inc="dpr%d" % slot, inc_amt=16)
            a2st_free[slot] = st
            if f == 15:
                a2_gate = mm
    phaseA_join = [("t", p.cnt.get("t", 0)), ("v", p.cnt.get("v", 0)),
                   ("s", p.cnt.get("s", 0)), ("dx0", p.cnt.get("dx0", 0)),
                   ("dx1", p.cnt.get("dx1", 0)), ("dw", p.cnt.get("dw", 0)),
                   ("dws", p.cnt.get("dws", 0))]
    t_rs = p.emit("gpsimd", lambda g: g.collective_compute(
        "ReduceScatter", OP.add, rg, [prt.ap().opt()], [rsout.ap().opt()]),
        wait=[("dpr0", p.cnt.get("dpr0", 0)), ("dpr1", p.cnt.get("dpr1", 0))],
        inc="cc")
    esA.close()
    esB1 = contextlib.ExitStack()
    SB1 = lambda nm, sh, dt: esB1.enter_context(nc.sbuf_tensor(nm, sh, dt))
    x1b = SB1("x1b", [128, 16, t], BF16)
    bp = SB1("bp", [128, 24 * t], BF16)
    stgF = SB1("stgF", [128, 4, 512], F32)
    srow = SB1("srow", [1, 3, 512], F32)
    lgf = SB1("lgf", [128, t], F32)
    lgTt = SB1("lgTt", [128, 128], F32)
    Eb = SB1("Eb", [128, 128], F32)
    Ea = SB1("Ea", [128, 128], F32)
    Ec = SB1("Ec", [128, 128], F32)
    Mb = SB1("Mb", [128, 128], F32)
    vc = SB1("vc", [128, 8], F32)
    cnb = SB1("cnb", [128, 128], F32)
    cnf = SB1("cnf", [128, t], BF16)
    mixs = SB1("mixs", [128, 8], F32)
    mixb = SB1("mixb", [128, 8], BF16)
    a1s = SB1("a1s", [128, 8], F32)
    esb = SB1("esb", [128, 1], F32)
    efb = SB1("efb", [8, t], BF16)
    wrb = SB1("wrb", [128, 16 * 128], BF16)
    tb = SB1("tb", [128, 2, 512], BF16)
    pall = esB1.enter_context(nc.psum_tensor("pall", [128, 8, 512], F32))
    ebs = bp[:, 0:8 * t]
    sgv = bp[:, 8 * t:16 * t]
    sgs = bp[:, 0:16 * t]
    comb = bp[:, 16 * t:24 * t]
    sq = lgf[:, 0:512]
    bank_rd = [None] * 8

    p.emit("gpsimd", lambda g: g.dma_start(out=mixs[:], in_=a0),
           wait=phaseA_join, inc="dw", inc_amt=16)
    p.emit("gpsimd", lambda g: g.dma_start(out=a1s[:], in_=a1), inc="dw", inc_amt=16)
    p.emit("sync", lambda s: s.dma_start(out=esb[:], in_=esc),
           wait=phaseA_join, inc="dws", inc_amt=16)
    t_pre_s1 = (("dws", p.cnt["dws"]))
    t_rb = p.emit("gpsimd", lambda g: g.dma_start(
        out=wrb[:].rearrange("p (a b) -> p a b", a=16),
        in_=wrout.rearrange("(a p) b -> p a b", p=128)), inc="dw", inc_amt=16)

    s0 = p.emit("scalar", lambda s: s.activation(out=Eb[:, 0:8], in_=mixs[:], func=AF.Square),
                wait=[t_rb, phaseA_join[2]], inc="s")
    s1 = p.emit("scalar", lambda s: s.activation(out=Eb[:, 8:16], in_=a1s[:], func=AF.Square),
                wait=[t_rb, s0], inc="s")
    v0 = p.emit("vector", lambda v: v.tensor_add(mixs[:], Eb[:, 0:8], Eb[:, 8:16]),
                wait=[s1, phaseA_join[1]], inc="v")
    v1_ = p.emit("vector", lambda v: v.reduce_sum(vc[:, 0:1], mixs[:], axis=AX.X), wait=[v0], inc="v")
    v2_ = p.emit("vector", lambda v: v.tensor_scalar_add(vc[:, 0:1], vc[:, 0:1], 1e-8), wait=[v1_], inc="v")
    v3_ = p.emit("vector", lambda v: v.reciprocal(vc[:, 1:2], vc[:, 0:1]), wait=[v2_], inc="v")
    v4_ = p.emit("vector", lambda v: v.tensor_scalar(mixs[:], mixs[:], vc[:, 1:2], None, OP.mult),
                 wait=[v3_], inc="v")
    v5_ = p.emit("vector", lambda v: v.tensor_scalar(mixs[:], mixs[:], esb[:], None, OP.mult),
                 wait=[v4_, t_pre_s1], inc="v")
    t_mix = p.emit("vector", lambda v: v.tensor_copy(mixb[:], mixs[:]), wait=[v5_], inc="v")

    pre1_tok = {}
    stg_rd = [phaseA_join[0], None, None, None]
    for j in range(16):
        for th in range(TH):
            slr = (j * TH + th) % 2
            slx = 2 + (j * TH + th) % 2
            ldr_ = p.emit("sync", lambda s, j=j, th=th, slr=slr: s.dma_start(
                out=stgF[:, slr, :], in_=rsout.ap()[128 * j:128 * (j + 1), 512 * th:512 * (th + 1)]),
                wait=[("cc", 1), stg_rd[slr]], inc="drs%d" % slr, inc_amt=16)
            ldx_ = p.emit("sync", lambda s, j=j, th=th, slx=slx: s.dma_start(
                out=stgF[:, slx, :], in_=xme[128 * j:128 * (j + 1), 512 * th:512 * (th + 1)]),
                wait=[stg_rd[slx]], inc="dxm%d" % (slx - 2), inc_amt=16)
            tk = p.emit("vector", lambda v, j=j, th=th, slr=slr, slx=slx: v.tensor_add(
                resA[:, j, 512 * th:512 * (th + 1)], stgF[:, slx, :], stgF[:, slr, :]),
                wait=[ldr_, ldx_], inc="v")
            pre1_tok[(j, th)] = tk
            stg_rd[slr] = tk
            stg_rd[slx] = tk

    srow_rd = [None]
    def emit_lnB1(src_ready, x1b_out):
        apply_toks = {}
        sq_rd = [None]
        for th in range(TH):
            mm1 = mm2 = None
            for j in range(16):
                sqt = p.emit("scalar", lambda s, j=j, th=th: s.activation(
                    out=sq, in_=resA[:, j, 512 * th:512 * (th + 1)], func=AF.Square),
                    wait=[src_ready[(j, th)], sq_rd[0]], inc="s")
                mm1 = p.emit("tensor", lambda tt, j=j, th=th: tt.matmul(
                    pall[0:1, 0, :], onesb[:, 0:1], resA[:, j, 512 * th:512 * (th + 1)],
                    start=(j == 0), stop=(j == 15)),
                    wait=[src_ready[(j, th)], bank_rd[0] if j == 0 else None], inc="t")
                mm2 = p.emit("tensor", lambda tt, j=j: tt.matmul(
                    pall[0:1, 1, :], onesb[:, 0:1], sq, start=(j == 0), stop=(j == 15)),
                    wait=[sqt, bank_rd[1] if j == 0 else None], inc="t")
                sq_rd[0] = mm2
            a_ = p.emit("vector", lambda v: v.tensor_scalar(
                srow[0:1, 0, :], pall[0:1, 0, :], 1.0 / 2048.0, None, OP.mult),
                wait=[mm1, srow_rd[0]], inc="v")
            bank_rd[0] = a_
            b2_ = p.emit("vector", lambda v: v.tensor_mul(
                srow[0:1, 2, :], srow[0:1, 0, :], srow[0:1, 0, :]), wait=[a_], inc="v")
            c_ = p.emit("vector", lambda v: v.scalar_tensor_tensor(
                srow[0:1, 2, :], pall[0:1, 1, :], 1.0 / 2048.0, srow[0:1, 2, :],
                OP.mult, OP.subtract),
                wait=[mm2, b2_], inc="v")
            bank_rd[1] = c_
            c2_ = p.emit("vector", lambda v: v.tensor_scalar_add(
                srow[0:1, 2, :], srow[0:1, 2, :], 1e-5), wait=[c_], inc="v")
            d_ = p.emit("scalar", lambda s: s.activation(
                out=srow[0:1, 1, :], in_=srow[0:1, 2, :], func=AF.Sqrt), wait=[c2_], inc="s")
            e_ = p.emit("vector", lambda v: v.reciprocal(srow[0:1, 2, :], srow[0:1, 1, :]),
                        wait=[d_], inc="v")
            bm = p.emit("tensor", lambda tt: tt.matmul(
                pall[:, 2, :], onesb[0:1, :], srow[0:1, 0, :], start=True, stop=True),
                wait=[a_, bank_rd[2]], inc="t")
            br = p.emit("tensor", lambda tt: tt.matmul(
                pall[:, 3, :], onesb[0:1, :], srow[0:1, 2, :], start=True, stop=True),
                wait=[e_, bank_rd[3]], inc="t")
            srow_rd[0] = br
            f2 = None
            for j in range(16):
                f1 = p.emit("vector", lambda v, j=j, th=th: v.tensor_sub(
                    resA[:, j, 512 * th:512 * (th + 1)],
                    resA[:, j, 512 * th:512 * (th + 1)], pall[:, 2, :]),
                    wait=[bm], inc="v")
                f2 = p.emit("vector", lambda v, j=j, th=th: v.tensor_mul(
                    resA[:, j, 512 * th:512 * (th + 1)],
                    resA[:, j, 512 * th:512 * (th + 1)], pall[:, 3, :]),
                    wait=[f1, br], inc="v")
                tokj = f2
                if x1b_out is not None:
                    tokj = p.emit("vector", lambda v, j=j, th=th: v.tensor_copy(
                        x1b_out[:, j, 512 * th:512 * (th + 1)],
                        resA[:, j, 512 * th:512 * (th + 1)]),
                        wait=[f2], inc="v")
                apply_toks[(j, th)] = tokj
            bank_rd[2] = f2
            bank_rd[3] = f2
        return apply_toks

    x1_tok = emit_lnB1(pre1_tok, x1b)

    lg_ev = [None] * TH
    for th in range(TH):
        mm = None
        for d in range(16):
            mm = p.emit("tensor", lambda tt, d=d, th=th: tt.matmul(
                pall[:, 4, :], wrb[:, 128 * d:128 * (d + 1)], x1b[:, d, 512 * th:512 * (th + 1)],
                start=(d == 0), stop=(d == 15)),
                wait=[x1_tok[(d, th)], t_rb, bank_rd[4] if d == 0 else None], inc="t")
        lg_ev[th] = p.emit("scalar", lambda s, th=th: s.copy(
            out=lgf[:, 512 * th:512 * (th + 1)], in_=pall[:, 4, :]), wait=[mm], inc="s")
        bank_rd[4] = lg_ev[th]
    cn_done = [None] * (t // 128)
    lgT_rd = [None]
    for tt_ in range(t // 128):
        th = (tt_ * 128) // 512
        tr1 = p.emit("tensor", lambda tt, tt_=tt_: tt.transpose(
            pall[:, 5, 0:128], lgf[:, 128 * tt_:128 * (tt_ + 1)], idb[:]),
            wait=[lg_ev[th], bank_rd[5]], inc="t")
        ev1 = p.emit("vector", lambda v: v.tensor_copy(lgTt[:], pall[:, 5, 0:128]),
                     wait=[tr1, lgT_rd[0]], inc="v")
        bank_rd[5] = ev1
        m0 = p.emit("vector", lambda v: v.reduce_max(vc[:, 0:1], lgTt[:], axis=AX.X, negate=True),
                    wait=[ev1], inc="v")
        ex = p.emit("scalar", lambda s: s.activation(out=Eb[:], in_=lgTt[:], func=AF.Exp,
                                                     bias=vc[:, 0:1]),
                    wait=[m0], inc="s")
        lgT_rd[0] = ex
        sS = p.emit("vector", lambda v: v.reduce_sum(vc[:, 1:2], Eb[:], axis=AX.X), wait=[ex], inc="v")
        vv1 = p.emit("vector", lambda v: v.reduce_max(vc[:, 2:3], Eb[:], axis=AX.X), wait=[ex], inc="v")
        mk1 = p.emit("vector", lambda v: v.tensor_scalar(Mb[:], Eb[:], vc[:, 2:3], None, OP.is_ge),
                     wait=[vv1], inc="v")
        e1 = p.emit("vector", lambda v: v.scalar_tensor_tensor(Ea[:], Mb[:], -1e30, Eb[:], OP.mult, OP.add),
                    wait=[mk1], inc="v")
        vv2 = p.emit("vector", lambda v: v.reduce_max(vc[:, 3:4], Ea[:], axis=AX.X), wait=[e1], inc="v")
        mk2 = p.emit("vector", lambda v: v.tensor_scalar(Mb[:], Ea[:], vc[:, 3:4], None, OP.is_ge),
                     wait=[vv2], inc="v")
        e2 = p.emit("vector", lambda v: v.scalar_tensor_tensor(Ec[:], Mb[:], -1e30, Ea[:], OP.mult, OP.add),
                    wait=[mk2], inc="v")
        vv3 = p.emit("vector", lambda v: v.reduce_max(vc[:, 4:5], Ec[:], axis=AX.X), wait=[e2], inc="v")
        mk3 = p.emit("vector", lambda v: v.tensor_scalar(Mb[:], Ec[:], vc[:, 4:5], None, OP.is_ge),
                     wait=[vv3], inc="v")
        e3 = p.emit("vector", lambda v: v.scalar_tensor_tensor(Ea[:], Mb[:], -1e30, Ec[:], OP.mult, OP.add),
                    wait=[mk3], inc="v")
        vv4 = p.emit("vector", lambda v: v.reduce_max(vc[:, 5:6], Ea[:], axis=AX.X), wait=[e3], inc="v")
        t1 = p.emit("vector", lambda v: v.tensor_add(vc[:, 6:7], vc[:, 2:3], vc[:, 3:4]),
                    wait=[vv2, vv1], inc="v")
        t2 = p.emit("vector", lambda v: v.tensor_add(vc[:, 7:8], vc[:, 6:7], vc[:, 4:5]),
                    wait=[t1, vv3], inc="v")
        t3 = p.emit("vector", lambda v: v.tensor_add(vc[:, 6:7], vc[:, 7:8], vc[:, 5:6]),
                    wait=[t2, vv4], inc="v")
        den = p.emit("vector", lambda v: v.scalar_tensor_tensor(vc[:, 7:8], vc[:, 1:2], 1e-8,
                                                                vc[:, 6:7], OP.mult, OP.add),
                     wait=[t3, sS], inc="v")
        rd = p.emit("vector", lambda v: v.reciprocal(vc[:, 6:7], vc[:, 7:8]), wait=[den], inc="v")
        mk4 = p.emit("vector", lambda v: v.tensor_scalar(Mb[:], Eb[:], vc[:, 5:6], None, OP.is_ge),
                     wait=[vv4], inc="v")
        cm = p.emit("vector", lambda v: v.tensor_mul(cnb[:], Eb[:], Mb[:]), wait=[mk4], inc="v")
        cs_ = p.emit("vector", lambda v: v.tensor_scalar(cnb[:], cnb[:], vc[:, 6:7], None, OP.mult),
                     wait=[cm, rd], inc="v")
        tr2 = p.emit("tensor", lambda tt, tt_=tt_: tt.transpose(
            pall[:, 5, 0:128], cnb[:], idb[:]), wait=[cs_, bank_rd[5]], inc="t")
        ev2 = p.emit("vector", lambda v, tt_=tt_: v.tensor_copy(
            cnf[:, 128 * tt_:128 * (tt_ + 1)], pall[:, 5, 0:128]), wait=[tr2], inc="v")
        bank_rd[5] = ev2
        cn_done[tt_] = ev2
    ef_ev = [None] * TH
    for th in range(TH):
        mm = p.emit("tensor", lambda tt, th=th: tt.matmul(
            pall[0:8, 6, :], mixb[:], cnf[:, 512 * th:512 * (th + 1)], start=True, stop=True),
            wait=[t_mix, bank_rd[6]] + cn_done[4 * th:4 * (th + 1)], inc="t")
        ef_ev[th] = p.emit("vector", lambda v, th=th: v.tensor_copy(
            efb[:, 512 * th:512 * (th + 1)], pall[0:8, 6, :]), wait=[mm], inc="v")
        bank_rd[6] = ef_ev[th]
    eb_tok = {}
    for k8 in range(8):
        for th in range(TH):
            b = 4 + (k8 * TH + th) % 2
            mm = p.emit("tensor", lambda tt, k8=k8, th=th, b=b: tt.matmul(
                pall[:, b, :], ind8b[:, 128 * k8:128 * (k8 + 1)], efb[0:8, 512 * th:512 * (th + 1)],
                start=True, stop=True),
                wait=[ef_ev[th], t_pre_gS, bank_rd[b]], inc="t")
            evx = p.emit("scalar", lambda s, k8=k8, th=th, b=b: s.copy(
                out=ebs[:, k8 * t + 512 * th:k8 * t + 512 * (th + 1)], in_=pall[:, b, :]),
                wait=[mm], inc="s")
            bank_rd[b] = evx
            eb_tok[(k8, th)] = evx

    def slab_loadB1(src, cols, gate, n_dma=16, rows=128):
        tk = None
        for d in range(n_dma):
            w = 8192 // n_dma
            tk = p.emit("gpsimd", lambda g, d=d, w=w: g.dma_start(
                out=wslab[:, w * d:w * (d + 1)], in_=src[rows * d:rows * (d + 1), cols]),
                wait=[gate if d == 0 else None], inc="dsl", inc_amt=16)
        return tk

    comb_tok = {}
    slab_cons = phaseA_join[0]
    sgv_rd = {}
    tb_rd = [None, None]
    for k8 in range(8):
        sg_tok = {}
        for ih in range(2):
            ldg = slab_loadB1(wgate, slice(1024 * k8 + 512 * ih, 1024 * k8 + 512 * (ih + 1)), slab_cons)
            for i4 in range(4):
                i = 4 * ih + i4
                for th in range(TH):
                    b = (i4 * TH + th) % 4
                    mm = p.emit("tensor", lambda tt, i4=i4, th=th, b=b: [tt.matmul(
                        pall[:, b, :], wslab[:, 512 * d + 128 * i4:512 * d + 128 * (i4 + 1)],
                        x1b[:, d, 512 * th:512 * (th + 1)],
                        start=(d == 0), stop=(d == 15)) for d in range(16)][-1],
                        wait=[ldg, x1_tok[(15, th)], bank_rd[b]], inc="t")
                    sg_tok[(i, th)] = p.emit("scalar", lambda s, i=i, th=th, b=b: s.activation(
                        out=sgv[:, i * t + 512 * th:i * t + 512 * (th + 1)], in_=pall[:, b, :],
                        func=AF.Silu),
                        wait=[mm, sgv_rd.get((i, th))], inc="s")
                    bank_rd[b] = sg_tok[(i, th)]
                    slab_cons = mm
        for ih in range(2):
            ldu = slab_loadB1(wup, slice(1024 * k8 + 512 * ih, 1024 * k8 + 512 * (ih + 1)), slab_cons)
            for i4 in range(4):
                i = 4 * ih + i4
                for th in range(TH):
                    b = (i4 * TH + th) % 4
                    mm = p.emit("tensor", lambda tt, i4=i4, th=th, b=b: [tt.matmul(
                        pall[:, b, :], wslab[:, 512 * d + 128 * i4:512 * d + 128 * (i4 + 1)],
                        x1b[:, d, 512 * th:512 * (th + 1)],
                        start=(d == 0), stop=(d == 15)) for d in range(16)][-1],
                        wait=[ldu, bank_rd[b]], inc="t")
                    slab_cons = mm
                    sltb = (i * TH + th) % 2
                    upe = p.emit("vector", lambda v, th=th, b=b, k8=k8, sltb=sltb: v.tensor_mul(
                        tb[:, sltb, :], pall[:, b, :],
                        ebs[:, k8 * t + 512 * th:k8 * t + 512 * (th + 1)]),
                        wait=[mm, eb_tok[(k8, th)], tb_rd[sltb]], inc="v")
                    bank_rd[b] = upe
                    if k8 == 0:
                        pr = p.emit("vector", lambda v, i=i, th=th, sltb=sltb: v.tensor_mul(
                            comb[:, i * t + 512 * th:i * t + 512 * (th + 1)],
                            sgv[:, i * t + 512 * th:i * t + 512 * (th + 1)], tb[:, sltb, :]),
                            wait=[upe, sg_tok[(i, th)]], inc="v")
                        comb_tok[(i, th)] = pr
                        tb_rd[sltb] = pr
                        sgv_rd[(i, th)] = pr
                    else:
                        pr = p.emit("vector", lambda v, i=i, th=th, sltb=sltb: v.tensor_mul(
                            tb[:, sltb, :], sgv[:, i * t + 512 * th:i * t + 512 * (th + 1)],
                            tb[:, sltb, :]),
                            wait=[upe, sg_tok[(i, th)]], inc="v")
                        sgv_rd[(i, th)] = pr
                        ad = p.emit("vector", lambda v, i=i, th=th, sltb=sltb: v.tensor_add(
                            comb[:, i * t + 512 * th:i * t + 512 * (th + 1)],
                            comb[:, i * t + 512 * th:i * t + 512 * (th + 1)], tb[:, sltb, :]),
                            wait=[pr, comb_tok[(i, th)]], inc="v")
                        comb_tok[(i, th)] = ad
                        tb_rd[sltb] = ad

    down_tok = {}
    for fh in range(2):
        ldd = slab_loadB1(wdown, slice(1024 * fh, 1024 * (fh + 1)), slab_cons, n_dma=8)
        for f8 in range(8):
            f = 8 * fh + f8
            for th in range(TH):
                b = (f8 * TH + th) % 4
                mm = p.emit("tensor", lambda tt, f8=f8, th=th, b=b: [tt.matmul(
                    pall[:, b, :], wslab[:, 1024 * ic + 128 * f8:1024 * ic + 128 * (f8 + 1)],
                    comb[:, ic * t + 512 * th:ic * t + 512 * (th + 1)],
                    start=(ic == 0), stop=(ic == 7)) for ic in range(8)][-1],
                    wait=[ldd, bank_rd[b]] + [comb_tok[(ic, th)] for ic in range(8)], inc="t")
                slab_cons = mm
                ra = p.emit("vector", lambda v, f=f, th=th, b=b: v.tensor_add(
                    resA[:, f, 512 * th:512 * (th + 1)],
                    resA[:, f, 512 * th:512 * (th + 1)], pall[:, b, :]),
                    wait=[mm, x1_tok[(f, th)]], inc="v")
                bank_rd[b] = ra
                down_tok[(f, th)] = ra

    sgs_tok = {}
    for qt in range(4):
        ldsg = slab_loadB1(wsg, slice(512 * qt, 512 * (qt + 1)), slab_cons)
        for f4 in range(4):
            f = 4 * qt + f4
            for th in range(TH):
                b = (f4 * TH + th) % 4
                mm = p.emit("tensor", lambda tt, f4=f4, th=th, b=b: [tt.matmul(
                    pall[:, b, :], wslab[:, 512 * d + 128 * f4:512 * d + 128 * (f4 + 1)],
                    x1b[:, d, 512 * th:512 * (th + 1)],
                    start=(d == 0), stop=(d == 15)) for d in range(16)][-1],
                    wait=[ldsg, bank_rd[b]], inc="t")
                slab_cons = mm
                sgs_tok[(f, th)] = p.emit("scalar", lambda s, f=f, th=th, b=b: s.activation(
                    out=sgs[:, f * t + 512 * th:f * t + 512 * (th + 1)], in_=pall[:, b, :],
                    func=AF.Silu),
                    wait=[mm], inc="s")
                bank_rd[b] = sgs_tok[(f, th)]
    for qt in range(4):
        ldsu = slab_loadB1(wsu, slice(512 * qt, 512 * (qt + 1)), slab_cons)
        for f4 in range(4):
            f = 4 * qt + f4
            for th in range(TH):
                b = (f4 * TH + th) % 4
                mm = p.emit("tensor", lambda tt, f4=f4, th=th, b=b: [tt.matmul(
                    pall[:, b, :], wslab[:, 512 * d + 128 * f4:512 * d + 128 * (f4 + 1)],
                    x1b[:, d, 512 * th:512 * (th + 1)],
                    start=(d == 0), stop=(d == 15)) for d in range(16)][-1],
                    wait=[ldsu, bank_rd[b]], inc="t")
                slab_cons = mm
                up2 = p.emit("vector", lambda v, f=f, th=th, b=b: v.tensor_mul(
                    sgs[:, f * t + 512 * th:f * t + 512 * (th + 1)],
                    sgs[:, f * t + 512 * th:f * t + 512 * (th + 1)], pall[:, b, :]),
                    wait=[mm, sgs_tok[(f, th)]], inc="v")
                bank_rd[b] = up2
                sgs_tok[(f, th)] = up2
    for qt in range(4):
        ldsd = slab_loadB1(wsd, slice(512 * qt, 512 * (qt + 1)), slab_cons)
        for f4 in range(4):
            f = 4 * qt + f4
            for th in range(TH):
                b = (f4 * TH + th) % 4
                mm = p.emit("tensor", lambda tt, f4=f4, th=th, b=b: [tt.matmul(
                    pall[:, b, :], wslab[:, 512 * d + 128 * f4:512 * d + 128 * (f4 + 1)],
                    sgs[:, d * t + 512 * th:d * t + 512 * (th + 1)],
                    start=(d == 0), stop=(d == 15)) for d in range(16)][-1],
                    wait=[ldsd, bank_rd[b]] + [sgs_tok[(d_, th)] for d_ in range(16)], inc="t")
                slab_cons = mm
                ra = p.emit("vector", lambda v, f=f, th=th, b=b: v.tensor_add(
                    resA[:, f, 512 * th:512 * (th + 1)],
                    resA[:, f, 512 * th:512 * (th + 1)], pall[:, b, :]),
                    wait=[mm, down_tok[(f, th)]], inc="v")
                bank_rd[b] = ra
                down_tok[(f, th)] = ra

    x2_tok = emit_lnB1(down_tok, None)
    ec_rd = [None, None]
    st_e = None
    for j in range(16):
        sl = j % 2
        e1_ = p.emit("vector", lambda v, j=j, sl=sl: v.tensor_copy(
            tb[:, sl, 0:64], resA[:, j, 0:64]),
            wait=[x2_tok[(j, 0)], ec_rd[sl]], inc="v")
        e2_ = p.emit("vector", lambda v, j=j, sl=sl: v.tensor_copy(
            tb[:, sl, 64:128], resA[:, j, t - 64:t]),
            wait=[x2_tok[(j, TH - 1)], e1_], inc="v")
        st_e = p.emit("sync", lambda s, j=j, sl=sl: s.dma_start(
            out=edgebuf.ap()[:, 128 * j:128 * (j + 1)], in_=tb[:, sl, 0:128]),
            wait=[e2_], inc="dse%d" % sl, inc_amt=16)
        ec_rd[sl] = st_e
    t_ag = p.emit("gpsimd", lambda g: g.collective_compute(
        "AllGather", OP.bypass, rg, [edgebuf.ap().opt()], [edgeall.ap().opt()]),
        wait=[("dse0", p.cnt.get("dse0", 0)), ("dse1", p.cnt.get("dse1", 0)), ("cc", 1)],
        inc="cc")
    phaseB1_join = [("t", p.cnt.get("t", 0)), ("v", p.cnt.get("v", 0)),
                    ("s", p.cnt.get("s", 0)), ("dsl", p.cnt.get("dsl", 0))]
    esB1.close()
    esB2 = contextlib.ExitStack()
    SB2 = lambda nm, sh, dt: esB2.enter_context(nc.sbuf_tensor(nm, sh, dt))
    XR = SB2("XR", [128, 16 * th_], BF16)
    ob = SB2("ob", [128, 16 * t], BF16)
    kbuf = SB2("kbuf", [128, 4 * th_], BF16)
    krob = SB2("krob", [128, 4 * th_], BF16)
    vbuf = SB2("vbuf", [128, 4 * th_], BF16)
    qhb = SB2("qhb", [128, t], BF16)
    qrob = SB2("qrob", [128, t], BF16)
    cs1b = SB2("cs1b", [128, th_], BF16)
    cs2b = SB2("cs2b", [128, th_], BF16)
    scr = SB2("scr", [128, 2, 512], F32)
    srowB = SB2("srowB", [1, 3, 512], F32)
    sqB = SB2("sqB", [128, 512], F32)
    attb = SB2("attb", [128, 256], BF16)
    attTb = SB2("attTb", [128, 256], BF16)
    g8 = SB2("g8", [128, 8], F32)
    gE = SB2("gE", [128, 8], F32)
    gE1 = SB2("gE1", [128, 8], F32)
    gM = SB2("gM", [128, 8], F32)
    rs = SB2("rs", [128, 4], F32)
    konst = SB2("konst", [128, 1], F32)
    pall7 = esB2.enter_context(nc.psum_tensor("pall7", [128, 7, 512], F32))
    pbt = esB2.enter_context(nc.psum_tensor("pbt", [128, 1024], BF16))
    pall = pall7
    x2hb = XR[:].rearrange("p (a b) -> p a b", a=16)
    lnreg = XR[:, 0:16 * t].rearrange("p (a b) -> p a b", a=16)
    gelu = ob[:].rearrange("p (a b) -> p a b", a=16)
    bank_rd = [None] * 8

    p.emit("gpsimd", lambda g: g.dma_start(out=cs1b[:], in_=cs1),
           wait=phaseB1_join, inc="dw", inc_amt=16)
    t_cs = p.emit("gpsimd", lambda g: g.dma_start(out=cs2b[:], in_=cs2), inc="dw", inc_amt=16)
    t_k0 = p.emit("vector", lambda v: v.memset(konst[:], 0.08838834764831845),
                  wait=[phaseB1_join[1]], inc="v")
    mmsk = p.emit("tensor", lambda tt: tt.matmul(
        pall[:, 6, 0:16], onesb[0:1, :], sinkb[:], start=True, stop=True),
        wait=phaseB1_join[:1], inc="t")
    t_sink = p.emit("vector", lambda v: v.tensor_copy(sinkbc[:], pall[:, 6, 0:16]),
                    wait=[mmsk], inc="v")
    bank_rd[6] = t_sink

    cast_tok = {}
    for j in range(16):
        cast_tok[j] = p.emit("vector", lambda v, j=j: v.tensor_copy(
            x2hb[:, j, HALO:HALO + t], resA[:, j, :]),
            wait=[x2_tok[(j, 0)], x2_tok[(j, TH - 1)], phaseB1_join[1]], inc="v")
    msL = p.emit("vector", lambda v: v.memset(x2hb[:, :, 0:HALO], 0.0),
                 wait=[phaseB1_join[1]], inc="v")
    msR = p.emit("vector", lambda v: v.memset(x2hb[:, :, t + HALO:t + 2 * HALO], 0.0),
                 wait=[phaseB1_join[1]], inc="v")
    selL = msL
    selR = msR
    for grp in range(2):
        eld = p.emit("gpsimd", lambda g, grp=grp: g.dma_start(
            out=wslab[:].rearrange("p (r x) -> p r x", r=4),
            in_=edgeall.ap().rearrange("(r p) x -> p r x", p=128)[:, 4 * grp:4 * (grp + 1), :]),
            wait=[("cc", 2), selL if grp == 1 else None, selR if grp == 1 else None]
                 + (phaseB1_join if grp == 0 else []),
            inc="dw", inc_amt=16)
        for rr in range(4):
            r_ = 4 * grp + rr
            selL = p.emit("vector", lambda v, rr=rr, r_=r_: v.scalar_tensor_tensor(
                x2hb[:, :, 0:HALO],
                wslab[:].rearrange("p (r j c) -> p r j c", r=4, j=16)[:, rr, :, 64:128],
                hwLb[:, r_:r_ + 1], x2hb[:, :, 0:HALO], OP.mult, OP.add),
                wait=[eld, selL, t_pre_s], inc="v")
            selR = p.emit("vector", lambda v, rr=rr, r_=r_: v.scalar_tensor_tensor(
                x2hb[:, :, t + HALO:t + 2 * HALO],
                wslab[:].rearrange("p (r j c) -> p r j c", r=4, j=16)[:, rr, :, 0:64],
                hwRb[:, r_:r_ + 1], x2hb[:, :, t + HALO:t + 2 * HALO], OP.mult, OP.add),
                wait=[eld, selR, t_pre_s], inc="v")
    x2h_done = [selL, selR] + list(cast_tok.values())

    def slab_loadB2(src_fn, n_dma, gate):
        tk = None
        w = 8192 // n_dma
        for d in range(n_dma):
            tk = p.emit("gpsimd", lambda g, d=d, w=w: g.dma_start(
                out=wslab[:, w * d:w * (d + 1)], in_=src_fn(d)),
                wait=[gate if d == 0 else None], inc="dsl", inc_amt=16)
        return tk

    slab_cons = tmax(selL, selR)
    TC = [(0, 512), (512, 1024), (1024, 1152)]
    kv_ev = {}
    for which, (wsrc, dst) in enumerate([(wkp, kbuf), (wv, vbuf)]):
        ld = slab_loadB2(lambda d, wsrc=wsrc: wsrc[128 * d:128 * (d + 1), :], 16,
                         slab_cons if which == 0 else slab_cons)
        for g4 in range(4):
            for ci, (c0, c1) in enumerate(TC):
                b = (g4 + ci) % 4
                mm = p.emit("tensor", lambda tt, g4=g4, c0=c0, c1=c1, b=b: [tt.matmul(
                    pall[:, b, 0:c1 - c0], wslab[:, 512 * d + 128 * g4:512 * d + 128 * (g4 + 1)],
                    x2hb[:, d, c0:c1], start=(d == 0), stop=(d == 15)) for d in range(16)][-1],
                    wait=[ld, bank_rd[b]] + x2h_done, inc="t")
                slab_cons = mm
                evk = p.emit("scalar", lambda s, dst=dst, g4=g4, c0=c0, c1=c1, b=b: s.copy(
                    out=dst[:, g4 * th_ + c0:g4 * th_ + c1], in_=pall[:, b, 0:c1 - c0]),
                    wait=[mm], inc="s")
                bank_rd[b] = evk
                kv_ev[(which, g4, ci)] = evk
    kro_tok = {}
    scr_rd = [None, None]
    for g4 in range(4):
        for ci, (c0, c1) in enumerate(TC):
            b = 4 + (g4 + ci) % 2
            mm = p.emit("tensor", lambda tt, g4=g4, c0=c0, c1=c1, b=b: tt.matmul(
                pall[:, b, 0:c1 - c0], permB[:], kbuf[:, g4 * th_ + c0:g4 * th_ + c1],
                start=True, stop=True),
                wait=[kv_ev[(0, g4, ci)], t_pre_gS, bank_rd[b]], inc="t")
            t1 = p.emit("vector", lambda v, g4=g4, c0=c0, c1=c1: v.tensor_mul(
                krob[:, g4 * th_ + c0:g4 * th_ + c1], kbuf[:, g4 * th_ + c0:g4 * th_ + c1],
                cs1b[:, c0:c1]),
                wait=[kv_ev[(0, g4, ci)], t_cs], inc="v")
            t2 = p.emit("vector", lambda v, c0=c0, c1=c1, b=b, sl=ci % 2: v.tensor_mul(
                scr[:, sl, 0:c1 - c0], pall[:, b, 0:c1 - c0], cs2b[:, c0:c1]),
                wait=[mm, t_cs, scr_rd[ci % 2]], inc="v")
            bank_rd[b] = t2
            t3 = p.emit("vector", lambda v, g4=g4, c0=c0, c1=c1, sl=ci % 2: v.tensor_add(
                krob[:, g4 * th_ + c0:g4 * th_ + c1], krob[:, g4 * th_ + c0:g4 * th_ + c1],
                scr[:, sl, 0:c1 - c0]),
                wait=[t1, t2], inc="v")
            scr_rd[ci % 2] = t3
            kro_tok[(g4, ci)] = t3
    vT_tok = {}
    for g4 in range(4):
        for kt in range(9):
            mm = p.emit("tensor", lambda tt, g4=g4, kt=kt: tt.transpose(
                pbt[:, 0:128] if kt % 2 == 0 else pbt[:, 128:256],
                vbuf[:, g4 * th_ + 128 * kt:g4 * th_ + 128 * (kt + 1)], idbB[:]),
                wait=[kv_ev[(1, g4, min(kt // 4, 2))], kv_ev[(1, g4, 2)],
                      bank_rd[7], kro_tok[(g4, 2)] if kt == 0 else None], inc="t")
            evv = p.emit("vector", lambda v, g4=g4, kt=kt: v.tensor_copy(
                kbuf[:, g4 * th_ + 128 * kt:g4 * th_ + 128 * (kt + 1)],
                pbt[:, 0:128] if kt % 2 == 0 else pbt[:, 128:256]),
                wait=[mm], inc="v")
            bank_rd[7] = evv
            vT_tok[(g4, kt)] = evv
    vT = kbuf

    o_tok = {}
    att_rd = [None]
    attT_rd = [None]
    qh_rd = [None]
    qro_rd = [None]
    ldq_cons = slab_cons
    ldq = None
    qev = None
    for h in range(16):
        g4 = h // 4
        if h % 4 == 0:
            ldq = slab_loadB2(lambda d, h=h: wqp[128 * d:128 * (d + 1),
                                                512 * (h // 4):512 * (h // 4 + 1)],
                              16, ldq_cons)
        hh = h % 4
        for thc in range(TH):
            b = thc % 2
            mm = p.emit("tensor", lambda tt, hh=hh, thc=thc, b=b: [tt.matmul(
                pall[:, b, :], wslab[:, 512 * d + 128 * hh:512 * d + 128 * (hh + 1)],
                x2hb[:, d, HALO + 512 * thc:HALO + 512 * (thc + 1)],
                start=(d == 0), stop=(d == 15)) for d in range(16)][-1],
                wait=[ldq, bank_rd[b]] + x2h_done, inc="t")
            ldq_cons = mm
            evq = p.emit("scalar", lambda s, thc=thc, b=b: s.copy(
                out=qhb[:, 512 * thc:512 * (thc + 1)], in_=pall[:, b, :]),
                wait=[mm, qh_rd[0]], inc="s")
            bank_rd[b] = evq
            if thc == TH - 1:
                qev = evq
        rope_toks = []
        for c in range(TH):
            b = 2 + c % 2
            mm = p.emit("tensor", lambda tt, c=c, b=b: tt.matmul(
                pall[:, b, :], permB[:], qhb[:, 512 * c:512 * (c + 1)], start=True, stop=True),
                wait=[qev, bank_rd[b]], inc="t")
            t1 = p.emit("vector", lambda v, c=c: v.tensor_mul(
                qrob[:, 512 * c:512 * (c + 1)], qhb[:, 512 * c:512 * (c + 1)],
                cs1b[:, HALO + 512 * c:HALO + 512 * (c + 1)]),
                wait=[qev, qro_rd[0], t_cs], inc="v")
            sl = c % 2
            t2 = p.emit("vector", lambda v, c=c, b=b, sl=sl: v.tensor_mul(
                scr[:, sl, 0:512], pall[:, b, :], cs2b[:, HALO + 512 * c:HALO + 512 * (c + 1)]),
                wait=[mm, scr_rd[sl]], inc="v")
            bank_rd[b] = t2
            t3 = p.emit("vector", lambda v, c=c, sl=sl: v.tensor_add(
                qrob[:, 512 * c:512 * (c + 1)], qrob[:, 512 * c:512 * (c + 1)], scr[:, sl, 0:512]),
                wait=[t1, t2], inc="v")
            scr_rd[sl] = t3
            rope_toks.append(t3)
        qh_rd[0] = rope_toks[-1]
        for i in range(8):
            mi = 0 if i == 0 else (2 if i == 7 else 1)
            b = 4 + i % 2
            sc_ = p.emit("tensor", lambda tt, h=h, i=i, g4=g4, b=b: tt.matmul(
                pall[:, b, 0:256], qrob[:, 128 * i:128 * (i + 1)],
                krob[:, g4 * th_ + 128 * i:g4 * th_ + 128 * i + 256],
                start=True, stop=True),
                wait=[rope_toks[(128 * i) // 512], rope_toks[(128 * i + 127) // 512],
                      kro_tok[(g4, 2)], bank_rd[b]], inc="t")
            m0 = p.emit("vector", lambda v, b=b: v.reduce_max(rs[:, 0:1], pall[:, b, 0:256],
                                                              axis=AX.X, negate=True),
                        wait=[sc_], inc="v")
            bb_ = p.emit("vector", lambda v, h=h: v.scalar_tensor_tensor(
                rs[:, 1:2], rs[:, 0:1], 0.08838834764831845, sinkbc[:, h:h + 1],
                OP.mult, OP.add),
                wait=[m0, t_sink], inc="v")
            ex = p.emit("scalar", lambda s, b=b: s.activation(
                out=scr[:, 0, 0:256], in_=pall[:, b, 0:256], func=AF.Exp,
                scale=konst[:], bias=rs[:, 1:2]),
                wait=[bb_, t_k0, scr_rd[0]], inc="s")
            bank_rd[b] = ex
            mk = p.emit("vector", lambda v, mi=mi: v.tensor_mul(
                scr[:, 1, 0:256], scr[:, 0, 0:256], maskb[:, mi, :]),
                wait=[ex, scr_rd[1], t_pre_gS], inc="v")
            scr_rd[0] = mk
            sm = p.emit("vector", lambda v: v.reduce_sum(rs[:, 2:3], scr[:, 1, 0:256], axis=AX.X),
                        wait=[mk], inc="v")
            rc = p.emit("vector", lambda v: v.reciprocal(rs[:, 3:4], rs[:, 2:3]), wait=[sm], inc="v")
            at = p.emit("vector", lambda v: v.tensor_scalar(
                attb[:], scr[:, 1, 0:256], rs[:, 3:4], None, OP.mult),
                wait=[rc, att_rd[0]], inc="v")
            scr_rd[1] = at
            ev2 = None
            for half in range(2):
                tr = p.emit("tensor", lambda tt, half=half: tt.transpose(
                    pbt[:, 128 * half:128 * (half + 1)], attb[:, 128 * half:128 * (half + 1)],
                    idbB[:]),
                    wait=[at, bank_rd[7]], inc="t")
                ev2 = p.emit("vector", lambda v, half=half: v.tensor_copy(
                    attTb[:, 128 * half:128 * (half + 1)], pbt[:, 128 * half:128 * (half + 1)]),
                    wait=[tr, attT_rd[0] if half == 0 else None], inc="v")
                bank_rd[7] = ev2
            att_rd[0] = ev2
            av = p.emit("tensor", lambda tt, g4=g4, i=i: [tt.matmul(
                pall[:, 6, 0:128], vT[:, g4 * th_ + 128 * (i + kk):g4 * th_ + 128 * (i + kk + 1)],
                attTb[:, 128 * kk:128 * (kk + 1)], start=(kk == 0), stop=(kk == 1))
                for kk in range(2)][-1],
                wait=[ev2, vT_tok[(g4, i)], vT_tok[(g4, i + 1)], bank_rd[6]], inc="t")
            attT_rd[0] = av
            oe = p.emit("scalar", lambda s, h=h, i=i: s.copy(
                out=ob[:, h * t + 128 * i:h * t + 128 * (i + 1)], in_=pall[:, 6, 0:128]),
                wait=[av, o_tok.get((h, i))], inc="s")
            bank_rd[6] = oe
            o_tok[(h, i)] = oe
        qro_rd[0] = sc_
    slab_cons = ldq_cons

    pre3_tok = {}
    for qt in range(4):
        ldo = slab_loadB2(lambda d, qt=qt: wo[128 * d:128 * (d + 1), 512 * qt:512 * (qt + 1)],
                          16, slab_cons)
        for f4 in range(4):
            f = 4 * qt + f4
            for thc in range(TH):
                b = (f4 * TH + thc) % 4
                mm = p.emit("tensor", lambda tt, f4=f4, thc=thc, b=b: [tt.matmul(
                    pall[:, b, :], wslab[:, 512 * d + 128 * f4:512 * d + 128 * (f4 + 1)],
                    ob[:, d * t + 512 * thc:d * t + 512 * (thc + 1)],
                    start=(d == 0), stop=(d == 15)) for d in range(16)][-1],
                    wait=[ldo, bank_rd[b], o_tok[(15, 7)]], inc="t")
                slab_cons = mm
                tk = p.emit("vector", lambda v, f=f, thc=thc, b=b: v.tensor_add(
                    resA[:, f, 512 * thc:512 * (thc + 1)],
                    resA[:, f, 512 * thc:512 * (thc + 1)], pall[:, b, :]),
                    wait=[mm, cast_tok[f]], inc="v")
                bank_rd[b] = tk
                pre3_tok[(f, thc)] = tk

    srow_rdB = [None]
    def emit_lnB2(src_ready, dst, gamma_beta=None, src_bf=None):
        apply_toks = {}
        sq_rd = [None]
        for thc in range(TH):
            mm1 = mm2 = None
            for j in range(16):
                srcap = (resA[:, j, 512 * thc:512 * (thc + 1)] if src_bf is None
                         else src_bf[:, j, 512 * thc:512 * (thc + 1)])
                sqt = p.emit("scalar", lambda s, srcap=srcap: s.activation(
                    out=sqB[:], in_=srcap, func=AF.Square),
                    wait=[src_ready[(j, thc)], sq_rd[0]], inc="s")
                mm1 = p.emit("tensor", lambda tt, srcap=srcap, j=j: tt.matmul(
                    pall[0:1, 0, :], onesb[:, 0:1] if src_bf is None else onesbB[:, 0:1],
                    srcap, start=(j == 0), stop=(j == 15)),
                    wait=[src_ready[(j, thc)], bank_rd[0] if j == 0 else None], inc="t")
                mm2 = p.emit("tensor", lambda tt, j=j: tt.matmul(
                    pall[0:1, 1, :], onesb[:, 0:1], sqB[:], start=(j == 0), stop=(j == 15)),
                    wait=[sqt, bank_rd[1] if j == 0 else None], inc="t")
                sq_rd[0] = mm2
            a_ = p.emit("vector", lambda v: v.tensor_scalar(
                srowB[0:1, 0, :], pall[0:1, 0, :], 1.0 / 2048.0, None, OP.mult),
                wait=[mm1, srow_rdB[0]], inc="v")
            bank_rd[0] = a_
            b2_ = p.emit("vector", lambda v: v.tensor_mul(
                srowB[0:1, 2, :], srowB[0:1, 0, :], srowB[0:1, 0, :]), wait=[a_], inc="v")
            c_ = p.emit("vector", lambda v: v.scalar_tensor_tensor(
                srowB[0:1, 2, :], pall[0:1, 1, :], 1.0 / 2048.0, srowB[0:1, 2, :],
                OP.mult, OP.subtract),
                wait=[mm2, b2_], inc="v")
            bank_rd[1] = c_
            c2_ = p.emit("vector", lambda v: v.tensor_scalar_add(
                srowB[0:1, 2, :], srowB[0:1, 2, :], 1e-5), wait=[c_], inc="v")
            d_ = p.emit("scalar", lambda s: s.activation(
                out=srowB[0:1, 1, :], in_=srowB[0:1, 2, :], func=AF.Sqrt), wait=[c2_], inc="s")
            e_ = p.emit("vector", lambda v: v.reciprocal(srowB[0:1, 2, :], srowB[0:1, 1, :]),
                        wait=[d_], inc="v")
            bm = p.emit("tensor", lambda tt: tt.matmul(
                pall[:, 2, :], onesb[0:1, :], srowB[0:1, 0, :], start=True, stop=True),
                wait=[a_, bank_rd[2]], inc="t")
            br = p.emit("tensor", lambda tt: tt.matmul(
                pall[:, 3, :], onesb[0:1, :], srowB[0:1, 2, :], start=True, stop=True),
                wait=[e_, bank_rd[3]], inc="t")
            srow_rdB[0] = br
            f2 = None
            for j in range(16):
                srcap = (resA[:, j, 512 * thc:512 * (thc + 1)] if src_bf is None
                         else src_bf[:, j, 512 * thc:512 * (thc + 1)])
                f1 = p.emit("vector", lambda v, srcap=srcap: v.tensor_sub(
                    scr[:, 0, 0:512], srcap, pall[:, 2, :]),
                    wait=[bm, scr_rd[0], src_ready[(j, thc)]], inc="v")
                if gamma_beta is None and dst is None:
                    f2 = p.emit("vector", lambda v, j=j, thc=thc: v.tensor_mul(
                        resA[:, j, 512 * thc:512 * (thc + 1)], scr[:, 0, 0:512], pall[:, 3, :]),
                        wait=[f1, br], inc="v")
                    tokj = f2
                else:
                    f2 = p.emit("vector", lambda v: v.tensor_mul(
                        scr[:, 1, 0:512], scr[:, 0, 0:512], pall[:, 3, :]),
                        wait=[f1, br, scr_rd[1]], inc="v")
                    gcol, bcol = gamma_beta
                    tokj = p.emit("vector", lambda v, j=j, thc=thc, gcol=gcol, bcol=bcol: v.tensor_scalar(
                        dst[:, j, 512 * thc:512 * (thc + 1)], scr[:, 1, 0:512],
                        lnpb[:, gcol * 16 + j:gcol * 16 + j + 1],
                        lnpb[:, bcol * 16 + j:bcol * 16 + j + 1],
                        OP.mult, OP.add),
                        wait=[f2], inc="v")
                    scr_rd[1] = tokj
                scr_rd[0] = f2
                apply_toks[(j, thc)] = tokj
            bank_rd[2] = f2
            bank_rd[3] = f2
        return apply_toks

    x3_tok = emit_lnB2(pre3_tok, None)
    ln1_tok = emit_lnB2(x3_tok, lnreg, gamma_beta=(0, 1))

    av2_tok = {}
    for qt in range(4):
        ldv = slab_loadB2(lambda d, qt=qt: wvselT[128 * d:128 * (d + 1), 512 * qt:512 * (qt + 1)],
                          16, slab_cons)
        for f4 in range(4):
            f = 4 * qt + f4
            for thc in range(TH):
                b = (f4 * TH + thc) % 4
                mm = p.emit("tensor", lambda tt, f4=f4, thc=thc, b=b: [tt.matmul(
                    pall[:, b, :], wslab[:, 512 * d + 128 * f4:512 * d + 128 * (f4 + 1)],
                    lnreg[:, d, 512 * thc:512 * (thc + 1)],
                    start=(d == 0), stop=(d == 15)) for d in range(16)][-1],
                    wait=[ldv, bank_rd[b]] + [ln1_tok[(d, thc)] for d in range(16)], inc="t")
                slab_cons = mm
                evv2 = p.emit("scalar", lambda s, f=f, thc=thc, b=b: s.copy(
                    out=ob[:, f * t + 512 * thc:f * t + 512 * (thc + 1)], in_=pall[:, b, :]),
                    wait=[mm, o_tok[(15, 7)] if f == 0 and thc == 0 else None], inc="s")
                bank_rd[b] = evv2
                av2_tok[(f, thc)] = evv2
    had_tok = {}
    pre_tok = {}
    for qt in range(4):
        ldp = slab_loadB2(lambda d, qt=qt: projT[128 * d:128 * (d + 1), 512 * qt:512 * (qt + 1)],
                          16, slab_cons)
        for f4 in range(4):
            f = 4 * qt + f4
            for thc in range(TH):
                b = (f4 * TH + thc) % 4
                mm = p.emit("tensor", lambda tt, f4=f4, thc=thc, b=b: [tt.matmul(
                    pall[:, b, :], wslab[:, 512 * d + 128 * f4:512 * d + 128 * (f4 + 1)],
                    ob[:, d * t + 512 * thc:d * t + 512 * (thc + 1)],
                    start=(d == 0), stop=(d == 15)) for d in range(16)][-1],
                    wait=[ldp, bank_rd[b]] + [av2_tok[(d, thc)] for d in range(16)], inc="t")
                slab_cons = mm
                hd = p.emit("vector", lambda v, f=f, thc=thc, b=b: v.tensor_add(
                    lnreg[:, f, 512 * thc:512 * (thc + 1)],
                    resA[:, f, 512 * thc:512 * (thc + 1)], pall[:, b, :]),
                    wait=[mm, ln1_tok[(f, thc)], x3_tok[(f, thc)]], inc="v")
                pr = p.emit("vector", lambda v, f=f, thc=thc, b=b: v.scalar_tensor_tensor(
                    resA[:, f, 512 * thc:512 * (thc + 1)],
                    resA[:, f, 512 * thc:512 * (thc + 1)], 2.0, pall[:, b, :],
                    OP.mult, OP.add),
                    wait=[hd], inc="v")
                bank_rd[b] = pr
                had_tok[(f, thc)] = hd
                pre_tok[(f, thc)] = pr

    ln2_tok = emit_lnB2(had_tok, lnreg, gamma_beta=(2, 3), src_bf=lnreg)

    lgG = wslab[:, 4096:6144].bitcast(F32)[0:8, :]
    gTb = wslab[0:8, 6144:7168]
    ldrr = p.emit("gpsimd", lambda g: g.dma_start(
        out=wslab[:, 0:128], in_=wre), wait=[slab_cons], inc="dsl", inc_amt=16)
    lg_ev = [None] * TH
    for thc in range(TH):
        mm = None
        for d in range(16):
            mm = p.emit("tensor", lambda tt, d=d, thc=thc: tt.matmul(
                pall[0:8, 4, :], wslab[:, 8 * d:8 * (d + 1)],
                lnreg[:, d, 512 * thc:512 * (thc + 1)],
                start=(d == 0), stop=(d == 15)),
                wait=[ldrr, ln2_tok[(d, thc)], bank_rd[4] if d == 0 else None], inc="t")
        lg_ev[thc] = p.emit("vector", lambda v, thc=thc: v.tensor_copy(
            lgG[:, 512 * thc:512 * (thc + 1)], pall[0:8, 4, :]), wait=[mm], inc="v")
        bank_rd[4] = lg_ev[thc]
    gT_done = []
    g8_rd = [None]
    for tt_ in range(t // 128):
        thc = (tt_ * 128) // 512
        tr = p.emit("tensor", lambda tt, tt_=tt_: tt.transpose(
            pall[:, 5, 0:8], lgG[0:8, 128 * tt_:128 * (tt_ + 1)], idb[0:8, 0:8]),
            wait=[lg_ev[thc], bank_rd[5]], inc="t")
        evg = p.emit("vector", lambda v: v.tensor_copy(g8[:], pall[:, 5, 0:8]),
                     wait=[tr, g8_rd[0]], inc="v")
        bank_rd[5] = evg
        m0 = p.emit("vector", lambda v: v.reduce_max(rs[:, 0:1], g8[:], axis=AX.X, negate=True),
                    wait=[evg], inc="v")
        exg = p.emit("scalar", lambda s: s.activation(out=gE[:], in_=g8[:], func=AF.Exp,
                                                      bias=rs[:, 0:1]),
                     wait=[m0], inc="s")
        g8_rd[0] = exg
        vv1 = p.emit("vector", lambda v: v.reduce_max(rs[:, 1:2], gE[:], axis=AX.X),
                     wait=[exg], inc="v")
        mk1 = p.emit("vector", lambda v: v.tensor_scalar(gM[:], gE[:], rs[:, 1:2], None, OP.is_ge),
                     wait=[vv1], inc="v")
        e1 = p.emit("vector", lambda v: v.scalar_tensor_tensor(gE1[:], gM[:], -1e30, gE[:],
                                                               OP.mult, OP.add),
                    wait=[mk1], inc="v")
        vv2 = p.emit("vector", lambda v: v.reduce_max(rs[:, 2:3], gE1[:], axis=AX.X),
                     wait=[e1], inc="v")
        dn = p.emit("vector", lambda v: v.tensor_add(rs[:, 3:4], rs[:, 1:2], rs[:, 2:3]),
                    wait=[vv2], inc="v")
        rc = p.emit("vector", lambda v: v.reciprocal(rs[:, 3:4], rs[:, 3:4]), wait=[dn], inc="v")
        mk2 = p.emit("vector", lambda v: v.tensor_scalar(gM[:], gE[:], rs[:, 2:3], None, OP.is_ge),
                     wait=[vv2, e1], inc="v")
        gm_ = p.emit("vector", lambda v: v.tensor_mul(gE[:], gE[:], gM[:]), wait=[mk2], inc="v")
        gs_ = p.emit("vector", lambda v: v.tensor_scalar(gE[:], gE[:], rs[:, 3:4], None, OP.mult),
                     wait=[gm_, rc], inc="v")
        tr2 = p.emit("tensor", lambda tt: tt.transpose(
            pall[0:8, 5, 0:128], gE[:], idb[:]),
            wait=[gs_, bank_rd[5]], inc="t")
        evg2 = p.emit("vector", lambda v, tt_=tt_: v.tensor_copy(
            gTb[0:8, 128 * tt_:128 * (tt_ + 1)] if False else gTb[:, 128 * tt_:128 * (tt_ + 1)],
            pall[0:8, 5, 0:128]),
            wait=[tr2], inc="v")
        bank_rd[5] = evg2
        g8_rd[0] = tr2
        gT_done.append(evg2)
    def gbv(e):
        if e < 4:
            return krob[:, e * t:(e + 1) * t]
        return vbuf[:, (e - 4) * t:(e - 3) * t]
    gb_tok = {}
    for e in range(8):
        for thc in range(TH):
            b = 4 + (e * TH + thc) % 2
            mm = p.emit("tensor", lambda tt, e=e, thc=thc, b=b: tt.matmul(
                pall[:, b, :], ind8b[:, 128 * e:128 * (e + 1)],
                gTb[:, 512 * thc:512 * (thc + 1)], start=True, stop=True),
                wait=[bank_rd[b], t_pre_gS] + gT_done[4 * thc:4 * (thc + 1)], inc="t")
            evb = p.emit("scalar", lambda s, e=e, thc=thc, b=b: s.copy(
                out=gbv(e)[:, 512 * thc:512 * (thc + 1)], in_=pall[:, b, :]),
                wait=[mm, kro_tok.get((3, 2)) if e == 0 and thc == 0 else None,
                      vT_tok[(3, 8)] if e == 4 and thc == 0 else None], inc="s")
            bank_rd[b] = evb
            gb_tok[(e, thc)] = evb
            slab_cons = mm

    gelu_tok = {}
    for e in range(8):
        for half in range(2):
            for s8 in range(4):
                fbase = 2048 * half + 512 * s8
                ldw1 = slab_loadB2(lambda d, e=e, fbase=fbase: mw1[e, 128 * d:128 * (d + 1),
                                                                  fbase:fbase + 512],
                                   16, slab_cons)
                for f4 in range(4):
                    lc = 4 * s8 + f4
                    for thc in range(TH):
                        b = (f4 * TH + thc) % 4
                        mm = p.emit("tensor", lambda tt, f4=f4, thc=thc, b=b: [tt.matmul(
                            pall[:, b, :], wslab[:, 512 * d + 128 * f4:512 * d + 128 * (f4 + 1)],
                            lnreg[:, d, 512 * thc:512 * (thc + 1)],
                            start=(d == 0), stop=(d == 15)) for d in range(16)][-1],
                            wait=[ldw1, bank_rd[b]] +
                                 ([ln2_tok[(d, thc)] for d in range(16)] if e == 0 else []),
                            inc="t")
                        slab_cons = mm
                        ge_ = p.emit("scalar", lambda s, lc=lc, thc=thc, b=b: s.activation(
                            out=gelu[:, lc, 512 * thc:512 * (thc + 1)], in_=pall[:, b, :],
                            func=AF.Gelu),
                            wait=[mm, gelu_tok.get((half, lc, thc)),
                                  av2_tok[(lc, thc)] if e == 0 and half == 0 else None], inc="s")
                        gmul = p.emit("vector", lambda v, lc=lc, thc=thc, e=e: v.tensor_mul(
                            gelu[:, lc, 512 * thc:512 * (thc + 1)],
                            gelu[:, lc, 512 * thc:512 * (thc + 1)],
                            gbv(e)[:, 512 * thc:512 * (thc + 1)]),
                            wait=[ge_, gb_tok[(e, thc)]], inc="v")
                        bank_rd[b] = gmul
                        gelu_tok[(half, lc, thc)] = gmul
            for qt in range(4):
                ldw2 = slab_loadB2(lambda d, e=e, half=half, qt=qt: mw2[
                    e, 2048 * half + 128 * d:2048 * half + 128 * (d + 1),
                    512 * qt:512 * (qt + 1)], 16, slab_cons)
                for f4 in range(4):
                    f = 4 * qt + f4
                    for thc in range(TH):
                        b = (f4 * TH + thc) % 4
                        mm = p.emit("tensor", lambda tt, f4=f4, thc=thc, b=b: [tt.matmul(
                            pall[:, b, :], wslab[:, 512 * d + 128 * f4:512 * d + 128 * (f4 + 1)],
                            gelu[:, d, 512 * thc:512 * (thc + 1)],
                            start=(d == 0), stop=(d == 15)) for d in range(16)][-1],
                            wait=[ldw2, bank_rd[b]] +
                                 [gelu_tok[(half, d, thc)] for d in range(16)], inc="t")
                        slab_cons = mm
                        ra = p.emit("vector", lambda v, f=f, thc=thc, b=b: v.tensor_add(
                            resA[:, f, 512 * thc:512 * (thc + 1)],
                            resA[:, f, 512 * thc:512 * (thc + 1)], pall[:, b, :]),
                            wait=[mm, pre_tok[(f, thc)]], inc="v")
                        bank_rd[b] = ra
                        pre_tok[(f, thc)] = ra

    x4_tok = emit_lnB2(pre_tok, None)
    st_last = None
    for j in range(16):
        st_last = p.emit("sync", lambda s, j=j: s.dma_start(
            out=outT[128 * j:128 * (j + 1), :], in_=resA[:, j, :]),
            wait=[x4_tok[(j, 0)], x4_tok[(j, TH - 1)]], inc="dso", inc_amt=16)
    p.emit("gpsimd", lambda g: None, wait=[st_last])
    with nc.Block() as block:
        p.build(nc, block)
    esB2.close()
    esS.close()
    return nc


def run_fused(inputs, t=1024):
    per_core = host_prep_fused(inputs, t=t)
    nc = build_fused(t=t)
    res = run_bass_kernel_spmd(nc, per_core, core_ids=list(range(NCORES)))
    return np.concatenate([r["outT"] for r in res.results], axis=1)


# version 03
# speedup vs baseline: 1.3352x; 1.2292x over previous
import numpy as np
import ml_dtypes
import concourse.bass as bass
import concourse.mybir as mybir
from concourse.bass_utils import run_bass_kernel_spmd

F32 = mybir.dt.float32
BF16 = mybir.dt.bfloat16
AF = mybir.ActivationFunctionType
OP = mybir.AluOpType
AX = mybir.AxisListType

NCORES = 8
D_MODEL = 2048
D_INNER = 4096
NCH = D_INNER // NCORES
NQ = NCH // 128

BF = ml_dtypes.bfloat16


class Prog:
    ENGINES = ("sync", "tensor", "vector", "scalar", "gpsimd")

    def __init__(self):
        self.ops = {e: [] for e in self.ENGINES}
        self.cnt = {}

    def emit(self, eng, fn, wait=(), inc=None, inc_amt=1):
        tok = None
        if inc is not None:
            self.cnt[inc] = self.cnt.get(inc, 0) + inc_amt
            tok = (inc, self.cnt[inc])
        waits = tuple(w for w in wait if w is not None)
        self.ops[eng].append((fn, waits, inc, inc_amt))
        return tok

    def build(self, nc, block):
        sems = {name: nc.semaphore(name).__enter__() for name in self.cnt}
        for e in self.ENGINES:
            if not self.ops[e]:
                continue
            ops = self.ops[e]

            def mk(ops=ops, e=e):
                def body(eng):
                    wm = {}
                    for fn, waits, inc, amt in ops:
                        for (s, v) in waits:
                            if wm.get(("w", s), 0) < v:
                                eng.wait_ge(sems[s], v)
                                wm[("w", s)] = v
                        instr = fn(eng)
                        if inc is not None:
                            instr.then_inc(sems[inc], amt)
                return body

            getattr(block, e)(mk())


def tmax(*toks):
    toks = [t for t in toks if t is not None]
    if not toks:
        return None
    sems = {t[0] for t in toks}
    assert len(sems) == 1, sems
    return max(toks, key=lambda t: t[1])


def _c(a):
    return np.ascontiguousarray(a)


def host_prep_A(inputs):
    W_in = inputs["L0_attn_W_in"]
    W_dt = inputs["L0_attn_W_dt"]
    b_dt = inputs["L0_attn_b_dt"]
    conv_w = inputs["L0_attn_conv_w"]
    conv_b = inputs["L0_attn_conv_b"]
    A_log = inputs["L0_attn_A_log"]
    D_param = inputs["L0_attn_D_param"]
    x = inputs["L0_attn_hidden_states"][0]
    xT = _c(x.T.astype(np.float32))

    ind_x = np.zeros((128, NQ * 128), np.float32)
    for q in range(NQ):
        for m in range(128):
            ind_x[16 * q + (m % 16), q * 128 + m] = 1.0
    ind_B = np.zeros((128, 64 * 128), np.float32)
    for j in range(64):
        ind_B[64 + j, j * 128:(j + 1) * 128] = 1.0
    ind_C = np.zeros((128, 16 * 128), np.float32)
    for n in range(16):
        for pp in range(128):
            ind_C[(pp // 16) * 16 + n, n * 128 + pp] = 1.0

    per_core = []
    for k in range(NCORES):
        cs = slice(NCH * k, NCH * (k + 1))
        rows = np.concatenate([
            np.arange(cs.start, cs.stop),
            np.arange(4096 + 64 * k, 4096 + 64 * (k + 1)),
            np.arange(4608 + 64 * k, 4608 + 64 * (k + 1)),
            np.arange(5120 + NCH * k, 5120 + NCH * (k + 1)),
            np.arange(9216, 9344),
        ])
        per_core.append(dict(
            xT=xT,
            winT=_c(W_in[rows].T),
            wdtT=_c(W_dt[cs].T),
            bdt=_c(b_dt[cs].reshape(NQ, 128).T),
            alogT=_c(A_log[cs].reshape(NQ, 128, 16).transpose(1, 0, 2).reshape(128, NQ * 16)),
            convw=_c(conv_w[cs, 0, :].reshape(NQ, 128, 4).transpose(1, 0, 2).reshape(128, NQ * 4)),
            convb=_c(conv_b[cs].reshape(NQ, 128).T),
            dpar=_c(D_param[cs].reshape(NQ, 128).T),
            ind_x=ind_x, ind_B=ind_B, ind_C=ind_C,
        ))
    return per_core


def build_A(T=8192, Tc=512, STAGE=5):
    nck = T // Tc
    nc = bass.Bass("TRN2", target_bir_lowering=False, debug=False)
    xT = nc.dram_tensor("xT", [D_MODEL, T], F32, kind="ExternalInput").ap()
    winT = nc.dram_tensor("winT", [D_MODEL, 1280], F32, kind="ExternalInput").ap()
    wdtT = nc.dram_tensor("wdtT", [128, NCH], F32, kind="ExternalInput").ap()
    bdt = nc.dram_tensor("bdt", [128, NQ], F32, kind="ExternalInput").ap()
    alogT = nc.dram_tensor("alogT", [128, NQ * 16], F32, kind="ExternalInput").ap()
    convw = nc.dram_tensor("convw", [128, NQ * 4], F32, kind="ExternalInput").ap()
    convb = nc.dram_tensor("convb", [128, NQ], F32, kind="ExternalInput").ap()
    dpar = nc.dram_tensor("dpar", [128, NQ], F32, kind="ExternalInput").ap()
    ind_x = nc.dram_tensor("ind_x", [128, NQ * 128], F32, kind="ExternalInput").ap()
    ind_B = nc.dram_tensor("ind_B", [128, 64 * 128], F32, kind="ExternalInput").ap()
    ind_C = nc.dram_tensor("ind_C", [128, 16 * 128], F32, kind="ExternalInput").ap()
    yg = nc.dram_tensor("yg", [NCH, T], BF16, kind="ExternalOutput").ap()

    p = Prog()
    import contextlib
    es = contextlib.ExitStack()
    wl = es.enter_context(nc.sbuf_tensor("wl", [128, 16 * 1280], BF16))
    wdt_sb = es.enter_context(nc.sbuf_tensor("wdt_sb", [128, NCH], BF16))
    indx_sb = es.enter_context(nc.sbuf_tensor("indx_sb", [128, NQ * 128], BF16))
    indB_sb = es.enter_context(nc.sbuf_tensor("indB_sb", [128, 64 * 128], BF16))
    indC_sb = es.enter_context(nc.sbuf_tensor("indC_sb", [128, 16 * 128], BF16))
    bdt2 = es.enter_context(nc.sbuf_tensor("bdt2", [128, NQ], F32))
    acol = es.enter_context(nc.sbuf_tensor("acol", [128, NQ * 16], F32))
    cw = es.enter_context(nc.sbuf_tensor("cw", [128, NQ * 4], F32))
    cb = es.enter_context(nc.sbuf_tensor("cb", [128, NQ], F32))
    dp = es.enter_context(nc.sbuf_tensor("dp", [128, NQ], F32))
    xc = es.enter_context(nc.sbuf_tensor("xc", [128, 2, 16 * Tc], BF16))
    zf = es.enter_context(nc.sbuf_tensor("zf", [128, NQ, Tc], F32))
    xBb = es.enter_context(nc.sbuf_tensor("xBb", [128, Tc], BF16))
    Cbuf = es.enter_context(nc.sbuf_tensor("Cbuf", [128, NQ, Tc], BF16))
    dtlb = es.enter_context(nc.sbuf_tensor("dtlb", [128, Tc], BF16))
    delta = es.enter_context(nc.sbuf_tensor("delta", [128, NQ, Tc], F32))
    ubuf = es.enter_context(nc.sbuf_tensor("ubuf", [128, NQ, Tc], F32))
    dubuf = es.enter_context(nc.sbuf_tensor("dubuf", [128, NQ, Tc], F32))
    ybuf = es.enter_context(nc.sbuf_tensor("ybuf", [128, NQ, Tc], F32))
    xrb = es.enter_context(nc.sbuf_tensor("xrb", [128, NQ, Tc + 3], F32))
    cacc = es.enter_context(nc.sbuf_tensor("cacc", [128, 2, Tc], F32))
    sp1 = es.enter_context(nc.sbuf_tensor("sp1", [128, Tc], F32))
    sp2 = es.enter_context(nc.sbuf_tensor("sp2", [128, Tc], F32))
    sp3 = es.enter_context(nc.sbuf_tensor("sp3", [128, Tc], F32))
    dAb = es.enter_context(nc.sbuf_tensor("dAb", [128, 2, Tc], F32))
    d1b = es.enter_context(nc.sbuf_tensor("d1b", [128, 2, Tc], F32))
    snb = es.enter_context(nc.sbuf_tensor("snb", [128, 2, Tc], F32))
    prodb = es.enter_context(nc.sbuf_tensor("prodb", [128, 2, Tc], F32))
    stateb = es.enter_context(nc.sbuf_tensor("stateb", [128, 64], F32))
    szb = es.enter_context(nc.sbuf_tensor("szb", [128, Tc], F32))
    ygb = es.enter_context(nc.sbuf_tensor("ygb", [128, 2, Tc], BF16))
    psmm = es.enter_context(nc.psum_tensor("psmm", [128, 2, 512], F32))
    psx = es.enter_context(nc.psum_tensor("psx", [128, 2, 512], F32))
    psB = es.enter_context(nc.psum_tensor("psB", [128, 2, 512], F32))
    psC = es.enter_context(nc.psum_tensor("psC", [128, 2, 512], F32))
    if True:
        for d in range(16):
            p.emit("gpsimd", lambda g, d=d: g.dma_start(
                out=wl[:, d * 1280:(d + 1) * 1280], in_=winT[128 * d:128 * (d + 1), :]),
                inc="dw", inc_amt=16)
        p.emit("gpsimd", lambda g: g.dma_start(out=wdt_sb[:], in_=wdtT), inc="dw", inc_amt=16)
        p.emit("gpsimd", lambda g: g.dma_start(out=indx_sb[:], in_=ind_x), inc="dw", inc_amt=16)
        p.emit("gpsimd", lambda g: g.dma_start(out=indB_sb[:], in_=ind_B), inc="dw", inc_amt=16)
        t_pre_g = p.emit("gpsimd", lambda g: g.dma_start(out=indC_sb[:], in_=ind_C), inc="dw", inc_amt=16)
        p.emit("sync", lambda s: s.dma_start(out=bdt2[:], in_=bdt), inc="dws", inc_amt=16)
        p.emit("sync", lambda s: s.dma_start(out=acol[:], in_=alogT), inc="dws", inc_amt=16)
        p.emit("sync", lambda s: s.dma_start(out=cw[:], in_=convw), inc="dws", inc_amt=16)
        p.emit("sync", lambda s: s.dma_start(out=cb[:], in_=convb), inc="dws", inc_amt=16)
        t_pre_s = p.emit("sync", lambda s: s.dma_start(out=dp[:], in_=dpar), inc="dws", inc_amt=16)
        t_wdt = t_indx = t_indB = t_indC = t_pre_g
        t_bdt = t_alog = t_cw = t_cb = t_dp = t_pre_s
        t_b2 = p.emit("vector", lambda v: v.tensor_scalar_mul(bdt2[:], bdt2[:], 2.0),
                      wait=[t_bdt], inc="v")
        t_ae = p.emit("scalar", lambda s: s.activation(out=acol[:], in_=acol[:], func=AF.Exp),
                      wait=[t_alog], inc="s")
        t_an = p.emit("vector", lambda v: v.tensor_scalar_mul(acol[:], acol[:], -1.0),
                      wait=[t_ae], inc="v")
        t_st0 = p.emit("vector", lambda v: v.memset(stateb[:], 0.0), inc="v")
        t_hist0 = p.emit("vector", lambda v: v.memset(xrb[:], 0.0), inc="v")

        xload = [None] * (nck + 2)
        xfree = [None] * (nck + 2)

        def load_chunk(k):
            tk = None
            sem = "dx%d" % (k % 2)
            for d in range(16):
                tk = p.emit("gpsimd", lambda g, d=d, k=k: g.dma_start(
                    out=xc[:, k % 2, d * Tc:(d + 1) * Tc],
                    in_=xT[128 * d:128 * (d + 1), k * Tc:(k + 1) * Tc]),
                    wait=[xfree[k - 2] if k >= 2 else None],
                    inc=sem, inc_amt=16)
            return tk

        xload[0] = load_chunk(0)
        if nck > 1:
            xload[1] = load_chunk(1)

        evac_free = [None, None]
        psx_free = [None, None]
        state_w = [None] * 64
        B_rd = [None, None]
        C_rd = [None, None]
        A_rd = [None, None]
        D_rd = [None, None]
        S_rd = [None, None]
        P_rd = [None, None]
        delta_war = [t_hist0] * NQ
        du_war = [None] * NQ
        u_war = [None] * NQ
        z_war = [None] * NQ
        xB_war = None
        C_war = [None] * NQ
        dtl_war = None
        xrb_war = [t_hist0] * NQ
        cacc_free = [None, None]
        sp_free = [t_hist0, None, None]
        szb_free = None
        ygb_free = [None, None]
        y_war = [None] * NQ
        sp1_hold = None

        store_last = [None, None]
        last_store = None
        for k in range(nck):
            mm_tok = [None] * 10
            ev = [None] * 10
            for j in range(10):
                slot = j % 2

                def mmz(t, j=j, slot=slot, k=k):
                    ins = None
                    for d in range(16):
                        ins = t.matmul(psmm[:, slot, :],
                                       wl[:, d * 1280 + j * 128:d * 1280 + (j + 1) * 128],
                                       xc[:, k % 2, d * Tc:(d + 1) * Tc],
                                       start=(d == 0), stop=(d == 15))
                    return ins

                mm_tok[j] = p.emit("tensor", mmz,
                                   wait=[xload[k], evac_free[slot],
                                         t_pre_g if k == 0 and j == 0 else None],
                                   inc="t")
                if j <= 3:
                    dst, war = zf[:, j, :], z_war[j]
                elif j == 4:
                    dst, war = xBb[:], xB_war
                elif j <= 8:
                    dst, war = Cbuf[:, j - 5, :], C_war[j - 5]
                else:
                    dst, war = dtlb[:], dtl_war
                ev[j] = p.emit("scalar",
                               lambda s, dst=dst, slot=slot: s.copy(out=dst, in_=psmm[:, slot, :]),
                               wait=[mm_tok[j], war], inc="s")
                evac_free[slot] = ev[j]
            if k + 2 < nck:
                xfree[k] = mm_tok[9]
                xload[k + 2] = load_chunk(k + 2)

            t_delta = [None] * NQ
            t_u = [None] * NQ
            t_du = [None] * NQ
            for q in range(NQ if STAGE >= 2 else 0):
                slot = q % 2
                mmd = p.emit("tensor", lambda t, q=q, slot=slot: t.matmul(
                    psx[:, slot, :], wdt_sb[:, q * 128:(q + 1) * 128], dtlb[:],
                    start=True, stop=True),
                    wait=[ev[9], t_wdt, psx_free[slot]], inc="t")
                if q == NQ - 1:
                    dtl_war = mmd
                a1 = p.emit("scalar", lambda s, slot=slot, q=q: s.activation(
                    out=sp1[:], in_=psx[:, slot, :], func=AF.Abs, bias=bdt2[:, q:q + 1]),
                    wait=[mmd, t_b2, sp_free[0]], inc="s")
                e1 = p.emit("scalar", lambda s: s.activation(
                    out=sp2[:], in_=sp1[:], func=AF.Exp, scale=-1.0),
                    wait=[a1, sp_free[1]], inc="s")
                l1 = p.emit("scalar", lambda s: s.activation(
                    out=sp2[:], in_=sp2[:], func=AF.Ln, bias=1.0),
                    wait=[e1], inc="s")
                t0 = p.emit("vector", lambda v, slot=slot, q=q: v.tensor_scalar_add(
                    sp3[:], psx[:, slot, :], bdt2[:, q:q + 1]),
                    wait=[mmd, a1, t_b2, sp_free[2]], inc="v")
                t1 = p.emit("vector", lambda v: v.tensor_add(sp3[:], sp3[:], sp1[:]),
                            wait=[t0, a1], inc="v")
                t_delta[q] = p.emit("vector", lambda v, q=q: v.scalar_tensor_tensor(
                    delta[:, q, :], sp3[:], 0.5, sp2[:], OP.mult, OP.add),
                    wait=[t1, l1, delta_war[q]], inc="v")
                sp_free[0] = t1
                sp_free[1] = t_delta[q]
                sp_free[2] = t_delta[q]
                psx_free[slot] = t1

                if STAGE < 3:
                    continue
                mmx = p.emit("tensor", lambda t, q=q, slot=slot: t.matmul(
                    psx[:, slot, :], indx_sb[:, q * 128:(q + 1) * 128], xBb[:],
                    start=True, stop=True),
                    wait=[ev[4], t_indx, psx_free[slot]], inc="t")
                exr = p.emit("vector", lambda v, q=q, slot=slot: v.tensor_copy(
                    xrb[:, q, 3:Tc + 3], psx[:, slot, :]),
                    wait=[mmx, xrb_war[q]], inc="v")
                psx_free[slot] = exr
                c0 = p.emit("vector", lambda v, q=q: v.tensor_scalar(
                    cacc[:, 0, :], xrb[:, q, 0:Tc], cw[:, 4 * q:4 * q + 1], None, OP.mult),
                    wait=[exr, t_cw, cacc_free[0]], inc="v")
                c1 = p.emit("vector", lambda v, q=q: v.scalar_tensor_tensor(
                    cacc[:, 1, :], xrb[:, q, 1:Tc + 1], cw[:, 4 * q + 1:4 * q + 2],
                    cacc[:, 0, :], OP.mult, OP.add),
                    wait=[c0, cacc_free[1]], inc="v")
                c2 = p.emit("vector", lambda v, q=q: v.scalar_tensor_tensor(
                    cacc[:, 0, :], xrb[:, q, 2:Tc + 2], cw[:, 4 * q + 2:4 * q + 3],
                    cacc[:, 1, :], OP.mult, OP.add),
                    wait=[c1], inc="v")
                c3 = p.emit("vector", lambda v, q=q: v.scalar_tensor_tensor(
                    cacc[:, 1, :], xrb[:, q, 3:Tc + 3], cw[:, 4 * q + 3:4 * q + 4],
                    cacc[:, 0, :], OP.mult, OP.add),
                    wait=[c2], inc="v")
                hq = p.emit("vector", lambda v, q=q: v.tensor_copy(
                    xrb[:, q, 0:3], xrb[:, q, Tc:Tc + 3]),
                    wait=[c3], inc="v")
                xrb_war[q] = hq
                t_u[q] = p.emit("scalar", lambda s, q=q: s.activation(
                    out=ubuf[:, q, :], in_=cacc[:, 1, :], func=AF.Silu, bias=cb[:, q:q + 1]),
                    wait=[c3, t_cb, u_war[q]], inc="s")
                cacc_free[0] = c3
                cacc_free[1] = t_u[q]
                t_du[q] = p.emit("vector", lambda v, q=q: v.tensor_mul(
                    dubuf[:, q, :], delta[:, q, :], ubuf[:, q, :]),
                    wait=[t_delta[q], t_u[q], du_war[q]], inc="v")

            tok_yadd = [None] * NQ
            dA_cur = [None] * 64
            for i in range(64 if STAGE >= 4 else 0):
                n, q = divmod(i, NQ)
                j = 16 * q + n
                sl = i % 2
                tB = p.emit("tensor", lambda t, j=j, sl=sl: t.matmul(
                    psB[:, sl, :], indB_sb[:, j * 128:(j + 1) * 128], xBb[:],
                    start=True, stop=True),
                    wait=[ev[4], t_indB, B_rd[sl]], inc="t")
                tC = p.emit("tensor", lambda t, n=n, q=q, sl=sl: t.matmul(
                    psC[:, sl, :], indC_sb[:, n * 128:(n + 1) * 128], Cbuf[:, q, :],
                    start=True, stop=True),
                    wait=[ev[5 + q], t_indC, C_rd[sl]], inc="t")
                if n == 15:
                    C_war[q] = tC
                if i == 63:
                    xB_war = tB
                tA = p.emit("scalar", lambda s, q=q, n=n, sl=sl: s.activation(
                    out=dAb[:, sl, :], in_=delta[:, q, :], func=AF.Exp,
                    scale=acol[:, q * 16 + n:q * 16 + n + 1]),
                    wait=[t_delta[q], t_an, A_rd[sl]], inc="s")
                dA_cur[i] = tA
                if n == 15:
                    delta_war[q] = tA
                td1 = p.emit("vector", lambda v, q=q, sl=sl: v.tensor_mul(
                    d1b[:, sl, :], dubuf[:, q, :], psB[:, sl, :]),
                    wait=[t_du[q], tB, D_rd[sl]], inc="v")
                B_rd[sl] = td1
                if n == 15:
                    du_war[q] = td1
                tscan = p.emit("vector", lambda v, i=i, sl=sl: v.tensor_tensor_scan(
                    snb[:, sl, :], dAb[:, sl, :], d1b[:, sl, :],
                    stateb[:, i:i + 1], OP.mult, OP.add),
                    wait=[tA, td1, state_w[i], S_rd[sl], t_st0], inc="v")
                A_rd[sl] = tscan
                D_rd[sl] = tscan
                state_w[i] = p.emit("vector", lambda v, i=i, sl=sl: v.tensor_copy(
                    stateb[:, i:i + 1], snb[:, sl, Tc - 1:Tc]),
                    wait=[tscan], inc="v")
                if STAGE < 5:
                    S_rd[sl] = state_w[i]
                    C_rd[sl] = tC
                    continue
                if n == 0:
                    tprod = p.emit("vector", lambda v, q=q, sl=sl: v.tensor_mul(
                        ybuf[:, q, :], snb[:, sl, :], psC[:, sl, :]),
                        wait=[tscan, tC, y_war[q]], inc="v")
                    tok_yadd[q] = tprod
                else:
                    tprod = p.emit("vector", lambda v, sl=sl: v.tensor_mul(
                        prodb[:, sl, :], snb[:, sl, :], psC[:, sl, :]),
                        wait=[tscan, tC, P_rd[sl]], inc="v")
                    ty = p.emit("vector", lambda v, q=q, sl=sl: v.tensor_add(
                        ybuf[:, q, :], ybuf[:, q, :], prodb[:, sl, :]),
                        wait=[tprod, tok_yadd[q]], inc="v")
                    tok_yadd[q] = ty
                    P_rd[sl] = ty
                C_rd[sl] = tprod
                S_rd[sl] = tmax(tprod, state_w[i])

            if STAGE < 5:
                for q in range(NQ):
                    sl = q % 2
                    if STAGE >= 3:
                        src, stok = ubuf, t_u[q]
                    elif STAGE >= 2:
                        src, stok = delta, t_delta[q]
                    else:
                        src, stok = zf, ev[q]
                    ygt = p.emit("vector", lambda v, q=q, sl=sl, src=src: v.tensor_copy(
                        ygb[:, sl, :], src[:, q, :]),
                        wait=[stok, ygb_free[sl]], inc="v")
                    if STAGE >= 2:
                        delta_war[q] = tmax(delta_war[q], None) if False else delta_war[q]
                    y_war[q] = ygt
                    if STAGE >= 3:
                        u_war[q] = ygt
                    if STAGE == 2:
                        delta_war[q] = ygt
                    if STAGE < 2:
                        z_war[q] = ygt
                    last_store = p.emit("sync", lambda s, q=q, k=k, sl=sl: s.dma_start(
                        out=yg[128 * q:128 * (q + 1), k * Tc:(k + 1) * Tc], in_=ygb[:, sl, :]),
                        wait=[ygt], inc="ds%d" % sl, inc_amt=16)
                    ygb_free[sl] = last_store
                    store_last[sl] = last_store
                continue
            for q in range(NQ):
                sl = q % 2
                pr = p.emit("vector", lambda v, q=q, sl=sl: v.tensor_scalar(
                    prodb[:, sl, :], ubuf[:, q, :], dp[:, q:q + 1], None, OP.mult),
                    wait=[t_u[q], t_dp, P_rd[sl]], inc="v")
                ya = p.emit("vector", lambda v, q=q, sl=sl: v.tensor_add(
                    ybuf[:, q, :], ybuf[:, q, :], prodb[:, sl, :]),
                    wait=[pr, tok_yadd[q]], inc="v")
                P_rd[sl] = ya
                u_war[q] = pr
                sz = p.emit("scalar", lambda s, q=q: s.activation(
                    out=szb[:], in_=zf[:, q, :], func=AF.Silu),
                    wait=[ev[q], szb_free], inc="s")
                z_war[q] = sz
                ygt = p.emit("vector", lambda v, q=q, sl=sl: v.tensor_mul(
                    ygb[:, sl, :], ybuf[:, q, :], szb[:]),
                    wait=[ya, sz, ygb_free[sl]], inc="v")
                szb_free = ygt
                y_war[q] = ygt
                last_store = p.emit("sync", lambda s, q=q, k=k, sl=sl: s.dma_start(
                    out=yg[128 * q:128 * (q + 1), k * Tc:(k + 1) * Tc], in_=ygb[:, sl, :]),
                    wait=[ygt], inc="ds%d" % sl, inc_amt=16)
                ygb_free[sl] = last_store
                store_last[sl] = last_store

        p.emit("gpsimd", lambda g: None, wait=[store_last[0], store_last[1]])
        with nc.Block() as block:
            p.build(nc, block)
    es.close()
    return nc


def run_A(inputs, T=8192, Tc=512, STAGE=5):
    per_core = host_prep_A(inputs)
    if T != 8192:
        for d in per_core:
            d["xT"] = _c(d["xT"][:, :T])
    nc = build_A(T=T, Tc=Tc, STAGE=STAGE)
    res = run_bass_kernel_spmd(nc, per_core, core_ids=list(range(NCORES)))
    return np.concatenate([r["yg"] for r in res.results], axis=0)


def kernel(**inputs):
    raise NotImplementedError("later launches pending")


def host_prep_B1(inputs, yg_full, t=1024):
    xT = _c(inputs["L0_attn_hidden_states"][0].T.astype(np.float32))
    ident = np.eye(128, dtype=np.float32)
    ones = np.ones((128, 128), np.float32)
    ind8 = np.zeros((8, 8 * 128), np.float32)
    for j in range(8):
        ind8[j, j * 128:(j + 1) * 128] = 1.0
    amp = inputs["L0_ffn_amplitudes"]
    shared = dict(
        woutT=_c(inputs["L0_attn_W_out"].T.astype(np.float32)),
        wrout=_c(inputs["L0_ffn_w_router"].astype(np.float32)),
        wgate=_c(inputs["L0_ffn_w_gate"].astype(np.float32)),
        wup=_c(inputs["L0_ffn_w_up"].astype(np.float32)),
        wdown=_c(inputs["L0_ffn_w_down"].astype(np.float32)),
        wsg=_c(inputs["L0_ffn_shared_w_gate"].astype(np.float32)),
        wsu=_c(inputs["L0_ffn_shared_w_up"].astype(np.float32)),
        wsd=_c(inputs["L0_ffn_shared_w_down"].astype(np.float32)),
        a0=_c(amp[:, :, 0].astype(np.float32)),
        a1=_c(amp[:, :, 1].astype(np.float32)),
        esc=_c(inputs["L0_ffn_expert_scale"].astype(np.float32).reshape(128, 1)),
        ident=ident, ones=ones, ind8=ind8,
    )
    per_core = []
    for k in range(NCORES):
        ts = slice(t * k, t * (k + 1))
        per_core.append(dict(ygT=_c(yg_full[:, ts]), xme=_c(xT[:, ts]), **shared))
    return per_core


def build_B1(t=1024):
    TH = t // 512
    nc = bass.Bass("TRN2", target_bir_lowering=False, debug=False)
    ygT = nc.dram_tensor("ygT", [D_INNER, t], BF16, kind="ExternalInput").ap()
    xme = nc.dram_tensor("xme", [D_MODEL, t], F32, kind="ExternalInput").ap()
    woutT = nc.dram_tensor("woutT", [D_INNER, D_MODEL], F32, kind="ExternalInput").ap()
    wrout = nc.dram_tensor("wrout", [D_MODEL, 128], F32, kind="ExternalInput").ap()
    wgate = nc.dram_tensor("wgate", [D_MODEL, 8192], F32, kind="ExternalInput").ap()
    wup = nc.dram_tensor("wup", [D_MODEL, 8192], F32, kind="ExternalInput").ap()
    wdown = nc.dram_tensor("wdown", [1024, D_MODEL], F32, kind="ExternalInput").ap()
    wsg = nc.dram_tensor("wsg", [D_MODEL, D_MODEL], F32, kind="ExternalInput").ap()
    wsu = nc.dram_tensor("wsu", [D_MODEL, D_MODEL], F32, kind="ExternalInput").ap()
    wsd = nc.dram_tensor("wsd", [D_MODEL, D_MODEL], F32, kind="ExternalInput").ap()
    a0 = nc.dram_tensor("a0", [128, 8], F32, kind="ExternalInput").ap()
    a1 = nc.dram_tensor("a1", [128, 8], F32, kind="ExternalInput").ap()
    esc = nc.dram_tensor("esc", [128, 1], F32, kind="ExternalInput").ap()
    ident = nc.dram_tensor("ident", [128, 128], F32, kind="ExternalInput").ap()
    ones = nc.dram_tensor("ones", [128, 128], F32, kind="ExternalInput").ap()
    ind8 = nc.dram_tensor("ind8", [8, 8 * 128], F32, kind="ExternalInput").ap()
    x2T = nc.dram_tensor("x2T", [D_MODEL, t], F32, kind="ExternalOutput").ap()

    p = Prog()
    import contextlib
    es = contextlib.ExitStack()
    SB = lambda nm, sh, dt: es.enter_context(nc.sbuf_tensor(nm, sh, dt))
    wslab = SB("wslab", [128, 8192], BF16)
    resA = SB("resA", [128, 16, t], F32)
    x1b = SB("x1b", [128, 16, t], BF16)
    bp = SB("bp", [128, 24 * t], BF16)
    ygs = SB("ygs", [128, 2, t], BF16)
    wos = SB("wos", [128, 4, 512], BF16)
    srow = SB("srow", [1, 6, 512], F32)
    lgf = SB("lgf", [128, t], F32)
    lgTt = SB("lgTt", [128, 128], F32)
    Eb = SB("Eb", [128, 128], F32)
    Ea = SB("Ea", [128, 128], F32)
    Ec = SB("Ec", [128, 128], F32)
    Mb = SB("Mb", [128, 128], F32)
    vc = SB("vc", [128, 8], F32)
    cnb = SB("cnb", [128, 128], F32)
    cnf = SB("cnf", [128, t], BF16)
    mixs = SB("mixs", [128, 8], F32)
    mixb = SB("mixb", [128, 8], BF16)
    a1s = SB("a1s", [128, 8], F32)
    esb = SB("esb", [128, 1], F32)
    efb = SB("efb", [8, t], BF16)
    ind8b = SB("ind8b", [8, 8 * 128], BF16)
    idb = SB("idb", [128, 128], F32)
    onesb = SB("onesb", [128, 128], F32)
    wrb = SB("wrb", [128, 16 * 128], BF16)
    stg = SB("stg", [128, 2, 512], F32)
    tb = SB("tb", [128, 2, 512], BF16)
    pall = es.enter_context(nc.psum_tensor("pall", [128, 8, 512], F32))
    ebs = bp[:, 0:8 * t]
    sgv = bp[:, 8 * t:16 * t]
    sgs = bp[:, 0:16 * t]
    comb = bp[:, 16 * t:24 * t]
    sq = lgf[:, 0:512]
    bank_rd = [None] * 8

    p.emit("sync", lambda s: s.dma_start(out=idb[:], in_=ident), inc="dws", inc_amt=16)
    p.emit("sync", lambda s: s.dma_start(out=onesb[:], in_=ones), inc="dws", inc_amt=16)
    t_pre_s = p.emit("sync", lambda s: s.dma_start(out=esb[:], in_=esc), inc="dws", inc_amt=16)
    p.emit("gpsimd", lambda g: g.dma_start(out=ind8b[:], in_=ind8), inc="dw", inc_amt=16)
    p.emit("gpsimd", lambda g: g.dma_start(out=mixs[:], in_=a0), inc="dw", inc_amt=16)
    p.emit("gpsimd", lambda g: g.dma_start(out=a1s[:], in_=a1), inc="dw", inc_amt=16)
    t_pre_g = p.emit("gpsimd", lambda g: g.dma_start(
        out=wrb[:].rearrange("p (a b) -> p a b", a=16),
        in_=wrout.rearrange("(a p) b -> p a b", p=128)), inc="dw", inc_amt=16)

    s0 = p.emit("scalar", lambda s: s.activation(out=Eb[:, 0:8], in_=mixs[:], func=AF.Square),
                wait=[t_pre_g], inc="s")
    s1 = p.emit("scalar", lambda s: s.activation(out=Eb[:, 8:16], in_=a1s[:], func=AF.Square),
                wait=[t_pre_g, s0], inc="s")
    v0 = p.emit("vector", lambda v: v.tensor_add(mixs[:], Eb[:, 0:8], Eb[:, 8:16]), wait=[s1], inc="v")
    v1_ = p.emit("vector", lambda v: v.reduce_sum(vc[:, 0:1], mixs[:], axis=AX.X), wait=[v0], inc="v")
    v2_ = p.emit("vector", lambda v: v.tensor_scalar_add(vc[:, 0:1], vc[:, 0:1], 1e-8), wait=[v1_], inc="v")
    v3_ = p.emit("vector", lambda v: v.reciprocal(vc[:, 1:2], vc[:, 0:1]), wait=[v2_], inc="v")
    v4_ = p.emit("vector", lambda v: v.tensor_scalar(mixs[:], mixs[:], vc[:, 1:2], None, OP.mult),
                 wait=[v3_], inc="v")
    v5_ = p.emit("vector", lambda v: v.tensor_scalar(mixs[:], mixs[:], esb[:], None, OP.mult),
                 wait=[v4_, t_pre_s], inc="v")
    t_mix = p.emit("vector", lambda v: v.tensor_copy(mixb[:], mixs[:]), wait=[v5_], inc="v")

    pre1_tok = {}
    wo_cons = {}
    yg_cons = {}
    stg_rd = [t_pre_s, t_pre_s]
    for jg in range(4):
        for k in range(32):
            slw = k % 4
            gate_w = wo_cons.get((jg, k - 4)) if k >= 4 else wo_cons.get((jg - 1, 28 + k))
            wld = p.emit("gpsimd", lambda g, jg=jg, k=k, slw=slw: g.dma_start(
                out=wos[:, slw, :], in_=woutT[128 * k:128 * (k + 1), 512 * jg:512 * (jg + 1)]),
                wait=[gate_w], inc="dwo%d" % slw, inc_amt=16)
            sly = k % 2
            gate_y = yg_cons.get((jg, k - 2)) if k >= 2 else yg_cons.get((jg - 1, 30 + k))
            yld = p.emit("gpsimd", lambda g, k=k, sly=sly: g.dma_start(
                out=ygs[:, sly, :], in_=ygT[128 * k:128 * (k + 1), :]),
                wait=[gate_y], inc="dyg%d" % sly, inc_amt=16)
            last = None
            for jj in range(4):
                for th in range(TH):
                    b = jj * 2 + th
                    last = p.emit("tensor", lambda tt, k=k, jj=jj, th=th, b=b, slw=slw, sly=sly: tt.matmul(
                        pall[:, b, :], wos[:, slw, 128 * jj:128 * (jj + 1)],
                        ygs[:, sly, 512 * th:512 * (th + 1)],
                        start=(k == 0), stop=(k == 31)),
                        wait=[wld, yld, bank_rd[b] if k == 0 else None], inc="t")
            wo_cons[(jg, k)] = last
            yg_cons[(jg, k)] = last
        for jj in range(4):
            for th in range(TH):
                b = jj * 2 + th
                j = 4 * jg + jj
                sl = (jg * 8 + jj * 2 + th) % 2
                ld = p.emit("sync", lambda s, j=j, th=th, sl=sl: s.dma_start(
                    out=stg[:, sl, :], in_=xme[128 * j:128 * (j + 1), 512 * th:512 * (th + 1)]),
                    wait=[stg_rd[sl]], inc="dst%d" % sl, inc_amt=16)
                tk = p.emit("vector", lambda v, j=j, th=th, b=b, sl=sl: v.tensor_add(
                    resA[:, j, 512 * th:512 * (th + 1)], stg[:, sl, :], pall[:, b, :]),
                    wait=[wo_cons[(jg, 31)], ld], inc="v")
                pre1_tok[(j, th)] = tk
                stg_rd[sl] = tk
                bank_rd[b] = tk

    srow_rd = [None]
    def emit_ln(src_ready, x1b_out):
        apply_toks = {}
        sq_rd = [None]
        for th in range(TH):
            mm1 = mm2 = None
            for j in range(16):
                sqt = p.emit("scalar", lambda s, j=j, th=th: s.activation(
                    out=sq, in_=resA[:, j, 512 * th:512 * (th + 1)], func=AF.Square),
                    wait=[src_ready[(j, th)], sq_rd[0]], inc="s")
                mm1 = p.emit("tensor", lambda tt, j=j, th=th: tt.matmul(
                    pall[0:1, 0, :], onesb[:, 0:1], resA[:, j, 512 * th:512 * (th + 1)],
                    start=(j == 0), stop=(j == 15)),
                    wait=[src_ready[(j, th)], bank_rd[0] if j == 0 else None], inc="t")
                mm2 = p.emit("tensor", lambda tt, j=j: tt.matmul(
                    pall[0:1, 1, :], onesb[:, 0:1], sq,
                    start=(j == 0), stop=(j == 15)),
                    wait=[sqt, bank_rd[1] if j == 0 else None], inc="t")
                sq_rd[0] = mm2
            a = p.emit("vector", lambda v: v.tensor_scalar(
                srow[0:1, 0, :], pall[0:1, 0, :], 1.0 / 2048.0, None, OP.mult),
                wait=[mm1, srow_rd[0]], inc="v")
            bank_rd[0] = a
            b2 = p.emit("vector", lambda v: v.tensor_mul(
                srow[0:1, 2, :], srow[0:1, 0, :], srow[0:1, 0, :]), wait=[a], inc="v")
            c = p.emit("vector", lambda v: v.scalar_tensor_tensor(
                srow[0:1, 3, :], pall[0:1, 1, :], 1.0 / 2048.0, srow[0:1, 2, :],
                OP.mult, OP.subtract),
                wait=[mm2, b2], inc="v")
            bank_rd[1] = c
            c2 = p.emit("vector", lambda v: v.tensor_scalar_add(
                srow[0:1, 3, :], srow[0:1, 3, :], 1e-5), wait=[c], inc="v")
            d = p.emit("scalar", lambda s: s.activation(
                out=srow[0:1, 4, :], in_=srow[0:1, 3, :], func=AF.Sqrt),
                wait=[c2], inc="s")
            e = p.emit("vector", lambda v: v.reciprocal(srow[0:1, 1, :], srow[0:1, 4, :]),
                       wait=[d], inc="v")
            bm = p.emit("tensor", lambda tt: tt.matmul(
                pall[:, 2, :], onesb[0:1, :], srow[0:1, 0, :], start=True, stop=True),
                wait=[a, bank_rd[2]], inc="t")
            br = p.emit("tensor", lambda tt: tt.matmul(
                pall[:, 3, :], onesb[0:1, :], srow[0:1, 1, :], start=True, stop=True),
                wait=[e, bank_rd[3]], inc="t")
            srow_rd[0] = br
            f2 = None
            for j in range(16):
                f1 = p.emit("vector", lambda v, j=j, th=th: v.tensor_sub(
                    resA[:, j, 512 * th:512 * (th + 1)],
                    resA[:, j, 512 * th:512 * (th + 1)], pall[:, 2, :]),
                    wait=[bm], inc="v")
                f2 = p.emit("vector", lambda v, j=j, th=th: v.tensor_mul(
                    resA[:, j, 512 * th:512 * (th + 1)],
                    resA[:, j, 512 * th:512 * (th + 1)], pall[:, 3, :]),
                    wait=[f1, br], inc="v")
                tokj = f2
                if x1b_out is not None:
                    tokj = p.emit("vector", lambda v, j=j, th=th: v.tensor_copy(
                        x1b_out[:, j, 512 * th:512 * (th + 1)],
                        resA[:, j, 512 * th:512 * (th + 1)]),
                        wait=[f2], inc="v")
                apply_toks[(j, th)] = tokj
            bank_rd[2] = f2
            bank_rd[3] = f2
        return apply_toks

    x1_tok = emit_ln(pre1_tok, x1b)

    lg_ev = [None] * TH
    for th in range(TH):
        mm = None
        for d in range(16):
            mm = p.emit("tensor", lambda tt, d=d, th=th: tt.matmul(
                pall[:, 4, :], wrb[:, 128 * d:128 * (d + 1)], x1b[:, d, 512 * th:512 * (th + 1)],
                start=(d == 0), stop=(d == 15)),
                wait=[x1_tok[(d, th)], t_pre_g, bank_rd[4] if d == 0 else None], inc="t")
        lg_ev[th] = p.emit("scalar", lambda s, th=th: s.copy(
            out=lgf[:, 512 * th:512 * (th + 1)], in_=pall[:, 4, :]), wait=[mm], inc="s")
        bank_rd[4] = lg_ev[th]
    cn_done = [None] * (t // 128)
    lgT_rd = [None]
    for tt_ in range(t // 128):
        th = (tt_ * 128) // 512
        tr1 = p.emit("tensor", lambda tt, tt_=tt_: tt.transpose(
            pall[:, 5, 0:128], lgf[:, 128 * tt_:128 * (tt_ + 1)], idb[:]),
            wait=[lg_ev[th], bank_rd[5]], inc="t")
        ev1 = p.emit("vector", lambda v: v.tensor_copy(lgTt[:], pall[:, 5, 0:128]),
                     wait=[tr1, lgT_rd[0]], inc="v")
        bank_rd[5] = ev1
        m0 = p.emit("vector", lambda v: v.reduce_max(vc[:, 0:1], lgTt[:], axis=AX.X, negate=True),
                    wait=[ev1], inc="v")
        ex = p.emit("scalar", lambda s: s.activation(out=Eb[:], in_=lgTt[:], func=AF.Exp,
                                                     bias=vc[:, 0:1]),
                    wait=[m0], inc="s")
        lgT_rd[0] = ex
        sS = p.emit("vector", lambda v: v.reduce_sum(vc[:, 1:2], Eb[:], axis=AX.X), wait=[ex], inc="v")
        vv1 = p.emit("vector", lambda v: v.reduce_max(vc[:, 2:3], Eb[:], axis=AX.X), wait=[ex], inc="v")
        mk1 = p.emit("vector", lambda v: v.tensor_scalar(Mb[:], Eb[:], vc[:, 2:3], None, OP.is_ge),
                     wait=[vv1], inc="v")
        e1 = p.emit("vector", lambda v: v.scalar_tensor_tensor(Ea[:], Mb[:], -1e30, Eb[:], OP.mult, OP.add),
                    wait=[mk1], inc="v")
        vv2 = p.emit("vector", lambda v: v.reduce_max(vc[:, 3:4], Ea[:], axis=AX.X), wait=[e1], inc="v")
        mk2 = p.emit("vector", lambda v: v.tensor_scalar(Mb[:], Ea[:], vc[:, 3:4], None, OP.is_ge),
                     wait=[vv2], inc="v")
        e2 = p.emit("vector", lambda v: v.scalar_tensor_tensor(Ec[:], Mb[:], -1e30, Ea[:], OP.mult, OP.add),
                    wait=[mk2], inc="v")
        vv3 = p.emit("vector", lambda v: v.reduce_max(vc[:, 4:5], Ec[:], axis=AX.X), wait=[e2], inc="v")
        mk3 = p.emit("vector", lambda v: v.tensor_scalar(Mb[:], Ec[:], vc[:, 4:5], None, OP.is_ge),
                     wait=[vv3], inc="v")
        e3 = p.emit("vector", lambda v: v.scalar_tensor_tensor(Ea[:], Mb[:], -1e30, Ec[:], OP.mult, OP.add),
                    wait=[mk3], inc="v")
        vv4 = p.emit("vector", lambda v: v.reduce_max(vc[:, 5:6], Ea[:], axis=AX.X), wait=[e3], inc="v")
        t1 = p.emit("vector", lambda v: v.tensor_add(vc[:, 6:7], vc[:, 2:3], vc[:, 3:4]),
                    wait=[vv2, vv1], inc="v")
        t2 = p.emit("vector", lambda v: v.tensor_add(vc[:, 7:8], vc[:, 6:7], vc[:, 4:5]),
                    wait=[t1, vv3], inc="v")
        t3 = p.emit("vector", lambda v: v.tensor_add(vc[:, 6:7], vc[:, 7:8], vc[:, 5:6]),
                    wait=[t2, vv4], inc="v")
        den = p.emit("vector", lambda v: v.scalar_tensor_tensor(vc[:, 7:8], vc[:, 1:2], 1e-8,
                                                                vc[:, 6:7], OP.mult, OP.add),
                     wait=[t3, sS], inc="v")
        rd = p.emit("vector", lambda v: v.reciprocal(vc[:, 6:7], vc[:, 7:8]), wait=[den], inc="v")
        mk4 = p.emit("vector", lambda v: v.tensor_scalar(Mb[:], Eb[:], vc[:, 5:6], None, OP.is_ge),
                     wait=[vv4], inc="v")
        cm = p.emit("vector", lambda v: v.tensor_mul(cnb[:], Eb[:], Mb[:]), wait=[mk4], inc="v")
        cs = p.emit("vector", lambda v: v.tensor_scalar(cnb[:], cnb[:], vc[:, 6:7], None, OP.mult),
                    wait=[cm, rd], inc="v")
        tr2 = p.emit("tensor", lambda tt, tt_=tt_: tt.transpose(
            pall[:, 5, 0:128], cnb[:], idb[:]), wait=[cs, bank_rd[5]], inc="t")
        ev2 = p.emit("vector", lambda v, tt_=tt_: v.tensor_copy(
            cnf[:, 128 * tt_:128 * (tt_ + 1)], pall[:, 5, 0:128]), wait=[tr2], inc="v")
        bank_rd[5] = ev2
        cn_done[tt_] = ev2
    ef_ev = [None] * TH
    for th in range(TH):
        mm = p.emit("tensor", lambda tt, th=th: tt.matmul(
            pall[0:8, 6, :], mixb[:], cnf[:, 512 * th:512 * (th + 1)], start=True, stop=True),
            wait=[t_mix, bank_rd[6]] + cn_done[4 * th:4 * (th + 1)], inc="t")
        ef_ev[th] = p.emit("vector", lambda v, th=th: v.tensor_copy(
            efb[:, 512 * th:512 * (th + 1)], pall[0:8, 6, :]), wait=[mm], inc="v")
        bank_rd[6] = ef_ev[th]
    eb_tok = {}
    for k8 in range(8):
        for th in range(TH):
            b = 4 + (k8 * TH + th) % 2
            mm = p.emit("tensor", lambda tt, k8=k8, th=th, b=b: tt.matmul(
                pall[:, b, :], ind8b[:, 128 * k8:128 * (k8 + 1)], efb[0:8, 512 * th:512 * (th + 1)],
                start=True, stop=True),
                wait=[ef_ev[th], t_pre_g, bank_rd[b]], inc="t")
            ev = p.emit("scalar", lambda s, k8=k8, th=th, b=b: s.copy(
                out=ebs[:, k8 * t + 512 * th:k8 * t + 512 * (th + 1)], in_=pall[:, b, :]),
                wait=[mm], inc="s")
            bank_rd[b] = ev
            eb_tok[(k8, th)] = ev

    def slab_load(src, cols, gate, n_dma=16, rows=128):
        tk = None
        for d in range(n_dma):
            w = 8192 // n_dma
            tk = p.emit("gpsimd", lambda g, d=d, w=w: g.dma_start(
                out=wslab[:, w * d:w * (d + 1)], in_=src[rows * d:rows * (d + 1), cols]),
                wait=[gate if d == 0 else None], inc="dsl", inc_amt=16)
        return tk

    comb_tok = {}
    slab_cons = None
    sgv_rd = {}
    tb_rd = [None, None]
    for k8 in range(8):
        sg_tok = {}
        for ih in range(2):
            ldg = slab_load(wgate, slice(1024 * k8 + 512 * ih, 1024 * k8 + 512 * (ih + 1)), slab_cons)
            for i4 in range(4):
                i = 4 * ih + i4
                for th in range(TH):
                    b = (i4 * TH + th) % 4
                    mm = p.emit("tensor", lambda tt, i4=i4, th=th, b=b: [tt.matmul(
                        pall[:, b, :], wslab[:, 512 * d + 128 * i4:512 * d + 128 * (i4 + 1)],
                        x1b[:, d, 512 * th:512 * (th + 1)],
                        start=(d == 0), stop=(d == 15)) for d in range(16)][-1],
                        wait=[ldg, x1_tok[(15, th)], bank_rd[b]], inc="t")
                    sg_tok[(i, th)] = p.emit("scalar", lambda s, i=i, th=th, b=b: s.activation(
                        out=sgv[:, i * t + 512 * th:i * t + 512 * (th + 1)], in_=pall[:, b, :],
                        func=AF.Silu),
                        wait=[mm, sgv_rd.get((i, th))], inc="s")
                    bank_rd[b] = sg_tok[(i, th)]
                    slab_cons = mm
        for ih in range(2):
            ldu = slab_load(wup, slice(1024 * k8 + 512 * ih, 1024 * k8 + 512 * (ih + 1)), slab_cons)
            for i4 in range(4):
                i = 4 * ih + i4
                for th in range(TH):
                    b = (i4 * TH + th) % 4
                    mm = p.emit("tensor", lambda tt, i4=i4, th=th, b=b: [tt.matmul(
                        pall[:, b, :], wslab[:, 512 * d + 128 * i4:512 * d + 128 * (i4 + 1)],
                        x1b[:, d, 512 * th:512 * (th + 1)],
                        start=(d == 0), stop=(d == 15)) for d in range(16)][-1],
                        wait=[ldu, bank_rd[b]], inc="t")
                    slab_cons = mm
                    sltb = (i * TH + th) % 2
                    upe = p.emit("vector", lambda v, th=th, b=b, k8=k8, sltb=sltb: v.tensor_mul(
                        tb[:, sltb, :], pall[:, b, :],
                        ebs[:, k8 * t + 512 * th:k8 * t + 512 * (th + 1)]),
                        wait=[mm, eb_tok[(k8, th)], tb_rd[sltb]], inc="v")
                    bank_rd[b] = upe
                    if k8 == 0:
                        pr = p.emit("vector", lambda v, i=i, th=th, sltb=sltb: v.tensor_mul(
                            comb[:, i * t + 512 * th:i * t + 512 * (th + 1)],
                            sgv[:, i * t + 512 * th:i * t + 512 * (th + 1)], tb[:, sltb, :]),
                            wait=[upe, sg_tok[(i, th)]], inc="v")
                        comb_tok[(i, th)] = pr
                        tb_rd[sltb] = pr
                        sgv_rd[(i, th)] = pr
                    else:
                        pr = p.emit("vector", lambda v, i=i, th=th, sltb=sltb: v.tensor_mul(
                            tb[:, sltb, :], sgv[:, i * t + 512 * th:i * t + 512 * (th + 1)],
                            tb[:, sltb, :]),
                            wait=[upe, sg_tok[(i, th)]], inc="v")
                        sgv_rd[(i, th)] = pr
                        ad = p.emit("vector", lambda v, i=i, th=th, sltb=sltb: v.tensor_add(
                            comb[:, i * t + 512 * th:i * t + 512 * (th + 1)],
                            comb[:, i * t + 512 * th:i * t + 512 * (th + 1)], tb[:, sltb, :]),
                            wait=[pr, comb_tok[(i, th)]], inc="v")
                        comb_tok[(i, th)] = ad
                        tb_rd[sltb] = ad

    down_tok = {}
    for fh in range(2):
        ldd = slab_load(wdown, slice(1024 * fh, 1024 * (fh + 1)), slab_cons, n_dma=8)
        for f8 in range(8):
            f = 8 * fh + f8
            for th in range(TH):
                b = (f8 * TH + th) % 4
                mm = p.emit("tensor", lambda tt, f8=f8, th=th, b=b: [tt.matmul(
                    pall[:, b, :], wslab[:, 1024 * ic + 128 * f8:1024 * ic + 128 * (f8 + 1)],
                    comb[:, ic * t + 512 * th:ic * t + 512 * (th + 1)],
                    start=(ic == 0), stop=(ic == 7)) for ic in range(8)][-1],
                    wait=[ldd, bank_rd[b]] + [comb_tok[(ic, th)] for ic in range(8)], inc="t")
                slab_cons = mm
                ra = p.emit("vector", lambda v, f=f, th=th, b=b: v.tensor_add(
                    resA[:, f, 512 * th:512 * (th + 1)],
                    resA[:, f, 512 * th:512 * (th + 1)], pall[:, b, :]),
                    wait=[mm, x1_tok[(f, th)]], inc="v")
                bank_rd[b] = ra
                down_tok[(f, th)] = ra

    sgs_tok = {}
    for qt in range(4):
        ldsg = slab_load(wsg, slice(512 * qt, 512 * (qt + 1)), slab_cons)
        for f4 in range(4):
            f = 4 * qt + f4
            for th in range(TH):
                b = (f4 * TH + th) % 4
                mm = p.emit("tensor", lambda tt, f4=f4, th=th, b=b: [tt.matmul(
                    pall[:, b, :], wslab[:, 512 * d + 128 * f4:512 * d + 128 * (f4 + 1)],
                    x1b[:, d, 512 * th:512 * (th + 1)],
                    start=(d == 0), stop=(d == 15)) for d in range(16)][-1],
                    wait=[ldsg, bank_rd[b]], inc="t")
                slab_cons = mm
                sgs_tok[(f, th)] = p.emit("scalar", lambda s, f=f, th=th, b=b: s.activation(
                    out=sgs[:, f * t + 512 * th:f * t + 512 * (th + 1)], in_=pall[:, b, :],
                    func=AF.Silu),
                    wait=[mm], inc="s")
                bank_rd[b] = sgs_tok[(f, th)]
    for qt in range(4):
        ldsu = slab_load(wsu, slice(512 * qt, 512 * (qt + 1)), slab_cons)
        for f4 in range(4):
            f = 4 * qt + f4
            for th in range(TH):
                b = (f4 * TH + th) % 4
                mm = p.emit("tensor", lambda tt, f4=f4, th=th, b=b: [tt.matmul(
                    pall[:, b, :], wslab[:, 512 * d + 128 * f4:512 * d + 128 * (f4 + 1)],
                    x1b[:, d, 512 * th:512 * (th + 1)],
                    start=(d == 0), stop=(d == 15)) for d in range(16)][-1],
                    wait=[ldsu, bank_rd[b]], inc="t")
                slab_cons = mm
                up2 = p.emit("vector", lambda v, f=f, th=th, b=b: v.tensor_mul(
                    sgs[:, f * t + 512 * th:f * t + 512 * (th + 1)],
                    sgs[:, f * t + 512 * th:f * t + 512 * (th + 1)], pall[:, b, :]),
                    wait=[mm, sgs_tok[(f, th)]], inc="v")
                bank_rd[b] = up2
                sgs_tok[(f, th)] = up2
    for qt in range(4):
        ldsd = slab_load(wsd, slice(512 * qt, 512 * (qt + 1)), slab_cons)
        for f4 in range(4):
            f = 4 * qt + f4
            for th in range(TH):
                b = (f4 * TH + th) % 4
                mm = p.emit("tensor", lambda tt, f4=f4, th=th, b=b: [tt.matmul(
                    pall[:, b, :], wslab[:, 512 * d + 128 * f4:512 * d + 128 * (f4 + 1)],
                    sgs[:, d * t + 512 * th:d * t + 512 * (th + 1)],
                    start=(d == 0), stop=(d == 15)) for d in range(16)][-1],
                    wait=[ldsd, bank_rd[b]] + [sgs_tok[(d_, th)] for d_ in range(16)], inc="t")
                slab_cons = mm
                ra = p.emit("vector", lambda v, f=f, th=th, b=b: v.tensor_add(
                    resA[:, f, 512 * th:512 * (th + 1)],
                    resA[:, f, 512 * th:512 * (th + 1)], pall[:, b, :]),
                    wait=[mm, down_tok[(f, th)]], inc="v")
                bank_rd[b] = ra
                down_tok[(f, th)] = ra

    x2_tok = emit_ln(down_tok, None)
    st_last = None
    for j in range(16):
        st_last = p.emit("sync", lambda s, j=j: s.dma_start(
            out=x2T[128 * j:128 * (j + 1), :], in_=resA[:, j, :]),
            wait=[x2_tok[(j, 0)], x2_tok[(j, TH - 1)]], inc="dso", inc_amt=16)
    p.emit("gpsimd", lambda g: None, wait=[st_last])
    with nc.Block() as block:
        p.build(nc, block)
    es.close()
    return nc


def run_B1(inputs, yg_full, t=1024):
    per_core = host_prep_B1(inputs, yg_full, t=t)
    nc = build_B1(t=t)
    res = run_bass_kernel_spmd(nc, per_core, core_ids=list(range(NCORES)))
    return np.concatenate([r["x2T"] for r in res.results], axis=1)

def host_prep_B2(inputs, x2_full, t=1024):
    HALO = 64
    T = x2_full.shape[1]
    def depair(n_heads):
        perm = []
        for h in range(n_heads):
            perm += [h * 128 + 2 * i for i in range(64)]
            perm += [h * 128 + 2 * i + 1 for i in range(64)]
        return np.array(perm)
    wqp = _c(inputs["L1_attn_wq"].astype(np.float32)[:, depair(16)])
    wkp = _c(inputs["L1_attn_wk"].astype(np.float32)[:, depair(4)])
    wv = _c(inputs["L1_attn_wv"].astype(np.float32))
    wo = _c(inputs["L1_attn_wo"].astype(np.float32))
    sink = _c(inputs["L1_attn_sink"].astype(np.float32).reshape(1, 16))
    qkv = inputs["L1_ffn_qkv_weight"].astype(np.float32)
    rows = np.concatenate([np.arange((h // 4) * 768 + 640, (h // 4) * 768 + 768)
                           for h in range(16)])
    wvselT = _c(qkv[rows].T)
    projT = _c(inputs["L1_ffn_proj_weight"].astype(np.float32).T)
    ident = np.eye(128, dtype=np.float32)
    ones = np.ones((128, 128), np.float32)
    ind8 = np.zeros((8, 8 * 128), np.float32)
    for j in range(8):
        ind8[j, j * 128:(j + 1) * 128] = 1.0
    permM = np.zeros((128, 128), np.float32)
    for i in range(64):
        permM[64 + i, i] = 1.0
        permM[i, 64 + i] = 1.0
    lnp = np.stack([inputs["L1_ffn_ln1_weight"], inputs["L1_ffn_ln1_bias"],
                    inputs["L1_ffn_ln2_weight"], inputs["L1_ffn_ln2_bias"]]
                   ).astype(np.float32).reshape(4, 16, 128).transpose(2, 0, 1)
    lnp = _c(lnp)
    band = np.zeros((128, 256), np.float32)
    for i in range(128):
        for j in range(256):
            if j - 128 <= i <= j:
                band[i, j] = 1.0
    shared = dict(
        wqp=wqp, wkp=wkp, wv=wv, wo=wo, sinkr=sink, wvselT=wvselT, projT=projT,
        wre=_c(inputs["L1_ffn_router_weight"].astype(np.float32).reshape(16, 128, 8).transpose(1, 0, 2).reshape(128, 128)),
        mw1=_c(inputs["L1_ffn_moe_w1"].astype(np.float32)),
        mw2=_c(inputs["L1_ffn_moe_w2"].astype(np.float32)),
        ident=ident, ones=ones, ind8=ind8, permM=permM, lnp=lnp,
    )
    per_core = []
    for c in range(NCORES):
        t0 = t * c
        x2h = np.zeros((D_MODEL, t + 2 * HALO), np.float32)
        lo, hi = t0 - HALO, t0 + t + HALO
        slo, shi = max(lo, 0), min(hi, T)
        x2h[:, slo - lo:shi - lo] = x2_full[:, slo:shi]
        pos = np.arange(lo, hi, dtype=np.float64)
        inv = 1.0 / (10000.0 ** (np.arange(64) / 64.0))
        f = pos[None, :] * inv[:, None]
        cos = np.cos(f); sin = np.sin(f)
        cs1 = np.concatenate([cos, cos], 0).astype(np.float32)
        cs2 = np.concatenate([-sin, sin], 0).astype(np.float32)
        m0 = band.copy()
        m7 = band.copy()
        if c == 0:
            for j in range(256):
                if j - HALO + 0 < 0:
                    m0[:, j] = 0.0
        if c == NCORES - 1:
            for j in range(256):
                if t0 + 128 * 7 - HALO + j >= T:
                    m7[:, j] = 0.0
        per_core.append(dict(x2h=_c(x2h), cs1=cs1, cs2=cs2,
                             mask0=m0, maskI=band, mask7=m7, **shared))
    return per_core


def build_B2(t=1024):
    TH = t // 512
    HALO = 64
    th_ = t + 2 * HALO
    nc = bass.Bass("TRN2", target_bir_lowering=False, debug=False)
    x2h = nc.dram_tensor("x2h", [D_MODEL, th_], F32, kind="ExternalInput").ap()
    wqp = nc.dram_tensor("wqp", [D_MODEL, 2048], F32, kind="ExternalInput").ap()
    wkp = nc.dram_tensor("wkp", [D_MODEL, 512], F32, kind="ExternalInput").ap()
    wv = nc.dram_tensor("wv", [D_MODEL, 512], F32, kind="ExternalInput").ap()
    wo = nc.dram_tensor("wo", [D_MODEL, 2048], F32, kind="ExternalInput").ap()
    sinkr = nc.dram_tensor("sinkr", [1, 16], F32, kind="ExternalInput").ap()
    wvselT = nc.dram_tensor("wvselT", [D_MODEL, 2048], F32, kind="ExternalInput").ap()
    projT = nc.dram_tensor("projT", [D_MODEL, 2048], F32, kind="ExternalInput").ap()
    wre = nc.dram_tensor("wre", [128, 128], F32, kind="ExternalInput").ap()
    mw1 = nc.dram_tensor("mw1", [8, D_MODEL, 4096], F32, kind="ExternalInput").ap()
    mw2 = nc.dram_tensor("mw2", [8, 4096, D_MODEL], F32, kind="ExternalInput").ap()
    ident = nc.dram_tensor("ident", [128, 128], F32, kind="ExternalInput").ap()
    ones = nc.dram_tensor("ones", [128, 128], F32, kind="ExternalInput").ap()
    ind8 = nc.dram_tensor("ind8", [8, 8 * 128], F32, kind="ExternalInput").ap()
    permM = nc.dram_tensor("permM", [128, 128], F32, kind="ExternalInput").ap()
    lnp = nc.dram_tensor("lnp", [128, 4 * 16], F32, kind="ExternalInput").ap()
    cs1 = nc.dram_tensor("cs1", [128, th_], F32, kind="ExternalInput").ap()
    cs2 = nc.dram_tensor("cs2", [128, th_], F32, kind="ExternalInput").ap()
    mask0 = nc.dram_tensor("mask0", [128, 256], F32, kind="ExternalInput").ap()
    maskI = nc.dram_tensor("maskI", [128, 256], F32, kind="ExternalInput").ap()
    mask7 = nc.dram_tensor("mask7", [128, 256], F32, kind="ExternalInput").ap()
    outT = nc.dram_tensor("outT", [D_MODEL, t], F32, kind="ExternalOutput").ap()

    p = Prog()
    import contextlib
    es = contextlib.ExitStack()
    SB = lambda nm, sh, dt: es.enter_context(nc.sbuf_tensor(nm, sh, dt))
    XR = SB("XR", [128, 16 * th_], BF16)
    P1 = SB("P1", [128, 16, t], F32)
    ob = SB("ob", [128, 16 * t], BF16)
    kbuf = SB("kbuf", [128, 4 * th_], BF16)
    krob = SB("krob", [128, 4 * th_], BF16)
    vbuf = SB("vbuf", [128, 4 * th_], BF16)
    qhb = SB("qhb", [128, t], BF16)
    qrob = SB("qrob", [128, t], BF16)
    wslab = SB("wslab", [128, 8192], BF16)
    cs1b = SB("cs1b", [128, th_], BF16)
    cs2b = SB("cs2b", [128, th_], BF16)
    maskb = SB("maskb", [128, 3, 256], BF16)
    scr = SB("scr", [128, 2, 512], F32)
    srow = SB("srow", [1, 3, 512], F32)
    sq = SB("sq", [128, 512], F32)
    stg = SB("stg", [128, 2, 512], F32)
    attb = SB("attb", [128, 256], BF16)
    attTb = SB("attTb", [128, 256], BF16)
    g8 = SB("g8", [128, 8], F32)
    gE = SB("gE", [128, 8], F32)
    gE1 = SB("gE1", [128, 8], F32)
    gM = SB("gM", [128, 8], F32)
    rs = SB("rs", [128, 4], F32)
    konst = SB("konst", [128, 1], F32)
    idb = SB("idb", [128, 128], F32)
    idbB = SB("idbB", [128, 128], BF16)
    onesb = SB("onesb", [128, 128], F32)
    onesbB = SB("onesbB", [128, 128], BF16)
    ind8b = SB("ind8b", [8, 8 * 128], BF16)
    permB = SB("permB", [128, 128], BF16)
    lnpb = SB("lnpb", [128, 4 * 16], F32)
    sinkbc = SB("sinkbc", [128, 16], F32)
    sinkb = SB("sinkb", [1, 16], F32)
    pall = es.enter_context(nc.psum_tensor("pall", [128, 7, 512], F32))
    pbt = es.enter_context(nc.psum_tensor("pbt", [128, 1024], BF16))
    x2hb = XR[:].rearrange("p (a b) -> p a b", a=16)
    lnreg = XR[:, 0:16 * t].rearrange("p (a b) -> p a b", a=16)
    gelu = ob[:].rearrange("p (a b) -> p a b", a=16)
    bank_rd = [None] * 8

    p.emit("sync", lambda s: s.dma_start(out=idb[:], in_=ident), inc="dws", inc_amt=16)
    p.emit("sync", lambda s: s.dma_start(out=onesb[:], in_=ones), inc="dws", inc_amt=16)
    p.emit("gpsimd", lambda g: g.dma_start(out=maskb[:, 0, :], in_=mask0), inc="dw", inc_amt=16)
    p.emit("gpsimd", lambda g: g.dma_start(out=maskb[:, 1, :], in_=maskI), inc="dw", inc_amt=16)
    p.emit("gpsimd", lambda g: g.dma_start(out=maskb[:, 2, :], in_=mask7), inc="dw", inc_amt=16)
    p.emit("sync", lambda s: s.dma_start(out=lnpb[:], in_=lnp), inc="dws", inc_amt=16)
    t_pre_s = p.emit("sync", lambda s: s.dma_start(out=sinkb[:], in_=sinkr), inc="dws", inc_amt=16)
    p.emit("gpsimd", lambda g: g.dma_start(out=idbB[:], in_=ident), inc="dw", inc_amt=16)
    p.emit("gpsimd", lambda g: g.dma_start(out=onesbB[:], in_=ones), inc="dw", inc_amt=16)
    p.emit("gpsimd", lambda g: g.dma_start(out=ind8b[:], in_=ind8), inc="dw", inc_amt=16)
    p.emit("gpsimd", lambda g: g.dma_start(out=permB[:], in_=permM), inc="dw", inc_amt=16)
    p.emit("gpsimd", lambda g: g.dma_start(out=cs1b[:], in_=cs1), inc="dw", inc_amt=16)
    p.emit("gpsimd", lambda g: g.dma_start(out=cs2b[:], in_=cs2), inc="dw", inc_amt=16)
    t_pre_g = None
    for d in range(16):
        t_pre_g = p.emit("gpsimd", lambda g, d=d: g.dma_start(
            out=x2hb[:, d, :], in_=x2h[128 * d:128 * (d + 1), :]), inc="dw", inc_amt=16)
    t_k0 = p.emit("vector", lambda v: v.memset(konst[:], 0.08838834764831845), inc="v")
    mmsk = p.emit("tensor", lambda tt: tt.matmul(
        pall[:, 6, 0:16], onesb[0:1, :], sinkb[:], start=True, stop=True),
        wait=[t_pre_s, ("dws", 6 * 16) if False else None], inc="t")
    t_sink = p.emit("vector", lambda v: v.tensor_copy(sinkbc[:], pall[:, 6, 0:16]),
                    wait=[mmsk], inc="v")
    bank_rd[6] = t_sink

    def slab_load(src_fn, n_dma, gate):
        tk = None
        w = 8192 // n_dma
        for d in range(n_dma):
            tk = p.emit("gpsimd", lambda g, d=d, w=w: g.dma_start(
                out=wslab[:, w * d:w * (d + 1)], in_=src_fn(d)),
                wait=[gate if d == 0 else None], inc="dsl", inc_amt=16)
        return tk

    slab_cons = None
    TC = [(0, 512), (512, 1024), (1024, 1152)]
    kv_ev = {}
    for which, (wsrc, dst) in enumerate([(wkp, kbuf), (wv, vbuf)]):
        ld = slab_load(lambda d, wsrc=wsrc: wsrc[128 * d:128 * (d + 1), :], 16, slab_cons)
        for g4 in range(4):
            for ci, (c0, c1) in enumerate(TC):
                b = (g4 + ci) % 4
                mm = p.emit("tensor", lambda tt, g4=g4, c0=c0, c1=c1, b=b: [tt.matmul(
                    pall[:, b, 0:c1 - c0], wslab[:, 512 * d + 128 * g4:512 * d + 128 * (g4 + 1)],
                    x2hb[:, d, c0:c1], start=(d == 0), stop=(d == 15)) for d in range(16)][-1],
                    wait=[ld, t_pre_g, bank_rd[b]], inc="t")
                slab_cons = mm
                ev = p.emit("scalar", lambda s, dst=dst, g4=g4, c0=c0, c1=c1, b=b: s.copy(
                    out=dst[:, g4 * th_ + c0:g4 * th_ + c1], in_=pall[:, b, 0:c1 - c0]),
                    wait=[mm], inc="s")
                bank_rd[b] = ev
                kv_ev[(which, g4, ci)] = ev
    kro_tok = {}
    scr_rd = [None, None]
    for g4 in range(4):
        mm = p.emit("tensor", lambda tt, g4=g4: [tt.matmul(
            pall[:, 4 + (g4 % 2), 0:512] if c1 - c0 == 512 else pall[:, 4 + (g4 % 2), 0:c1 - c0],
            permB[:], kbuf[:, g4 * th_ + c0:g4 * th_ + c1], start=True, stop=True)
            for (c0, c1) in [TC[0]]][-1],
            wait=[kv_ev[(0, g4, 0)], bank_rd[4 + (g4 % 2)]], inc="t")
        toks = []
        for ci, (c0, c1) in enumerate(TC):
            b = 4 + (g4 + ci) % 2
            if ci > 0 or True:
                mm = p.emit("tensor", lambda tt, g4=g4, c0=c0, c1=c1, b=b: tt.matmul(
                    pall[:, b, 0:c1 - c0], permB[:], kbuf[:, g4 * th_ + c0:g4 * th_ + c1],
                    start=True, stop=True),
                    wait=[kv_ev[(0, g4, ci)], bank_rd[b]], inc="t")
            t1 = p.emit("vector", lambda v, g4=g4, c0=c0, c1=c1: v.tensor_mul(
                krob[:, g4 * th_ + c0:g4 * th_ + c1], kbuf[:, g4 * th_ + c0:g4 * th_ + c1],
                cs1b[:, c0:c1]),
                wait=[kv_ev[(0, g4, ci)], kro_tok.get((g4 - 1, ci), None)], inc="v")
            t2 = p.emit("vector", lambda v, c0=c0, c1=c1, b=b, sl=ci % 2: v.tensor_mul(
                scr[:, sl, 0:c1 - c0], pall[:, b, 0:c1 - c0], cs2b[:, c0:c1]),
                wait=[mm, scr_rd[ci % 2]], inc="v")
            bank_rd[b] = t2
            t3 = p.emit("vector", lambda v, g4=g4, c0=c0, c1=c1, sl=ci % 2: v.tensor_add(
                krob[:, g4 * th_ + c0:g4 * th_ + c1], krob[:, g4 * th_ + c0:g4 * th_ + c1],
                scr[:, sl, 0:c1 - c0]),
                wait=[t1, t2], inc="v")
            scr_rd[ci % 2] = t3
            toks.append(t3)
            kro_tok[(g4, ci)] = t3
    vT_tok = {}
    for g4 in range(4):
        for kt in range(9):
            mm = p.emit("tensor", lambda tt, g4=g4, kt=kt: tt.transpose(
                pbt[:, 0:128] if kt % 2 == 0 else pbt[:, 128:256],
                vbuf[:, g4 * th_ + 128 * kt:g4 * th_ + 128 * (kt + 1)], idbB[:]),
                wait=[kv_ev[(1, g4, min(kt // 4, 2))], kv_ev[(1, g4, 2)],
                      bank_rd[7], kro_tok[(g4, 2)] if kt == 0 else None], inc="t")
            ev = p.emit("vector", lambda v, g4=g4, kt=kt: v.tensor_copy(
                kbuf[:, g4 * th_ + 128 * kt:g4 * th_ + 128 * (kt + 1)],
                pbt[:, 0:128] if kt % 2 == 0 else pbt[:, 128:256]),
                wait=[mm], inc="v")
            bank_rd[7] = ev
            vT_tok[(g4, kt)] = ev
    vT = kbuf

    o_tok = {}
    att_rd = [None]
    attT_rd = [None]
    qh_rd = [None]
    qro_rd = [None]
    ldq_cons = slab_cons
    for h in range(16):
        g4 = h // 4
        ldq = slab_load(lambda d, h=h: wqp[128 * d:128 * (d + 1),
                                           512 * (h // 4):512 * (h // 4 + 1)],
                        16, ldq_cons) if h % 4 == 0 else ldq
        if h % 4 == 0:
            ldq_cons_next = None
        hh = h % 4
        for thc in range(TH):
            b = thc % 2
            mm = p.emit("tensor", lambda tt, hh=hh, thc=thc, b=b: [tt.matmul(
                pall[:, b, :], wslab[:, 512 * d + 128 * hh:512 * d + 128 * (hh + 1)],
                x2hb[:, d, HALO + 512 * thc:HALO + 512 * (thc + 1)],
                start=(d == 0), stop=(d == 15)) for d in range(16)][-1],
                wait=[ldq, t_pre_g, bank_rd[b]], inc="t")
            ldq_cons_next = mm
            ev = p.emit("scalar", lambda s, thc=thc, b=b: s.copy(
                out=qhb[:, 512 * thc:512 * (thc + 1)], in_=pall[:, b, :]),
                wait=[mm, qh_rd[0]], inc="s")
            bank_rd[b] = ev
            if thc == TH - 1:
                qev = ev
        if h % 4 == 3:
            ldq_cons = ldq_cons_next
        b = 2
        mm = p.emit("tensor", lambda tt, b=b: [tt.matmul(
            pall[:, b, :], permB[:], qhb[:, 512 * c:512 * (c + 1)], start=True, stop=True)
            for c in [0]][-1],
            wait=[qev, bank_rd[b]], inc="t")
        rope_toks = []
        for c in range(TH):
            b = 2 + c % 2
            if c > 0:
                mm = p.emit("tensor", lambda tt, c=c, b=b: tt.matmul(
                    pall[:, b, :], permB[:], qhb[:, 512 * c:512 * (c + 1)], start=True, stop=True),
                    wait=[qev, bank_rd[b]], inc="t")
            t1 = p.emit("vector", lambda v, c=c: v.tensor_mul(
                qrob[:, 512 * c:512 * (c + 1)], qhb[:, 512 * c:512 * (c + 1)],
                cs1b[:, HALO + 512 * c:HALO + 512 * (c + 1)]),
                wait=[qev, qro_rd[0]], inc="v")
            sl = c % 2
            t2 = p.emit("vector", lambda v, c=c, b=b, sl=sl: v.tensor_mul(
                scr[:, sl, 0:512], pall[:, b, :], cs2b[:, HALO + 512 * c:HALO + 512 * (c + 1)]),
                wait=[mm, scr_rd[sl]], inc="v")
            bank_rd[b] = t2
            t3 = p.emit("vector", lambda v, c=c, sl=sl: v.tensor_add(
                qrob[:, 512 * c:512 * (c + 1)], qrob[:, 512 * c:512 * (c + 1)], scr[:, sl, 0:512]),
                wait=[t1, t2], inc="v")
            scr_rd[sl] = t3
            rope_toks.append(t3)
        qh_rd[0] = rope_toks[-1]
        for i in range(8):
            mi = 0 if i == 0 else (2 if i == 7 else 1)
            b = 4 + i % 2
            sc = p.emit("tensor", lambda tt, h=h, i=i, g4=g4, b=b: tt.matmul(
                pall[:, b, 0:256], qrob[:, 128 * i:128 * (i + 1)],
                krob[:, g4 * th_ + 128 * i:g4 * th_ + 128 * i + 256],
                start=True, stop=True),
                wait=[rope_toks[(128 * i) // 512], rope_toks[(128 * i + 127) // 512],
                      kro_tok[(g4, 2)], bank_rd[b]], inc="t")
            m0 = p.emit("vector", lambda v, b=b: v.reduce_max(rs[:, 0:1], pall[:, b, 0:256],
                                                              axis=AX.X, negate=True),
                        wait=[sc], inc="v")
            bb = p.emit("vector", lambda v, h=h: v.scalar_tensor_tensor(
                rs[:, 1:2], rs[:, 0:1], 0.08838834764831845, sinkbc[:, h:h + 1],
                OP.mult, OP.add),
                wait=[m0, t_sink], inc="v")
            ex = p.emit("scalar", lambda s, b=b: s.activation(
                out=scr[:, 0, 0:256], in_=pall[:, b, 0:256], func=AF.Exp,
                scale=konst[:], bias=rs[:, 1:2]),
                wait=[bb, t_k0, scr_rd[0]], inc="s")
            bank_rd[b] = ex
            mk = p.emit("vector", lambda v, mi=mi: v.tensor_mul(
                scr[:, 1, 0:256], scr[:, 0, 0:256], maskb[:, mi, :]),
                wait=[ex, scr_rd[1]], inc="v")
            scr_rd[0] = mk
            sm = p.emit("vector", lambda v: v.reduce_sum(rs[:, 2:3], scr[:, 1, 0:256], axis=AX.X),
                        wait=[mk], inc="v")
            rc = p.emit("vector", lambda v: v.reciprocal(rs[:, 3:4], rs[:, 2:3]), wait=[sm], inc="v")
            at = p.emit("vector", lambda v: v.tensor_scalar(
                attb[:], scr[:, 1, 0:256], rs[:, 3:4], None, OP.mult),
                wait=[rc, att_rd[0]], inc="v")
            scr_rd[1] = at
            ev2 = None
            for half in range(2):
                tr = p.emit("tensor", lambda tt, half=half: tt.transpose(
                    pbt[:, 128 * half:128 * (half + 1)], attb[:, 128 * half:128 * (half + 1)],
                    idbB[:]),
                    wait=[at, bank_rd[7]], inc="t")
                ev2 = p.emit("vector", lambda v, half=half: v.tensor_copy(
                    attTb[:, 128 * half:128 * (half + 1)], pbt[:, 128 * half:128 * (half + 1)]),
                    wait=[tr, attT_rd[0] if half == 0 else None], inc="v")
                bank_rd[7] = ev2
            att_rd[0] = ev2
            av = p.emit("tensor", lambda tt, g4=g4, i=i: [tt.matmul(
                pall[:, 6, 0:128], vT[:, g4 * th_ + 128 * (i + kk):g4 * th_ + 128 * (i + kk + 1)],
                attTb[:, 128 * kk:128 * (kk + 1)], start=(kk == 0), stop=(kk == 1))
                for kk in range(2)][-1],
                wait=[ev2, vT_tok[(g4, i)], vT_tok[(g4, i + 1)], bank_rd[6]], inc="t")
            attT_rd[0] = av
            oe = p.emit("scalar", lambda s, h=h, i=i: s.copy(
                out=ob[:, h * t + 128 * i:h * t + 128 * (i + 1)], in_=pall[:, 6, 0:128]),
                wait=[av, o_tok.get((h, i))], inc="s")
            bank_rd[6] = oe
            o_tok[(h, i)] = oe
        qro_rd[0] = sc
    slab_cons = ldq_cons

    pre3_tok = {}
    stg_rd = [None, None]
    for qt in range(4):
        ldo = slab_load(lambda d, qt=qt: wo[128 * d:128 * (d + 1), 512 * qt:512 * (qt + 1)],
                        16, slab_cons if qt == 0 else slab_cons)
        for f4 in range(4):
            f = 4 * qt + f4
            for thc in range(TH):
                b = (f4 * TH + thc) % 4
                mm = p.emit("tensor", lambda tt, f4=f4, thc=thc, b=b: [tt.matmul(
                    pall[:, b, :], wslab[:, 512 * d + 128 * f4:512 * d + 128 * (f4 + 1)],
                    ob[:, d * t + 512 * thc:d * t + 512 * (thc + 1)],
                    start=(d == 0), stop=(d == 15)) for d in range(16)][-1],
                    wait=[ldo, bank_rd[b], o_tok[(15, 7)]], inc="t")
                slab_cons = mm
                sl = (f * TH + thc) % 2
                ld = p.emit("sync", lambda s, f=f, thc=thc, sl=sl: s.dma_start(
                    out=stg[:, sl, :], in_=x2h[128 * f:128 * (f + 1),
                                               HALO + 512 * thc:HALO + 512 * (thc + 1)]),
                    wait=[stg_rd[sl]], inc="dst%d" % sl, inc_amt=16)
                tk = p.emit("vector", lambda v, f=f, thc=thc, b=b, sl=sl: v.tensor_add(
                    P1[:, f, 512 * thc:512 * (thc + 1)], stg[:, sl, :], pall[:, b, :]),
                    wait=[mm, ld], inc="v")
                bank_rd[b] = tk
                stg_rd[sl] = tk
                pre3_tok[(f, thc)] = tk

    srow_rd = [None]
    def emit_ln(src_ready, dst, gamma_beta=None, src_bf=None, inplace=True):
        apply_toks = {}
        sq_rd = [None]
        for thc in range(TH):
            mm1 = mm2 = None
            for j in range(16):
                srcap = (P1[:, j, 512 * thc:512 * (thc + 1)] if src_bf is None
                         else src_bf[:, j, 512 * thc:512 * (thc + 1)])
                sqt = p.emit("scalar", lambda s, srcap=srcap: s.activation(
                    out=sq[:], in_=srcap, func=AF.Square),
                    wait=[src_ready[(j, thc)], sq_rd[0]], inc="s")
                mm1 = p.emit("tensor", lambda tt, srcap=srcap, j=j: tt.matmul(
                    pall[0:1, 0, :], onesb[:, 0:1] if src_bf is None else onesbB[:, 0:1],
                    srcap, start=(j == 0), stop=(j == 15)),
                    wait=[src_ready[(j, thc)], bank_rd[0] if j == 0 else None], inc="t")
                mm2 = p.emit("tensor", lambda tt, j=j: tt.matmul(
                    pall[0:1, 1, :], onesb[:, 0:1], sq[:], start=(j == 0), stop=(j == 15)),
                    wait=[sqt, bank_rd[1] if j == 0 else None], inc="t")
                sq_rd[0] = mm2
            a = p.emit("vector", lambda v: v.tensor_scalar(
                srow[0:1, 0, :], pall[0:1, 0, :], 1.0 / 2048.0, None, OP.mult),
                wait=[mm1, srow_rd[0]], inc="v")
            bank_rd[0] = a
            b2 = p.emit("vector", lambda v: v.tensor_mul(
                srow[0:1, 2, :], srow[0:1, 0, :], srow[0:1, 0, :]), wait=[a], inc="v")
            c = p.emit("vector", lambda v: v.scalar_tensor_tensor(
                srow[0:1, 2, :], pall[0:1, 1, :], 1.0 / 2048.0, srow[0:1, 2, :],
                OP.mult, OP.subtract),
                wait=[mm2, b2], inc="v")
            bank_rd[1] = c
            c2 = p.emit("vector", lambda v: v.tensor_scalar_add(
                srow[0:1, 2, :], srow[0:1, 2, :], 1e-5), wait=[c], inc="v")
            d = p.emit("scalar", lambda s: s.activation(
                out=srow[0:1, 1, :], in_=srow[0:1, 2, :], func=AF.Sqrt), wait=[c2], inc="s")
            e = p.emit("vector", lambda v: v.reciprocal(srow[0:1, 2, :], srow[0:1, 1, :]),
                       wait=[d], inc="v")
            bm = p.emit("tensor", lambda tt: tt.matmul(
                pall[:, 2, :], onesb[0:1, :], srow[0:1, 0, :], start=True, stop=True),
                wait=[a, bank_rd[2]], inc="t")
            br = p.emit("tensor", lambda tt: tt.matmul(
                pall[:, 3, :], onesb[0:1, :], srow[0:1, 2, :], start=True, stop=True),
                wait=[e, bank_rd[3]], inc="t")
            srow_rd[0] = br
            f2 = None
            for j in range(16):
                srcap = (P1[:, j, 512 * thc:512 * (thc + 1)] if src_bf is None
                         else src_bf[:, j, 512 * thc:512 * (thc + 1)])
                f1 = p.emit("vector", lambda v, srcap=srcap: v.tensor_sub(
                    scr[:, 0, 0:512], srcap, pall[:, 2, :]),
                    wait=[bm, scr_rd[0], src_ready[(j, thc)]], inc="v")
                if gamma_beta is None and dst is None:
                    f2 = p.emit("vector", lambda v, j=j, thc=thc: v.tensor_mul(
                        P1[:, j, 512 * thc:512 * (thc + 1)], scr[:, 0, 0:512], pall[:, 3, :]),
                        wait=[f1, br], inc="v")
                    tokj = f2
                else:
                    f2 = p.emit("vector", lambda v: v.tensor_mul(
                        scr[:, 1, 0:512], scr[:, 0, 0:512], pall[:, 3, :]),
                        wait=[f1, br, scr_rd[1]], inc="v")
                    gcol, bcol = gamma_beta
                    tokj = p.emit("vector", lambda v, j=j, thc=thc, gcol=gcol, bcol=bcol: v.tensor_scalar(
                        dst[:, j, 512 * thc:512 * (thc + 1)], scr[:, 1, 0:512],
                        lnpb[:, gcol * 16 + j:gcol * 16 + j + 1],
                        lnpb[:, bcol * 16 + j:bcol * 16 + j + 1],
                        OP.mult, OP.add),
                        wait=[f2], inc="v")
                    scr_rd[1] = tokj
                scr_rd[0] = f2
                apply_toks[(j, thc)] = tokj
            bank_rd[2] = f2
            bank_rd[3] = f2
        return apply_toks

    x3_tok = emit_ln(pre3_tok, None)
    ln1_tok = emit_ln(x3_tok, lnreg, gamma_beta=(0, 1))

    av2_tok = {}
    for qt in range(4):
        ldv = slab_load(lambda d, qt=qt: wvselT[128 * d:128 * (d + 1), 512 * qt:512 * (qt + 1)],
                        16, slab_cons)
        for f4 in range(4):
            f = 4 * qt + f4
            for thc in range(TH):
                b = (f4 * TH + thc) % 4
                mm = p.emit("tensor", lambda tt, f4=f4, thc=thc, b=b: [tt.matmul(
                    pall[:, b, :], wslab[:, 512 * d + 128 * f4:512 * d + 128 * (f4 + 1)],
                    lnreg[:, d, 512 * thc:512 * (thc + 1)],
                    start=(d == 0), stop=(d == 15)) for d in range(16)][-1],
                    wait=[ldv, bank_rd[b]] + [ln1_tok[(d, thc)] for d in range(16)], inc="t")
                slab_cons = mm
                ev = p.emit("scalar", lambda s, f=f, thc=thc, b=b: s.copy(
                    out=ob[:, f * t + 512 * thc:f * t + 512 * (thc + 1)], in_=pall[:, b, :]),
                    wait=[mm], inc="s")
                bank_rd[b] = ev
                av2_tok[(f, thc)] = ev
    had_tok = {}
    pre_tok = {}
    for qt in range(4):
        ldp = slab_load(lambda d, qt=qt: projT[128 * d:128 * (d + 1), 512 * qt:512 * (qt + 1)],
                        16, slab_cons)
        for f4 in range(4):
            f = 4 * qt + f4
            for thc in range(TH):
                b = (f4 * TH + thc) % 4
                mm = p.emit("tensor", lambda tt, f4=f4, thc=thc, b=b: [tt.matmul(
                    pall[:, b, :], wslab[:, 512 * d + 128 * f4:512 * d + 128 * (f4 + 1)],
                    ob[:, d * t + 512 * thc:d * t + 512 * (thc + 1)],
                    start=(d == 0), stop=(d == 15)) for d in range(16)][-1],
                    wait=[ldp, bank_rd[b]] + [av2_tok[(d, thc)] for d in range(16)], inc="t")
                slab_cons = mm
                hd = p.emit("vector", lambda v, f=f, thc=thc, b=b: v.tensor_add(
                    lnreg[:, f, 512 * thc:512 * (thc + 1)],
                    P1[:, f, 512 * thc:512 * (thc + 1)], pall[:, b, :]),
                    wait=[mm, ln1_tok[(f, thc)], x3_tok[(f, thc)]], inc="v")
                pr = p.emit("vector", lambda v, f=f, thc=thc, b=b: v.scalar_tensor_tensor(
                    P1[:, f, 512 * thc:512 * (thc + 1)],
                    P1[:, f, 512 * thc:512 * (thc + 1)], 2.0, pall[:, b, :],
                    OP.mult, OP.add),
                    wait=[hd], inc="v")
                bank_rd[b] = pr
                had_tok[(f, thc)] = hd
                pre_tok[(f, thc)] = pr

    ln2_tok = emit_ln(had_tok, lnreg, gamma_beta=(2, 3), src_bf=lnreg)

    lgG = wslab[:, 4096:6144].bitcast(F32)[0:8, :]
    gTb = wslab[0:8, 6144:7168]
    ldr = p.emit("gpsimd", lambda g: g.dma_start(
        out=wslab[:, 0:128], in_=wre), wait=[slab_cons], inc="dsl", inc_amt=16)
    lg_ev = [None] * TH
    for thc in range(TH):
        mm = None
        for d in range(16):
            mm = p.emit("tensor", lambda tt, d=d, thc=thc: tt.matmul(
                pall[0:8, 4, :], wslab[:, 8 * d:8 * (d + 1)],
                lnreg[:, d, 512 * thc:512 * (thc + 1)],
                start=(d == 0), stop=(d == 15)),
                wait=[ldr, ln2_tok[(d, thc)], bank_rd[4] if d == 0 else None], inc="t")
        lg_ev[thc] = p.emit("vector", lambda v, thc=thc: v.tensor_copy(
            lgG[:, 512 * thc:512 * (thc + 1)], pall[0:8, 4, :]), wait=[mm], inc="v")
        bank_rd[4] = lg_ev[thc]
    gT_done = []
    g8_rd = [None]
    for tt_ in range(t // 128):
        thc = (tt_ * 128) // 512
        tr = p.emit("tensor", lambda tt, tt_=tt_: tt.transpose(
            pall[:, 5, 0:8], lgG[0:8, 128 * tt_:128 * (tt_ + 1)], idb[0:8, 0:8]),
            wait=[lg_ev[thc], bank_rd[5]], inc="t")
        ev = p.emit("vector", lambda v: v.tensor_copy(g8[:], pall[:, 5, 0:8]),
                    wait=[tr, g8_rd[0]], inc="v")
        bank_rd[5] = ev
        m0 = p.emit("vector", lambda v: v.reduce_max(rs[:, 0:1], g8[:], axis=AX.X, negate=True),
                    wait=[ev], inc="v")
        exg = p.emit("scalar", lambda s: s.activation(out=gE[:], in_=g8[:], func=AF.Exp,
                                                      bias=rs[:, 0:1]),
                     wait=[m0], inc="s")
        g8_rd[0] = exg
        vv1 = p.emit("vector", lambda v: v.reduce_max(rs[:, 1:2], gE[:], axis=AX.X),
                     wait=[exg], inc="v")
        mk1 = p.emit("vector", lambda v: v.tensor_scalar(gM[:], gE[:], rs[:, 1:2], None, OP.is_ge),
                     wait=[vv1], inc="v")
        e1 = p.emit("vector", lambda v: v.scalar_tensor_tensor(gE1[:], gM[:], -1e30, gE[:],
                                                               OP.mult, OP.add),
                    wait=[mk1], inc="v")
        vv2 = p.emit("vector", lambda v: v.reduce_max(rs[:, 2:3], gE1[:], axis=AX.X),
                     wait=[e1], inc="v")
        dn = p.emit("vector", lambda v: v.tensor_add(rs[:, 3:4], rs[:, 1:2], rs[:, 2:3]),
                    wait=[vv2], inc="v")
        rc = p.emit("vector", lambda v: v.reciprocal(rs[:, 3:4], rs[:, 3:4]), wait=[dn], inc="v")
        mk2 = p.emit("vector", lambda v: v.tensor_scalar(gM[:], gE[:], rs[:, 2:3], None, OP.is_ge),
                     wait=[vv2, e1], inc="v")
        gm = p.emit("vector", lambda v: v.tensor_mul(gE[:], gE[:], gM[:]), wait=[mk2], inc="v")
        gs = p.emit("vector", lambda v: v.tensor_scalar(gE[:], gE[:], rs[:, 3:4], None, OP.mult),
                    wait=[gm, rc], inc="v")
        tr2 = p.emit("tensor", lambda tt: tt.transpose(
            pall[0:8, 5, 0:128], gE[:], idb[:]),
            wait=[gs, bank_rd[5]], inc="t")
        ev2 = p.emit("vector", lambda v, tt_=tt_: v.tensor_copy(
            gTb[0:8, 128 * tt_:128 * (tt_ + 1)], pall[0:8, 5, 0:128]),
            wait=[tr2], inc="v")
        bank_rd[5] = ev2
        g8_rd[0] = tr2
        gT_done.append(ev2)
    def gbv(e):
        if e < 4:
            return krob[:, e * t:(e + 1) * t]
        return vbuf[:, (e - 4) * t:(e - 3) * t]
    gb_tok = {}
    for e in range(8):
        for thc in range(TH):
            b = 4 + (e * TH + thc) % 2
            mm = p.emit("tensor", lambda tt, e=e, thc=thc, b=b: tt.matmul(
                pall[:, b, :], ind8b[:, 128 * e:128 * (e + 1)],
                gTb[0:8, 512 * thc:512 * (thc + 1)], start=True, stop=True),
                wait=[bank_rd[b], t_pre_g] + gT_done[4 * thc:4 * (thc + 1)], inc="t")
            ev = p.emit("scalar", lambda s, e=e, thc=thc, b=b: s.copy(
                out=gbv(e)[:, 512 * thc:512 * (thc + 1)], in_=pall[:, b, :]),
                wait=[mm, kro_tok.get((3, 2)) if e == 0 and thc == 0 else None], inc="s")
            bank_rd[b] = ev
            gb_tok[(e, thc)] = ev
            slab_cons = mm

    gelu_tok = {}
    for e in range(8):
        for half in range(2):
            for s8 in range(4):
                fbase = 2048 * half + 512 * s8
                ldw1 = slab_load(lambda d, e=e, fbase=fbase: mw1[e, 128 * d:128 * (d + 1),
                                                                 fbase:fbase + 512],
                                 16, slab_cons)
                for f4 in range(4):
                    lc = 4 * s8 + f4
                    for thc in range(TH):
                        b = (f4 * TH + thc) % 4
                        mm = p.emit("tensor", lambda tt, f4=f4, thc=thc, b=b: [tt.matmul(
                            pall[:, b, :], wslab[:, 512 * d + 128 * f4:512 * d + 128 * (f4 + 1)],
                            lnreg[:, d, 512 * thc:512 * (thc + 1)],
                            start=(d == 0), stop=(d == 15)) for d in range(16)][-1],
                            wait=[ldw1, bank_rd[b]] +
                                 ([ln2_tok[(d, thc)] for d in range(16)] if e == 0 else []),
                            inc="t")
                        slab_cons = mm
                        ge = p.emit("scalar", lambda s, lc=lc, thc=thc, b=b: s.activation(
                            out=gelu[:, lc, 512 * thc:512 * (thc + 1)], in_=pall[:, b, :],
                            func=AF.Gelu),
                            wait=[mm, gelu_tok.get((half, lc, thc)),
                                  av2_tok[(lc, thc)] if e == 0 and half == 0 else None], inc="s")
                        gmul = p.emit("vector", lambda v, lc=lc, thc=thc, e=e: v.tensor_mul(
                            gelu[:, lc, 512 * thc:512 * (thc + 1)],
                            gelu[:, lc, 512 * thc:512 * (thc + 1)],
                            gbv(e)[:, 512 * thc:512 * (thc + 1)]),
                            wait=[ge, gb_tok[(e, thc)]], inc="v")
                        bank_rd[b] = gmul
                        gelu_tok[(half, lc, thc)] = gmul
            for qt in range(4):
                ldw2 = slab_load(lambda d, e=e, half=half, qt=qt: mw2[
                    e, 2048 * half + 128 * d:2048 * half + 128 * (d + 1),
                    512 * qt:512 * (qt + 1)], 16, slab_cons)
                for f4 in range(4):
                    f = 4 * qt + f4
                    for thc in range(TH):
                        b = (f4 * TH + thc) % 4
                        mm = p.emit("tensor", lambda tt, f4=f4, thc=thc, b=b: [tt.matmul(
                            pall[:, b, :], wslab[:, 512 * d + 128 * f4:512 * d + 128 * (f4 + 1)],
                            gelu[:, d, 512 * thc:512 * (thc + 1)],
                            start=(d == 0), stop=(d == 15)) for d in range(16)][-1],
                            wait=[ldw2, bank_rd[b]] +
                                 [gelu_tok[(half, d, thc)] for d in range(16)], inc="t")
                        slab_cons = mm
                        ra = p.emit("vector", lambda v, f=f, thc=thc, b=b: v.tensor_add(
                            P1[:, f, 512 * thc:512 * (thc + 1)],
                            P1[:, f, 512 * thc:512 * (thc + 1)], pall[:, b, :]),
                            wait=[mm, pre_tok[(f, thc)]], inc="v")
                        bank_rd[b] = ra
                        pre_tok[(f, thc)] = ra

    x4_tok = emit_ln(pre_tok, None)
    st_last = None
    for j in range(16):
        st_last = p.emit("sync", lambda s, j=j: s.dma_start(
            out=outT[128 * j:128 * (j + 1), :], in_=P1[:, j, :]),
            wait=[x4_tok[(j, 0)], x4_tok[(j, TH - 1)]], inc="dso", inc_amt=16)
    p.emit("gpsimd", lambda g: None, wait=[st_last])
    with nc.Block() as block:
        p.build(nc, block)
    es.close()
    return nc


def run_B2(inputs, x2_full, t=1024):
    per_core = host_prep_B2(inputs, x2_full, t=t)
    nc = build_B2(t=t)
    res = run_bass_kernel_spmd(nc, per_core, core_ids=list(range(NCORES)))
    return np.concatenate([r["outT"] for r in res.results], axis=1)


def kernel(**inputs):
    out = run_fused(inputs)
    return np.ascontiguousarray(out.T).reshape(1, 8192, 2048)

def host_prep_fused(inputs, t=1024):
    A = host_prep_A(inputs)
    xT = A[0]["xT"]
    B1shared = host_prep_B1(inputs, np.zeros((1, 1), BF), t=t)[0]
    for k in ("ygT", "xme", "woutT"):
        B1shared.pop(k, None)
    B2pc = host_prep_B2(inputs, np.zeros((D_MODEL, NCORES * t), np.float32), t=t)
    woutT = _c(inputs["L0_attn_W_out"].T.astype(np.float32))
    per_core = []
    for c in range(NCORES):
        d = dict(A[c])
        d.update(B1shared)
        b2 = dict(B2pc[c])
        b2.pop("x2h", None)
        for k in ("ident", "ones", "ind8"):
            b2.pop(k, None)
        d.update(b2)
        d["xme"] = _c(xT[:, t * c:t * (c + 1)])
        d["woutTs"] = _c(woutT[NCH * c:NCH * (c + 1), :])
        hwL = np.zeros((128, 8), np.float32)
        hwR = np.zeros((128, 8), np.float32)
        if c > 0:
            hwL[:, c - 1] = 1.0
        if c < NCORES - 1:
            hwR[:, c + 1] = 1.0
        d["hwL"] = hwL
        d["hwR"] = hwR
        per_core.append(d)
    return per_core


def build_fused(T=8192, Tc=512, t=1024):
    nck = T // Tc
    TH = t // 512
    HALO = 64
    th_ = t + 2 * HALO
    nc = bass.Bass("TRN2", target_bir_lowering=False, debug=False)
    DR = {}
    def DI(name, shape, dt=F32):
        DR[name] = nc.dram_tensor(name, shape, dt, kind="ExternalInput").ap()
        return DR[name]
    xT = DI("xT", [D_MODEL, T]); winT = DI("winT", [D_MODEL, 1280])
    wdtT = DI("wdtT", [128, NCH]); bdt = DI("bdt", [128, NQ])
    alogT = DI("alogT", [128, NQ * 16]); convw = DI("convw", [128, NQ * 4])
    convb = DI("convb", [128, NQ]); dpar = DI("dpar", [128, NQ])
    ind_x = DI("ind_x", [128, NQ * 128]); ind_B = DI("ind_B", [128, 64 * 128])
    ind_C = DI("ind_C", [128, 16 * 128])
    woutTs = DI("woutTs", [NCH, D_MODEL]); xme = DI("xme", [D_MODEL, t])
    wrout = DI("wrout", [D_MODEL, 128]); wgate = DI("wgate", [D_MODEL, 8192])
    wup = DI("wup", [D_MODEL, 8192]); wdown = DI("wdown", [1024, D_MODEL])
    wsg = DI("wsg", [D_MODEL, D_MODEL]); wsu = DI("wsu", [D_MODEL, D_MODEL])
    wsd = DI("wsd", [D_MODEL, D_MODEL])
    a0 = DI("a0", [128, 8]); a1 = DI("a1", [128, 8]); esc = DI("esc", [128, 1])
    ident = DI("ident", [128, 128]); ones = DI("ones", [128, 128])
    ind8 = DI("ind8", [8, 8 * 128])
    wqp = DI("wqp", [D_MODEL, 2048]); wkp = DI("wkp", [D_MODEL, 512])
    wv = DI("wv", [D_MODEL, 512]); wo = DI("wo", [D_MODEL, 2048])
    sinkr = DI("sinkr", [1, 16]); wvselT = DI("wvselT", [D_MODEL, 2048])
    projT = DI("projT", [D_MODEL, 2048]); wre = DI("wre", [128, 128])
    mw1 = DI("mw1", [8, D_MODEL, 4096]); mw2 = DI("mw2", [8, 4096, D_MODEL])
    permM = DI("permM", [128, 128]); lnp = DI("lnp", [128, 4 * 16])
    cs1 = DI("cs1", [128, th_]); cs2 = DI("cs2", [128, th_])
    mask0 = DI("mask0", [128, 256]); maskI = DI("maskI", [128, 256])
    mask7 = DI("mask7", [128, 256])
    hwL = DI("hwL", [128, 8]); hwR = DI("hwR", [128, 8])
    outT = nc.dram_tensor("outT", [D_MODEL, t], F32, kind="ExternalOutput").ap()
    prt = nc.dram_tensor("prt", [NCORES * D_MODEL, t], BF16, kind="Internal")
    rsout = nc.dram_tensor("rsout", [D_MODEL, t], BF16, kind="Internal")
    edgebuf = nc.dram_tensor("edgebuf", [128, 16 * 128], BF16, kind="Internal")
    edgeall = nc.dram_tensor("edgeall", [NCORES * 128, 16 * 128], BF16, kind="Internal")
    rg = [list(range(NCORES))]

    p = Prog()
    import contextlib
    esS = contextlib.ExitStack()
    SBs = lambda nm, sh, dt: esS.enter_context(nc.sbuf_tensor(nm, sh, dt))
    resA = SBs("resA", [128, 16, t], F32)
    wslab = SBs("wslab", [128, 8192], BF16)
    idb = SBs("idb", [128, 128], F32)
    onesb = SBs("onesb", [128, 128], F32)
    idbB = SBs("idbB", [128, 128], BF16)
    onesbB = SBs("onesbB", [128, 128], BF16)
    ind8b = SBs("ind8b", [8, 8 * 128], BF16)
    permB = SBs("permB", [128, 128], BF16)
    lnpb = SBs("lnpb", [128, 4 * 16], F32)
    sinkbc = SBs("sinkbc", [128, 16], F32)
    sinkb = SBs("sinkb", [1, 16], F32)
    maskb = SBs("maskb", [128, 3, 256], BF16)
    hwLb = SBs("hwLb", [128, 8], F32)
    hwRb = SBs("hwRb", [128, 8], F32)
    resF = resA[:].rearrange("p a b -> p (a b)")

    p.emit("sync", lambda s: s.dma_start(out=idb[:], in_=ident), inc="dws", inc_amt=16)
    p.emit("sync", lambda s: s.dma_start(out=onesb[:], in_=ones), inc="dws", inc_amt=16)
    p.emit("sync", lambda s: s.dma_start(out=lnpb[:], in_=lnp), inc="dws", inc_amt=16)
    p.emit("sync", lambda s: s.dma_start(out=sinkb[:], in_=sinkr), inc="dws", inc_amt=16)
    p.emit("sync", lambda s: s.dma_start(out=hwLb[:], in_=hwL), inc="dws", inc_amt=16)
    t_pre_s = p.emit("sync", lambda s: s.dma_start(out=hwRb[:], in_=hwR), inc="dws", inc_amt=16)
    p.emit("gpsimd", lambda g: g.dma_start(out=idbB[:], in_=ident), inc="dw", inc_amt=16)
    p.emit("gpsimd", lambda g: g.dma_start(out=onesbB[:], in_=ones), inc="dw", inc_amt=16)
    p.emit("gpsimd", lambda g: g.dma_start(out=ind8b[:], in_=ind8), inc="dw", inc_amt=16)
    p.emit("gpsimd", lambda g: g.dma_start(out=permB[:], in_=permM), inc="dw", inc_amt=16)
    p.emit("gpsimd", lambda g: g.dma_start(out=maskb[:, 0, :], in_=mask0), inc="dw", inc_amt=16)
    p.emit("gpsimd", lambda g: g.dma_start(out=maskb[:, 1, :], in_=maskI), inc="dw", inc_amt=16)
    t_pre_gS = p.emit("gpsimd", lambda g: g.dma_start(out=maskb[:, 2, :], in_=mask7), inc="dw", inc_amt=16)
    esA = contextlib.ExitStack()
    SA = lambda nm, sh, dt: esA.enter_context(nc.sbuf_tensor(nm, sh, dt))
    wl = SA("wl", [128, 16 * 1280], BF16)
    xcA = SA("xcA", [128, 16 * Tc], BF16)
    wdt_sb = SA("wdt_sb", [128, NCH], BF16)
    indx_sb = SA("indx_sb", [128, NQ * 128], BF16)
    indB_sb = SA("indB_sb", [128, 64 * 128], BF16)
    indC_sb = SA("indC_sb", [128, 16 * 128], BF16)
    Cbuf = SA("Cbuf", [128, NQ, Tc], BF16)
    xBb = SA("xBb", [128, Tc], BF16)
    dtlb = SA("dtlb", [128, Tc], BF16)
    ygb = SA("ygb", [128, NQ, Tc], BF16)
    wouts_sb = SA("wouts_sb", [128, NQ * 2048], BF16)
    dAb = SA("dAb", [128, 2, Tc], F32)
    d1b = SA("d1b", [128, 2, Tc], F32)
    snb = SA("snb", [128, 2, Tc], F32)
    prodbA = SA("prodbA", [128, 2, Tc], F32)
    stateb = SA("stateb", [128, 64], F32)
    bdt2 = SA("bdt2", [128, NQ], F32)
    acol = SA("acol", [128, NQ * 16], F32)
    cwS = SA("cwS", [128, NQ * 4], F32)
    cbS = SA("cbS", [128, NQ], F32)
    dpS = SA("dpS", [128, NQ], F32)
    psmm = esA.enter_context(nc.psum_tensor("psmm", [128, 2, 512], F32))
    psx = esA.enter_context(nc.psum_tensor("psx", [128, 2, 512], F32))
    psB = esA.enter_context(nc.psum_tensor("psB", [128, 2, 512], F32))
    psC = esA.enter_context(nc.psum_tensor("psC", [128, 2, 512], F32))
    delta = resF[:, 0:2048].rearrange("p (a b) -> p a b", a=NQ)
    ubuf = resF[:, 2048:4096].rearrange("p (a b) -> p a b", a=NQ)
    dubuf = resF[:, 4096:6144].rearrange("p (a b) -> p a b", a=NQ)
    ybuf = resF[:, 6144:8192].rearrange("p (a b) -> p a b", a=NQ)
    zf = resF[:, 8192:10240].rearrange("p (a b) -> p a b", a=NQ)
    xrb = resF[:, 10240:12300].rearrange("p (a b) -> p a b", a=NQ)
    cacc = resF[:, 12300:13324].rearrange("p (a b) -> p a b", a=2)
    sp1 = resF[:, 13324:13836]
    sp2 = resF[:, 13836:14348]
    sp3 = resF[:, 14348:14860]
    szb = sp3
    a2s = resF[:, 14860:15884].rearrange("p (a b) -> p a b", a=2).bitcast(BF16)[:, :, 0:512]
    xcv = [xcA[:].rearrange("p (a b) -> p a b", a=16),
           wslab[:].rearrange("p (a b) -> p a b", a=16)]

    for d in range(16):
        p.emit("gpsimd", lambda g, d=d: g.dma_start(
            out=wl[:, d * 1280:(d + 1) * 1280], in_=winT[128 * d:128 * (d + 1), :]),
            inc="dw", inc_amt=16)
    p.emit("gpsimd", lambda g: g.dma_start(out=wdt_sb[:], in_=wdtT), inc="dw", inc_amt=16)
    p.emit("gpsimd", lambda g: g.dma_start(out=indx_sb[:], in_=ind_x), inc="dw", inc_amt=16)
    p.emit("gpsimd", lambda g: g.dma_start(out=indB_sb[:], in_=ind_B), inc="dw", inc_amt=16)
    p.emit("gpsimd", lambda g: g.dma_start(out=indC_sb[:], in_=ind_C), inc="dw", inc_amt=16)
    for qq in range(NQ):
        p.emit("gpsimd", lambda g, qq=qq: g.dma_start(
            out=wouts_sb[:, qq * 2048:(qq + 1) * 2048], in_=woutTs[128 * qq:128 * (qq + 1), :]),
            inc="dw", inc_amt=16)
    t_pre_g = p.emit("gpsimd", lambda g: g.dma_start(out=acol[:], in_=alogT), inc="dw", inc_amt=16)
    p.emit("sync", lambda s: s.dma_start(out=bdt2[:], in_=bdt), inc="dws", inc_amt=16)
    p.emit("sync", lambda s: s.dma_start(out=cwS[:], in_=convw), inc="dws", inc_amt=16)
    p.emit("sync", lambda s: s.dma_start(out=cbS[:], in_=convb), inc="dws", inc_amt=16)
    t_pre_s = p.emit("sync", lambda s: s.dma_start(out=dpS[:], in_=dpar), inc="dws", inc_amt=16)
    t_bdt = t_alog = t_cw = t_cb = t_dp = t_pre_s
    t_wdt = t_indx = t_indB = t_indC = t_pre_g
    t_b2 = p.emit("vector", lambda v: v.tensor_scalar_mul(bdt2[:], bdt2[:], 2.0),
                  wait=[t_bdt], inc="v")
    t_ae = p.emit("scalar", lambda s: s.activation(out=acol[:], in_=acol[:], func=AF.Exp),
                  wait=[t_pre_g], inc="s")
    t_an = p.emit("vector", lambda v: v.tensor_scalar_mul(acol[:], acol[:], -1.0),
                  wait=[t_ae], inc="v")
    t_st0 = p.emit("vector", lambda v: v.memset(stateb[:], 0.0), inc="v")
    t_hist0 = p.emit("vector", lambda v: v.memset(xrb[:], 0.0), inc="v")

    xload = [None] * (nck + 2)
    xfree = [None] * (nck + 2)

    def load_chunk(k):
        tk = None
        sem = "dx%d" % (k % 2)
        for d in range(16):
            tk = p.emit("gpsimd", lambda g, d=d, k=k: g.dma_start(
                out=xcv[k % 2][:, d, :],
                in_=xT[128 * d:128 * (d + 1), k * Tc:(k + 1) * Tc]),
                wait=[xfree[k - 2] if k >= 2 else None],
                inc=sem, inc_amt=16)
        return tk

    xload[0] = load_chunk(0)
    if nck > 1:
        xload[1] = load_chunk(1)

    evac_free = [None, None]
    psx_free = [None, None]
    state_w = [None] * 64
    B_rd = [None, None]
    C_rd = [None, None]
    A_rd = [None, None]
    D_rd = [None, None]
    S_rd = [None, None]
    P_rd = [None, None]
    delta_war = [t_hist0] * NQ
    du_war = [None] * NQ
    u_war = [None] * NQ
    z_war = [None] * NQ
    xB_war = None
    C_war = [None] * NQ
    dtl_war = None
    xrb_war = [t_hist0] * NQ
    cacc_free = [None, None]
    sp_free = [t_hist0, None, None]
    szb_free = None
    y_war = [None] * NQ
    a2s_free = [None, None]
    a2st_free = [None, None]
    a2_gate = None
    for k in range(nck):
        mm_tok = [None] * 10
        ev = [None] * 10
        for j in range(10):
            slot = j % 2

            def mmz(tt, j=j, slot=slot, k=k):
                ins = None
                for d in range(16):
                    ins = tt.matmul(psmm[:, slot, :],
                                    wl[:, d * 1280 + j * 128:d * 1280 + (j + 1) * 128],
                                    xcv[k % 2][:, d, :],
                                    start=(d == 0), stop=(d == 15))
                return ins

            mm_tok[j] = p.emit("tensor", mmz,
                               wait=[xload[k], evac_free[slot],
                                     t_pre_g if k == 0 and j == 0 else None],
                               inc="t")
            if j <= 3:
                dst, war = zf[:, j, :], z_war[j]
            elif j == 4:
                dst, war = xBb[:], xB_war
            elif j <= 8:
                dst, war = Cbuf[:, j - 5, :], C_war[j - 5]
            else:
                dst, war = dtlb[:], dtl_war
            ev[j] = p.emit("scalar",
                           lambda s, dst=dst, slot=slot: s.copy(out=dst, in_=psmm[:, slot, :]),
                           wait=[mm_tok[j], war], inc="s")
            evac_free[slot] = ev[j]
        if k + 2 < nck:
            xfree[k] = mm_tok[9]
            xload[k + 2] = load_chunk(k + 2)

        t_delta = [None] * NQ
        t_u = [None] * NQ
        t_du = [None] * NQ
        for q in range(NQ):
            slot = q % 2
            mmd = p.emit("tensor", lambda tt, q=q, slot=slot: tt.matmul(
                psx[:, slot, :], wdt_sb[:, q * 128:(q + 1) * 128], dtlb[:],
                start=True, stop=True),
                wait=[ev[9], t_wdt, psx_free[slot]], inc="t")
            if q == NQ - 1:
                dtl_war = mmd
            a1_ = p.emit("scalar", lambda s, slot=slot, q=q: s.activation(
                out=sp1, in_=psx[:, slot, :], func=AF.Abs, bias=bdt2[:, q:q + 1]),
                wait=[mmd, t_b2, sp_free[0]], inc="s")
            e1 = p.emit("scalar", lambda s: s.activation(
                out=sp2, in_=sp1, func=AF.Exp, scale=-1.0),
                wait=[a1_, sp_free[1]], inc="s")
            l1 = p.emit("scalar", lambda s: s.activation(
                out=sp2, in_=sp2, func=AF.Ln, bias=1.0),
                wait=[e1], inc="s")
            t0 = p.emit("vector", lambda v, slot=slot, q=q: v.tensor_scalar_add(
                sp3, psx[:, slot, :], bdt2[:, q:q + 1]),
                wait=[mmd, a1_, t_b2, sp_free[2]], inc="v")
            t1 = p.emit("vector", lambda v: v.tensor_add(sp3, sp3, sp1),
                        wait=[t0, a1_], inc="v")
            t_delta[q] = p.emit("vector", lambda v, q=q: v.scalar_tensor_tensor(
                delta[:, q, :], sp3, 0.5, sp2, OP.mult, OP.add),
                wait=[t1, l1, delta_war[q]], inc="v")
            sp_free[0] = t1
            sp_free[1] = t_delta[q]
            sp_free[2] = t_delta[q]
            psx_free[slot] = t1

            mmx = p.emit("tensor", lambda tt, q=q, slot=slot: tt.matmul(
                psx[:, slot, :], indx_sb[:, q * 128:(q + 1) * 128], xBb[:],
                start=True, stop=True),
                wait=[ev[4], t_indx, psx_free[slot]], inc="t")
            exr = p.emit("vector", lambda v, q=q, slot=slot: v.tensor_copy(
                xrb[:, q, 3:Tc + 3], psx[:, slot, :]),
                wait=[mmx, xrb_war[q]], inc="v")
            psx_free[slot] = exr
            c0 = p.emit("vector", lambda v, q=q: v.tensor_scalar(
                cacc[:, 0, :], xrb[:, q, 0:Tc], cwS[:, 4 * q:4 * q + 1], None, OP.mult),
                wait=[exr, t_cw, cacc_free[0]], inc="v")
            c1 = p.emit("vector", lambda v, q=q: v.scalar_tensor_tensor(
                cacc[:, 1, :], xrb[:, q, 1:Tc + 1], cwS[:, 4 * q + 1:4 * q + 2],
                cacc[:, 0, :], OP.mult, OP.add),
                wait=[c0, cacc_free[1]], inc="v")
            c2 = p.emit("vector", lambda v, q=q: v.scalar_tensor_tensor(
                cacc[:, 0, :], xrb[:, q, 2:Tc + 2], cwS[:, 4 * q + 2:4 * q + 3],
                cacc[:, 1, :], OP.mult, OP.add),
                wait=[c1], inc="v")
            c3 = p.emit("vector", lambda v, q=q: v.scalar_tensor_tensor(
                cacc[:, 1, :], xrb[:, q, 3:Tc + 3], cwS[:, 4 * q + 3:4 * q + 4],
                cacc[:, 0, :], OP.mult, OP.add),
                wait=[c2], inc="v")
            hq = p.emit("vector", lambda v, q=q: v.tensor_copy(
                xrb[:, q, 0:3], xrb[:, q, Tc:Tc + 3]),
                wait=[c3], inc="v")
            xrb_war[q] = hq
            t_u[q] = p.emit("scalar", lambda s, q=q: s.activation(
                out=ubuf[:, q, :], in_=cacc[:, 1, :], func=AF.Silu, bias=cbS[:, q:q + 1]),
                wait=[c3, t_cb, u_war[q]], inc="s")
            cacc_free[0] = c3
            cacc_free[1] = t_u[q]
            t_du[q] = p.emit("vector", lambda v, q=q: v.tensor_mul(
                dubuf[:, q, :], delta[:, q, :], ubuf[:, q, :]),
                wait=[t_delta[q], t_u[q], du_war[q]], inc="v")

        tok_yadd = [None] * NQ
        for i in range(64):
            n, q = divmod(i, NQ)
            j = 16 * q + n
            sl = i % 2
            tB = p.emit("tensor", lambda tt, j=j, sl=sl: tt.matmul(
                psB[:, sl, :], indB_sb[:, j * 128:(j + 1) * 128], xBb[:],
                start=True, stop=True),
                wait=[ev[4], t_indB, B_rd[sl]], inc="t")
            tC = p.emit("tensor", lambda tt, n=n, q=q, sl=sl: tt.matmul(
                psC[:, sl, :], indC_sb[:, n * 128:(n + 1) * 128], Cbuf[:, q, :],
                start=True, stop=True),
                wait=[ev[5 + q], t_indC, C_rd[sl]], inc="t")
            if n == 15:
                C_war[q] = tC
            if i == 63:
                xB_war = tB
            tA = p.emit("scalar", lambda s, q=q, n=n, sl=sl: s.activation(
                out=dAb[:, sl, :], in_=delta[:, q, :], func=AF.Exp,
                scale=acol[:, q * 16 + n:q * 16 + n + 1]),
                wait=[t_delta[q], t_an, A_rd[sl]], inc="s")
            if n == 15:
                delta_war[q] = tA
            td1 = p.emit("vector", lambda v, q=q, sl=sl: v.tensor_mul(
                d1b[:, sl, :], dubuf[:, q, :], psB[:, sl, :]),
                wait=[t_du[q], tB, D_rd[sl]], inc="v")
            B_rd[sl] = td1
            if n == 15:
                du_war[q] = td1
            tscan = p.emit("vector", lambda v, i=i, sl=sl: v.tensor_tensor_scan(
                snb[:, sl, :], dAb[:, sl, :], d1b[:, sl, :],
                stateb[:, i:i + 1], OP.mult, OP.add),
                wait=[tA, td1, state_w[i], S_rd[sl], t_st0], inc="v")
            A_rd[sl] = tscan
            D_rd[sl] = tscan
            state_w[i] = p.emit("vector", lambda v, i=i, sl=sl: v.tensor_copy(
                stateb[:, i:i + 1], snb[:, sl, Tc - 1:Tc]),
                wait=[tscan], inc="v")
            if n == 0:
                tprod = p.emit("vector", lambda v, q=q, sl=sl: v.tensor_mul(
                    ybuf[:, q, :], snb[:, sl, :], psC[:, sl, :]),
                    wait=[tscan, tC, y_war[q]], inc="v")
                tok_yadd[q] = tprod
            else:
                tprod = p.emit("vector", lambda v, sl=sl: v.tensor_mul(
                    prodbA[:, sl, :], snb[:, sl, :], psC[:, sl, :]),
                    wait=[tscan, tC, P_rd[sl]], inc="v")
                ty = p.emit("vector", lambda v, q=q, sl=sl: v.tensor_add(
                    ybuf[:, q, :], ybuf[:, q, :], prodbA[:, sl, :]),
                    wait=[tprod, tok_yadd[q]], inc="v")
                tok_yadd[q] = ty
                P_rd[sl] = ty
            C_rd[sl] = tprod
            S_rd[sl] = tmax(tprod, state_w[i])

        ygt_tok = [None] * NQ
        for q in range(NQ):
            sl = q % 2
            pr = p.emit("vector", lambda v, q=q, sl=sl: v.tensor_scalar(
                prodbA[:, sl, :], ubuf[:, q, :], dpS[:, q:q + 1], None, OP.mult),
                wait=[t_u[q], t_dp, P_rd[sl]], inc="v")
            ya = p.emit("vector", lambda v, q=q, sl=sl: v.tensor_add(
                ybuf[:, q, :], ybuf[:, q, :], prodbA[:, sl, :]),
                wait=[pr, tok_yadd[q]], inc="v")
            P_rd[sl] = ya
            u_war[q] = pr
            sz = p.emit("scalar", lambda s, q=q: s.activation(
                out=szb, in_=zf[:, q, :], func=AF.Silu),
                wait=[ev[q], szb_free], inc="s")
            z_war[q] = sz
            ygt = p.emit("vector", lambda v, q=q: v.tensor_mul(
                ygb[:, q, :], ybuf[:, q, :], szb),
                wait=[ya, sz, a2_gate], inc="v")
            szb_free = ygt
            sp_free[2] = ygt
            y_war[q] = ygt
            ygt_tok[q] = ygt
        r_, cblk = divmod(k, TH)
        for f in range(16):
            slot = f % 2
            mm = p.emit("tensor", lambda tt, f=f, slot=slot: [tt.matmul(
                psmm[:, slot, :], wouts_sb[:, q2 * 2048 + 128 * f:q2 * 2048 + 128 * (f + 1)],
                ygb[:, q2, :], start=(q2 == 0), stop=(q2 == 3)) for q2 in range(NQ)][-1],
                wait=[ygt_tok[0], ygt_tok[1], ygt_tok[2], ygt_tok[3],
                      t_pre_g, evac_free[slot]], inc="t")
            eva = p.emit("scalar", lambda s, slot=slot: s.copy(
                out=a2s[:, slot, :], in_=psmm[:, slot, :]),
                wait=[mm, a2st_free[slot]], inc="s")
            evac_free[slot] = eva
            st = p.emit("sync", lambda s, f=f, r_=r_, cblk=cblk, slot=slot: s.dma_start(
                out=prt.ap()[2048 * r_ + 128 * f:2048 * r_ + 128 * (f + 1),
                             512 * cblk:512 * (cblk + 1)],
                in_=a2s[:, slot, :]),
                wait=[eva], inc="dpr%d" % slot, inc_amt=16)
            a2st_free[slot] = st
            if f == 15:
                a2_gate = mm
    phaseA_join = [("t", p.cnt.get("t", 0)), ("v", p.cnt.get("v", 0)),
                   ("s", p.cnt.get("s", 0)), ("dx0", p.cnt.get("dx0", 0)),
                   ("dx1", p.cnt.get("dx1", 0)), ("dw", p.cnt.get("dw", 0)),
                   ("dws", p.cnt.get("dws", 0))]
    t_rs = p.emit("gpsimd", lambda g: g.collective_compute(
        "ReduceScatter", OP.add, rg, [prt.ap().opt()], [rsout.ap().opt()]),
        wait=[("dpr0", p.cnt.get("dpr0", 0)), ("dpr1", p.cnt.get("dpr1", 0))],
        inc="cc")
    esA.close()

    def make_streamer(sem_prefix, init_gate):
        state = {"cons": [init_gate, init_gate], "par": 0}

        def load(src_fn, n_dma, extra=None):
            h = state["par"]
            state["par"] ^= 1
            w = 4096 // n_dma
            tk = None
            for dd in range(n_dma):
                tk = p.emit("gpsimd", lambda g, dd=dd, w=w, h=h: g.dma_start(
                    out=wslab[:, 4096 * h + w * dd:4096 * h + w * (dd + 1)],
                    in_=src_fn(dd)),
                    wait=[state["cons"][h] if dd == 0 else None,
                          extra if dd == 0 else None],
                    inc="%s%d" % (sem_prefix, h), inc_amt=16)
            return h, tk

        def consumed(h, tok):
            state["cons"][h] = tok
        return load, consumed, state
    esB1 = contextlib.ExitStack()
    SB1 = lambda nm, sh, dt: esB1.enter_context(nc.sbuf_tensor(nm, sh, dt))
    x1b = SB1("x1b", [128, 16, t], BF16)
    bp = SB1("bp", [128, 24 * t], BF16)
    stgF = SB1("stgF", [128, 4, 512], F32)
    srow = SB1("srow", [1, 3, 512], F32)
    lgf = SB1("lgf", [128, t], F32)
    lgTt = SB1("lgTt", [128, 128], F32)
    Eb = SB1("Eb", [128, 128], F32)
    Ea = SB1("Ea", [128, 128], F32)
    Ec = SB1("Ec", [128, 128], F32)
    Mb = SB1("Mb", [128, 128], F32)
    vc = SB1("vc", [128, 8], F32)
    cnb = SB1("cnb", [128, 128], F32)
    cnf = SB1("cnf", [128, t], BF16)
    mixs = SB1("mixs", [128, 8], F32)
    mixb = SB1("mixb", [128, 8], BF16)
    a1s = SB1("a1s", [128, 8], F32)
    esb = SB1("esb", [128, 1], F32)
    efb = SB1("efb", [8, t], BF16)
    wrb = SB1("wrb", [128, 16 * 128], BF16)
    tb = SB1("tb", [128, 2, 512], BF16)
    pall = esB1.enter_context(nc.psum_tensor("pall", [128, 8, 512], F32))
    ebs = bp[:, 0:8 * t]
    sgv = bp[:, 8 * t:16 * t]
    sgs = bp[:, 0:16 * t]
    comb = bp[:, 16 * t:24 * t]
    sq = lgf[:, 0:512]
    bank_rd = [None] * 8

    p.emit("gpsimd", lambda g: g.dma_start(out=mixs[:], in_=a0),
           wait=phaseA_join, inc="dw", inc_amt=16)
    p.emit("gpsimd", lambda g: g.dma_start(out=a1s[:], in_=a1), inc="dw", inc_amt=16)
    p.emit("sync", lambda s: s.dma_start(out=esb[:], in_=esc),
           wait=phaseA_join, inc="dws", inc_amt=16)
    t_pre_s1 = (("dws", p.cnt["dws"]))
    t_rb = p.emit("gpsimd", lambda g: g.dma_start(
        out=wrb[:].rearrange("p (a b) -> p a b", a=16),
        in_=wrout.rearrange("(a p) b -> p a b", p=128)), inc="dw", inc_amt=16)

    s0 = p.emit("scalar", lambda s: s.activation(out=Eb[:, 0:8], in_=mixs[:], func=AF.Square),
                wait=[t_rb, phaseA_join[2]], inc="s")
    s1 = p.emit("scalar", lambda s: s.activation(out=Eb[:, 8:16], in_=a1s[:], func=AF.Square),
                wait=[t_rb, s0], inc="s")
    v0 = p.emit("vector", lambda v: v.tensor_add(mixs[:], Eb[:, 0:8], Eb[:, 8:16]),
                wait=[s1, phaseA_join[1]], inc="v")
    v1_ = p.emit("vector", lambda v: v.reduce_sum(vc[:, 0:1], mixs[:], axis=AX.X), wait=[v0], inc="v")
    v2_ = p.emit("vector", lambda v: v.tensor_scalar_add(vc[:, 0:1], vc[:, 0:1], 1e-8), wait=[v1_], inc="v")
    v3_ = p.emit("vector", lambda v: v.reciprocal(vc[:, 1:2], vc[:, 0:1]), wait=[v2_], inc="v")
    v4_ = p.emit("vector", lambda v: v.tensor_scalar(mixs[:], mixs[:], vc[:, 1:2], None, OP.mult),
                 wait=[v3_], inc="v")
    v5_ = p.emit("vector", lambda v: v.tensor_scalar(mixs[:], mixs[:], esb[:], None, OP.mult),
                 wait=[v4_, t_pre_s1], inc="v")
    t_mix = p.emit("vector", lambda v: v.tensor_copy(mixb[:], mixs[:]), wait=[v5_], inc="v")

    pre1_tok = {}
    stg_rd = [phaseA_join[0], None, None, None]
    for j in range(16):
        for th in range(TH):
            slr = (j * TH + th) % 2
            slx = 2 + (j * TH + th) % 2
            ldr_ = p.emit("sync", lambda s, j=j, th=th, slr=slr: s.dma_start(
                out=stgF[:, slr, :].bitcast(BF16)[:, 0:512],
                in_=rsout.ap()[128 * j:128 * (j + 1), 512 * th:512 * (th + 1)]),
                wait=[("cc", 1), stg_rd[slr]], inc="drs%d" % slr, inc_amt=16)
            ldx_ = p.emit("sync", lambda s, j=j, th=th, slx=slx: s.dma_start(
                out=stgF[:, slx, :], in_=xme[128 * j:128 * (j + 1), 512 * th:512 * (th + 1)]),
                wait=[stg_rd[slx]], inc="dxm%d" % (slx - 2), inc_amt=16)
            tk = p.emit("vector", lambda v, j=j, th=th, slr=slr, slx=slx: v.tensor_add(
                resA[:, j, 512 * th:512 * (th + 1)], stgF[:, slx, :],
                stgF[:, slr, :].bitcast(BF16)[:, 0:512]),
                wait=[ldr_, ldx_], inc="v")
            pre1_tok[(j, th)] = tk
            stg_rd[slr] = tk
            stg_rd[slx] = tk

    srow_rd = [None]
    def emit_lnB1(src_ready, x1b_out):
        apply_toks = {}
        sq_rd = [None]
        for th in range(TH):
            mm1 = mm2 = None
            for j in range(16):
                sqt = p.emit("scalar", lambda s, j=j, th=th: s.activation(
                    out=sq, in_=resA[:, j, 512 * th:512 * (th + 1)], func=AF.Square),
                    wait=[src_ready[(j, th)], sq_rd[0]], inc="s")
                mm1 = p.emit("tensor", lambda tt, j=j, th=th: tt.matmul(
                    pall[0:1, 0, :], onesb[:, 0:1], resA[:, j, 512 * th:512 * (th + 1)],
                    start=(j == 0), stop=(j == 15)),
                    wait=[src_ready[(j, th)], bank_rd[0] if j == 0 else None], inc="t")
                mm2 = p.emit("tensor", lambda tt, j=j: tt.matmul(
                    pall[0:1, 1, :], onesb[:, 0:1], sq, start=(j == 0), stop=(j == 15)),
                    wait=[sqt, bank_rd[1] if j == 0 else None], inc="t")
                sq_rd[0] = mm2
            a_ = p.emit("vector", lambda v: v.tensor_scalar(
                srow[0:1, 0, :], pall[0:1, 0, :], 1.0 / 2048.0, None, OP.mult),
                wait=[mm1, srow_rd[0]], inc="v")
            bank_rd[0] = a_
            b2_ = p.emit("vector", lambda v: v.tensor_mul(
                srow[0:1, 2, :], srow[0:1, 0, :], srow[0:1, 0, :]), wait=[a_], inc="v")
            c_ = p.emit("vector", lambda v: v.scalar_tensor_tensor(
                srow[0:1, 2, :], pall[0:1, 1, :], 1.0 / 2048.0, srow[0:1, 2, :],
                OP.mult, OP.subtract),
                wait=[mm2, b2_], inc="v")
            bank_rd[1] = c_
            c2_ = p.emit("vector", lambda v: v.tensor_scalar_add(
                srow[0:1, 2, :], srow[0:1, 2, :], 1e-5), wait=[c_], inc="v")
            d_ = p.emit("scalar", lambda s: s.activation(
                out=srow[0:1, 1, :], in_=srow[0:1, 2, :], func=AF.Sqrt), wait=[c2_], inc="s")
            e_ = p.emit("vector", lambda v: v.reciprocal(srow[0:1, 2, :], srow[0:1, 1, :]),
                        wait=[d_], inc="v")
            bm = p.emit("tensor", lambda tt: tt.matmul(
                pall[:, 2, :], onesb[0:1, :], srow[0:1, 0, :], start=True, stop=True),
                wait=[a_, bank_rd[2]], inc="t")
            br = p.emit("tensor", lambda tt: tt.matmul(
                pall[:, 3, :], onesb[0:1, :], srow[0:1, 2, :], start=True, stop=True),
                wait=[e_, bank_rd[3]], inc="t")
            srow_rd[0] = br
            f2 = None
            for j in range(16):
                f1 = p.emit("vector", lambda v, j=j, th=th: v.tensor_sub(
                    resA[:, j, 512 * th:512 * (th + 1)],
                    resA[:, j, 512 * th:512 * (th + 1)], pall[:, 2, :]),
                    wait=[bm], inc="v")
                f2 = p.emit("vector", lambda v, j=j, th=th: v.tensor_mul(
                    resA[:, j, 512 * th:512 * (th + 1)],
                    resA[:, j, 512 * th:512 * (th + 1)], pall[:, 3, :]),
                    wait=[f1, br], inc="v")
                tokj = f2
                if x1b_out is not None:
                    tokj = p.emit("vector", lambda v, j=j, th=th: v.tensor_copy(
                        x1b_out[:, j, 512 * th:512 * (th + 1)],
                        resA[:, j, 512 * th:512 * (th + 1)]),
                        wait=[f2], inc="v")
                apply_toks[(j, th)] = tokj
            bank_rd[2] = f2
            bank_rd[3] = f2
        return apply_toks

    x1_tok = emit_lnB1(pre1_tok, x1b)

    lg_ev = [None] * TH
    for th in range(TH):
        mm = None
        for d in range(16):
            mm = p.emit("tensor", lambda tt, d=d, th=th: tt.matmul(
                pall[:, 4, :], wrb[:, 128 * d:128 * (d + 1)], x1b[:, d, 512 * th:512 * (th + 1)],
                start=(d == 0), stop=(d == 15)),
                wait=[x1_tok[(d, th)], t_rb, bank_rd[4] if d == 0 else None], inc="t")
        lg_ev[th] = p.emit("scalar", lambda s, th=th: s.copy(
            out=lgf[:, 512 * th:512 * (th + 1)], in_=pall[:, 4, :]), wait=[mm], inc="s")
        bank_rd[4] = lg_ev[th]
    cn_done = [None] * (t // 128)
    lgT_rd = [None]
    for tt_ in range(t // 128):
        th = (tt_ * 128) // 512
        tr1 = p.emit("tensor", lambda tt, tt_=tt_: tt.transpose(
            pall[:, 5, 0:128], lgf[:, 128 * tt_:128 * (tt_ + 1)], idb[:]),
            wait=[lg_ev[th], bank_rd[5]], inc="t")
        ev1 = p.emit("vector", lambda v: v.tensor_copy(lgTt[:], pall[:, 5, 0:128]),
                     wait=[tr1, lgT_rd[0]], inc="v")
        bank_rd[5] = ev1
        m0 = p.emit("vector", lambda v: v.reduce_max(vc[:, 0:1], lgTt[:], axis=AX.X, negate=True),
                    wait=[ev1], inc="v")
        ex = p.emit("scalar", lambda s: s.activation(out=Eb[:], in_=lgTt[:], func=AF.Exp,
                                                     bias=vc[:, 0:1]),
                    wait=[m0], inc="s")
        lgT_rd[0] = ex
        sS = p.emit("vector", lambda v: v.reduce_sum(vc[:, 1:2], Eb[:], axis=AX.X), wait=[ex], inc="v")
        vv1 = p.emit("vector", lambda v: v.reduce_max(vc[:, 2:3], Eb[:], axis=AX.X), wait=[ex], inc="v")
        mk1 = p.emit("vector", lambda v: v.tensor_scalar(Mb[:], Eb[:], vc[:, 2:3], None, OP.is_ge),
                     wait=[vv1], inc="v")
        e1 = p.emit("vector", lambda v: v.scalar_tensor_tensor(Ea[:], Mb[:], -1e30, Eb[:], OP.mult, OP.add),
                    wait=[mk1], inc="v")
        vv2 = p.emit("vector", lambda v: v.reduce_max(vc[:, 3:4], Ea[:], axis=AX.X), wait=[e1], inc="v")
        mk2 = p.emit("vector", lambda v: v.tensor_scalar(Mb[:], Ea[:], vc[:, 3:4], None, OP.is_ge),
                     wait=[vv2], inc="v")
        e2 = p.emit("vector", lambda v: v.scalar_tensor_tensor(Ec[:], Mb[:], -1e30, Ea[:], OP.mult, OP.add),
                    wait=[mk2], inc="v")
        vv3 = p.emit("vector", lambda v: v.reduce_max(vc[:, 4:5], Ec[:], axis=AX.X), wait=[e2], inc="v")
        mk3 = p.emit("vector", lambda v: v.tensor_scalar(Mb[:], Ec[:], vc[:, 4:5], None, OP.is_ge),
                     wait=[vv3], inc="v")
        e3 = p.emit("vector", lambda v: v.scalar_tensor_tensor(Ea[:], Mb[:], -1e30, Ec[:], OP.mult, OP.add),
                    wait=[mk3], inc="v")
        vv4 = p.emit("vector", lambda v: v.reduce_max(vc[:, 5:6], Ea[:], axis=AX.X), wait=[e3], inc="v")
        t1 = p.emit("vector", lambda v: v.tensor_add(vc[:, 6:7], vc[:, 2:3], vc[:, 3:4]),
                    wait=[vv2, vv1], inc="v")
        t2 = p.emit("vector", lambda v: v.tensor_add(vc[:, 7:8], vc[:, 6:7], vc[:, 4:5]),
                    wait=[t1, vv3], inc="v")
        t3 = p.emit("vector", lambda v: v.tensor_add(vc[:, 6:7], vc[:, 7:8], vc[:, 5:6]),
                    wait=[t2, vv4], inc="v")
        den = p.emit("vector", lambda v: v.scalar_tensor_tensor(vc[:, 7:8], vc[:, 1:2], 1e-8,
                                                                vc[:, 6:7], OP.mult, OP.add),
                     wait=[t3, sS], inc="v")
        rd = p.emit("vector", lambda v: v.reciprocal(vc[:, 6:7], vc[:, 7:8]), wait=[den], inc="v")
        mk4 = p.emit("vector", lambda v: v.tensor_scalar(Mb[:], Eb[:], vc[:, 5:6], None, OP.is_ge),
                     wait=[vv4], inc="v")
        cm = p.emit("vector", lambda v: v.tensor_mul(cnb[:], Eb[:], Mb[:]), wait=[mk4], inc="v")
        cs_ = p.emit("vector", lambda v: v.tensor_scalar(cnb[:], cnb[:], vc[:, 6:7], None, OP.mult),
                     wait=[cm, rd], inc="v")
        tr2 = p.emit("tensor", lambda tt, tt_=tt_: tt.transpose(
            pall[:, 5, 0:128], cnb[:], idb[:]), wait=[cs_, bank_rd[5]], inc="t")
        ev2 = p.emit("vector", lambda v, tt_=tt_: v.tensor_copy(
            cnf[:, 128 * tt_:128 * (tt_ + 1)], pall[:, 5, 0:128]), wait=[tr2], inc="v")
        bank_rd[5] = ev2
        cn_done[tt_] = ev2
    ef_ev = [None] * TH
    for th in range(TH):
        mm = p.emit("tensor", lambda tt, th=th: tt.matmul(
            pall[0:8, 6, :], mixb[:], cnf[:, 512 * th:512 * (th + 1)], start=True, stop=True),
            wait=[t_mix, bank_rd[6]] + cn_done[4 * th:4 * (th + 1)], inc="t")
        ef_ev[th] = p.emit("vector", lambda v, th=th: v.tensor_copy(
            efb[:, 512 * th:512 * (th + 1)], pall[0:8, 6, :]), wait=[mm], inc="v")
        bank_rd[6] = ef_ev[th]
    eb_tok = {}
    for k8 in range(8):
        for th in range(TH):
            b = 4 + (k8 * TH + th) % 2
            mm = p.emit("tensor", lambda tt, k8=k8, th=th, b=b: tt.matmul(
                pall[:, b, :], ind8b[:, 128 * k8:128 * (k8 + 1)], efb[0:8, 512 * th:512 * (th + 1)],
                start=True, stop=True),
                wait=[ef_ev[th], t_pre_gS, bank_rd[b]], inc="t")
            evx = p.emit("scalar", lambda s, k8=k8, th=th, b=b: s.copy(
                out=ebs[:, k8 * t + 512 * th:k8 * t + 512 * (th + 1)], in_=pall[:, b, :]),
                wait=[mm], inc="s")
            bank_rd[b] = evx
            eb_tok[(k8, th)] = evx

    ldB1, coB1, stB1 = make_streamer("dsA", phaseA_join[0])

    comb_tok = {}
    sgv_rd = {}
    tb_rd = [None, None]
    for k8 in range(8):
        sg_tok = {}
        for qh_ in range(4):
            h, ldg = ldB1(lambda d, k8=k8, qh_=qh_: wgate[128 * d:128 * (d + 1),
                          1024 * k8 + 256 * qh_:1024 * k8 + 256 * (qh_ + 1)], 16)
            mm = None
            for f2 in range(2):
                i = 2 * qh_ + f2
                for th in range(TH):
                    b_ = (i * TH + th) % 4
                    mm = p.emit("tensor", lambda tt, f2=f2, th=th, b_=b_, h=h: [tt.matmul(
                        pall[:, b_, :], wslab[:, 4096 * h + 256 * d + 128 * f2:
                                              4096 * h + 256 * d + 128 * (f2 + 1)],
                        x1b[:, d, 512 * th:512 * (th + 1)],
                        start=(d == 0), stop=(d == 15)) for d in range(16)][-1],
                        wait=[ldg, x1_tok[(15, th)], bank_rd[b_]], inc="t")
                    sg_tok[(i, th)] = p.emit("scalar", lambda s, i=i, th=th, b_=b_: s.activation(
                        out=sgv[:, i * t + 512 * th:i * t + 512 * (th + 1)], in_=pall[:, b_, :],
                        func=AF.Silu),
                        wait=[mm, sgv_rd.get((i, th))], inc="s")
                    bank_rd[b_] = sg_tok[(i, th)]
            coB1(h, mm)
        for qh_ in range(4):
            h, ldu = ldB1(lambda d, k8=k8, qh_=qh_: wup[128 * d:128 * (d + 1),
                          1024 * k8 + 256 * qh_:1024 * k8 + 256 * (qh_ + 1)], 16)
            mm = None
            for f2 in range(2):
                i = 2 * qh_ + f2
                for th in range(TH):
                    b_ = (i * TH + th) % 4
                    mm = p.emit("tensor", lambda tt, f2=f2, th=th, b_=b_, h=h: [tt.matmul(
                        pall[:, b_, :], wslab[:, 4096 * h + 256 * d + 128 * f2:
                                              4096 * h + 256 * d + 128 * (f2 + 1)],
                        x1b[:, d, 512 * th:512 * (th + 1)],
                        start=(d == 0), stop=(d == 15)) for d in range(16)][-1],
                        wait=[ldu, bank_rd[b_]], inc="t")
                    sltb = (i * TH + th) % 2
                    upe = p.emit("vector", lambda v, th=th, b_=b_, k8=k8, sltb=sltb: v.tensor_mul(
                        tb[:, sltb, :], pall[:, b_, :],
                        ebs[:, k8 * t + 512 * th:k8 * t + 512 * (th + 1)]),
                        wait=[mm, eb_tok[(k8, th)], tb_rd[sltb]], inc="v")
                    bank_rd[b_] = upe
                    if k8 == 0:
                        pr = p.emit("vector", lambda v, i=i, th=th, sltb=sltb: v.tensor_mul(
                            comb[:, i * t + 512 * th:i * t + 512 * (th + 1)],
                            sgv[:, i * t + 512 * th:i * t + 512 * (th + 1)], tb[:, sltb, :]),
                            wait=[upe, sg_tok[(i, th)]], inc="v")
                        comb_tok[(i, th)] = pr
                        tb_rd[sltb] = pr
                        sgv_rd[(i, th)] = pr
                    else:
                        pr = p.emit("vector", lambda v, i=i, th=th, sltb=sltb: v.tensor_mul(
                            tb[:, sltb, :], sgv[:, i * t + 512 * th:i * t + 512 * (th + 1)],
                            tb[:, sltb, :]),
                            wait=[upe, sg_tok[(i, th)]], inc="v")
                        sgv_rd[(i, th)] = pr
                        ad = p.emit("vector", lambda v, i=i, th=th, sltb=sltb: v.tensor_add(
                            comb[:, i * t + 512 * th:i * t + 512 * (th + 1)],
                            comb[:, i * t + 512 * th:i * t + 512 * (th + 1)], tb[:, sltb, :]),
                            wait=[pr, comb_tok[(i, th)]], inc="v")
                        comb_tok[(i, th)] = ad
                        tb_rd[sltb] = ad
            coB1(h, mm)

    down_tok = {}
    for fq in range(4):
        h, ldd = ldB1(lambda ic, fq=fq: wdown[128 * ic:128 * (ic + 1),
                      512 * fq:512 * (fq + 1)], 8)
        mm = None
        for f4 in range(4):
            f = 4 * fq + f4
            for th in range(TH):
                b_ = (f4 * TH + th) % 4
                mm = p.emit("tensor", lambda tt, f4=f4, th=th, b_=b_, h=h: [tt.matmul(
                    pall[:, b_, :], wslab[:, 4096 * h + 512 * ic + 128 * f4:
                                          4096 * h + 512 * ic + 128 * (f4 + 1)],
                    comb[:, ic * t + 512 * th:ic * t + 512 * (th + 1)],
                    start=(ic == 0), stop=(ic == 7)) for ic in range(8)][-1],
                    wait=[ldd, bank_rd[b_]] + [comb_tok[(ic, th)] for ic in range(8)], inc="t")
                ra = p.emit("vector", lambda v, f=f, th=th, b_=b_: v.tensor_add(
                    resA[:, f, 512 * th:512 * (th + 1)],
                    resA[:, f, 512 * th:512 * (th + 1)], pall[:, b_, :]),
                    wait=[mm, x1_tok[(f, th)]], inc="v")
                bank_rd[b_] = ra
                down_tok[(f, th)] = ra
        coB1(h, mm)

    sgs_tok = {}
    for oc in range(8):
        h, ldsg = ldB1(lambda d, oc=oc: wsg[128 * d:128 * (d + 1),
                       256 * oc:256 * (oc + 1)], 16)
        mm = None
        for f2 in range(2):
            f = 2 * oc + f2
            for th in range(TH):
                b_ = (f2 * TH + th) % 4
                mm = p.emit("tensor", lambda tt, f2=f2, th=th, b_=b_, h=h: [tt.matmul(
                    pall[:, b_, :], wslab[:, 4096 * h + 256 * d + 128 * f2:
                                          4096 * h + 256 * d + 128 * (f2 + 1)],
                    x1b[:, d, 512 * th:512 * (th + 1)],
                    start=(d == 0), stop=(d == 15)) for d in range(16)][-1],
                    wait=[ldsg, bank_rd[b_]], inc="t")
                sgs_tok[(f, th)] = p.emit("scalar", lambda s, f=f, th=th, b_=b_: s.activation(
                    out=sgs[:, f * t + 512 * th:f * t + 512 * (th + 1)], in_=pall[:, b_, :],
                    func=AF.Silu),
                    wait=[mm, sgv_rd.get((f, th)) if f < 8 else None,
                          eb_tok.get((f - 8, th)) if f >= 8 else None], inc="s")
                bank_rd[b_] = sgs_tok[(f, th)]
        coB1(h, mm)
    for oc in range(8):
        h, ldsu = ldB1(lambda d, oc=oc: wsu[128 * d:128 * (d + 1),
                       256 * oc:256 * (oc + 1)], 16)
        mm = None
        for f2 in range(2):
            f = 2 * oc + f2
            for th in range(TH):
                b_ = (f2 * TH + th) % 4
                mm = p.emit("tensor", lambda tt, f2=f2, th=th, b_=b_, h=h: [tt.matmul(
                    pall[:, b_, :], wslab[:, 4096 * h + 256 * d + 128 * f2:
                                          4096 * h + 256 * d + 128 * (f2 + 1)],
                    x1b[:, d, 512 * th:512 * (th + 1)],
                    start=(d == 0), stop=(d == 15)) for d in range(16)][-1],
                    wait=[ldsu, bank_rd[b_]], inc="t")
                up2 = p.emit("vector", lambda v, f=f, th=th, b_=b_: v.tensor_mul(
                    sgs[:, f * t + 512 * th:f * t + 512 * (th + 1)],
                    sgs[:, f * t + 512 * th:f * t + 512 * (th + 1)], pall[:, b_, :]),
                    wait=[mm, sgs_tok[(f, th)]], inc="v")
                bank_rd[b_] = up2
                sgs_tok[(f, th)] = up2
        coB1(h, up2)
    for oc in range(8):
        h, ldsd = ldB1(lambda d, oc=oc: wsd[128 * d:128 * (d + 1),
                       256 * oc:256 * (oc + 1)], 16)
        mm = None
        for f2 in range(2):
            f = 2 * oc + f2
            for th in range(TH):
                b_ = (f2 * TH + th) % 4
                mm = p.emit("tensor", lambda tt, f2=f2, th=th, b_=b_, h=h: [tt.matmul(
                    pall[:, b_, :], wslab[:, 4096 * h + 256 * d + 128 * f2:
                                          4096 * h + 256 * d + 128 * (f2 + 1)],
                    sgs[:, d * t + 512 * th:d * t + 512 * (th + 1)],
                    start=(d == 0), stop=(d == 15)) for d in range(16)][-1],
                    wait=[ldsd, bank_rd[b_]] + [sgs_tok[(d_, th)] for d_ in range(16)], inc="t")
                ra = p.emit("vector", lambda v, f=f, th=th, b_=b_: v.tensor_add(
                    resA[:, f, 512 * th:512 * (th + 1)],
                    resA[:, f, 512 * th:512 * (th + 1)], pall[:, b_, :]),
                    wait=[mm, down_tok[(f, th)]], inc="v")
                bank_rd[b_] = ra
                down_tok[(f, th)] = ra
        coB1(h, mm)

    x2_tok = emit_lnB1(down_tok, None)
    ec_rd = [None, None]
    st_e = None
    for j in range(16):
        sl = j % 2
        e1_ = p.emit("vector", lambda v, j=j, sl=sl: v.tensor_copy(
            tb[:, sl, 0:64], resA[:, j, 0:64]),
            wait=[x2_tok[(j, 0)], ec_rd[sl]], inc="v")
        e2_ = p.emit("vector", lambda v, j=j, sl=sl: v.tensor_copy(
            tb[:, sl, 64:128], resA[:, j, t - 64:t]),
            wait=[x2_tok[(j, TH - 1)], e1_], inc="v")
        st_e = p.emit("sync", lambda s, j=j, sl=sl: s.dma_start(
            out=edgebuf.ap()[:, 128 * j:128 * (j + 1)], in_=tb[:, sl, 0:128]),
            wait=[e2_], inc="dse%d" % sl, inc_amt=16)
        ec_rd[sl] = st_e
    t_ag = p.emit("gpsimd", lambda g: g.collective_compute(
        "AllGather", OP.bypass, rg, [edgebuf.ap().opt()], [edgeall.ap().opt()]),
        wait=[("dse0", p.cnt.get("dse0", 0)), ("dse1", p.cnt.get("dse1", 0)), ("cc", 1)],
        inc="cc")
    phaseB1_join = [("t", p.cnt.get("t", 0)), ("v", p.cnt.get("v", 0)),
                    ("s", p.cnt.get("s", 0)), ("dsl", p.cnt.get("dsl", 0))]
    esB1.close()
    esB2 = contextlib.ExitStack()
    SB2 = lambda nm, sh, dt: esB2.enter_context(nc.sbuf_tensor(nm, sh, dt))
    XR = SB2("XR", [128, 16 * th_], BF16)
    ob = SB2("ob", [128, 16 * t], BF16)
    kbuf = SB2("kbuf", [128, 4 * th_], BF16)
    krob = SB2("krob", [128, 4 * th_], BF16)
    vbuf = SB2("vbuf", [128, 4 * th_], BF16)
    qhb = SB2("qhb", [128, t], BF16)
    qrob = SB2("qrob", [128, t], BF16)
    cs1b = SB2("cs1b", [128, th_], BF16)
    cs2b = SB2("cs2b", [128, th_], BF16)
    scr = SB2("scr", [128, 2, 512], F32)
    srowB = SB2("srowB", [1, 3, 512], F32)
    sqB = SB2("sqB", [128, 512], F32)
    attb = SB2("attb", [128, 256], BF16)
    attTb = SB2("attTb", [128, 256], BF16)
    g8 = SB2("g8", [128, 8], F32)
    gE = SB2("gE", [128, 8], F32)
    gE1 = SB2("gE1", [128, 8], F32)
    gM = SB2("gM", [128, 8], F32)
    rs = SB2("rs", [128, 4], F32)
    konst = SB2("konst", [128, 1], F32)
    pall7 = esB2.enter_context(nc.psum_tensor("pall7", [128, 7, 512], F32))
    pbt = esB2.enter_context(nc.psum_tensor("pbt", [128, 1024], BF16))
    pall = pall7
    x2hb = XR[:].rearrange("p (a b) -> p a b", a=16)
    lnreg = XR[:, 0:16 * t].rearrange("p (a b) -> p a b", a=16)
    gelu = ob[:].rearrange("p (a b) -> p a b", a=16)
    bank_rd = [None] * 8

    p.emit("gpsimd", lambda g: g.dma_start(out=cs1b[:], in_=cs1),
           wait=phaseB1_join, inc="dw", inc_amt=16)
    t_cs = p.emit("gpsimd", lambda g: g.dma_start(out=cs2b[:], in_=cs2), inc="dw", inc_amt=16)
    t_k0 = p.emit("vector", lambda v: v.memset(konst[:], 0.08838834764831845),
                  wait=[phaseB1_join[1]], inc="v")
    mmsk = p.emit("tensor", lambda tt: tt.matmul(
        pall[:, 6, 0:16], onesb[0:1, :], sinkb[:], start=True, stop=True),
        wait=phaseB1_join[:1], inc="t")
    t_sink = p.emit("vector", lambda v: v.tensor_copy(sinkbc[:], pall[:, 6, 0:16]),
                    wait=[mmsk], inc="v")
    bank_rd[6] = t_sink

    cast_tok = {}
    for j in range(16):
        cast_tok[j] = p.emit("vector", lambda v, j=j: v.tensor_copy(
            x2hb[:, j, HALO:HALO + t], resA[:, j, :]),
            wait=[x2_tok[(j, 0)], x2_tok[(j, TH - 1)], phaseB1_join[1]], inc="v")
    msL = p.emit("vector", lambda v: v.memset(x2hb[:, :, 0:HALO], 0.0),
                 wait=[phaseB1_join[1]], inc="v")
    msR = p.emit("vector", lambda v: v.memset(x2hb[:, :, t + HALO:t + 2 * HALO], 0.0),
                 wait=[phaseB1_join[1]], inc="v")
    selL = msL
    selR = msR
    for grp in range(2):
        eld = p.emit("gpsimd", lambda g, grp=grp: g.dma_start(
            out=wslab[:].rearrange("p (r x) -> p r x", r=4),
            in_=edgeall.ap().rearrange("(r p) x -> p r x", p=128)[:, 4 * grp:4 * (grp + 1), :]),
            wait=[("cc", 2), selL if grp == 1 else None, selR if grp == 1 else None]
                 + (phaseB1_join if grp == 0 else []),
            inc="dw", inc_amt=16)
        for rr in range(4):
            r_ = 4 * grp + rr
            selL = p.emit("vector", lambda v, rr=rr, r_=r_: v.scalar_tensor_tensor(
                x2hb[:, :, 0:HALO],
                wslab[:].rearrange("p (r j c) -> p r j c", r=4, j=16)[:, rr, :, 64:128],
                hwLb[:, r_:r_ + 1], x2hb[:, :, 0:HALO], OP.mult, OP.add),
                wait=[eld, selL, t_pre_s], inc="v")
            selR = p.emit("vector", lambda v, rr=rr, r_=r_: v.scalar_tensor_tensor(
                x2hb[:, :, t + HALO:t + 2 * HALO],
                wslab[:].rearrange("p (r j c) -> p r j c", r=4, j=16)[:, rr, :, 0:64],
                hwRb[:, r_:r_ + 1], x2hb[:, :, t + HALO:t + 2 * HALO], OP.mult, OP.add),
                wait=[eld, selR, t_pre_s], inc="v")
    x2h_done = [selL, selR] + list(cast_tok.values())

    ldB2, coB2, stB2 = make_streamer("dsB", tmax(selL, selR))
    TC = [(0, 512), (512, 1024), (1024, 1152)]
    kv_ev = {}
    for which, (wsrc, dst) in enumerate([(wkp, kbuf), (wv, vbuf)]):
        for hp2 in range(2):
            h, ld = ldB2(lambda d, wsrc=wsrc, hp2=hp2: wsrc[128 * d:128 * (d + 1),
                         256 * hp2:256 * (hp2 + 1)], 16)
            mm = None
            for g2 in range(2):
                g4 = 2 * hp2 + g2
                for ci, (c0, c1) in enumerate(TC):
                    b_ = (g2 + ci) % 4
                    mm = p.emit("tensor", lambda tt, g2=g2, c0=c0, c1=c1, b_=b_, h=h: [tt.matmul(
                        pall[:, b_, 0:c1 - c0],
                        wslab[:, 4096 * h + 256 * d + 128 * g2:4096 * h + 256 * d + 128 * (g2 + 1)],
                        x2hb[:, d, c0:c1], start=(d == 0), stop=(d == 15)) for d in range(16)][-1],
                        wait=[ld, bank_rd[b_]] + x2h_done, inc="t")
                    evk = p.emit("scalar", lambda s, dst=dst, g4=g4, c0=c0, c1=c1, b_=b_: s.copy(
                        out=dst[:, g4 * th_ + c0:g4 * th_ + c1], in_=pall[:, b_, 0:c1 - c0]),
                        wait=[mm], inc="s")
                    bank_rd[b_] = evk
                    kv_ev[(which, g4, ci)] = evk
            coB2(h, mm)
    kro_tok = {}
    scr_rd = [None, None]
    for g4 in range(4):
        for ci, (c0, c1) in enumerate(TC):
            b_ = 4 + (g4 + ci) % 2
            mm = p.emit("tensor", lambda tt, g4=g4, c0=c0, c1=c1, b_=b_: tt.matmul(
                pall[:, b_, 0:c1 - c0], permB[:], kbuf[:, g4 * th_ + c0:g4 * th_ + c1],
                start=True, stop=True),
                wait=[kv_ev[(0, g4, ci)], t_pre_gS, bank_rd[b_]], inc="t")
            t1 = p.emit("vector", lambda v, g4=g4, c0=c0, c1=c1: v.tensor_mul(
                krob[:, g4 * th_ + c0:g4 * th_ + c1], kbuf[:, g4 * th_ + c0:g4 * th_ + c1],
                cs1b[:, c0:c1]),
                wait=[kv_ev[(0, g4, ci)], t_cs], inc="v")
            t2 = p.emit("vector", lambda v, c0=c0, c1=c1, b_=b_, sl=ci % 2: v.tensor_mul(
                scr[:, sl, 0:c1 - c0], pall[:, b_, 0:c1 - c0], cs2b[:, c0:c1]),
                wait=[mm, t_cs, scr_rd[ci % 2]], inc="v")
            bank_rd[b_] = t2
            t3 = p.emit("vector", lambda v, g4=g4, c0=c0, c1=c1, sl=ci % 2: v.tensor_add(
                krob[:, g4 * th_ + c0:g4 * th_ + c1], krob[:, g4 * th_ + c0:g4 * th_ + c1],
                scr[:, sl, 0:c1 - c0]),
                wait=[t1, t2], inc="v")
            scr_rd[ci % 2] = t3
            kro_tok[(g4, ci)] = t3
    vT_tok = {}
    for g4 in range(4):
        for kt in range(9):
            mm = p.emit("tensor", lambda tt, g4=g4, kt=kt: tt.transpose(
                pbt[:, 0:128] if kt % 2 == 0 else pbt[:, 128:256],
                vbuf[:, g4 * th_ + 128 * kt:g4 * th_ + 128 * (kt + 1)], idbB[:]),
                wait=[kv_ev[(1, g4, min(kt // 4, 2))], kv_ev[(1, g4, 2)],
                      bank_rd[7], kro_tok[(g4, 2)] if kt == 0 else None], inc="t")
            evv = p.emit("vector", lambda v, g4=g4, kt=kt: v.tensor_copy(
                kbuf[:, g4 * th_ + 128 * kt:g4 * th_ + 128 * (kt + 1)],
                pbt[:, 0:128] if kt % 2 == 0 else pbt[:, 128:256]),
                wait=[mm], inc="v")
            bank_rd[7] = evv
            vT_tok[(g4, kt)] = evv
    vT = kbuf

    o_tok = {}
    att_rd = [None]
    attT_rd = [None]
    qh_rd = [None]
    qro_rd = [None]
    ldq = None
    hq_ = 0
    qev = None
    for h_att in range(16):
        g4 = h_att // 4
        if h_att % 2 == 0:
            hq_, ldq = ldB2(lambda d, h_att=h_att: wqp[128 * d:128 * (d + 1),
                            256 * (h_att // 2):256 * (h_att // 2 + 1)], 16)
        hh = h_att % 2
        for thc in range(TH):
            b_ = thc % 2
            mm = p.emit("tensor", lambda tt, hh=hh, thc=thc, b_=b_, hq_=hq_: [tt.matmul(
                pall[:, b_, :], wslab[:, 4096 * hq_ + 256 * d + 128 * hh:
                                      4096 * hq_ + 256 * d + 128 * (hh + 1)],
                x2hb[:, d, HALO + 512 * thc:HALO + 512 * (thc + 1)],
                start=(d == 0), stop=(d == 15)) for d in range(16)][-1],
                wait=[ldq, bank_rd[b_]] + x2h_done, inc="t")
            if h_att % 2 == 1:
                coB2(hq_, mm)
            evq = p.emit("scalar", lambda s, thc=thc, b_=b_: s.copy(
                out=qhb[:, 512 * thc:512 * (thc + 1)], in_=pall[:, b_, :]),
                wait=[mm, qh_rd[0]], inc="s")
            bank_rd[b_] = evq
            if thc == TH - 1:
                qev = evq
        rope_toks = []
        for c in range(TH):
            b_ = 2 + c % 2
            mm = p.emit("tensor", lambda tt, c=c, b_=b_: tt.matmul(
                pall[:, b_, :], permB[:], qhb[:, 512 * c:512 * (c + 1)], start=True, stop=True),
                wait=[qev, bank_rd[b_]], inc="t")
            t1 = p.emit("vector", lambda v, c=c: v.tensor_mul(
                qrob[:, 512 * c:512 * (c + 1)], qhb[:, 512 * c:512 * (c + 1)],
                cs1b[:, HALO + 512 * c:HALO + 512 * (c + 1)]),
                wait=[qev, qro_rd[0], t_cs], inc="v")
            sl = c % 2
            t2 = p.emit("vector", lambda v, c=c, b_=b_, sl=sl: v.tensor_mul(
                scr[:, sl, 0:512], pall[:, b_, :], cs2b[:, HALO + 512 * c:HALO + 512 * (c + 1)]),
                wait=[mm, scr_rd[sl]], inc="v")
            bank_rd[b_] = t2
            t3 = p.emit("vector", lambda v, c=c, sl=sl: v.tensor_add(
                qrob[:, 512 * c:512 * (c + 1)], qrob[:, 512 * c:512 * (c + 1)], scr[:, sl, 0:512]),
                wait=[t1, t2], inc="v")
            scr_rd[sl] = t3
            rope_toks.append(t3)
        qh_rd[0] = rope_toks[-1]
        for i in range(8):
            mi = 0 if i == 0 else (2 if i == 7 else 1)
            b_ = 4 + i % 2
            sc_ = p.emit("tensor", lambda tt, i=i, g4=g4, b_=b_: tt.matmul(
                pall[:, b_, 0:256], qrob[:, 128 * i:128 * (i + 1)],
                krob[:, g4 * th_ + 128 * i:g4 * th_ + 128 * i + 256],
                start=True, stop=True),
                wait=[rope_toks[(128 * i) // 512], rope_toks[(128 * i + 127) // 512],
                      kro_tok[(g4, 2)], bank_rd[b_]], inc="t")
            m0 = p.emit("vector", lambda v, b_=b_: v.reduce_max(rs[:, 0:1], pall[:, b_, 0:256],
                                                                axis=AX.X, negate=True),
                        wait=[sc_], inc="v")
            bb_ = p.emit("vector", lambda v, h_att=h_att: v.scalar_tensor_tensor(
                rs[:, 1:2], rs[:, 0:1], 0.08838834764831845, sinkbc[:, h_att:h_att + 1],
                OP.mult, OP.add),
                wait=[m0, t_sink], inc="v")
            ex = p.emit("scalar", lambda s, b_=b_: s.activation(
                out=scr[:, 0, 0:256], in_=pall[:, b_, 0:256], func=AF.Exp,
                scale=konst[:], bias=rs[:, 1:2]),
                wait=[bb_, t_k0, scr_rd[0]], inc="s")
            bank_rd[b_] = ex
            mk = p.emit("vector", lambda v, mi=mi: v.tensor_mul(
                scr[:, 1, 0:256], scr[:, 0, 0:256], maskb[:, mi, :]),
                wait=[ex, scr_rd[1], t_pre_gS], inc="v")
            scr_rd[0] = mk
            sm = p.emit("vector", lambda v: v.reduce_sum(rs[:, 2:3], scr[:, 1, 0:256], axis=AX.X),
                        wait=[mk], inc="v")
            rc = p.emit("vector", lambda v: v.reciprocal(rs[:, 3:4], rs[:, 2:3]), wait=[sm], inc="v")
            at = p.emit("vector", lambda v: v.tensor_scalar(
                attb[:], scr[:, 1, 0:256], rs[:, 3:4], None, OP.mult),
                wait=[rc, att_rd[0]], inc="v")
            scr_rd[1] = at
            ev2 = None
            for half in range(2):
                tr = p.emit("tensor", lambda tt, half=half: tt.transpose(
                    pbt[:, 128 * half:128 * (half + 1)], attb[:, 128 * half:128 * (half + 1)],
                    idbB[:]),
                    wait=[at, bank_rd[7]], inc="t")
                ev2 = p.emit("vector", lambda v, half=half: v.tensor_copy(
                    attTb[:, 128 * half:128 * (half + 1)], pbt[:, 128 * half:128 * (half + 1)]),
                    wait=[tr, attT_rd[0] if half == 0 else None], inc="v")
                bank_rd[7] = ev2
            att_rd[0] = ev2
            av = p.emit("tensor", lambda tt, g4=g4, i=i: [tt.matmul(
                pall[:, 6, 0:128], vT[:, g4 * th_ + 128 * (i + kk):g4 * th_ + 128 * (i + kk + 1)],
                attTb[:, 128 * kk:128 * (kk + 1)], start=(kk == 0), stop=(kk == 1))
                for kk in range(2)][-1],
                wait=[ev2, vT_tok[(g4, i)], vT_tok[(g4, i + 1)], bank_rd[6]], inc="t")
            attT_rd[0] = av
            oe = p.emit("scalar", lambda s, h_att=h_att, i=i: s.copy(
                out=ob[:, h_att * t + 128 * i:h_att * t + 128 * (i + 1)], in_=pall[:, 6, 0:128]),
                wait=[av, o_tok.get((h_att, i))], inc="s")
            bank_rd[6] = oe
            o_tok[(h_att, i)] = oe
        qro_rd[0] = sc_

    pre3_tok = {}
    for oc in range(8):
        h, ldo = ldB2(lambda d, oc=oc: wo[128 * d:128 * (d + 1), 256 * oc:256 * (oc + 1)], 16)
        mm = None
        for f2 in range(2):
            f = 2 * oc + f2
            for thc in range(TH):
                b_ = (f2 * TH + thc) % 4
                mm = p.emit("tensor", lambda tt, f2=f2, thc=thc, b_=b_, h=h: [tt.matmul(
                    pall[:, b_, :], wslab[:, 4096 * h + 256 * d + 128 * f2:
                                          4096 * h + 256 * d + 128 * (f2 + 1)],
                    ob[:, d * t + 512 * thc:d * t + 512 * (thc + 1)],
                    start=(d == 0), stop=(d == 15)) for d in range(16)][-1],
                    wait=[ldo, bank_rd[b_], o_tok[(15, 7)]], inc="t")
                tk = p.emit("vector", lambda v, f=f, thc=thc, b_=b_: v.tensor_add(
                    resA[:, f, 512 * thc:512 * (thc + 1)],
                    resA[:, f, 512 * thc:512 * (thc + 1)], pall[:, b_, :]),
                    wait=[mm, cast_tok[f]], inc="v")
                bank_rd[b_] = tk
                pre3_tok[(f, thc)] = tk
        coB2(h, mm)
    srow_rdB = [None]
    def emit_lnB2(src_ready, dst, gamma_beta=None, src_bf=None):
        apply_toks = {}
        sq_rd = [None]
        for thc in range(TH):
            mm1 = mm2 = None
            for j in range(16):
                srcap = (resA[:, j, 512 * thc:512 * (thc + 1)] if src_bf is None
                         else src_bf[:, j, 512 * thc:512 * (thc + 1)])
                sqt = p.emit("scalar", lambda s, srcap=srcap: s.activation(
                    out=sqB[:], in_=srcap, func=AF.Square),
                    wait=[src_ready[(j, thc)], sq_rd[0]], inc="s")
                mm1 = p.emit("tensor", lambda tt, srcap=srcap, j=j: tt.matmul(
                    pall[0:1, 0, :], onesb[:, 0:1] if src_bf is None else onesbB[:, 0:1],
                    srcap, start=(j == 0), stop=(j == 15)),
                    wait=[src_ready[(j, thc)], bank_rd[0] if j == 0 else None], inc="t")
                mm2 = p.emit("tensor", lambda tt, j=j: tt.matmul(
                    pall[0:1, 1, :], onesb[:, 0:1], sqB[:], start=(j == 0), stop=(j == 15)),
                    wait=[sqt, bank_rd[1] if j == 0 else None], inc="t")
                sq_rd[0] = mm2
            a_ = p.emit("vector", lambda v: v.tensor_scalar(
                srowB[0:1, 0, :], pall[0:1, 0, :], 1.0 / 2048.0, None, OP.mult),
                wait=[mm1, srow_rdB[0]], inc="v")
            bank_rd[0] = a_
            b2_ = p.emit("vector", lambda v: v.tensor_mul(
                srowB[0:1, 2, :], srowB[0:1, 0, :], srowB[0:1, 0, :]), wait=[a_], inc="v")
            c_ = p.emit("vector", lambda v: v.scalar_tensor_tensor(
                srowB[0:1, 2, :], pall[0:1, 1, :], 1.0 / 2048.0, srowB[0:1, 2, :],
                OP.mult, OP.subtract),
                wait=[mm2, b2_], inc="v")
            bank_rd[1] = c_
            c2_ = p.emit("vector", lambda v: v.tensor_scalar_add(
                srowB[0:1, 2, :], srowB[0:1, 2, :], 1e-5), wait=[c_], inc="v")
            d_ = p.emit("scalar", lambda s: s.activation(
                out=srowB[0:1, 1, :], in_=srowB[0:1, 2, :], func=AF.Sqrt), wait=[c2_], inc="s")
            e_ = p.emit("vector", lambda v: v.reciprocal(srowB[0:1, 2, :], srowB[0:1, 1, :]),
                        wait=[d_], inc="v")
            bm = p.emit("tensor", lambda tt: tt.matmul(
                pall[:, 2, :], onesb[0:1, :], srowB[0:1, 0, :], start=True, stop=True),
                wait=[a_, bank_rd[2]], inc="t")
            br = p.emit("tensor", lambda tt: tt.matmul(
                pall[:, 3, :], onesb[0:1, :], srowB[0:1, 2, :], start=True, stop=True),
                wait=[e_, bank_rd[3]], inc="t")
            srow_rdB[0] = br
            f2 = None
            for j in range(16):
                srcap = (resA[:, j, 512 * thc:512 * (thc + 1)] if src_bf is None
                         else src_bf[:, j, 512 * thc:512 * (thc + 1)])
                f1 = p.emit("vector", lambda v, srcap=srcap: v.tensor_sub(
                    scr[:, 0, 0:512], srcap, pall[:, 2, :]),
                    wait=[bm, scr_rd[0], src_ready[(j, thc)]], inc="v")
                if gamma_beta is None and dst is None:
                    f2 = p.emit("vector", lambda v, j=j, thc=thc: v.tensor_mul(
                        resA[:, j, 512 * thc:512 * (thc + 1)], scr[:, 0, 0:512], pall[:, 3, :]),
                        wait=[f1, br], inc="v")
                    tokj = f2
                else:
                    f2 = p.emit("vector", lambda v: v.tensor_mul(
                        scr[:, 1, 0:512], scr[:, 0, 0:512], pall[:, 3, :]),
                        wait=[f1, br, scr_rd[1]], inc="v")
                    gcol, bcol = gamma_beta
                    tokj = p.emit("vector", lambda v, j=j, thc=thc, gcol=gcol, bcol=bcol: v.tensor_scalar(
                        dst[:, j, 512 * thc:512 * (thc + 1)], scr[:, 1, 0:512],
                        lnpb[:, gcol * 16 + j:gcol * 16 + j + 1],
                        lnpb[:, bcol * 16 + j:bcol * 16 + j + 1],
                        OP.mult, OP.add),
                        wait=[f2], inc="v")
                    scr_rd[1] = tokj
                scr_rd[0] = f2
                apply_toks[(j, thc)] = tokj
            bank_rd[2] = f2
            bank_rd[3] = f2
        return apply_toks

    x3_tok = emit_lnB2(pre3_tok, None)
    ln1_tok = emit_lnB2(x3_tok, lnreg, gamma_beta=(0, 1))

    av2_tok = {}
    for oc in range(8):
        h, ldv = ldB2(lambda d, oc=oc: wvselT[128 * d:128 * (d + 1), 256 * oc:256 * (oc + 1)], 16)
        mm = None
        for f2 in range(2):
            f = 2 * oc + f2
            for thc in range(TH):
                b_ = (f2 * TH + thc) % 4
                mm = p.emit("tensor", lambda tt, f2=f2, thc=thc, b_=b_, h=h: [tt.matmul(
                    pall[:, b_, :], wslab[:, 4096 * h + 256 * d + 128 * f2:
                                          4096 * h + 256 * d + 128 * (f2 + 1)],
                    lnreg[:, d, 512 * thc:512 * (thc + 1)],
                    start=(d == 0), stop=(d == 15)) for d in range(16)][-1],
                    wait=[ldv, bank_rd[b_]] + [ln1_tok[(d, thc)] for d in range(16)], inc="t")
                evv2 = p.emit("scalar", lambda s, f=f, thc=thc, b_=b_: s.copy(
                    out=ob[:, f * t + 512 * thc:f * t + 512 * (thc + 1)], in_=pall[:, b_, :]),
                    wait=[mm, o_tok[(15, 7)] if f == 0 and thc == 0 else None], inc="s")
                bank_rd[b_] = evv2
                av2_tok[(f, thc)] = evv2
        coB2(h, mm)
    had_tok = {}
    pre_tok = {}
    for oc in range(8):
        h, ldp = ldB2(lambda d, oc=oc: projT[128 * d:128 * (d + 1), 256 * oc:256 * (oc + 1)], 16)
        mm = None
        for f2 in range(2):
            f = 2 * oc + f2
            for thc in range(TH):
                b_ = (f2 * TH + thc) % 4
                mm = p.emit("tensor", lambda tt, f2=f2, thc=thc, b_=b_, h=h: [tt.matmul(
                    pall[:, b_, :], wslab[:, 4096 * h + 256 * d + 128 * f2:
                                          4096 * h + 256 * d + 128 * (f2 + 1)],
                    ob[:, d * t + 512 * thc:d * t + 512 * (thc + 1)],
                    start=(d == 0), stop=(d == 15)) for d in range(16)][-1],
                    wait=[ldp, bank_rd[b_]] + [av2_tok[(d, thc)] for d in range(16)], inc="t")
                hd = p.emit("vector", lambda v, f=f, thc=thc, b_=b_: v.tensor_add(
                    lnreg[:, f, 512 * thc:512 * (thc + 1)],
                    resA[:, f, 512 * thc:512 * (thc + 1)], pall[:, b_, :]),
                    wait=[mm, ln1_tok[(f, thc)], x3_tok[(f, thc)]], inc="v")
                pr = p.emit("vector", lambda v, f=f, thc=thc, b_=b_: v.scalar_tensor_tensor(
                    resA[:, f, 512 * thc:512 * (thc + 1)],
                    resA[:, f, 512 * thc:512 * (thc + 1)], 2.0, pall[:, b_, :],
                    OP.mult, OP.add),
                    wait=[hd], inc="v")
                bank_rd[b_] = pr
                had_tok[(f, thc)] = hd
                pre_tok[(f, thc)] = pr
        coB2(h, pr)

    ln2_tok = emit_lnB2(had_tok, lnreg, gamma_beta=(2, 3), src_bf=lnreg)

    lgG = wslab[:, 4096:6144].bitcast(F32)[0:8, :]
    gTb = wslab[0:8, 6144:7168]
    ldrr = p.emit("gpsimd", lambda g: g.dma_start(
        out=wslab[:, 0:128], in_=wre), wait=[stB2["cons"][0]], inc="dsl", inc_amt=16)
    lg_ev = [None] * TH
    for thc in range(TH):
        mm = None
        for d in range(16):
            mm = p.emit("tensor", lambda tt, d=d, thc=thc: tt.matmul(
                pall[0:8, 4, :], wslab[:, 8 * d:8 * (d + 1)],
                lnreg[:, d, 512 * thc:512 * (thc + 1)],
                start=(d == 0), stop=(d == 15)),
                wait=[ldrr, ln2_tok[(d, thc)], bank_rd[4] if d == 0 else None], inc="t")
        lg_ev[thc] = p.emit("vector", lambda v, thc=thc: v.tensor_copy(
            lgG[:, 512 * thc:512 * (thc + 1)], pall[0:8, 4, :]),
            wait=[mm, stB2["cons"][1]], inc="v")
        bank_rd[4] = lg_ev[thc]
    gT_done = []
    g8_rd = [None]
    for tt_ in range(t // 128):
        thc = (tt_ * 128) // 512
        tr = p.emit("tensor", lambda tt, tt_=tt_: tt.transpose(
            pall[:, 5, 0:8], lgG[0:8, 128 * tt_:128 * (tt_ + 1)], idb[0:8, 0:8]),
            wait=[lg_ev[thc], bank_rd[5]], inc="t")
        evg = p.emit("vector", lambda v: v.tensor_copy(g8[:], pall[:, 5, 0:8]),
                     wait=[tr, g8_rd[0]], inc="v")
        bank_rd[5] = evg
        m0 = p.emit("vector", lambda v: v.reduce_max(rs[:, 0:1], g8[:], axis=AX.X, negate=True),
                    wait=[evg], inc="v")
        exg = p.emit("scalar", lambda s: s.activation(out=gE[:], in_=g8[:], func=AF.Exp,
                                                      bias=rs[:, 0:1]),
                     wait=[m0], inc="s")
        g8_rd[0] = exg
        vv1 = p.emit("vector", lambda v: v.reduce_max(rs[:, 1:2], gE[:], axis=AX.X),
                     wait=[exg], inc="v")
        mk1 = p.emit("vector", lambda v: v.tensor_scalar(gM[:], gE[:], rs[:, 1:2], None, OP.is_ge),
                     wait=[vv1], inc="v")
        e1 = p.emit("vector", lambda v: v.scalar_tensor_tensor(gE1[:], gM[:], -1e30, gE[:],
                                                               OP.mult, OP.add),
                    wait=[mk1], inc="v")
        vv2 = p.emit("vector", lambda v: v.reduce_max(rs[:, 2:3], gE1[:], axis=AX.X),
                     wait=[e1], inc="v")
        dn = p.emit("vector", lambda v: v.tensor_add(rs[:, 3:4], rs[:, 1:2], rs[:, 2:3]),
                    wait=[vv2], inc="v")
        rc = p.emit("vector", lambda v: v.reciprocal(rs[:, 3:4], rs[:, 3:4]), wait=[dn], inc="v")
        mk2 = p.emit("vector", lambda v: v.tensor_scalar(gM[:], gE[:], rs[:, 2:3], None, OP.is_ge),
                     wait=[vv2, e1], inc="v")
        gm_ = p.emit("vector", lambda v: v.tensor_mul(gE[:], gE[:], gM[:]), wait=[mk2], inc="v")
        gs_ = p.emit("vector", lambda v: v.tensor_scalar(gE[:], gE[:], rs[:, 3:4], None, OP.mult),
                     wait=[gm_, rc], inc="v")
        tr2 = p.emit("tensor", lambda tt: tt.transpose(
            pall[0:8, 5, 0:128], gE[:], idb[:]),
            wait=[gs_, bank_rd[5]], inc="t")
        evg2 = p.emit("vector", lambda v, tt_=tt_: v.tensor_copy(
            gTb[:, 128 * tt_:128 * (tt_ + 1)], pall[0:8, 5, 0:128]),
            wait=[tr2, stB2["cons"][1] if tt_ == 0 else None], inc="v")
        bank_rd[5] = evg2
        g8_rd[0] = tr2
        gT_done.append(evg2)
    def gbv(e):
        if e < 4:
            return krob[:, e * t:(e + 1) * t]
        return vbuf[:, (e - 4) * t:(e - 3) * t]
    gb_tok = {}
    gb_last_mm = None
    for e in range(8):
        for thc in range(TH):
            b_ = 4 + (e * TH + thc) % 2
            mm = p.emit("tensor", lambda tt, e=e, thc=thc, b_=b_: tt.matmul(
                pall[:, b_, :], ind8b[:, 128 * e:128 * (e + 1)],
                gTb[:, 512 * thc:512 * (thc + 1)], start=True, stop=True),
                wait=[bank_rd[b_], t_pre_gS] + gT_done[4 * thc:4 * (thc + 1)], inc="t")
            evb = p.emit("scalar", lambda s, e=e, thc=thc, b_=b_: s.copy(
                out=gbv(e)[:, 512 * thc:512 * (thc + 1)], in_=pall[:, b_, :]),
                wait=[mm, kro_tok.get((3, 2)) if e == 0 and thc == 0 else None,
                      vT_tok[(3, 8)] if e == 4 and thc == 0 else None], inc="s")
            bank_rd[b_] = evb
            gb_tok[(e, thc)] = evb
            gb_last_mm = mm
    stB2["cons"][0] = gb_last_mm
    stB2["cons"][1] = gb_last_mm

    gelu_tok = {}
    for e in range(8):
        for half in range(2):
            for sx in range(8):
                fbase = 2048 * half + 256 * sx
                h, ldw1 = ldB2(lambda d, e=e, fbase=fbase: mw1[e, 128 * d:128 * (d + 1),
                                                              fbase:fbase + 256], 16)
                mm = None
                for f2 in range(2):
                    lc = 2 * sx + f2
                    for thc in range(TH):
                        b_ = (f2 * TH + thc) % 4
                        mm = p.emit("tensor", lambda tt, f2=f2, thc=thc, b_=b_, h=h: [tt.matmul(
                            pall[:, b_, :], wslab[:, 4096 * h + 256 * d + 128 * f2:
                                                  4096 * h + 256 * d + 128 * (f2 + 1)],
                            lnreg[:, d, 512 * thc:512 * (thc + 1)],
                            start=(d == 0), stop=(d == 15)) for d in range(16)][-1],
                            wait=[ldw1, bank_rd[b_]] +
                                 ([ln2_tok[(d, thc)] for d in range(16)] if e == 0 else []),
                            inc="t")
                        ge_ = p.emit("scalar", lambda s, lc=lc, thc=thc, b_=b_: s.activation(
                            out=gelu[:, lc, 512 * thc:512 * (thc + 1)], in_=pall[:, b_, :],
                            func=AF.Gelu),
                            wait=[mm, gelu_tok.get((half, lc, thc)),
                                  av2_tok[(lc, thc)] if e == 0 and half == 0 else None], inc="s")
                        gmul = p.emit("vector", lambda v, lc=lc, thc=thc, e=e: v.tensor_mul(
                            gelu[:, lc, 512 * thc:512 * (thc + 1)],
                            gelu[:, lc, 512 * thc:512 * (thc + 1)],
                            gbv(e)[:, 512 * thc:512 * (thc + 1)]),
                            wait=[ge_, gb_tok[(e, thc)]], inc="v")
                        bank_rd[b_] = gmul
                        gelu_tok[(half, lc, thc)] = gmul
                coB2(h, mm)
            for qx in range(8):
                h, ldw2 = ldB2(lambda d, e=e, half=half, qx=qx: mw2[
                    e, 2048 * half + 128 * d:2048 * half + 128 * (d + 1),
                    256 * qx:256 * (qx + 1)], 16)
                mm = None
                for f2 in range(2):
                    f = 2 * qx + f2
                    for thc in range(TH):
                        b_ = (f2 * TH + thc) % 4
                        mm = p.emit("tensor", lambda tt, f2=f2, thc=thc, b_=b_, h=h: [tt.matmul(
                            pall[:, b_, :], wslab[:, 4096 * h + 256 * d + 128 * f2:
                                                  4096 * h + 256 * d + 128 * (f2 + 1)],
                            gelu[:, d, 512 * thc:512 * (thc + 1)],
                            start=(d == 0), stop=(d == 15)) for d in range(16)][-1],
                            wait=[ldw2, bank_rd[b_]] +
                                 [gelu_tok[(half, d, thc)] for d in range(16)], inc="t")
                        ra = p.emit("vector", lambda v, f=f, thc=thc, b_=b_: v.tensor_add(
                            resA[:, f, 512 * thc:512 * (thc + 1)],
                            resA[:, f, 512 * thc:512 * (thc + 1)], pall[:, b_, :]),
                            wait=[mm, pre_tok[(f, thc)]], inc="v")
                        bank_rd[b_] = ra
                        pre_tok[(f, thc)] = ra
                coB2(h, mm)

    x4_tok = emit_lnB2(pre_tok, None)
    st_last = None
    for j in range(16):
        st_last = p.emit("sync", lambda s, j=j: s.dma_start(
            out=outT[128 * j:128 * (j + 1), :], in_=resA[:, j, :]),
            wait=[x4_tok[(j, 0)], x4_tok[(j, TH - 1)]], inc="dso", inc_amt=16)
    p.emit("gpsimd", lambda g: None, wait=[st_last])
    with nc.Block() as block:
        p.build(nc, block)
    esB2.close()
    esS.close()
    return nc


def run_fused(inputs, t=1024):
    per_core = host_prep_fused(inputs, t=t)
    nc = build_fused(t=t)
    res = run_bass_kernel_spmd(nc, per_core, core_ids=list(range(NCORES)))
    return np.concatenate([r["outT"] for r in res.results], axis=1)
